# version 02
# baseline (speedup 1.0000x reference)
import numpy as np
from contextlib import ExitStack

import concourse.bass as bass
import concourse.mybir as mybir
from concourse.bass_utils import run_bass_kernel_spmd

F32 = mybir.dt.float32
BF16 = mybir.dt.bfloat16
ALU = mybir.AluOpType
AF = mybir.ActivationFunctionType
AX = mybir.AxisListType

NCORES = 8
T = 4096
D = 1024
NDC = D // 128
NTT = T // 128
TC = 512
NTCH = T // TC
NH = 16
HD = 64
NPAIR = 8
I0 = 938
I0P = 1024
E0 = 8
E1 = 31
LN_EPS = 1e-5

ENGS = ("pe", "act", "dve", "pool", "sync")
DEBUG_B2 = False


class Op:
    __slots__ = ("eng", "fn", "deps", "step", "cum")


class Bar:
    __slots__ = ("snap",)


class Prog:
    def __init__(self):
        self.items = {e: [] for e in ENGS}
        self.cum = {e: 0 for e in ENGS}

    def op(self, eng, fn, deps=(), step=1):
        o = Op()
        o.eng = eng
        o.fn = fn
        o.deps = tuple(d for d in deps if d is not None)
        o.step = step
        self.cum[eng] += step
        o.cum = self.cum[eng]
        self.items[eng].append(o)
        return o

    def dma(self, eng, fn, deps=()):
        return self.op(eng, fn, deps, step=16)

    def barrier(self):
        snap = dict(self.cum)
        for e in ENGS:
            b = Bar()
            b.snap = snap
            self.items[e].append(b)

    def emit(self, nc, block, es):
        sems = {e: es.enter_context(nc.semaphore(f"ch_{e}")) for e in ENGS}
        engmap = {"pe": block.tensor, "act": block.scalar, "dve": block.vector,
                  "pool": block.gpsimd, "sync": block.sync}
        self.barrier()
        for ename in ENGS:
            deco = engmap[ename]
            items = self.items[ename]

            def section(e, ename=ename, items=items):
                waited = {x: 0 for x in ENGS}
                for it in items:
                    if isinstance(it, Bar):
                        for oe in ENGS:
                            c = it.snap[oe]
                            if oe != ename and waited[oe] < c:
                                e.wait_ge(sems[oe], c)
                                waited[oe] = c
                        continue
                    needs = {}
                    for d in it.deps:
                        if needs.get(d.eng, 0) < d.cum:
                            needs[d.eng] = d.cum
                    for oe, c in needs.items():
                        if waited[oe] < c:
                            e.wait_ge(sems[oe], c)
                            waited[oe] = c
                    ins = it.fn(e)
                    ins.then_inc(sems[ename], it.step)
                    if it.eng == ename:
                        waited[ename] = max(waited[ename], 0)

            deco(section)


class Ctx:
    def __init__(self, nc, es, prog):
        self.nc = nc
        self.es = es
        self.P = prog
        self.hbm = {}
        self.ph = None
        self._consts = {}

    def sbuf(self, name, shape, dtype, scope=None):
        return (scope or self.ph).enter_context(
            self.nc.sbuf_tensor(name, shape, dtype))

    def psum(self, name, shape, dtype=F32, scope=None):
        return (scope or self.ph).enter_context(
            self.nc.psum_tensor(name, shape, dtype))

    def constant(self, value):
        value = float(value)
        if value in self._consts:
            return self._consts[value]
        t = self.es.enter_context(
            self.nc.sbuf_tensor(f"const_{len(self._consts)}", [128, 1], F32))
        op = self.P.op("dve", lambda e: e.memset(t[:], value))
        self.nc.const_aps.register(F32, value, t[:])
        self._consts[value] = op
        return op


def _perm_qk_cols():
    perm = []
    for h in range(NH):
        base = h * HD
        perm.extend(base + 2 * np.arange(32))
        perm.extend(base + 2 * np.arange(32) + 1)
    return np.array(perm)


def prep_host(inputs):
    g = {k: np.asarray(v) for k, v in inputs.items()}
    out = {}
    perm = _perm_qk_cols()
    out["wq"] = np.ascontiguousarray(g["L0_attn_w_q"][:, perm])
    out["wk"] = np.ascontiguousarray(g["L0_attn_w_kv"][:, :D][:, perm])
    out["wv"] = np.ascontiguousarray(g["L0_attn_w_kv"][:, D:])
    out["wo"] = np.ascontiguousarray(g["L0_attn_w_o"])
    out["lnw"] = np.ascontiguousarray(g["L0_attn_ln_w"].reshape(1, D))
    out["lnb"] = np.ascontiguousarray(g["L0_attn_ln_b"].reshape(1, D))
    out["theta"] = np.asarray(g["L0_attn_theta"], dtype=np.float32).reshape(1, 1)
    out["gw0T"] = np.ascontiguousarray(g["L0_ffn_gate_weight"].T)

    def prep_gu(gu):
        o = np.zeros((2 * I0P, D), dtype=np.float32)
        gp, up = gu[:I0], gu[I0:]
        for ic in range(8):
            lo, hi = ic * 128, min((ic + 1) * 128, I0)
            n = hi - lo
            if n > 0:
                o[ic * 256: ic * 256 + n] = gp[lo:hi]
                o[ic * 256 + 128: ic * 256 + 128 + n] = up[lo:hi]
        return o

    def prep_dn(dn):
        o = np.zeros((I0P, D), dtype=np.float32)
        o[:I0] = dn.T
        return o

    out["gu0"] = np.ascontiguousarray(np.stack(
        [prep_gu(g["L0_ffn_shared_gate_up"]).T] +
        [prep_gu(g["L0_ffn_experts_gate_up"][e]).T for e in range(E0)]))
    out["dn0"] = np.ascontiguousarray(np.stack(
        [prep_dn(g["L0_ffn_shared_down"])] +
        [prep_dn(g["L0_ffn_experts_down"][e]) for e in range(E0)]))
    out["wq1"] = np.ascontiguousarray(g["L1_attn_Wq"].T)
    out["wk1"] = np.ascontiguousarray(g["L1_attn_Wk"].T)
    out["wv1"] = np.ascontiguousarray(g["L1_attn_Wv"].T)
    out["bq1"] = np.ascontiguousarray(g["L1_attn_bq"].reshape(NDC, 128).T)
    out["bk1"] = np.ascontiguousarray(g["L1_attn_bk"].reshape(NDC, 128).T)
    out["bv1"] = np.ascontiguousarray(g["L1_attn_bv"].reshape(1, D))
    out["w1f"] = np.ascontiguousarray(np.stack(
        [g["L1_ffn_ws1"]] + [g["L1_ffn_We1"][e] for e in range(E1)]))
    out["w2f"] = np.ascontiguousarray(np.stack(
        [g["L1_ffn_ws2"]] + [g["L1_ffn_We2"][e] for e in range(E1)]))
    out["b1f"] = np.ascontiguousarray(np.stack(
        [g["L1_ffn_bs1"]] + [g["L1_ffn_Be1"][e] for e in range(E1)]
    ).reshape(32, NDC, 128).transpose(2, 0, 1))
    out["bs2"] = np.ascontiguousarray(g["L1_ffn_bs2"].reshape(NDC, 128).T)
    out["be2"] = np.ascontiguousarray(g["L1_ffn_Be2"])
    out["wr1"] = np.ascontiguousarray(g["L1_ffn_Wr"])
    out["br1"] = np.ascontiguousarray(g["L1_ffn_br"].reshape(1, E1))
    xs = [np.ascontiguousarray(g["L0_attn_x"][b]) for b in range(NCORES)]
    return out, xs


WEIGHT_SPECS = {
    "wq": ([D, D], F32), "wk": ([D, D], F32), "wv": ([D, D], F32),
    "wo": ([D, D], F32), "lnw": ([1, D], F32), "lnb": ([1, D], F32),
    "theta": ([1, 1], F32), "gw0T": ([D, E0], F32),
    "gu0": ([E0 + 1, D, 2 * I0P], F32), "dn0": ([E0 + 1, I0P, D], F32),
    "wq1": ([D, D], F32), "wk1": ([D, D], F32), "wv1": ([D, D], F32),
    "bq1": ([128, NDC], F32), "bk1": ([128, NDC], F32), "bv1": ([1, D], F32),
    "w1f": ([32, D, D], F32), "w2f": ([32, D, D], F32),
    "b1f": ([128, 32, NDC], F32), "bs2": ([128, NDC], F32),
    "be2": ([E1, D], F32), "wr1": ([D, E1], F32), "br1": ([1, E1], F32),
    "x": ([T, D], F32),
}
INTER_SPECS = {
    "xn0": ([T, D], BF16), "h0": ([T, D], BF16),
    "xn0T": ([D, T], BF16), "h0T": ([D, T], BF16),
    "cw0": ([E0, T], F32),
    "s0": ([D, T], BF16), "p0": ([D, T], BF16),
    "x1": ([T, D], F32), "xn1": ([T, D], BF16), "xn1T": ([D, T], BF16),
    "g1": ([E1, T], F32), "q1": ([D, T], BF16),
    "attn1": ([T, T], BF16), "s1": ([D, T], BF16), "p1": ([D, T], BF16),
    "out": ([T, D], F32),
    "dbg_q": ([128, T], BF16), "dbg_k": ([128, T], BF16),
    "dbg_C": ([128, T], F32), "dbg_S": ([128, T], F32),
    "dbg_y": ([128, NDC * 1024], F32), "dbg_h": ([128, 8 * TC], BF16),
    "dbg_v": ([128, 32 * 130], BF16), "dbg_o": ([128, NPAIR * T], BF16),
}


def ph_b0(P, ctx):
    x, xn0, h0 = ctx.hbm["x"], ctx.hbm["xn0"], ctx.hbm["h0"]
    ceps = ctx.constant(LN_EPS)
    cm1 = ctx.constant(-1.0)
    cmh = ctx.constant(-0.5)
    lnw_h, lnb_h = ctx.hbm["lnw"], ctx.hbm["lnb"]
    xbuf = [ctx.sbuf(f"b0_x{j}", [128, D], F32) for j in range(2)]
    sq = [ctx.sbuf(f"b0_sq{j}", [128, D], F32) for j in range(2)]
    xnf = [ctx.sbuf(f"b0_xn{j}", [128, D], F32) for j in range(2)]
    z2f = [ctx.sbuf(f"b0_z2{j}", [128, D], F32) for j in range(2)]
    hbf = [ctx.sbuf(f"b0_h{j}", [128, D], BF16) for j in range(2)]
    xnbf = [ctx.sbuf(f"b0_xnb{j}", [128, D], BF16) for j in range(2)]
    st = [ctx.sbuf(f"b0_st{j}", [128, 8], F32) for j in range(2)]
    lnw_bc = ctx.sbuf("b0_lnwbc", [128, D], F32)
    lnb_bc = ctx.sbuf("b0_lnbbc", [128, D], F32)
    lnrow = ctx.sbuf("b0_lnrow", [1, 2 * D], F32)
    ones1 = ctx.sbuf("b0_ones", [1, 128], F32)
    pb = [ctx.psum(f"b0_pb{i}", [128, 512], F32) for i in range(4)]

    ldr = P.dma("sync", lambda e: e.dma_start(out=lnrow[:, 0:D], in_=lnw_h[:]))
    ldr2 = P.dma("sync", lambda e: e.dma_start(out=lnrow[:, D:2 * D], in_=lnb_h[:]))
    on = P.op("dve", lambda e: e.memset(ones1[:], 1.0))
    mms, cps = [], []
    for k in range(4):
        src = lnrow[:, (k % 2) * 512 + (k // 2) * D: (k % 2) * 512 + (k // 2) * D + 512]
        mms.append(P.op("pe", lambda e, pbk=pb[k], s=src: e.matmul(
            pbk[:], ones1[:], s, start=True, stop=True), deps=[ldr, ldr2, on]))
    for k in range(4):
        dst = (lnw_bc if k < 2 else lnb_bc)[:, (k % 2) * 512:(k % 2) * 512 + 512]
        cps.append(P.op("dve", lambda e, d=dst, pbk=pb[k]: e.tensor_copy(d, pbk[:]),
                        deps=[mms[k]]))
    C = 1.0 / D
    prev = {}
    for i in range(NTT):
        j = i % 2
        xb, sqb, xnb, z2b, hb, xnbb, stb = (xbuf[j], sq[j], xnf[j], z2f[j],
                                            hbf[j], xnbf[j], st[j])
        sx, sx2 = stb[:, 0:1], stb[:, 1:2]
        mu, nv, rstd = stb[:, 2:3], stb[:, 3:4], stb[:, 4:5]
        mu2, nv2, rstd2 = stb[:, 5:6], stb[:, 6:7], stb[:, 7:8]
        pv = prev.get(j, {})
        ld = P.dma("sync", lambda e, i=i, xb=xb: e.dma_start(
            out=xb[:], in_=x[i * 128:(i + 1) * 128, :]),
            deps=[pv.get("xn"), pv.get("sq1")])
        summ = P.op("dve", lambda e, sx=sx, xb=xb: e.reduce_sum(sx, xb[:], axis=AX.X),
                    deps=[ld, pv.get("nv")])
        sq1 = P.op("act", lambda e, sqb=sqb, xb=xb, sx2=sx2: e.activation(
            sqb[:], xb[:], AF.Square, bias=0.0, scale=1.0, accum_out=sx2),
            deps=[ld, pv.get("t1"), pv.get("nv")])
        ex2 = P.op("dve", lambda e, sx2=sx2: e.tensor_scalar(
            sx2, sx2, C, None, ALU.mult), deps=[sq1])
        muo = P.op("dve", lambda e, mu=mu, sx=sx: e.tensor_scalar(
            mu, sx, C, None, ALU.mult), deps=[summ])
        nvo = P.op("dve", lambda e, nv=nv, mu=mu, sx2=sx2: e.scalar_tensor_tensor(
            nv, mu, mu, sx2, ALU.mult, ALU.subtract), deps=[muo, ex2])
        lnv = P.op("act", lambda e, nv=nv: e.activation(
            nv, nv, AF.Ln, bias=LN_EPS, scale=-1.0), deps=[nvo, ceps, cm1])
        rst = P.op("act", lambda e, rstd=rstd, nv=nv: e.activation(
            rstd, nv, AF.Exp, bias=0.0, scale=-0.5), deps=[lnv, cmh])
        xn = P.op("dve", lambda e, xnb=xnb, xb=xb, mu=mu, rstd=rstd: e.tensor_scalar(
            xnb[:], xb[:], mu, rstd, ALU.subtract, ALU.mult),
            deps=[muo, rst, pv.get("z2"), pv.get("xnbf"), pv.get("sum2"), pv.get("sq2")])
        sum2 = P.op("dve", lambda e, mu2=mu2, xnb=xnb: e.reduce_sum(
            mu2, xnb[:], axis=AX.X), deps=[xn])
        sq2 = P.op("act", lambda e, sqb=sqb, xnb=xnb, nv2=nv2: e.activation(
            sqb[:], xnb[:], AF.Square, bias=0.0, scale=1.0, accum_out=nv2),
            deps=[xn, sq1])
        ex2b = P.op("dve", lambda e, nv2=nv2: e.tensor_scalar(
            nv2, nv2, C, None, ALU.mult), deps=[sq2])
        mu2o = P.op("dve", lambda e, mu2=mu2: e.tensor_scalar(
            mu2, mu2, C, None, ALU.mult), deps=[sum2])
        nv2o = P.op("dve", lambda e, nv2=nv2, mu2=mu2: e.scalar_tensor_tensor(
            nv2, mu2, mu2, nv2, ALU.mult, ALU.subtract), deps=[mu2o, ex2b])
        lnv2 = P.op("act", lambda e, nv2=nv2: e.activation(
            nv2, nv2, AF.Ln, bias=LN_EPS, scale=-1.0), deps=[nv2o, ceps, cm1])
        rst2 = P.op("act", lambda e, rstd2=rstd2, nv2=nv2: e.activation(
            rstd2, nv2, AF.Exp, bias=0.0, scale=-0.5), deps=[lnv2, cmh])
        z2 = P.op("dve", lambda e, z2b=z2b, xnb=xnb, mu2=mu2, rstd2=rstd2:
                  e.tensor_scalar(z2b[:], xnb[:], mu2, rstd2, ALU.subtract, ALU.mult),
                  deps=[mu2o, rst2, pv.get("t1")])
        t1 = P.op("dve", lambda e, sqb=sqb, z2b=z2b: e.tensor_mul(
            sqb[:], z2b[:], lnw_bc[:]), deps=[z2, sq2] + cps[:2])
        h = P.op("dve", lambda e, hb=hb, sqb=sqb: e.tensor_add(
            hb[:], sqb[:], lnb_bc[:]), deps=[t1, pv.get("sth")] + cps[2:])
        xnbf_o = P.op("act", lambda e, xnbb=xnbb, xnb=xnb: e.activation(
            xnbb[:], xnb[:], AF.Copy, bias=0.0, scale=1.0),
            deps=[xn, pv.get("stx")])
        sth = P.dma("pool", lambda e, i=i, hb=hb: e.dma_start(
            out=h0[i * 128:(i + 1) * 128, :], in_=hb[:]), deps=[h])
        stx = P.dma("pool", lambda e, i=i, xnbb=xnbb: e.dma_start(
            out=xn0[i * 128:(i + 1) * 128, :], in_=xnbb[:]), deps=[xnbf_o])
        prev[j] = dict(xn=xn, sq1=sq1, z2=z2, t1=t1, sth=sth, stx=stx,
                       xnbf=xnbf_o, nv=nv2o, sum2=sum2, sq2=sq2)


def ph_b1(P, ctx):
    xn0, h0 = ctx.hbm["xn0"], ctx.hbm["h0"]
    xn0T, h0T = ctx.hbm["xn0T"], ctx.hbm["h0T"]
    gw0T_h, cw0 = ctx.hbm["gw0T"], ctx.hbm["cw0"]

    xt = [[ctx.sbuf(f"b1_xt{j}_{dc}", [128, TC], BF16) for dc in range(NDC)]
          for j in range(2)]
    ht = [[ctx.sbuf(f"b1_ht{j}_{dc}", [128, TC], BF16) for dc in range(NDC)]
          for j in range(2)]
    gw = ctx.sbuf("b1_gw", [128, NDC * E0], BF16)
    lg = [ctx.sbuf(f"b1_lg{j}", [128, 4 * E0], F32) for j in range(2)]
    sc2 = [ctx.sbuf(f"b1_sc2{j}", [128, 4 * E0], F32) for j in range(2)]
    tmp = [ctx.sbuf(f"b1_tmp{j}", [128, 4 * E0], F32) for j in range(2)]
    msk = [ctx.sbuf(f"b1_msk{j}", [128, 4 * E0], F32) for j in range(2)]
    cwt = [ctx.sbuf(f"b1_cwt{j}", [128, 4 * E0], F32) for j in range(2)]
    r1 = [ctx.sbuf(f"b1_r1{j}", [128, 4], F32) for j in range(2)]
    r2 = [ctx.sbuf(f"b1_r2{j}", [128, 4], F32) for j in range(2)]
    cwfm = ctx.sbuf("b1_cwfm", [E0, T], F32)
    ident = ctx.sbuf("b1_id", [128, 128], F32)
    ia = ctx.sbuf("b1_ia", [128, 128], F32)
    plg = [ctx.psum(f"b1_plg{j}", [128, E0], F32) for j in range(2)]
    ptr = [ctx.psum(f"b1_ptr{j}", [E0, 128], F32) for j in range(2)]

    i1 = P.op("pool", lambda e: e.iota(ia[:], pattern=[[0, 128]], base=0,
              channel_multiplier=1, allow_small_or_imprecise_dtypes=True))
    i2 = P.op("pool", lambda e: e.iota(ident[:], pattern=[[1, 128]], base=0,
              channel_multiplier=0, allow_small_or_imprecise_dtypes=True))
    idt = P.op("dve", lambda e: e.tensor_tensor(ident[:], ia[:], ident[:],
               ALU.is_equal), deps=[i1, i2])
    ldw = [P.dma("pool", lambda e, dc=dc: e.dma_start(
        out=gw[:, dc * E0:(dc + 1) * E0],
        in_=gw0T_h[dc * 128:(dc + 1) * 128, :])) for dc in range(NDC)]

    prev = {}
    evcw_by_k = {}
    ev_by_k = {}
    for i in range(NTCH):
        j = i % 2
        pv = prev.get(j, {})
        lds, ldsh = [], []
        for dc in range(NDC):
            lds.append(P.dma("sync", lambda e, i=i, dc=dc, t=xt[j][dc]: e.dma_start(
                out=t[:], in_=xn0[i * TC:(i + 1) * TC, dc * 128:(dc + 1) * 128],
                transpose=True), deps=pv.get("xt_cons", [])))
            ldsh.append(P.dma("sync", lambda e, i=i, dc=dc, t=ht[j][dc]: e.dma_start(
                out=t[:], in_=h0[i * TC:(i + 1) * TC, dc * 128:(dc + 1) * 128],
                transpose=True), deps=pv.get("ht_cons", [])))
        stws = []
        for dc in range(NDC):
            stws.append(P.dma("pool", lambda e, i=i, dc=dc, t=xt[j][dc]: e.dma_start(
                out=xn0T[dc * 128:(dc + 1) * 128, i * TC:(i + 1) * TC], in_=t[:]),
                deps=[lds[dc]]))
            stws.append(P.dma("pool", lambda e, i=i, dc=dc, t=ht[j][dc]: e.dma_start(
                out=h0T[dc * 128:(dc + 1) * 128, i * TC:(i + 1) * TC], in_=t[:]),
                deps=[ldsh[dc]]))
        mmops, evops = [], []
        for tt in range(4):
            k = i * 4 + tt
            jj = k % 2

            def mmfn(e, jj=jj, j=j, tt=tt):
                last = None
                for dc in range(NDC):
                    last = e.matmul(plg[jj][:],
                                    xt[j][dc][:, tt * 128:(tt + 1) * 128],
                                    gw[:, dc * E0:(dc + 1) * E0],
                                    start=(dc == 0), stop=(dc == NDC - 1))
                return last
            mm = P.op("pe", mmfn,
                      deps=lds + ldw + [ev_by_k.get(k - 2), pv.get("ev%d" % tt)])
            ev = P.op("dve", lambda e, j=j, tt=tt, jj=jj: e.tensor_copy(
                lg[j][:, tt * E0:(tt + 1) * E0], plg[jj][:]),
                deps=[mm, pv.get("chain_last")])
            ev_by_k[k] = ev
            mmops.append(mm)
            evops.append(ev)
        lgv = lg[j][:].rearrange("p (s e) -> p s e", e=E0)
        sc2v = sc2[j][:].rearrange("p (s e) -> p s e", e=E0)
        tmpv = tmp[j][:].rearrange("p (s e) -> p s e", e=E0)
        mskv = msk[j][:].rearrange("p (s e) -> p s e", e=E0)
        cwtv = cwt[j][:].rearrange("p (s e) -> p s e", e=E0)
        r1b = r1[j][:].rearrange("p (s o) -> p s o", o=1).to_broadcast([128, 4, E0])
        r2b = r2[j][:].rearrange("p (s o) -> p s o", o=1).to_broadcast([128, 4, E0])
        c1 = P.op("dve", lambda e, lgv=lgv, r=r1[j]: e.tensor_reduce(
            r[:].rearrange("p (s o) -> p s o", o=1), lgv, axis=AX.X, op=ALU.max, negate=True),
            deps=evops + [pv.get("c_last")])
        c2 = P.op("dve", lambda e, sc2v=sc2v, lgv=lgv, r1b=r1b: e.tensor_tensor(
            sc2v, lgv, r1b, ALU.add), deps=[c1] + [pv.get("c2w")])
        c3 = P.op("act", lambda e, s=sc2[j]: e.activation(
            s[:], s[:], AF.Exp, bias=0.0, scale=1.0), deps=[c2])
        c4 = P.op("dve", lambda e, sc2v=sc2v, r=r1[j]: e.tensor_reduce(
            r[:].rearrange("p (s o) -> p s o", o=1), sc2v, axis=AX.X, op=ALU.add), deps=[c3])
        c5 = P.op("dve", lambda e, r=r1[j]: e.reciprocal(r[:], r[:]), deps=[c4])
        c6 = P.op("dve", lambda e, sc2v=sc2v, r1b=r1b: e.tensor_tensor(
            sc2v, sc2v, r1b, ALU.mult), deps=[c5])
        c7 = P.op("dve", lambda e, sc2v=sc2v, r=r1[j]: e.tensor_reduce(
            r[:].rearrange("p (s o) -> p s o", o=1), sc2v, axis=AX.X, op=ALU.max), deps=[c6])
        c8 = P.op("dve", lambda e, mskv=mskv, sc2v=sc2v, r1b=r1b: e.tensor_tensor(
            mskv, sc2v, r1b, ALU.is_equal), deps=[c7])
        c9 = P.op("dve", lambda e, tmpv=tmpv, sc2v=sc2v, mskv=mskv: e.tensor_tensor(
            tmpv, sc2v, mskv, ALU.mult), deps=[c8])
        c10 = P.op("dve", lambda e, tmpv=tmpv, sc2v=sc2v: e.tensor_tensor(
            tmpv, sc2v, tmpv, ALU.subtract), deps=[c9])
        c11 = P.op("dve", lambda e, tmpv=tmpv, r=r2[j]: e.tensor_reduce(
            r[:].rearrange("p (s o) -> p s o", o=1), tmpv, axis=AX.X, op=ALU.max), deps=[c10])
        c12 = P.op("dve", lambda e, cwtv=cwtv, tmpv=tmpv, r2b=r2b: e.tensor_tensor(
            cwtv, tmpv, r2b, ALU.is_equal), deps=[c11])
        c13 = P.op("dve", lambda e, mskv=mskv, cwtv=cwtv: e.tensor_tensor(
            mskv, mskv, cwtv, ALU.add), deps=[c12])
        c14 = P.op("dve", lambda e, r1=r1[j], r2=r2[j]: e.tensor_tensor(
            r1[:], r1[:], r2[:], ALU.add), deps=[c11, c8])
        c15 = P.op("dve", lambda e, r1=r1[j]: e.tensor_scalar(
            r1[:], r1[:], 1e-8, None, ALU.add), deps=[c14])
        c16 = P.op("dve", lambda e, r1=r1[j]: e.reciprocal(r1[:], r1[:]), deps=[c15])
        c17 = P.op("dve", lambda e, cwtv=cwtv, sc2v=sc2v, mskv=mskv: e.tensor_tensor(
            cwtv, sc2v, mskv, ALU.mult), deps=[c13, c12])
        c18 = P.op("dve", lambda e, cwtv=cwtv, r1b=r1b: e.tensor_tensor(
            cwtv, cwtv, r1b, ALU.mult), deps=[c16, c17])
        tpev = []
        for tt in range(4):
            k = i * 4 + tt
            jj = k % 2
            tpo = P.op("pe", lambda e, j=j, tt=tt, jj=jj: e.transpose(
                ptr[jj][:], cwt[j][:, tt * E0:(tt + 1) * E0], ident[:]),
                deps=[c18, idt, evcw_by_k.get(k - 2)])
            evc = P.op("dve", lambda e, i=i, tt=tt, jj=jj: e.tensor_copy(
                cwfm[:, i * TC + tt * 128: i * TC + (tt + 1) * 128], ptr[jj][:]),
                deps=[tpo])
            evcw_by_k[k] = evc
            tpev.append(evc)
        prev[j] = {
            "xt_cons": mmops + [stws[0]] + stws[::2],
            "ht_cons": stws[1::2],
            "ev0": evops[0], "ev1": evops[1], "ev2": evops[2], "ev3": evops[3],
            "chain_last": c18, "c_last": c18, "c2w": c18,
        }
    stcw = P.dma("pool", lambda e: e.dma_start(out=cw0[:], in_=cwfm[:]),
                 deps=list(evcw_by_k.values()))


def ph_b2(P, ctx):
    h0T = ctx.hbm["h0T"]
    wq_h, wk_h, wv_h, wo_h = (ctx.hbm["wq"], ctx.hbm["wk"], ctx.hbm["wv"],
                              ctx.hbm["wo"])
    theta_h, s0 = ctx.hbm["theta"], ctx.hbm["s0"]

    TOP = 2595.0 * np.log10(np.float32(1.0 + 4000.0 / 200.0))
    CS = float(0.125 / np.sqrt(2.0))
    BS = float(1.0 / np.sqrt(2.0))
    INV2PI = float(np.float32(1.0 / (2 * np.pi)))
    MAGIC = 12582912.0
    CW1 = 6.25
    CW2 = 0.032958984375
    CW3 = 0.000226322804586232
    MEL_C = float(np.log(10.0) / 2595.0 * (TOP / 31.0))
    cpre = [ctx.constant(v) for v in (MEL_C, CS, BS)]

    fcol = ctx.sbuf("b2_fcol", [32, 1], F32)
    thc = ctx.sbuf("b2_thc", [32, 1], F32)
    throw = ctx.sbuf("b2_throw", [1, 1], F32)
    ones32 = ctx.sbuf("b2_ones32", [1, 32], F32)
    ones64 = ctx.sbuf("b2_ones64", [1, 64], F32)
    onescol = ctx.sbuf("b2_onescol", [128, 1], BF16)
    C128 = ctx.sbuf("b2_C128", [128, T], F32)
    S128 = ctx.sbuf("b2_S128", [128, T], F32)
    o_all = ctx.sbuf("b2_oall", [128, NPAIR * T], BF16)
    v_all = ctx.sbuf("b2_vall", [128, 32 * 130], BF16)
    vv = v_all[:].rearrange("p (k c) -> p k c", c=130)
    qbf = ctx.sbuf("b2_qbf", [128, T], BF16)
    kbf = ctx.sbuf("b2_kbf", [128, T], BF16)
    qt32 = ctx.sbuf("b2_qt", [128, TC], F32)
    qsw = ctx.sbuf("b2_qsw", [128, TC], F32)
    qcs = ctx.sbuf("b2_qcs", [128, TC], F32)
    ttb = [ctx.sbuf(f"b2_tt{j}", [128, TC], BF16) for j in range(2)]
    svb = [ctx.sbuf(f"b2_sv{h}", [128, 1], F32) for h in range(2)]
    drow = ctx.sbuf("b2_drow", [1, TC], F32)
    bcsb = ctx.sbuf("b2_bc", [64, TC], F32)
    _tabscope = ExitStack()
    ang = ctx.sbuf("b2_ang", [32, T], F32, scope=_tabscope)
    kk = ctx.sbuf("b2_kk", [32, T], F32, scope=_tabscope)
    red = ctx.sbuf("b2_red", [32, T], F32, scope=_tabscope)
    sin32 = ang
    cos32 = kk
    pth = ctx.psum("b2_pth", [32, 1], F32, scope=_tabscope)

    _f32 = np.float32
    _top = _f32(2595.0) * np.log10(_f32(21.0))
    _lin = np.linspace(_f32(0.0), _top, 32, dtype=_f32)
    _mel = np.power(_f32(10.0), _lin / _f32(2595.0), dtype=_f32) - _f32(1.0)
    FB = (_f32(200.0) * _mel / _f32(1000.0)).astype(_f32)
    fbrow = ctx.sbuf("b2_fbrow", [1, 32], F32, scope=_tabscope)
    one11 = ctx.sbuf("b2_one11", [1, 1], F32, scope=_tabscope)
    ldth = P.dma("sync", lambda e: e.dma_start(out=throw[:], in_=theta_h[:]))
    o32 = P.op("dve", lambda e: e.memset(ones32[:], 1.0))
    o64 = P.op("dve", lambda e: e.memset(ones64[:], 1.0))
    ocol = P.op("dve", lambda e: e.memset(onescol[:], 1.0))
    o11 = P.op("dve", lambda e: e.memset(one11[:], 1.0))
    fbsets = [P.op("dve", lambda e, i=i: e.memset(fbrow[:, i:i + 1],
                   float(FB[i]))) for i in range(32)]
    t220 = P.op("dve", lambda e: e.tensor_scalar(
        throw[:], throw[:], float(np.float32(1.0) / np.float32(220.0)),
        None, ALU.mult), deps=[ldth])
    mth = P.op("pe", lambda e: e.matmul(pth[:], ones32[:], throw[:],
               start=True, stop=True), deps=[t220, o32])
    evth = P.op("dve", lambda e: e.tensor_copy(thc[:], pth[:]), deps=[mth])
    mfb = P.op("pe", lambda e: e.transpose(pth[:], fbrow[:], one11[:]),
               deps=fbsets + [o11, evth])
    evfb = P.op("dve", lambda e: e.tensor_copy(fcol[:], pth[:]), deps=[mfb])
    frq = P.op("dve", lambda e: e.tensor_mul(fcol[:], fcol[:], thc[:]),
               deps=[evfb, evth])
    iot = P.op("pool", lambda e: e.iota(ang[:], pattern=[[1, T]], base=0,
               channel_multiplier=0, allow_small_or_imprecise_dtypes=True))
    a1 = P.op("dve", lambda e: e.tensor_scalar(ang[:], ang[:], fcol[:, 0:1],
              None, ALU.mult), deps=[iot, frq])
    k1 = P.op("dve", lambda e: e.tensor_scalar(kk[:], ang[:], INV2PI, MAGIC,
              ALU.mult, ALU.add), deps=[a1])
    k2 = P.op("dve", lambda e: e.tensor_scalar(kk[:], kk[:], MAGIC, None,
              ALU.subtract), deps=[k1])
    rr1 = P.op("dve", lambda e: e.scalar_tensor_tensor(red[:], kk[:], -CW1,
               ang[:], ALU.mult, ALU.add), deps=[k2])
    rr2 = P.op("dve", lambda e: e.scalar_tensor_tensor(red[:], kk[:], -CW2,
               red[:], ALU.mult, ALU.add), deps=[rr1])
    rr3 = P.op("dve", lambda e: e.scalar_tensor_tensor(red[:], kk[:], -CW3,
               red[:], ALU.mult, ALU.add), deps=[rr2])
    sn = P.op("act", lambda e: e.activation(sin32[:], red[:], AF.Sin,
              bias=0.0, scale=1.0), deps=[rr3])
    w1 = P.op("dve", lambda e: e.tensor_scalar(red[:], red[:],
              float(np.pi / 2), None, ALU.add), deps=[sn])
    w2 = P.op("dve", lambda e: e.tensor_scalar(kk[:], red[:], float(np.pi),
              None, ALU.is_gt), deps=[w1])
    w3 = P.op("dve", lambda e: e.scalar_tensor_tensor(red[:], kk[:],
              float(-2 * np.pi), red[:], ALU.mult, ALU.add), deps=[w2])
    cs_ = P.op("act", lambda e: e.activation(cos32[:], red[:], AF.Sin,
               bias=0.0, scale=1.0), deps=[w3])
    tabops = []
    for blk in range(4):
        sl = slice(blk * 32, (blk + 1) * 32)
        eng = "dve" if blk % 2 == 0 else "pool"
        tabops.append(P.op(eng, lambda e, sl=sl: e.tensor_copy(
            C128[sl, :], cos32[:]), deps=[cs_]))
        if blk % 2 == 0:
            tabops.append(P.op(eng, lambda e, sl=sl: e.tensor_scalar(
                S128[sl, :], sin32[:], -1.0, None, ALU.mult), deps=[sn, cs_]))
        else:
            tabops.append(P.op(eng, lambda e, sl=sl: e.tensor_copy(
                S128[sl, :], sin32[:]), deps=[sn, cs_]))


    mo1 = P.op("dve", lambda e: e.memset(vv[:, :, 64:65], 1.0))
    mo2 = P.op("dve", lambda e: e.memset(vv[:, :, 129:130], 1.0))
    _tabscope.close()
    _qkvscope = ExitStack()
    wqp = [ctx.sbuf(f"b2_wqp{j}", [128, D], BF16, scope=_qkvscope) for j in range(2)]
    wkp = [ctx.sbuf(f"b2_wkp{j}", [128, D], BF16, scope=_qkvscope) for j in range(2)]
    wvp = [ctx.sbuf(f"b2_wvp{j}", [128, D], BF16, scope=_qkvscope) for j in range(2)]
    hbuf = [ctx.sbuf(f"b2_h{j}", [128, NDC * TC], BF16, scope=_qkvscope)
            for j in range(2)]
    pa = ctx.psum("b2_pa", [128, TC], F32, scope=_qkvscope)
    pb = ctx.psum("b2_pb", [128, TC], F32, scope=_qkvscope)
    pv = ctx.psum("b2_pv", [128, 128], F32, scope=_qkvscope)
    po = [ctx.psum(f"b2_po{j}", [65, TC], F32, scope=_qkvscope) for j in range(2)]
    pbc = ctx.psum("b2_pbc", [64, TC], F32, scope=_qkvscope)
    psv = ctx.psum("b2_psv", [65, 1], F32, scope=_qkvscope)

    tabdep = tabops + [cs_, sn]

    def wsrc(w, p):
        return w[:, p * 128:(p + 1) * 128].rearrange("(c p) f -> p c f", p=128)

    w_cons = {("q", 0): [], ("q", 1): [], ("k", 0): [], ("k", 1): [],
              ("v", 0): [], ("v", 1): []}
    h_cons = {0: [], 1: []}
    ep_by_kt = {}
    av_by_kt = {}
    last_qev = None
    last_kev = None
    last_vev = None
    last_sve = None
    last_smm = None
    prev_d4 = None
    prev_d5 = {}
    qkv_done_pair = []

    for p in range(NPAIR):
        jw = p % 2
        extra = tabdep if p < 2 else []
        lwq = P.dma("pool", lambda e, p=p, t=wqp[jw]: e.dma_start(
            out=t[:].rearrange("p (c f) -> p c f", f=128), in_=wsrc(wq_h, p)),
            deps=w_cons[("q", jw)] + extra)
        lwk = P.dma("pool", lambda e, p=p, t=wkp[jw]: e.dma_start(
            out=t[:].rearrange("p (c f) -> p c f", f=128), in_=wsrc(wk_h, p)),
            deps=w_cons[("k", jw)] + extra)
        lwv = P.dma("pool", lambda e, p=p, t=wvp[jw]: e.dma_start(
            out=t[:].rearrange("p (c f) -> p c f", f=128), in_=wsrc(wv_h, p)),
            deps=w_cons[("v", jw)] + extra)
        qmms, kmms, vmms, vevs, ropes = [], [], [], [], []
        for tc in range(NTCH):
            j = tc % 2
            hb = hbuf[j]
            ldh = P.dma("sync", lambda e, tc=tc, hb=hb: e.dma_start(
                out=hb[:].rearrange("p (c t) -> p c t", t=TC),
                in_=h0T[:, tc * TC:(tc + 1) * TC].rearrange(
                    "(c p) t -> p c t", p=128)),
                deps=h_cons[j] + (tabdep if p == 0 and tc < 2 else []))
            h_cons[j] = []

            def qfn(e, w=wqp[jw], hb=hb):
                last = None
                for dc in range(NDC):
                    last = e.matmul(pa[:], w[:, dc * 128:(dc + 1) * 128],
                                    hb[:, dc * TC:(dc + 1) * TC],
                                    start=(dc == 0), stop=(dc == NDC - 1))
                return last
            qmm = P.op("pe", qfn, deps=[ldh, lwq, last_qev, ep_by_kt.get(30)])
            qev = P.op("dve", lambda e: e.tensor_copy(qt32[:], pa[:]),
                       deps=[qmm, ropes[-1] if ropes else None])
            sw_ops = []
            for blk in range(4):
                d_, s_ = slice(blk * 32, (blk + 1) * 32),                     slice((blk ^ 1) * 32, ((blk ^ 1) + 1) * 32)
                sw_ops.append(P.op("pool" if blk % 2 else "dve",
                              lambda e, d=d_, s=s_: e.tensor_copy(
                                  qsw[d, :], qt32[s, :]), deps=[qev]))
            qc1 = P.op("pool", lambda e, tc=tc: e.tensor_mul(
                qcs[:], qt32[:], C128[:, tc * TC:(tc + 1) * TC]),
                deps=[qev] + tabops)
            qc2 = P.op("dve", lambda e, tc=tc: e.tensor_mul(
                qsw[:], qsw[:], S128[:, tc * TC:(tc + 1) * TC]),
                deps=sw_ops + tabops)
            qc3 = P.op("dve", lambda e, tc=tc: e.tensor_add(
                qbf[:, tc * TC:(tc + 1) * TC], qcs[:], qsw[:]),
                deps=[qc1, qc2, last_smm])
            last_qev = qev

            def kfn(e, w=wkp[jw], hb=hb):
                last = None
                for dc in range(NDC):
                    last = e.matmul(pb[:], w[:, dc * 128:(dc + 1) * 128],
                                    hb[:, dc * TC:(dc + 1) * TC],
                                    start=(dc == 0), stop=(dc == NDC - 1))
                return last
            kmm = P.op("pe", kfn, deps=[ldh, lwk, last_kev, ep_by_kt.get(31)])
            kev = P.op("dve", lambda e: e.tensor_copy(qt32[:], pb[:]),
                       deps=[kmm, qc1, qc3])
            ksw_ops = []
            for blk in range(4):
                d_, s_ = slice(blk * 32, (blk + 1) * 32),                     slice((blk ^ 1) * 32, ((blk ^ 1) + 1) * 32)
                ksw_ops.append(P.op("pool" if blk % 2 else "dve",
                               lambda e, d=d_, s=s_: e.tensor_copy(
                                   qsw[d, :], qt32[s, :]), deps=[kev, qc2]))
            kc1 = P.op("pool", lambda e, tc=tc: e.tensor_mul(
                qcs[:], qt32[:], C128[:, tc * TC:(tc + 1) * TC]),
                deps=[kev, qc3])
            kc2 = P.op("dve", lambda e, tc=tc: e.tensor_mul(
                qsw[:], qsw[:], S128[:, tc * TC:(tc + 1) * TC]), deps=ksw_ops)
            kc3 = P.op("dve", lambda e, tc=tc: e.tensor_add(
                kbf[:, tc * TC:(tc + 1) * TC], qcs[:], qsw[:]),
                deps=[kc1, kc2, last_smm])
            last_kev = kev
            ropes.extend([qc3, kc3])
            qmms.append(qmm)
            kmms.append(kmm)

            for tt in range(4):
                kt = tc * 4 + tt
                def vfn(e, w=wvp[jw], hb=hb, tt=tt):
                    last = None
                    for dc in range(NDC):
                        last = e.matmul(
                            pv[:], hb[:, dc * TC + tt * 128:
                                      dc * TC + (tt + 1) * 128],
                            w[:, dc * 128:(dc + 1) * 128],
                            start=(dc == 0), stop=(dc == NDC - 1))
                    return last
                vmm = P.op("pe", vfn, deps=[ldh, lwv, last_vev])
                vev = P.op("dve", lambda e, kt=kt: e.tensor_copy(
                    vv[:, kt, 0:130].rearrange("p (b c) -> p b c", b=2, c=65)
                    [:, :, 0:64],
                    pv[:].rearrange("p (b c) -> p b c", b=2, c=64)),
                    deps=[vmm] + av_by_kt.get(kt, []))
                av_by_kt[kt] = []
                vmms.append(vmm)
                vevs.append(vev)
                last_vev = vev
            h_cons[j] = [qmm, kmm] + vmms[-4:]
        if "dbg_q" in ctx.hbm and p == 0:
            P.dma("sync", lambda e: e.dma_start(out=ctx.hbm["dbg_C"][:],
                  in_=C128[:]), deps=tabdep)
            P.dma("sync", lambda e: e.dma_start(out=ctx.hbm["dbg_S"][:],
                  in_=S128[:]), deps=tabdep)
            P.dma("sync", lambda e: e.dma_start(out=ctx.hbm["dbg_q"][:],
                  in_=qbf[:]), deps=ropes)
            P.dma("sync", lambda e: e.dma_start(out=ctx.hbm["dbg_k"][:],
                  in_=kbf[:]), deps=ropes)
            P.dma("sync", lambda e: e.dma_start(out=ctx.hbm["dbg_v"][:],
                  in_=v_all[:]), deps=vevs)
        w_cons[("q", jw)] = [qmms[-1]]
        w_cons[("k", jw)] = [kmms[-1]]
        w_cons[("v", jw)] = [vmms[-1]]
        qkv_done_pair = ropes + vevs

        svops = []
        for h in range(2):
            def svfn(e, h=h):
                last = None
                for kt in range(32):
                    last = e.matmul(psv[:], vv[:, kt, h * 65:(h + 1) * 65],
                                    onescol[:], start=(kt == 0), stop=(kt == 31))
                return last
            svm = P.op("pe", svfn,
                       deps=vevs + [ocol, mo1, mo2, last_sve])
            sve = P.op("dve", lambda e, h=h: e.tensor_scalar(
                svb[h][0:65, 0:1], psv[:], 0.5, None, ALU.mult),
                deps=[svm, prev_d5.get(0), prev_d5.get(1)])
            svops.append(sve)
            last_sve = sve

        for hq in range(2 * NTCH):
            h, qc = hq // NTCH, hq % NTCH
            jo = hq % 2
            av_last = None
            for kt in range(32):
                ps = pa if kt % 2 == 0 else pb
                prev_ep = ep_by_kt.get(kt - 2 if kt >= 2 else kt + 30)
                smm = P.op("pe", lambda e, h=h, qc=qc, kt=kt, ps=ps: e.matmul(
                    ps[:], kbf[h * 64:(h + 1) * 64, kt * 128:(kt + 1) * 128],
                    qbf[h * 64:(h + 1) * 64, qc * TC:(qc + 1) * TC],
                    start=True, stop=True),
                    deps=[ropes[-1], ropes[-2], prev_ep,
                          last_qev if hq == 0 and kt == 0 else None,
                          last_kev if hq == 0 and kt == 0 else None])
                ttw = av_by_kt.get(kt - 2 if kt >= 2 else kt + 30, [])
                ep = P.op("act", lambda e, kt=kt, ps=ps: e.activation(
                    ttb[kt % 2][:], ps[:], AF.Square, bias=BS, scale=CS),
                    deps=[smm] + ttw + cpre)
                ep_by_kt[kt] = ep
                avm = P.op("pe", lambda e, h=h, kt=kt, jo=jo: e.matmul(
                    po[jo][:], vv[:, kt, h * 65:(h + 1) * 65], ttb[kt % 2][:],
                    start=(kt == 0), stop=(kt == 31)),
                    deps=[ep, vevs[-1]] +
                    ([prev_d5.get(jo)] if kt == 0 else []))
                av_by_kt[kt] = [avm]
                av_last = avm
                last_smm = smm
            d1 = P.op("dve", lambda e, jo=jo, h=h: e.tensor_scalar(
                drow[:], po[jo][64:65, :], svb[h][64:65, 0:1], None, ALU.add),
                deps=[av_last, svops[h], prev_d4])
            d2 = P.op("dve", lambda e: e.reciprocal(drow[:], drow[:]),
                      deps=[d1])
            d3 = P.op("pe", lambda e: e.matmul(
                pbc[:], ones64[:], drow[:], start=True, stop=True),
                deps=[d2, o64, prev_d4])
            d4 = P.op("dve", lambda e: e.tensor_copy(bcsb[:], pbc[:]),
                      deps=[d3, prev_d5.get(1 - jo)])
            d5 = P.op("dve", lambda e, jo=jo, h=h, p=p, qc=qc:
                      e.scalar_tensor_tensor(
                          o_all[h * 64:(h + 1) * 64,
                                p * T + qc * TC: p * T + (qc + 1) * TC],
                          po[jo][0:64, :], svb[h][0:64, 0:1], bcsb[:],
                          ALU.add, ALU.mult), deps=[d4, av_last, svops[h]])
            prev_d4 = d4
            prev_d5[jo] = d5

    _qkvscope.close()
    wo_sb = ctx.sbuf("b2_wo", [128, NPAIR * D], BF16)
    evb = [ctx.sbuf(f"b2_ev{j}", [128, TC], BF16) for j in range(2)]
    pw = [ctx.psum(f"b2_pw{j}", [128, TC], F32) for j in range(2)]
    oall_done = [prev_d5.get(0), prev_d5.get(1)]
    if "dbg_o" in ctx.hbm:
        P.dma("sync", lambda e: e.dma_start(out=ctx.hbm["dbg_o"][:],
              in_=o_all[:]), deps=[d for d in oall_done if d is not None])
    lws = [P.dma("pool", lambda e, p=p: e.dma_start(
        out=wo_sb[:, p * D:(p + 1) * D],
        in_=wo_h[p * 128:(p + 1) * 128, :]), deps=oall_done)
        for p in range(NPAIR)]
    prev = {}
    k = 0
    for dc in range(NDC):
        for qc in range(NTCH):
            j = k % 2
            def mfn(e, dc=dc, qc=qc, j=j):
                last = None
                for p in range(NPAIR):
                    last = e.matmul(
                        pw[j][:],
                        wo_sb[:, p * D + dc * 128: p * D + (dc + 1) * 128],
                        o_all[:, p * T + qc * TC: p * T + (qc + 1) * TC],
                        start=(p == 0), stop=(p == NPAIR - 1))
                return last
            mm = P.op("pe", mfn, deps=lws + oall_done + [prev.get(j)])
            evo = P.op("dve", lambda e, j=j: e.tensor_copy(evb[j][:], pw[j][:]),
                       deps=[mm, prev.get(("st", j))])
            sto = P.dma("sync", lambda e, dc=dc, qc=qc, j=j: e.dma_start(
                out=s0[dc * 128:(dc + 1) * 128, qc * TC:(qc + 1) * TC],
                in_=evb[j][:]), deps=[evo])
            prev[j] = evo
            prev[("st", j)] = sto
            k += 1


def ph_b4(P, ctx):
    xn0T, cw0, s0, p0 = (ctx.hbm["xn0T"], ctx.hbm["cw0"], ctx.hbm["s0"],
                         ctx.hbm["p0"])
    gu0_h, dn0_h = ctx.hbm["gu0"], ctx.hbm["dn0"]
    NE = E0 + 1
    QW = 1024
    NQ = T // QW

    y0 = ctx.sbuf("b4_y0", [128, NDC * QW], F32)
    xnq = ctx.sbuf("b4_xnq", [128, NDC * QW], BF16)
    gu = [ctx.sbuf(f"b4_gu{j}", [128, NDC * 2 * I0P], BF16) for j in range(2)]
    dn = [ctx.sbuf(f"b4_dn{j}", [128, 8 * D], BF16) for j in range(2)]
    sg = [ctx.sbuf(f"b4_sg{j}", [128, TC], F32) for j in range(2)]
    hsb = ctx.sbuf("b4_h", [128, 8 * TC], BF16)
    cwbc = ctx.sbuf("b4_cwbc", [128, TC], BF16)
    cwrow = ctx.sbuf("b4_cwrow", [1, TC], F32)
    ones1 = ctx.sbuf("b4_ones1", [1, 128], F32)
    gsb = [ctx.sbuf(f"b4_g{j}", [128, TC], BF16) for j in range(2)]
    ssb = [ctx.sbuf(f"b4_s{j}", [128, TC], BF16) for j in range(2)]
    psb = [ctx.sbuf(f"b4_p{j}", [128, TC], BF16) for j in range(2)]
    pg = [ctx.psum(f"b4_pg{j}", [128, TC], F32) for j in range(2)]
    pu = [ctx.psum(f"b4_pu{j}", [128, TC], F32) for j in range(2)]
    py = [ctx.psum(f"b4_py{j}", [128, TC], F32) for j in range(2)]
    pbc = ctx.psum("b4_pbc", [128, TC], F32)

    o1 = P.op("dve", lambda e: e.memset(ones1[:], 1.0))
    gu_cons = {0: [], 1: []}
    dn_cons = {0: [], 1: []}
    xn_cons = []
    y_cons = []
    h_cons = {}
    sg_cons = {}
    cw_cons = []
    bc_cons = []
    pg_cons = {}
    pu_cons = {}
    py_cons = {}
    bc_readers = []
    ya_prev = {}
    prev_comb = {}

    for q in range(NQ):
        ldx = P.dma("sync", lambda e, q=q: e.dma_start(
            out=xnq[:].rearrange("p (c t) -> p c t", t=QW),
            in_=xn0T[:, q * QW:(q + 1) * QW].rearrange(
                "(c p) t -> p c t", p=128)), deps=xn_cons)
        xn_cons = []
        ms = [P.op("dve", lambda e, dc=dc: e.memset(
            y0[:, dc * QW:(dc + 1) * QW], 0.0), deps=y_cons) for dc in range(NDC)]
        y_cons = []
        for en in range(NE):
            jw = en % 2
            lg = P.dma("pool", lambda e, en=en, t=gu[jw]: e.dma_start(
                out=t[:].rearrange("p (c r) -> p c r", r=2 * I0P),
                in_=gu0_h[en].rearrange("(c p) r -> p c r", p=128)),
                deps=gu_cons[jw])
            ldn = P.dma("pool", lambda e, en=en, t=dn[jw]: e.dma_start(
                out=t[:].rearrange("p (c d) -> p c d", d=D),
                in_=dn0_h[en].rearrange("(c p) d -> p c d", p=128)),
                deps=dn_cons[jw])
            gu_cons[jw], dn_cons[jw] = [], []
            for tcc in range(QW // TC):
                col0 = q * QW + tcc * TC
                if en > 0:
                    lcw = P.dma("sync", lambda e, en=en, col0=col0: e.dma_start(
                        out=cwrow[:], in_=cw0[en - 1:en, col0:col0 + TC]),
                        deps=cw_cons)
                    mbc = P.op("pe", lambda e: e.matmul(
                        pbc[:], ones1[:], cwrow[:], start=True, stop=True),
                        deps=[lcw, o1] + bc_cons)
                    ebc = P.op("dve", lambda e: e.tensor_copy(cwbc[:], pbc[:]),
                               deps=[mbc] + bc_readers)
                    cw_cons = [mbc]
                    bc_cons = [ebc]
                    bc_readers = []
                hops = []
                for ic in range(8):
                    jj = ic % 2
                    def gfn(e, w=gu[jw], ic=ic, tcc=tcc, jj=jj):
                        last = None
                        for dc in range(NDC):
                            last = e.matmul(
                                pg[jj][:],
                                w[:, dc * 2 * I0P + (2 * ic) * 128:
                                  dc * 2 * I0P + (2 * ic + 1) * 128],
                                xnq[:, dc * QW + tcc * TC:
                                    dc * QW + tcc * TC + TC],
                                start=(dc == 0), stop=(dc == NDC - 1))
                        return last
                    def ufn(e, w=gu[jw], ic=ic, tcc=tcc, jj=jj):
                        last = None
                        for dc in range(NDC):
                            last = e.matmul(
                                pu[jj][:],
                                w[:, dc * 2 * I0P + (2 * ic + 1) * 128:
                                  dc * 2 * I0P + (2 * ic + 2) * 128],
                                xnq[:, dc * QW + tcc * TC:
                                    dc * QW + tcc * TC + TC],
                                start=(dc == 0), stop=(dc == NDC - 1))
                        return last
                    gm = P.op("pe", gfn, deps=[lg, ldx] + pg_cons.get(jj, []))
                    um = P.op("pe", ufn, deps=[lg, ldx] + pu_cons.get(jj, []))
                    sl = P.op("act", lambda e, jj=jj: e.activation(
                        sg[jj][:], pg[jj][:], AF.Silu, bias=0.0, scale=1.0),
                        deps=[gm] + sg_cons.get(jj, []))
                    pg_cons[jj] = [sl]
                    hm = P.op("dve", lambda e, jj=jj, ic=ic: e.tensor_mul(
                        hsb[:, ic * TC:(ic + 1) * TC], sg[jj][:], pu[jj][:]),
                        deps=[sl, um] + h_cons.get(ic, []))
                    h_cons[ic] = [hm]
                    pu_cons[jj] = [hm]
                    sg_cons[jj] = [hm]
                    if en > 0:
                        hc = P.op("pool", lambda e, ic=ic: e.tensor_mul(
                            hsb[:, ic * TC:(ic + 1) * TC],
                            hsb[:, ic * TC:(ic + 1) * TC], cwbc[:]),
                            deps=[hm] + bc_cons)
                        hops.append(hc)
                        bc_readers.append(hc)
                    else:
                        hops.append(hm)
                for dc in range(NDC):
                    jj = dc % 2
                    def dfn(e, w=dn[jw], dc=dc, jj=jj):
                        last = None
                        for ic in range(8):
                            last = e.matmul(
                                py[jj][:],
                                w[:, ic * D + dc * 128: ic * D + (dc + 1) * 128],
                                hsb[:, ic * TC:(ic + 1) * TC],
                                start=(ic == 0), stop=(ic == 7))
                        return last
                    dm = P.op("pe", dfn, deps=hops + py_cons.get(jj, []))
                    ya = P.op("dve", lambda e, dc=dc, tcc=tcc, jj=jj:
                              e.tensor_add(
                                  y0[:, dc * QW + tcc * TC:
                                     dc * QW + tcc * TC + TC],
                                  y0[:, dc * QW + tcc * TC:
                                     dc * QW + tcc * TC + TC], py[jj][:]),
                              deps=[dm, ya_prev.get((dc, tcc))] + ms)
                    ya_prev[(dc, tcc)] = ya
                    py_cons[jj] = [ya]
                    y_cons.append(ya)
                    last_dm = dm
                if "dbg_h" in ctx.hbm and q == 0 and en == 0 and tcc == 0:
                    P.dma("sync", lambda e: e.dma_start(
                        out=ctx.hbm["dbg_h"][:], in_=hsb[:]), deps=hops)
                for ic in range(8):
                    h_cons[ic] = [last_dm]
                gu_cons[jw].extend([hops[-1]])
                dn_cons[jw].append(py_cons[0][0] if 0 in py_cons else None)
                dn_cons[jw].append(py_cons[1][0] if 1 in py_cons else None)
            gu_cons[jw] = [x for x in gu_cons[jw] if x]
            dn_cons[jw] = [x for x in dn_cons[jw] if x]
            xn_cons = y_cons[-1:]
        if "dbg_y" in ctx.hbm and q == 0:
            P.dma("sync", lambda e: e.dma_start(
                out=ctx.hbm["dbg_y"][:], in_=y0[:]), deps=list(y_cons))
        for tcc in range(QW // TC):
            for dc in range(NDC):
                k = tcc * NDC + dc
                jj = k % 2
                col0 = q * QW + tcc * TC
                lds = P.dma("sync", lambda e, dc=dc, col0=col0, jj=jj:
                            e.dma_start(out=ssb[jj][:],
                                        in_=s0[dc * 128:(dc + 1) * 128,
                                               col0:col0 + TC]),
                            deps=[prev_comb.get(("p", jj))])
                gg = P.op("act", lambda e, dc=dc, tcc=tcc, jj=jj: e.activation(
                    gsb[jj][:], y0[:, dc * QW + tcc * TC: dc * QW + tcc * TC + TC],
                    AF.Silu, bias=0.0, scale=1.0),
                    deps=y_cons + [prev_comb.get(("g", jj))])
                pp = P.op("dve", lambda e, jj=jj: e.tensor_mul(
                    psb[jj][:], gsb[jj][:], ssb[jj][:]),
                    deps=[gg, lds, prev_comb.get(("st", jj))])
                st = P.dma("sync", lambda e, dc=dc, col0=col0, jj=jj:
                           e.dma_start(out=p0[dc * 128:(dc + 1) * 128,
                                              col0:col0 + TC], in_=psb[jj][:]),
                           deps=[pp])
                prev_comb[("g", jj)] = pp
                prev_comb[("p", jj)] = pp
                prev_comb[("st", jj)] = st
                y_cons.append(gg)


def ph_b5(P, ctx):
    x, p0, x1, xn1 = (ctx.hbm["x"], ctx.hbm["p0"], ctx.hbm["x1"],
                      ctx.hbm["xn1"])
    ceps = ctx.constant(LN_EPS)
    cm1 = ctx.constant(-1.0)
    cmh = ctx.constant(-0.5)
    xb = [ctx.sbuf(f"b5_x{j}", [128, D], F32) for j in range(2)]
    pb_ = [ctx.sbuf(f"b5_p{j}", [128, D], BF16) for j in range(2)]
    x1b = [ctx.sbuf(f"b5_x1{j}", [128, D], F32) for j in range(2)]
    sq = [ctx.sbuf(f"b5_sq{j}", [128, D], F32) for j in range(2)]
    xnb = [ctx.sbuf(f"b5_xn{j}", [128, D], BF16) for j in range(2)]
    st = [ctx.sbuf(f"b5_st{j}", [128, 8], F32) for j in range(2)]
    C = 1.0 / D
    prev = {}
    for i in range(NTT):
        j = i % 2
        pv = prev.get(j, {})
        ld = P.dma("sync", lambda e, i=i, t=xb[j]: e.dma_start(
            out=t[:], in_=x[i * 128:(i + 1) * 128, :]), deps=[pv.get("add")])
        ldp = P.dma("sync", lambda e, i=i, t=pb_[j]: e.dma_start(
            out=t[:], in_=p0[:, i * 128:(i + 1) * 128], transpose=True),
            deps=[pv.get("add")])
        add = P.op("dve", lambda e, j=j: e.tensor_add(
            x1b[j][:], xb[j][:], pb_[j][:]),
            deps=[ld, ldp, pv.get("stx1"), pv.get("xn"), pv.get("sq1"),
                  pv.get("sum")])
        stx1 = P.dma("pool", lambda e, i=i, j=j: e.dma_start(
            out=x1[i * 128:(i + 1) * 128, :], in_=x1b[j][:]), deps=[add])
        stb = st[j]
        sx, sx2 = stb[:, 0:1], stb[:, 1:2]
        mu, nv, rstd = stb[:, 2:3], stb[:, 3:4], stb[:, 4:5]
        summ = P.op("dve", lambda e, sx=sx, j=j: e.reduce_sum(
            sx, x1b[j][:], axis=AX.X), deps=[add])
        sq1 = P.op("act", lambda e, j=j, sx2=sx2: e.activation(
            sq[j][:], x1b[j][:], AF.Square, bias=0.0, scale=1.0,
            accum_out=sx2), deps=[add, pv.get("sq_w")])
        ex2 = P.op("dve", lambda e, sx2=sx2: e.tensor_scalar(
            sx2, sx2, C, None, ALU.mult), deps=[sq1])
        muo = P.op("dve", lambda e, mu=mu, sx=sx: e.tensor_scalar(
            mu, sx, C, None, ALU.mult), deps=[summ])
        nvo = P.op("dve", lambda e, nv=nv, mu=mu, sx2=sx2:
                   e.scalar_tensor_tensor(nv, mu, mu, sx2, ALU.mult,
                                          ALU.subtract), deps=[muo, ex2])
        lnv = P.op("act", lambda e, nv=nv: e.activation(
            nv, nv, AF.Ln, bias=LN_EPS, scale=-1.0), deps=[nvo, ceps, cm1])
        rst = P.op("act", lambda e, rstd=rstd, nv=nv: e.activation(
            rstd, nv, AF.Exp, bias=0.0, scale=-0.5), deps=[lnv, cmh])
        xn = P.op("dve", lambda e, j=j, mu=mu, rstd=rstd: e.tensor_scalar(
            xnb[j][:], x1b[j][:], mu, rstd, ALU.subtract, ALU.mult),
            deps=[muo, rst, pv.get("stxn")])
        stxn = P.dma("pool", lambda e, i=i, j=j: e.dma_start(
            out=xn1[i * 128:(i + 1) * 128, :], in_=xnb[j][:]), deps=[xn])
        prev[j] = dict(add=add, stx1=stx1, xn=xn, sq1=sq1, sum=summ,
                       sq_w=nvo, stxn=stxn)


def ph_b6(P, ctx):
    xn1, xn1T = ctx.hbm["xn1"], ctx.hbm["xn1T"]
    wr1_h, br1_h, g1 = ctx.hbm["wr1"], ctx.hbm["br1"], ctx.hbm["g1"]

    xt = [[ctx.sbuf(f"b6_xt{j}_{dc}", [128, TC], BF16) for dc in range(NDC)]
          for j in range(2)]
    wr = ctx.sbuf("b6_wr", [128, NDC * E1], BF16)
    brrow = ctx.sbuf("b6_br", [1, E1], F32)
    brbc = ctx.sbuf("b6_brbc", [128, E1], F32)
    ones1 = ctx.sbuf("b6_ones", [1, 128], F32)
    lg = [ctx.sbuf(f"b6_lg{j}", [128, 4 * E1], F32) for j in range(2)]
    p2b = [ctx.sbuf(f"b6_p2{j}", [128, 4 * E1], F32) for j in range(2)]
    tmp = [ctx.sbuf(f"b6_tmp{j}", [128, 4 * E1], F32) for j in range(2)]
    msk = [ctx.sbuf(f"b6_msk{j}", [128, 4 * E1], F32) for j in range(2)]
    gt = [ctx.sbuf(f"b6_gt{j}", [128, 4 * E1], F32) for j in range(2)]
    r1 = [ctx.sbuf(f"b6_r1{j}", [128, 4], F32) for j in range(2)]
    gfm = ctx.sbuf("b6_gfm", [E1, T], F32)
    ident = ctx.sbuf("b6_id", [128, 128], F32)
    ia = ctx.sbuf("b6_ia", [128, 128], F32)
    plg = [ctx.psum(f"b6_plg{j}", [128, E1], F32) for j in range(2)]
    ptr = [ctx.psum(f"b6_ptr{j}", [E1, 128], F32) for j in range(2)]
    pbc = ctx.psum("b6_pbc", [128, E1], F32)

    i1 = P.op("pool", lambda e: e.iota(ia[:], pattern=[[0, 128]], base=0,
              channel_multiplier=1, allow_small_or_imprecise_dtypes=True))
    i2 = P.op("pool", lambda e: e.iota(ident[:], pattern=[[1, 128]], base=0,
              channel_multiplier=0, allow_small_or_imprecise_dtypes=True))
    idt = P.op("dve", lambda e: e.tensor_tensor(ident[:], ia[:], ident[:],
               ALU.is_equal), deps=[i1, i2])
    ldw = [P.dma("pool", lambda e, dc=dc: e.dma_start(
        out=wr[:, dc * E1:(dc + 1) * E1],
        in_=wr1_h[dc * 128:(dc + 1) * 128, :])) for dc in range(NDC)]
    ldb = P.dma("sync", lambda e: e.dma_start(out=brrow[:], in_=br1_h[:]))
    o1 = P.op("dve", lambda e: e.memset(ones1[:], 1.0))
    mbc = P.op("pe", lambda e: e.matmul(pbc[:], ones1[:], brrow[:],
               start=True, stop=True), deps=[ldb, o1])
    ebc = P.op("dve", lambda e: e.tensor_copy(brbc[:], pbc[:]), deps=[mbc])

    prev = {}
    ev_by_k = {}
    evg_by_k = {}
    for i in range(NTCH):
        j = i % 2
        pv = prev.get(j, {})
        lds = []
        for dc in range(NDC):
            lds.append(P.dma("sync", lambda e, i=i, dc=dc, t=xt[j][dc]:
                       e.dma_start(out=t[:],
                                   in_=xn1[i * TC:(i + 1) * TC,
                                           dc * 128:(dc + 1) * 128],
                                   transpose=True), deps=pv.get("xt_cons", [])))
        stws = [P.dma("pool", lambda e, i=i, dc=dc, t=xt[j][dc]: e.dma_start(
            out=xn1T[dc * 128:(dc + 1) * 128, i * TC:(i + 1) * TC], in_=t[:]),
            deps=[lds[dc]]) for dc in range(NDC)]
        mmops, evops = [], []
        for tt in range(4):
            k = i * 4 + tt
            jj = k % 2

            def mmfn(e, j=j, tt=tt, jj=jj):
                last = None
                for dc in range(NDC):
                    last = e.matmul(plg[jj][:],
                                    xt[j][dc][:, tt * 128:(tt + 1) * 128],
                                    wr[:, dc * E1:(dc + 1) * E1],
                                    start=(dc == 0), stop=(dc == NDC - 1))
                return last
            mm = P.op("pe", mmfn, deps=lds + ldw + [ev_by_k.get(k - 2)])
            ev = P.op("dve", lambda e, j=j, tt=tt, jj=jj: e.tensor_add(
                lg[j][:, tt * E1:(tt + 1) * E1], plg[jj][:], brbc[:]),
                deps=[mm, ebc, pv.get("chain_last")])
            ev_by_k[k] = ev
            mmops.append(mm)
            evops.append(ev)
        lgv = lg[j][:].rearrange("p (s e) -> p s e", e=E1)
        p2v = p2b[j][:].rearrange("p (s e) -> p s e", e=E1)
        tmpv = tmp[j][:].rearrange("p (s e) -> p s e", e=E1)
        mskv = msk[j][:].rearrange("p (s e) -> p s e", e=E1)
        gtv = gt[j][:].rearrange("p (s e) -> p s e", e=E1)
        r1v = r1[j][:].rearrange("p (s o) -> p s o", o=1)
        r1b = r1[j][:].rearrange("p (s o) -> p s o", o=1).to_broadcast(
            [128, 4, E1])
        c1 = P.op("dve", lambda e, lgv=lgv, r1v=r1v: e.tensor_reduce(
            r1v, lgv, axis=AX.X, op=ALU.max, negate=True),
            deps=evops + [pv.get("c_last")])
        c2 = P.op("dve", lambda e, p2v=p2v, lgv=lgv, r1b=r1b: e.tensor_tensor(
            p2v, lgv, r1b, ALU.add), deps=[c1, pv.get("c_last")])
        c3 = P.op("act", lambda e, s=p2b[j]: e.activation(
            s[:], s[:], AF.Exp, bias=0.0, scale=1.0), deps=[c2])
        c4 = P.op("dve", lambda e, p2v=p2v, r1v=r1v: e.tensor_reduce(
            r1v, p2v, axis=AX.X, op=ALU.add), deps=[c3])
        c5 = P.op("dve", lambda e, r=r1[j]: e.reciprocal(r[:], r[:]), deps=[c4])
        c6 = P.op("dve", lambda e, p2v=p2v, r1b=r1b: e.tensor_tensor(
            p2v, p2v, r1b, ALU.mult), deps=[c5])
        c7 = P.op("dve", lambda e, p2v=p2v, r1v=r1v: e.tensor_reduce(
            r1v, p2v, axis=AX.X, op=ALU.max), deps=[c6])
        c8 = P.op("dve", lambda e, mskv=mskv, p2v=p2v, r1b=r1b: e.tensor_tensor(
            mskv, p2v, r1b, ALU.is_equal), deps=[c7])
        c9 = P.op("dve", lambda e, tmpv=tmpv, p2v=p2v, mskv=mskv:
                  e.tensor_tensor(tmpv, p2v, mskv, ALU.mult), deps=[c8])
        c10 = P.op("dve", lambda e, tmpv=tmpv, p2v=p2v: e.tensor_tensor(
            tmpv, p2v, tmpv, ALU.subtract), deps=[c9])
        c11 = P.op("dve", lambda e, tmpv=tmpv, r1v=r1v: e.tensor_reduce(
            r1v, tmpv, axis=AX.X, op=ALU.max), deps=[c10])
        c12 = P.op("dve", lambda e, gtv=gtv, tmpv=tmpv, r1b=r1b:
                   e.tensor_tensor(gtv, tmpv, r1b, ALU.is_equal), deps=[c11])
        c13 = P.op("dve", lambda e, mskv=mskv, gtv=gtv: e.tensor_tensor(
            mskv, mskv, gtv, ALU.add), deps=[c12])
        c14 = P.op("dve", lambda e, tmpv=tmpv, gtv=gtv: e.tensor_tensor(
            gtv, tmpv, gtv, ALU.mult), deps=[c13])
        c15 = P.op("dve", lambda e, tmpv=tmpv, gtv=gtv: e.tensor_tensor(
            tmpv, tmpv, gtv, ALU.subtract), deps=[c14])
        c16 = P.op("dve", lambda e, tmpv=tmpv, r1v=r1v: e.tensor_reduce(
            r1v, tmpv, axis=AX.X, op=ALU.max), deps=[c15])
        c17 = P.op("dve", lambda e, gtv=gtv, tmpv=tmpv, r1b=r1b:
                   e.tensor_tensor(gtv, tmpv, r1b, ALU.is_equal), deps=[c16])
        c18 = P.op("dve", lambda e, mskv=mskv, gtv=gtv: e.tensor_tensor(
            mskv, mskv, gtv, ALU.add), deps=[c17])
        c19 = P.op("dve", lambda e, gtv=gtv, p2v=p2v, mskv=mskv:
                   e.tensor_tensor(gtv, p2v, mskv, ALU.mult), deps=[c18])
        tpev = []
        for tt in range(4):
            k = i * 4 + tt
            jj = k % 2
            tpo = P.op("pe", lambda e, j=j, tt=tt, jj=jj: e.transpose(
                ptr[jj][:], gt[j][:, tt * E1:(tt + 1) * E1], ident[:]),
                deps=[c19, idt, evg_by_k.get(k - 2)])
            evc = P.op("dve", lambda e, i=i, tt=tt, jj=jj: e.tensor_copy(
                gfm[:, i * TC + tt * 128: i * TC + (tt + 1) * 128], ptr[jj][:]),
                deps=[tpo])
            evg_by_k[k] = evc
            tpev.append(evc)
        prev[j] = {"xt_cons": mmops + stws, "chain_last": c19, "c_last": c19}
    P.dma("pool", lambda e: e.dma_start(out=g1[:], in_=gfm[:]),
          deps=list(evg_by_k.values()))


def ph_b7(P, ctx):
    xn1T = ctx.hbm["xn1T"]
    wq1_h, wk1_h, wv1_h = ctx.hbm["wq1"], ctx.hbm["wk1"], ctx.hbm["wv1"]
    bq1_h, bk1_h, bv1_h = ctx.hbm["bq1"], ctx.hbm["bk1"], ctx.hbm["bv1"]
    q1, attn1, s1 = ctx.hbm["q1"], ctx.hbm["attn1"], ctx.hbm["s1"]

    v_tm = ctx.sbuf("b7_vtm", [128, 32 * D], BF16)
    bq = ctx.sbuf("b7_bq", [128, NDC], F32)
    bk = ctx.sbuf("b7_bk", [128, NDC], F32)
    bvrow = ctx.sbuf("b7_bvrow", [1, D], F32)
    bvbc = ctx.sbuf("b7_bvbc", [128, D], F32)
    ones1 = ctx.sbuf("b7_ones", [1, 128], F32)
    _sc2 = ExitStack()
    k_fm = ctx.sbuf("b7_kfm", [128, NDC * T], BF16, scope=_sc2)
    pq = [ctx.psum(f"b7_pq{j}", [128, TC], F32, scope=_sc2) for j in range(2)]
    pv_ = [ctx.psum(f"b7_pv{j}", [128, TC], F32, scope=_sc2) for j in range(2)]
    pbc = ctx.psum("b7_pbc", [128, TC], F32, scope=_sc2)
    _sc3 = ExitStack()
    wq_sb = ctx.sbuf("b7_wq", [128, NDC * D], BF16, scope=_sc3)
    wk_sb = ctx.sbuf("b7_wk", [128, NDC * D], BF16, scope=_sc3)
    wv_sb = ctx.sbuf("b7_wv", [128, NDC * D], BF16, scope=_sc3)
    xnb = [ctx.sbuf(f"b7_xn{j}", [128, NDC * TC], BF16, scope=_sc3)
           for j in range(2)]
    qev = [ctx.sbuf(f"b7_qe{j}", [128, TC], BF16, scope=_sc3) for j in range(2)]

    ldbq = P.dma("sync", lambda e: e.dma_start(out=bq[:], in_=bq1_h[:]))
    ldbk = P.dma("sync", lambda e: e.dma_start(out=bk[:], in_=bk1_h[:]))
    ldbv = P.dma("sync", lambda e: e.dma_start(out=bvrow[:], in_=bv1_h[:]))
    o1 = P.op("dve", lambda e: e.memset(ones1[:], 1.0))
    mb1 = P.op("pe", lambda e: e.matmul(pbc[:], ones1[:], bvrow[:, 0:TC],
               start=True, stop=True), deps=[ldbv, o1])
    eb1 = P.op("dve", lambda e: e.tensor_copy(bvbc[:, 0:TC], pbc[:]),
               deps=[mb1])
    mb2 = P.op("pe", lambda e: e.matmul(pbc[:], ones1[:], bvrow[:, TC:D],
               start=True, stop=True), deps=[eb1])
    eb2 = P.op("dve", lambda e: e.tensor_copy(bvbc[:, TC:D], pbc[:]),
               deps=[mb2])
    lws = []
    for dc in range(NDC):
        lws.append(P.dma("pool", lambda e, dc=dc: e.dma_start(
            out=wq_sb[:, dc * D:(dc + 1) * D],
            in_=wq1_h[dc * 128:(dc + 1) * 128, :])))
        lws.append(P.dma("pool", lambda e, dc=dc: e.dma_start(
            out=wk_sb[:, dc * D:(dc + 1) * D],
            in_=wk1_h[dc * 128:(dc + 1) * 128, :])))
        lws.append(P.dma("pool", lambda e, dc=dc: e.dma_start(
            out=wv_sb[:, dc * D:(dc + 1) * D],
            in_=wv1_h[dc * 128:(dc + 1) * 128, :])))

    xn_cons = {0: [], 1: []}
    pq_cons = {0: [], 1: []}
    pv_cons = {0: [], 1: []}
    qst_prev = {0: None, 1: None}
    last_qkv = []
    for tc in range(NTCH):
        j = tc % 2
        ldx = P.dma("sync", lambda e, tc=tc, t=xnb[j]: e.dma_start(
            out=t[:].rearrange("p (c t) -> p c t", t=TC),
            in_=xn1T[:, tc * TC:(tc + 1) * TC].rearrange(
                "(c p) t -> p c t", p=128)), deps=xn_cons[j])
        xn_cons[j] = []
        for fc in range(NDC):
            kq = (tc * NDC + fc)
            jj = kq % 2
            def qfn(e, j=j, fc=fc, jj=jj):
                last = None
                for dc in range(NDC):
                    last = e.matmul(pq[jj][:],
                                    wq_sb[:, dc * D + fc * 128:
                                          dc * D + (fc + 1) * 128],
                                    xnb[j][:, dc * TC:(dc + 1) * TC],
                                    start=(dc == 0), stop=(dc == NDC - 1))
                return last
            qm = P.op("pe", qfn, deps=[ldx] + lws + pq_cons[jj])
            qe = P.op("dve", lambda e, fc=fc, jj=jj: e.tensor_scalar(
                qev[jj][:], pq[jj][:], bq[:, fc:fc + 1], None, ALU.add),
                deps=[qm, qst_prev[jj]])
            pq_cons[jj] = [qe]
            qs = P.dma("sync", lambda e, fc=fc, tc=tc, jj=jj: e.dma_start(
                out=q1[fc * 128:(fc + 1) * 128, tc * TC:(tc + 1) * TC],
                in_=qev[jj][:]), deps=[qe])
            qst_prev[jj] = qs
            xn_cons[j].append(qm)
            def kfn(e, j=j, fc=fc, jj=jj):
                last = None
                for dc in range(NDC):
                    last = e.matmul(pv_[jj][:],
                                    wk_sb[:, dc * D + fc * 128:
                                          dc * D + (fc + 1) * 128],
                                    xnb[j][:, dc * TC:(dc + 1) * TC],
                                    start=(dc == 0), stop=(dc == NDC - 1))
                return last
            km = P.op("pe", kfn, deps=[ldx] + lws + pv_cons[jj])
            ke = P.op("dve", lambda e, fc=fc, tc=tc, jj=jj: e.tensor_scalar(
                k_fm[:, fc * T + tc * TC: fc * T + tc * TC + TC], pv_[jj][:],
                bk[:, fc:fc + 1], None, ALU.add), deps=[km, ldbk])
            pv_cons[jj] = [ke]
            xn_cons[j].append(km)
            last_qkv = [qs, ke]
        for tt in range(4):
            kt = tc * 4 + tt
            for fh in range(2):
                jj = (kt * 2 + fh) % 2
                def vfn(e, j=j, tt=tt, fh=fh, jj=jj):
                    last = None
                    for dc in range(NDC):
                        last = e.matmul(
                            pv_[jj][:],
                            xnb[j][:, dc * TC + tt * 128:
                                dc * TC + (tt + 1) * 128],
                            wv_sb[:, dc * D + fh * TC: dc * D + (fh + 1) * TC],
                            start=(dc == 0), stop=(dc == NDC - 1))
                    return last
                vm = P.op("pe", vfn, deps=[ldx] + lws + pv_cons[jj])
                ve = P.op("dve", lambda e, kt=kt, fh=fh, jj=jj: e.tensor_add(
                    v_tm[:, kt * D + fh * TC: kt * D + (fh + 1) * TC],
                    pv_[jj][:], bvbc[:, fh * TC:(fh + 1) * TC]),
                    deps=[vm, eb1, eb2])
                pv_cons[jj] = [ve]
                xn_cons[j].append(vm)
                last_qkv.append(ve)

    _sc3.close()
    srow = ctx.sbuf("b7_srow", [128, T], F32, scope=_sc2)
    zap = ctx.sbuf("b7_zap", [128, T], F32, scope=_sc2)
    mx8 = ctx.sbuf("b7_mx8", [128, 8], F32, scope=_sc2)
    dcol = ctx.sbuf("b7_dcol", [128, 1], F32, scope=_sc2)
    asp = ctx.sbuf("b7_asp", [128, T], BF16, scope=_sc2)
    qt = [ctx.sbuf(f"b7_qt{j}", [128, NDC * 128], BF16, scope=_sc2)
          for j in range(2)]
    ep_prev = {0: None, 1: None}
    att_sts = []
    sc_prev = None
    mr_prev = None
    for qt_i in range(NTT):
        j = qt_i % 2
        ldq = P.dma("sync", lambda e, qt_i=qt_i, t=qt[j]: e.dma_start(
            out=t[:].rearrange("p (c f) -> p c f", f=128),
            in_=q1[:, qt_i * 128:(qt_i + 1) * 128].rearrange(
                "(c p) f -> p c f", p=128)),
            deps=last_qkv + (ep_prev[j] or []))
        evs = []
        for kc in range(NTCH):
            jj = kc % 2
            def sfn(e, j=j, kc=kc, jj=jj):
                last = None
                for ec in range(NDC):
                    last = e.matmul(pq[jj][:],
                                    qt[j][:, ec * 128:(ec + 1) * 128],
                                    k_fm[:, ec * T + kc * TC:
                                         ec * T + kc * TC + TC],
                                    start=(ec == 0), stop=(ec == NDC - 1))
                return last
            sm = P.op("pe", sfn, deps=[ldq] + last_qkv + pq_cons[jj])
            se = P.op("dve", lambda e, kc=kc, jj=jj: e.tensor_copy(
                srow[:, kc * TC:(kc + 1) * TC], pq[jj][:]),
                deps=[sm, sc_prev])
            pq_cons[jj] = [se]
            evs.append(se)
        cz = P.op("dve", lambda e: e.tensor_copy(zap[:], srow[:]),
                  deps=evs + [mr_prev])
        mrs = cz
        for r in range(4):
            m8 = P.op("dve", lambda e: e.max(out=mx8[:], in_=zap[:]),
                      deps=[mrs])
            mrs = P.op("dve", lambda e: e.match_replace(
                out=zap[:], in_to_replace=mx8[:], in_values=zap[:],
                imm_value=-1e30), deps=[m8])
        mk = P.op("dve", lambda e: e.tensor_tensor(zap[:], zap[:], srow[:],
                  ALU.not_equal), deps=[mrs])
        sp = P.op("dve", lambda e: e.scalar_tensor_tensor(
            asp[:], srow[:], 1.0, zap[:], ALU.mult, ALU.mult,
            accum_out=dcol[:]), deps=[mk, att_sts[-1] if att_sts else None])
        dn1 = P.op("dve", lambda e: e.tensor_scalar(
            dcol[:], dcol[:], 1e-10, None, ALU.add), deps=[sp])
        dn2 = P.op("dve", lambda e: e.reciprocal(dcol[:], dcol[:]), deps=[dn1])
        nrm = P.op("dve", lambda e: e.tensor_scalar(
            asp[:], asp[:], dcol[:, 0:1], None, ALU.mult), deps=[dn2])
        sc_prev = nrm
        mr_prev = sp
        ast = P.dma("sync", lambda e, qt_i=qt_i: e.dma_start(
            out=attn1[qt_i * 128:(qt_i + 1) * 128, :], in_=asp[:]),
            deps=[nrm])
        att_sts.append(ast)
        ep_prev[j] = [sm]

    _sc2.close()
    at = ctx.sbuf("b7_at", [128, 32 * TC], BF16)
    ev1 = [ctx.sbuf(f"b7_ev1{j}", [128, TC], BF16) for j in range(2)]
    pa = [ctx.psum(f"b7_pa{j}", [128, TC], F32) for j in range(2)]
    at_cons = []
    prev_st = {0: None, 1: None}
    prev_ev = {0: None, 1: None}
    for qc in range(NTCH):
        lats = [P.dma("sync", lambda e, qc=qc, kt=kt: e.dma_start(
            out=at[:, kt * TC:(kt + 1) * TC],
            in_=attn1[qc * TC:(qc + 1) * TC, kt * 128:(kt + 1) * 128],
            transpose=True), deps=att_sts + at_cons) for kt in range(32)]
        at_cons = []
        for vc in range(NDC):
            jj = vc % 2
            def avfn(e, qc=qc, vc=vc, jj=jj):
                last = None
                for kt in range(32):
                    last = e.matmul(pa[jj][:],
                                    v_tm[:, kt * D + vc * 128:
                                         kt * D + (vc + 1) * 128],
                                    at[:, kt * TC:(kt + 1) * TC],
                                    start=(kt == 0), stop=(kt == 31))
                return last
            av = P.op("pe", avfn, deps=lats + last_qkv + [prev_ev[jj]])
            eo = P.op("dve", lambda e, jj=jj: e.tensor_copy(ev1[jj][:],
                      pa[jj][:]), deps=[av, prev_st[jj]])
            st = P.dma("sync", lambda e, qc=qc, vc=vc, jj=jj: e.dma_start(
                out=s1[vc * 128:(vc + 1) * 128, qc * TC:(qc + 1) * TC],
                in_=ev1[jj][:]), deps=[eo])
            prev_ev[jj] = eo
            prev_st[jj] = st
            at_cons.append(av)


def ph_b8(P, ctx):
    xn1T, g1, s1, p1 = (ctx.hbm["xn1T"], ctx.hbm["g1"], ctx.hbm["s1"],
                        ctx.hbm["p1"])
    w1f_h, w2f_h = ctx.hbm["w1f"], ctx.hbm["w2f"]
    b1f_h, bs2_h, be2_h = ctx.hbm["b1f"], ctx.hbm["bs2"], ctx.hbm["be2"]
    NE = 32
    HW = 2048
    NHALF = T // HW

    y1 = ctx.sbuf("b8_y1", [128, NDC * HW], F32)
    xnh = ctx.sbuf("b8_xnh", [128, NDC * HW], BF16)
    w1 = [ctx.sbuf(f"b8_w1{j}", [128, NDC * D], BF16) for j in range(2)]
    w2 = [ctx.sbuf(f"b8_w2{j}", [128, 8 * D], BF16) for j in range(2)]
    b1 = ctx.sbuf("b8_b1", [128, NE * NDC], F32)
    bs2 = ctx.sbuf("b8_bs2", [128, NDC], F32)
    be2 = ctx.sbuf("b8_be2", [E1, NDC * 128], BF16)
    g1b = ctx.sbuf("b8_g1b", [E1, HW], BF16)
    grow = ctx.sbuf("b8_grow", [1, TC], F32)
    gbc = ctx.sbuf("b8_gbc", [128, TC], BF16)
    ones1 = ctx.sbuf("b8_ones1", [1, 128], F32)
    hsb = ctx.sbuf("b8_h", [128, 8 * TC], BF16)
    tsc = [ctx.sbuf(f"b8_t{j}", [128, TC], F32) for j in range(2)]
    gsb = [ctx.sbuf(f"b8_g{j}", [128, TC], BF16) for j in range(2)]
    ssb = [ctx.sbuf(f"b8_s{j}", [128, TC], BF16) for j in range(2)]
    psb = [ctx.sbuf(f"b8_p{j}", [128, TC], BF16) for j in range(2)]
    ph_ = [ctx.psum(f"b8_ph{j}", [128, TC], F32) for j in range(2)]
    py = [ctx.psum(f"b8_py{j}", [128, TC], F32) for j in range(2)]
    pbc = ctx.psum("b8_pbc", [128, TC], F32)

    o1 = P.op("dve", lambda e: e.memset(ones1[:], 1.0))
    ldb1 = P.dma("sync", lambda e: e.dma_start(
        out=b1[:], in_=b1f_h[:].rearrange("p e c -> p (e c)")))
    ldbs2 = P.dma("sync", lambda e: e.dma_start(out=bs2[:], in_=bs2_h[:]))
    ldbe2 = P.dma("pool", lambda e: e.dma_start(
        out=be2[:], in_=be2_h[:]))
    w1_cons = {0: [], 1: []}
    w2_cons = {0: [], 1: []}
    xn_cons = []
    y_cons = []
    h_cons = {}
    ts_cons = {}
    gb_cons = []
    gr_cons = []
    ph_cons = {}
    py_cons = {}
    ya_prev = {}
    prev_comb = {}
    g1_cons = []

    for half in range(NHALF):
        col0h = half * HW
        ldx = P.dma("sync", lambda e, col0h=col0h: e.dma_start(
            out=xnh[:].rearrange("p (c t) -> p c t", t=HW),
            in_=xn1T[:, col0h:col0h + HW].rearrange("(c p) t -> p c t", p=128)),
            deps=xn_cons)
        xn_cons = []
        ldg = P.dma("pool", lambda e, col0h=col0h: e.dma_start(
            out=g1b[:], in_=g1[:, col0h:col0h + HW]), deps=g1_cons)
        g1_cons = []
        ms = [P.op("dve", lambda e, dc=dc: e.memset(
            y1[:, dc * HW:(dc + 1) * HW], 0.0), deps=y_cons)
            for dc in range(NDC)]
        y_cons = []
        for en in range(NE):
            jw = en % 2
            lw1 = P.dma("pool", lambda e, en=en, t=w1[jw]: e.dma_start(
                out=t[:].rearrange("p (c i) -> p c i", i=D),
                in_=w1f_h[en].rearrange("(c p) i -> p c i", p=128)),
                deps=w1_cons[jw])
            lw2 = P.dma("pool", lambda e, en=en, t=w2[jw]: e.dma_start(
                out=t[:].rearrange("p (c d) -> p c d", d=D),
                in_=w2f_h[en].rearrange("(c p) d -> p c d", p=128)),
                deps=w2_cons[jw])
            w1_cons[jw], w2_cons[jw] = [], []
            for tcc in range(HW // TC):
                col0 = tcc * TC
                if en > 0:
                    lgr = P.dma("sync", lambda e, en=en, col0=col0: e.dma_start(
                        out=grow[:], in_=g1[en - 1:en,
                                            col0h + col0:col0h + col0 + TC]),
                        deps=gr_cons)
                    mgb = P.op("pe", lambda e: e.matmul(
                        pbc[:], ones1[:], grow[:], start=True, stop=True),
                        deps=[lgr, o1] + gb_cons)
                    egb = P.op("dve", lambda e: e.tensor_copy(gbc[:], pbc[:]),
                               deps=[mgb] + gb_cons)
                    gr_cons = [mgb]
                    gb_cons = [egb]
                hops = []
                for ic in range(8):
                    jj = ic % 2
                    def hfn(e, jw=jw, ic=ic, tcc=tcc, jj=jj):
                        last = None
                        for dc in range(NDC):
                            last = e.matmul(
                                ph_[jj][:],
                                w1[jw][:, dc * D + ic * 128:
                                       dc * D + (ic + 1) * 128],
                                xnh[:, dc * HW + tcc * TC:
                                    dc * HW + tcc * TC + TC],
                                start=(dc == 0), stop=(dc == NDC - 1))
                        return last
                    hm = P.op("pe", hfn, deps=[lw1, ldx] + ph_cons.get(jj, []))
                    ge = P.op("act", lambda e, en=en, ic=ic, jj=jj:
                              e.activation(hsb[:, ic * TC:(ic + 1) * TC],
                                           ph_[jj][:], AF.Gelu,
                                           bias=b1[:, en * NDC + ic:
                                                   en * NDC + ic + 1],
                                           scale=1.0),
                              deps=[hm, ldb1] + h_cons.get(ic, []))
                    ph_cons[jj] = [ge]
                    if en > 0:
                        hc = P.op("pool", lambda e, ic=ic: e.tensor_mul(
                            hsb[:, ic * TC:(ic + 1) * TC],
                            hsb[:, ic * TC:(ic + 1) * TC], gbc[:]),
                            deps=[ge] + gb_cons)
                        gb_cons.append(hc)
                        hops.append(hc)
                    else:
                        hops.append(ge)
                last_dm = None
                for dc in range(NDC):
                    jj = dc % 2
                    def dfn(e, jw=jw, dc=dc, en=en, tcc=tcc, jj=jj):
                        last = None
                        for ic in range(8):
                            stop = (ic == 7) and (en != 0)
                            last = e.matmul(
                                py[jj][:],
                                w2[jw][:, ic * D + dc * 128:
                                       ic * D + (dc + 1) * 128],
                                hsb[:, ic * TC:(ic + 1) * TC],
                                start=(ic == 0), stop=stop)
                        if en == 0:
                            last = e.matmul(
                                py[jj][:], be2[:, dc * 128:(dc + 1) * 128],
                                g1b[:, tcc * TC:(tcc + 1) * TC],
                                start=False, stop=True)
                        return last
                    dm = P.op("pe", dfn,
                              deps=hops + py_cons.get(jj, []) +
                              ([ldbe2, ldg] if en == 0 else []))
                    ya = P.op("dve", lambda e, dc=dc, tcc=tcc, jj=jj:
                              e.tensor_add(
                                  y1[:, dc * HW + tcc * TC:
                                     dc * HW + tcc * TC + TC],
                                  y1[:, dc * HW + tcc * TC:
                                     dc * HW + tcc * TC + TC], py[jj][:]),
                              deps=[dm, ya_prev.get((dc, tcc))] + ms)
                    ya_prev[(dc, tcc)] = ya
                    py_cons[jj] = [ya]
                    y_cons.append(ya)
                    last_dm = dm
                for ic in range(8):
                    h_cons[ic] = [last_dm]
                w1_cons[jw].append(hops[-1])
                w2_cons[jw].append(y_cons[-1])
            xn_cons = y_cons[-1:]
            g1_cons = y_cons[-1:]
        for tcc in range(HW // TC):
            for dc in range(NDC):
                k = tcc * NDC + dc
                jj = k % 2
                colg = col0h + tcc * TC
                lds = P.dma("sync", lambda e, dc=dc, colg=colg, jj=jj:
                            e.dma_start(out=ssb[jj][:],
                                        in_=s1[dc * 128:(dc + 1) * 128,
                                               colg:colg + TC]),
                            deps=[prev_comb.get(("s", jj))])
                tadd = P.op("dve", lambda e, dc=dc, tcc=tcc, jj=jj:
                            e.tensor_add(
                                tsc[jj][:],
                                y1[:, dc * HW + tcc * TC:
                                   dc * HW + tcc * TC + TC],
                                xnh[:, dc * HW + tcc * TC:
                                    dc * HW + tcc * TC + TC]),
                            deps=y_cons + [prev_comb.get(("t", jj))])
                gg = P.op("act", lambda e, dc=dc, jj=jj: e.activation(
                    gsb[jj][:], tsc[jj][:], AF.Silu,
                    bias=bs2[:, dc:dc + 1], scale=1.0),
                    deps=[tadd, ldbs2, prev_comb.get(("g", jj))])
                pp = P.op("dve", lambda e, jj=jj: e.tensor_mul(
                    psb[jj][:], gsb[jj][:], ssb[jj][:]),
                    deps=[gg, lds, prev_comb.get(("st", jj))])
                st = P.dma("sync", lambda e, dc=dc, colg=colg, jj=jj:
                           e.dma_start(out=p1[dc * 128:(dc + 1) * 128,
                                              colg:colg + TC],
                                       in_=psb[jj][:]), deps=[pp])
                prev_comb[("t", jj)] = gg
                prev_comb[("g", jj)] = pp
                prev_comb[("s", jj)] = pp
                prev_comb[("st", jj)] = st
                y_cons.append(tadd)
                xn_cons.append(tadd)


def ph_b9(P, ctx):
    x1, p1, out = ctx.hbm["x1"], ctx.hbm["p1"], ctx.hbm["out"]
    xb = [ctx.sbuf(f"b9_x{j}", [128, D], F32) for j in range(2)]
    pb_ = [ctx.sbuf(f"b9_p{j}", [128, D], BF16) for j in range(2)]
    ob = [ctx.sbuf(f"b9_o{j}", [128, D], F32) for j in range(2)]
    prev = {}
    for i in range(NTT):
        j = i % 2
        pv = prev.get(j, {})
        ld = P.dma("sync", lambda e, i=i, t=xb[j]: e.dma_start(
            out=t[:], in_=x1[i * 128:(i + 1) * 128, :]), deps=[pv.get("add")])
        ldp = P.dma("sync", lambda e, i=i, t=pb_[j]: e.dma_start(
            out=t[:], in_=p1[:, i * 128:(i + 1) * 128], transpose=True),
            deps=[pv.get("add")])
        add = P.op("dve", lambda e, j=j: e.tensor_add(
            ob[j][:], xb[j][:], pb_[j][:]), deps=[ld, ldp, pv.get("st")])
        st = P.dma("pool", lambda e, i=i, j=j: e.dma_start(
            out=out[i * 128:(i + 1) * 128, :], in_=ob[j][:]), deps=[add])
        prev[j] = dict(add=add, st=st)


PHASES = {
    "b0": (ph_b0, ["x", "lnw", "lnb"], ["xn0", "h0"]),
    "b1": (ph_b1, ["xn0", "h0", "gw0T"], ["xn0T", "h0T", "cw0"]),
    "b2": (ph_b2, ["h0T", "wq", "wk", "wv", "wo", "theta"], ["s0"]),
    "b2dbg": (ph_b2, ["h0T", "wq", "wk", "wv", "wo", "theta"],
              ["s0", "dbg_q", "dbg_k", "dbg_v", "dbg_o", "dbg_C", "dbg_S"]),
    "b4": (ph_b4, ["xn0T", "cw0", "s0", "gu0", "dn0"], ["p0"]),
    "b4dbg": (ph_b4, ["xn0T", "cw0", "s0", "gu0", "dn0"],
              ["p0", "dbg_y", "dbg_h"]),
    "b5": (ph_b5, ["x", "p0"], ["x1", "xn1"]),
    "b6": (ph_b6, ["xn1", "wr1", "br1"], ["xn1T", "g1"]),
    "b7": (ph_b7, ["xn1T", "wq1", "wk1", "wv1", "bq1", "bk1", "bv1"],
           ["q1", "attn1", "s1"]),
    "b8": (ph_b8, ["xn1T", "g1", "s1", "w1f", "w2f", "b1f", "bs2", "be2"],
           ["p1"]),
    "b9": (ph_b9, ["x1", "p1"], ["out"]),
}


INTERNAL_WRITES = set()


def build_group(phase_names, feed_names):
    nc = bass.Bass("TRN2", target_bir_lowering=False)
    es = ExitStack()
    P = Prog()
    ctx = Ctx(nc, es, P)
    reads, writes = set(), set()
    for pn in phase_names:
        _, r, w = PHASES[pn]
        reads |= set(r)
        writes |= set(w)
    in_names = sorted((reads - writes) | (reads & feed_names))
    out_names = sorted(writes - INTERNAL_WRITES)
    for n in in_names:
        shape, dt = (WEIGHT_SPECS.get(n) or INTER_SPECS.get(n))
        ctx.hbm[n] = nc.dram_tensor(n, shape, dt, kind="ExternalInput").ap()
    for n in out_names:
        shape, dt = INTER_SPECS[n]
        ctx.hbm[n] = nc.dram_tensor(n, shape, dt, kind="ExternalOutput").ap()
    for n in sorted(writes & INTERNAL_WRITES):
        shape, dt = INTER_SPECS[n]
        ctx.hbm[n] = nc.dram_tensor(n, shape, dt, kind="Internal").ap()
    for pi, pn in enumerate(phase_names):
        fn, _, _ = PHASES[pn]
        ctx.ph = ExitStack()
        fn(P, ctx)
        ctx.ph.close()
        P.barrier()
    with nc.Block() as block:
        P.emit(nc, block, es)
    return nc, in_names, out_names


def run_groups(groups, shared, xs, keep=()):
    store = [dict(shared) for _ in range(NCORES)]
    for b in range(NCORES):
        store[b]["x"] = xs[b]
    for gi, gnames in enumerate(groups):
        nc, in_names, out_names = build_group(gnames, set(shared.keys()) | {"x"})
        in_maps = [{n: store[b][n] for n in in_names} for b in range(NCORES)]
        res = run_bass_kernel_spmd(nc, in_maps, core_ids=list(range(NCORES)))
        for b in range(NCORES):
            for n in out_names:
                store[b][n] = np.asarray(res.results[b][n])
    return store


ALL_PHASES = ["b0", "b1", "b2", "b4", "b5", "b6", "b7", "b8", "b9"]
FUSED = True
GROUPS = [ALL_PHASES] if FUSED else [[p] for p in ALL_PHASES]


def kernel(**inputs):
    global INTERNAL_WRITES
    if FUSED:
        INTERNAL_WRITES = set(INTER_SPECS) - {"out"}
    shared, xs = prep_host(inputs)
    store = run_groups(GROUPS, shared, xs)
    return np.stack([store[b]["out"] for b in range(NCORES)])


# version 03
# speedup vs baseline: 1.0745x; 1.0745x over previous
import numpy as np
from contextlib import ExitStack

import concourse.bass as bass
import concourse.mybir as mybir
from concourse.bass_utils import run_bass_kernel_spmd

F32 = mybir.dt.float32
BF16 = mybir.dt.bfloat16
ALU = mybir.AluOpType
AF = mybir.ActivationFunctionType
AX = mybir.AxisListType

NCORES = 8
T = 4096
D = 1024
NDC = D // 128
NTT = T // 128
TC = 512
NTCH = T // TC
NH = 16
HD = 64
NPAIR = 8
I0 = 938
I0P = 1024
E0 = 8
E1 = 31
LN_EPS = 1e-5

ENGS = ("pe", "act", "dve", "pool", "sync")
DEBUG_B2 = False


class Op:
    __slots__ = ("eng", "fn", "deps", "step", "cum")


class Bar:
    __slots__ = ("snap",)


class Prog:
    def __init__(self):
        self.items = {e: [] for e in ENGS}
        self.cum = {e: 0 for e in ENGS}

    def op(self, eng, fn, deps=(), step=1):
        o = Op()
        o.eng = eng
        o.fn = fn
        o.deps = tuple(d for d in deps if d is not None)
        o.step = step
        self.cum[eng] += step
        o.cum = self.cum[eng]
        self.items[eng].append(o)
        return o

    def dma(self, eng, fn, deps=()):
        return self.op(eng, fn, deps, step=16)

    def barrier(self):
        snap = dict(self.cum)
        for e in ENGS:
            b = Bar()
            b.snap = snap
            self.items[e].append(b)

    def emit(self, nc, block, es):
        sems = {e: es.enter_context(nc.semaphore(f"ch_{e}")) for e in ENGS}
        engmap = {"pe": block.tensor, "act": block.scalar, "dve": block.vector,
                  "pool": block.gpsimd, "sync": block.sync}
        self.barrier()
        for ename in ENGS:
            deco = engmap[ename]
            items = self.items[ename]

            def section(e, ename=ename, items=items):
                waited = {x: 0 for x in ENGS}
                for it in items:
                    if isinstance(it, Bar):
                        for oe in ENGS:
                            c = it.snap[oe]
                            if oe != ename and waited[oe] < c:
                                e.wait_ge(sems[oe], c)
                                waited[oe] = c
                        continue
                    needs = {}
                    for d in it.deps:
                        if needs.get(d.eng, 0) < d.cum:
                            needs[d.eng] = d.cum
                    for oe, c in needs.items():
                        if waited[oe] < c:
                            e.wait_ge(sems[oe], c)
                            waited[oe] = c
                    ins = it.fn(e)
                    ins.then_inc(sems[ename], it.step)
                    if it.eng == ename:
                        waited[ename] = max(waited[ename], 0)

            deco(section)


class Ctx:
    def __init__(self, nc, es, prog):
        self.nc = nc
        self.es = es
        self.P = prog
        self.hbm = {}
        self.ph = None
        self._consts = {}

    def sbuf(self, name, shape, dtype, scope=None):
        return (scope or self.ph).enter_context(
            self.nc.sbuf_tensor(name, shape, dtype))

    def psum(self, name, shape, dtype=F32, scope=None):
        return (scope or self.ph).enter_context(
            self.nc.psum_tensor(name, shape, dtype))

    def constant(self, value):
        value = float(value)
        if value in self._consts:
            return self._consts[value]
        t = self.es.enter_context(
            self.nc.sbuf_tensor(f"const_{len(self._consts)}", [128, 1], F32))
        op = self.P.op("dve", lambda e: e.memset(t[:], value))
        self.nc.const_aps.register(F32, value, t[:])
        self._consts[value] = op
        return op


def _perm_qk_cols():
    perm = []
    for h in range(NH):
        base = h * HD
        perm.extend(base + 2 * np.arange(32))
        perm.extend(base + 2 * np.arange(32) + 1)
    return np.array(perm)


def prep_host(inputs):
    g = {k: np.asarray(v) for k, v in inputs.items()}
    out = {}
    perm = _perm_qk_cols()
    out["wq"] = np.ascontiguousarray(g["L0_attn_w_q"][:, perm])
    out["wk"] = np.ascontiguousarray(g["L0_attn_w_kv"][:, :D][:, perm])
    out["wv"] = np.ascontiguousarray(g["L0_attn_w_kv"][:, D:])
    out["wo"] = np.ascontiguousarray(g["L0_attn_w_o"])
    out["lnw"] = np.ascontiguousarray(g["L0_attn_ln_w"].reshape(1, D))
    out["lnb"] = np.ascontiguousarray(g["L0_attn_ln_b"].reshape(1, D))
    out["theta"] = np.asarray(g["L0_attn_theta"], dtype=np.float32).reshape(1, 1)
    out["gw0T"] = np.ascontiguousarray(g["L0_ffn_gate_weight"].T)

    def prep_gu(gu):
        o = np.zeros((2 * I0P, D), dtype=np.float32)
        gp, up = gu[:I0], gu[I0:]
        for ic in range(8):
            lo, hi = ic * 128, min((ic + 1) * 128, I0)
            n = hi - lo
            if n > 0:
                o[ic * 256: ic * 256 + n] = gp[lo:hi]
                o[ic * 256 + 128: ic * 256 + 128 + n] = up[lo:hi]
        return o

    def prep_dn(dn):
        o = np.zeros((I0P, D), dtype=np.float32)
        o[:I0] = dn.T
        return o

    out["gu0"] = np.ascontiguousarray(np.stack(
        [prep_gu(g["L0_ffn_shared_gate_up"]).T] +
        [prep_gu(g["L0_ffn_experts_gate_up"][e]).T for e in range(E0)]))
    out["dn0"] = np.ascontiguousarray(np.stack(
        [prep_dn(g["L0_ffn_shared_down"])] +
        [prep_dn(g["L0_ffn_experts_down"][e]) for e in range(E0)]))
    out["wq1"] = np.ascontiguousarray(g["L1_attn_Wq"].T)
    out["wk1"] = np.ascontiguousarray(g["L1_attn_Wk"].T)
    out["wv1"] = np.ascontiguousarray(g["L1_attn_Wv"].T)
    out["bq1"] = np.ascontiguousarray(g["L1_attn_bq"].reshape(NDC, 128).T)
    out["bk1"] = np.ascontiguousarray(g["L1_attn_bk"].reshape(NDC, 128).T)
    out["bv1"] = np.ascontiguousarray(g["L1_attn_bv"].reshape(1, D))
    out["w1f"] = np.ascontiguousarray(np.stack(
        [g["L1_ffn_ws1"]] + [g["L1_ffn_We1"][e] for e in range(E1)]))
    out["w2f"] = np.ascontiguousarray(np.stack(
        [g["L1_ffn_ws2"]] + [g["L1_ffn_We2"][e] for e in range(E1)]))
    out["b1f"] = np.ascontiguousarray(np.stack(
        [g["L1_ffn_bs1"]] + [g["L1_ffn_Be1"][e] for e in range(E1)]
    ).reshape(32, NDC, 128).transpose(2, 0, 1))
    out["bs2"] = np.ascontiguousarray(g["L1_ffn_bs2"].reshape(NDC, 128).T)
    out["be2"] = np.ascontiguousarray(g["L1_ffn_Be2"])
    out["wr1"] = np.ascontiguousarray(g["L1_ffn_Wr"])
    out["br1"] = np.ascontiguousarray(g["L1_ffn_br"].reshape(1, E1))
    xs = [np.ascontiguousarray(g["L0_attn_x"][b]) for b in range(NCORES)]
    return out, xs


WEIGHT_SPECS = {
    "wq": ([D, D], F32), "wk": ([D, D], F32), "wv": ([D, D], F32),
    "wo": ([D, D], F32), "lnw": ([1, D], F32), "lnb": ([1, D], F32),
    "theta": ([1, 1], F32), "gw0T": ([D, E0], F32),
    "gu0": ([E0 + 1, D, 2 * I0P], F32), "dn0": ([E0 + 1, I0P, D], F32),
    "wq1": ([D, D], F32), "wk1": ([D, D], F32), "wv1": ([D, D], F32),
    "bq1": ([128, NDC], F32), "bk1": ([128, NDC], F32), "bv1": ([1, D], F32),
    "w1f": ([32, D, D], F32), "w2f": ([32, D, D], F32),
    "b1f": ([128, 32, NDC], F32), "bs2": ([128, NDC], F32),
    "be2": ([E1, D], F32), "wr1": ([D, E1], F32), "br1": ([1, E1], F32),
    "x": ([T, D], F32),
}
INTER_SPECS = {
    "xn0": ([T, D], BF16), "h0": ([T, D], BF16),
    "xn0T": ([D, T], BF16), "h0T": ([D, T], BF16),
    "cw0": ([E0, T], F32),
    "s0": ([D, T], BF16), "p0": ([D, T], BF16),
    "x1": ([T, D], F32), "xn1": ([T, D], BF16), "xn1T": ([D, T], BF16),
    "g1": ([E1, T], F32), "q1": ([D, T], BF16),
    "attn1": ([T, T], BF16), "s1": ([D, T], BF16), "p1": ([D, T], BF16),
    "out": ([T, D], F32),
    "dbg_q": ([128, T], BF16), "dbg_k": ([128, T], BF16),
    "dbg_C": ([128, T], F32), "dbg_S": ([128, T], F32),
    "dbg_y": ([128, NDC * 1024], F32), "dbg_h": ([128, 8 * TC], BF16),
    "dbg_v": ([128, 32 * 130], BF16), "dbg_o": ([128, NPAIR * T], BF16),
}


def ph_b0(P, ctx):
    x, xn0, h0 = ctx.hbm["x"], ctx.hbm["xn0"], ctx.hbm["h0"]
    ceps = ctx.constant(LN_EPS)
    cm1 = ctx.constant(-1.0)
    cmh = ctx.constant(-0.5)
    lnw_h, lnb_h = ctx.hbm["lnw"], ctx.hbm["lnb"]
    xbuf = [ctx.sbuf(f"b0_x{j}", [128, D], F32) for j in range(2)]
    sq = [ctx.sbuf(f"b0_sq{j}", [128, D], F32) for j in range(2)]
    xnf = [ctx.sbuf(f"b0_xn{j}", [128, D], F32) for j in range(2)]
    z2f = [ctx.sbuf(f"b0_z2{j}", [128, D], F32) for j in range(2)]
    hbf = [ctx.sbuf(f"b0_h{j}", [128, D], BF16) for j in range(2)]
    xnbf = [ctx.sbuf(f"b0_xnb{j}", [128, D], BF16) for j in range(2)]
    st = [ctx.sbuf(f"b0_st{j}", [128, 8], F32) for j in range(2)]
    lnw_bc = ctx.sbuf("b0_lnwbc", [128, D], F32)
    lnb_bc = ctx.sbuf("b0_lnbbc", [128, D], F32)
    lnrow = ctx.sbuf("b0_lnrow", [1, 2 * D], F32)
    ones1 = ctx.sbuf("b0_ones", [1, 128], F32)
    pb = [ctx.psum(f"b0_pb{i}", [128, 512], F32) for i in range(4)]

    ldr = P.dma("sync", lambda e: e.dma_start(out=lnrow[:, 0:D], in_=lnw_h[:]))
    ldr2 = P.dma("sync", lambda e: e.dma_start(out=lnrow[:, D:2 * D], in_=lnb_h[:]))
    on = P.op("dve", lambda e: e.memset(ones1[:], 1.0))
    mms, cps = [], []
    for k in range(4):
        src = lnrow[:, (k % 2) * 512 + (k // 2) * D: (k % 2) * 512 + (k // 2) * D + 512]
        mms.append(P.op("pe", lambda e, pbk=pb[k], s=src: e.matmul(
            pbk[:], ones1[:], s, start=True, stop=True), deps=[ldr, ldr2, on]))
    for k in range(4):
        dst = (lnw_bc if k < 2 else lnb_bc)[:, (k % 2) * 512:(k % 2) * 512 + 512]
        cps.append(P.op("dve", lambda e, d=dst, pbk=pb[k]: e.tensor_copy(d, pbk[:]),
                        deps=[mms[k]]))
    C = 1.0 / D
    prev = {}
    for i in range(NTT):
        j = i % 2
        xb, sqb, xnb, z2b, hb, xnbb, stb = (xbuf[j], sq[j], xnf[j], z2f[j],
                                            hbf[j], xnbf[j], st[j])
        sx, sx2 = stb[:, 0:1], stb[:, 1:2]
        mu, nv, rstd = stb[:, 2:3], stb[:, 3:4], stb[:, 4:5]
        mu2, nv2, rstd2 = stb[:, 5:6], stb[:, 6:7], stb[:, 7:8]
        pv = prev.get(j, {})
        ld = P.dma("sync", lambda e, i=i, xb=xb: e.dma_start(
            out=xb[:], in_=x[i * 128:(i + 1) * 128, :]),
            deps=[pv.get("xn"), pv.get("sq1")])
        summ = P.op("dve", lambda e, sx=sx, xb=xb: e.reduce_sum(sx, xb[:], axis=AX.X),
                    deps=[ld, pv.get("nv")])
        sq1 = P.op("act", lambda e, sqb=sqb, xb=xb, sx2=sx2: e.activation(
            sqb[:], xb[:], AF.Square, bias=0.0, scale=1.0, accum_out=sx2),
            deps=[ld, pv.get("t1"), pv.get("nv")])
        ex2 = P.op("dve", lambda e, sx2=sx2: e.tensor_scalar(
            sx2, sx2, C, None, ALU.mult), deps=[sq1])
        muo = P.op("dve", lambda e, mu=mu, sx=sx: e.tensor_scalar(
            mu, sx, C, None, ALU.mult), deps=[summ])
        nvo = P.op("dve", lambda e, nv=nv, mu=mu, sx2=sx2: e.scalar_tensor_tensor(
            nv, mu, mu, sx2, ALU.mult, ALU.subtract), deps=[muo, ex2])
        lnv = P.op("act", lambda e, nv=nv: e.activation(
            nv, nv, AF.Ln, bias=LN_EPS, scale=-1.0), deps=[nvo, ceps, cm1])
        rst = P.op("act", lambda e, rstd=rstd, nv=nv: e.activation(
            rstd, nv, AF.Exp, bias=0.0, scale=-0.5), deps=[lnv, cmh])
        xn = P.op("dve", lambda e, xnb=xnb, xb=xb, mu=mu, rstd=rstd: e.tensor_scalar(
            xnb[:], xb[:], mu, rstd, ALU.subtract, ALU.mult),
            deps=[muo, rst, pv.get("z2"), pv.get("xnbf"), pv.get("sum2"), pv.get("sq2")])
        sum2 = P.op("dve", lambda e, mu2=mu2, xnb=xnb: e.reduce_sum(
            mu2, xnb[:], axis=AX.X), deps=[xn])
        sq2 = P.op("act", lambda e, sqb=sqb, xnb=xnb, nv2=nv2: e.activation(
            sqb[:], xnb[:], AF.Square, bias=0.0, scale=1.0, accum_out=nv2),
            deps=[xn, sq1])
        ex2b = P.op("dve", lambda e, nv2=nv2: e.tensor_scalar(
            nv2, nv2, C, None, ALU.mult), deps=[sq2])
        mu2o = P.op("dve", lambda e, mu2=mu2: e.tensor_scalar(
            mu2, mu2, C, None, ALU.mult), deps=[sum2])
        nv2o = P.op("dve", lambda e, nv2=nv2, mu2=mu2: e.scalar_tensor_tensor(
            nv2, mu2, mu2, nv2, ALU.mult, ALU.subtract), deps=[mu2o, ex2b])
        lnv2 = P.op("act", lambda e, nv2=nv2: e.activation(
            nv2, nv2, AF.Ln, bias=LN_EPS, scale=-1.0), deps=[nv2o, ceps, cm1])
        rst2 = P.op("act", lambda e, rstd2=rstd2, nv2=nv2: e.activation(
            rstd2, nv2, AF.Exp, bias=0.0, scale=-0.5), deps=[lnv2, cmh])
        z2 = P.op("dve", lambda e, z2b=z2b, xnb=xnb, mu2=mu2, rstd2=rstd2:
                  e.tensor_scalar(z2b[:], xnb[:], mu2, rstd2, ALU.subtract, ALU.mult),
                  deps=[mu2o, rst2, pv.get("t1")])
        t1 = P.op("dve", lambda e, sqb=sqb, z2b=z2b: e.tensor_mul(
            sqb[:], z2b[:], lnw_bc[:]), deps=[z2, sq2] + cps[:2])
        h = P.op("dve", lambda e, hb=hb, sqb=sqb: e.tensor_add(
            hb[:], sqb[:], lnb_bc[:]), deps=[t1, pv.get("sth")] + cps[2:])
        xnbf_o = P.op("act", lambda e, xnbb=xnbb, xnb=xnb: e.activation(
            xnbb[:], xnb[:], AF.Copy, bias=0.0, scale=1.0),
            deps=[xn, pv.get("stx")])
        sth = P.dma("pool", lambda e, i=i, hb=hb: e.dma_start(
            out=h0[i * 128:(i + 1) * 128, :], in_=hb[:]), deps=[h])
        stx = P.dma("pool", lambda e, i=i, xnbb=xnbb: e.dma_start(
            out=xn0[i * 128:(i + 1) * 128, :], in_=xnbb[:]), deps=[xnbf_o])
        prev[j] = dict(xn=xn, sq1=sq1, z2=z2, t1=t1, sth=sth, stx=stx,
                       xnbf=xnbf_o, nv=nv2o, sum2=sum2, sq2=sq2)


def ph_b1(P, ctx):
    xn0, h0 = ctx.hbm["xn0"], ctx.hbm["h0"]
    xn0T, h0T = ctx.hbm["xn0T"], ctx.hbm["h0T"]
    gw0T_h, cw0 = ctx.hbm["gw0T"], ctx.hbm["cw0"]

    xt = [[ctx.sbuf(f"b1_xt{j}_{dc}", [128, TC], BF16) for dc in range(NDC)]
          for j in range(2)]
    ht = [[ctx.sbuf(f"b1_ht{j}_{dc}", [128, TC], BF16) for dc in range(NDC)]
          for j in range(2)]
    gw = ctx.sbuf("b1_gw", [128, NDC * E0], BF16)
    lg = [ctx.sbuf(f"b1_lg{j}", [128, 4 * E0], F32) for j in range(2)]
    sc2 = [ctx.sbuf(f"b1_sc2{j}", [128, 4 * E0], F32) for j in range(2)]
    tmp = [ctx.sbuf(f"b1_tmp{j}", [128, 4 * E0], F32) for j in range(2)]
    msk = [ctx.sbuf(f"b1_msk{j}", [128, 4 * E0], F32) for j in range(2)]
    cwt = [ctx.sbuf(f"b1_cwt{j}", [128, 4 * E0], F32) for j in range(2)]
    r1 = [ctx.sbuf(f"b1_r1{j}", [128, 4], F32) for j in range(2)]
    r2 = [ctx.sbuf(f"b1_r2{j}", [128, 4], F32) for j in range(2)]
    cwfm = ctx.sbuf("b1_cwfm", [E0, T], F32)
    ident = ctx.sbuf("b1_id", [128, 128], F32)
    ia = ctx.sbuf("b1_ia", [128, 128], F32)
    plg = [ctx.psum(f"b1_plg{j}", [128, E0], F32) for j in range(2)]
    ptr = [ctx.psum(f"b1_ptr{j}", [E0, 128], F32) for j in range(2)]

    i1 = P.op("pool", lambda e: e.iota(ia[:], pattern=[[0, 128]], base=0,
              channel_multiplier=1, allow_small_or_imprecise_dtypes=True))
    i2 = P.op("pool", lambda e: e.iota(ident[:], pattern=[[1, 128]], base=0,
              channel_multiplier=0, allow_small_or_imprecise_dtypes=True))
    idt = P.op("dve", lambda e: e.tensor_tensor(ident[:], ia[:], ident[:],
               ALU.is_equal), deps=[i1, i2])
    ldw = [P.dma("pool", lambda e, dc=dc: e.dma_start(
        out=gw[:, dc * E0:(dc + 1) * E0],
        in_=gw0T_h[dc * 128:(dc + 1) * 128, :])) for dc in range(NDC)]

    prev = {}
    evcw_by_k = {}
    ev_by_k = {}
    for i in range(NTCH):
        j = i % 2
        pv = prev.get(j, {})
        lds, ldsh = [], []
        for dc in range(NDC):
            lds.append(P.dma("sync", lambda e, i=i, dc=dc, t=xt[j][dc]: e.dma_start(
                out=t[:], in_=xn0[i * TC:(i + 1) * TC, dc * 128:(dc + 1) * 128],
                transpose=True), deps=pv.get("xt_cons", [])))
            ldsh.append(P.dma("sync", lambda e, i=i, dc=dc, t=ht[j][dc]: e.dma_start(
                out=t[:], in_=h0[i * TC:(i + 1) * TC, dc * 128:(dc + 1) * 128],
                transpose=True), deps=pv.get("ht_cons", [])))
        stws = []
        for dc in range(NDC):
            stws.append(P.dma("pool", lambda e, i=i, dc=dc, t=xt[j][dc]: e.dma_start(
                out=xn0T[dc * 128:(dc + 1) * 128, i * TC:(i + 1) * TC], in_=t[:]),
                deps=[lds[dc]]))
            stws.append(P.dma("pool", lambda e, i=i, dc=dc, t=ht[j][dc]: e.dma_start(
                out=h0T[dc * 128:(dc + 1) * 128, i * TC:(i + 1) * TC], in_=t[:]),
                deps=[ldsh[dc]]))
        mmops, evops = [], []
        for tt in range(4):
            k = i * 4 + tt
            jj = k % 2

            def mmfn(e, jj=jj, j=j, tt=tt):
                last = None
                for dc in range(NDC):
                    last = e.matmul(plg[jj][:],
                                    xt[j][dc][:, tt * 128:(tt + 1) * 128],
                                    gw[:, dc * E0:(dc + 1) * E0],
                                    start=(dc == 0), stop=(dc == NDC - 1))
                return last
            mm = P.op("pe", mmfn,
                      deps=lds + ldw + [ev_by_k.get(k - 2), pv.get("ev%d" % tt)])
            ev = P.op("dve", lambda e, j=j, tt=tt, jj=jj: e.tensor_copy(
                lg[j][:, tt * E0:(tt + 1) * E0], plg[jj][:]),
                deps=[mm, pv.get("chain_last")])
            ev_by_k[k] = ev
            mmops.append(mm)
            evops.append(ev)
        lgv = lg[j][:].rearrange("p (s e) -> p s e", e=E0)
        sc2v = sc2[j][:].rearrange("p (s e) -> p s e", e=E0)
        tmpv = tmp[j][:].rearrange("p (s e) -> p s e", e=E0)
        mskv = msk[j][:].rearrange("p (s e) -> p s e", e=E0)
        cwtv = cwt[j][:].rearrange("p (s e) -> p s e", e=E0)
        r1b = r1[j][:].rearrange("p (s o) -> p s o", o=1).to_broadcast([128, 4, E0])
        r2b = r2[j][:].rearrange("p (s o) -> p s o", o=1).to_broadcast([128, 4, E0])
        c1 = P.op("dve", lambda e, lgv=lgv, r=r1[j]: e.tensor_reduce(
            r[:].rearrange("p (s o) -> p s o", o=1), lgv, axis=AX.X, op=ALU.max, negate=True),
            deps=evops + [pv.get("c_last")])
        c2 = P.op("dve", lambda e, sc2v=sc2v, lgv=lgv, r1b=r1b: e.tensor_tensor(
            sc2v, lgv, r1b, ALU.add), deps=[c1] + [pv.get("c2w")])
        c3 = P.op("act", lambda e, s=sc2[j]: e.activation(
            s[:], s[:], AF.Exp, bias=0.0, scale=1.0), deps=[c2])
        c4 = P.op("dve", lambda e, sc2v=sc2v, r=r1[j]: e.tensor_reduce(
            r[:].rearrange("p (s o) -> p s o", o=1), sc2v, axis=AX.X, op=ALU.add), deps=[c3])
        c5 = P.op("dve", lambda e, r=r1[j]: e.reciprocal(r[:], r[:]), deps=[c4])
        c6 = P.op("dve", lambda e, sc2v=sc2v, r1b=r1b: e.tensor_tensor(
            sc2v, sc2v, r1b, ALU.mult), deps=[c5])
        c7 = P.op("dve", lambda e, sc2v=sc2v, r=r1[j]: e.tensor_reduce(
            r[:].rearrange("p (s o) -> p s o", o=1), sc2v, axis=AX.X, op=ALU.max), deps=[c6])
        c8 = P.op("dve", lambda e, mskv=mskv, sc2v=sc2v, r1b=r1b: e.tensor_tensor(
            mskv, sc2v, r1b, ALU.is_equal), deps=[c7])
        c9 = P.op("dve", lambda e, tmpv=tmpv, sc2v=sc2v, mskv=mskv: e.tensor_tensor(
            tmpv, sc2v, mskv, ALU.mult), deps=[c8])
        c10 = P.op("dve", lambda e, tmpv=tmpv, sc2v=sc2v: e.tensor_tensor(
            tmpv, sc2v, tmpv, ALU.subtract), deps=[c9])
        c11 = P.op("dve", lambda e, tmpv=tmpv, r=r2[j]: e.tensor_reduce(
            r[:].rearrange("p (s o) -> p s o", o=1), tmpv, axis=AX.X, op=ALU.max), deps=[c10])
        c12 = P.op("dve", lambda e, cwtv=cwtv, tmpv=tmpv, r2b=r2b: e.tensor_tensor(
            cwtv, tmpv, r2b, ALU.is_equal), deps=[c11])
        c13 = P.op("dve", lambda e, mskv=mskv, cwtv=cwtv: e.tensor_tensor(
            mskv, mskv, cwtv, ALU.add), deps=[c12])
        c14 = P.op("dve", lambda e, r1=r1[j], r2=r2[j]: e.tensor_tensor(
            r1[:], r1[:], r2[:], ALU.add), deps=[c11, c8])
        c15 = P.op("dve", lambda e, r1=r1[j]: e.tensor_scalar(
            r1[:], r1[:], 1e-8, None, ALU.add), deps=[c14])
        c16 = P.op("dve", lambda e, r1=r1[j]: e.reciprocal(r1[:], r1[:]), deps=[c15])
        c17 = P.op("dve", lambda e, cwtv=cwtv, sc2v=sc2v, mskv=mskv: e.tensor_tensor(
            cwtv, sc2v, mskv, ALU.mult), deps=[c13, c12])
        c18 = P.op("dve", lambda e, cwtv=cwtv, r1b=r1b: e.tensor_tensor(
            cwtv, cwtv, r1b, ALU.mult), deps=[c16, c17])
        tpev = []
        for tt in range(4):
            k = i * 4 + tt
            jj = k % 2
            tpo = P.op("pe", lambda e, j=j, tt=tt, jj=jj: e.transpose(
                ptr[jj][:], cwt[j][:, tt * E0:(tt + 1) * E0], ident[:]),
                deps=[c18, idt, evcw_by_k.get(k - 2)])
            evc = P.op("dve", lambda e, i=i, tt=tt, jj=jj: e.tensor_copy(
                cwfm[:, i * TC + tt * 128: i * TC + (tt + 1) * 128], ptr[jj][:]),
                deps=[tpo])
            evcw_by_k[k] = evc
            tpev.append(evc)
        prev[j] = {
            "xt_cons": mmops + [stws[0]] + stws[::2],
            "ht_cons": stws[1::2],
            "ev0": evops[0], "ev1": evops[1], "ev2": evops[2], "ev3": evops[3],
            "chain_last": c18, "c_last": c18, "c2w": c18,
        }
    stcw = P.dma("pool", lambda e: e.dma_start(out=cw0[:], in_=cwfm[:]),
                 deps=list(evcw_by_k.values()))


def ph_b2(P, ctx):
    h0T = ctx.hbm["h0T"]
    wq_h, wk_h, wv_h, wo_h = (ctx.hbm["wq"], ctx.hbm["wk"], ctx.hbm["wv"],
                              ctx.hbm["wo"])
    theta_h, s0 = ctx.hbm["theta"], ctx.hbm["s0"]

    TOP = 2595.0 * np.log10(np.float32(1.0 + 4000.0 / 200.0))
    CS = float(0.125 / np.sqrt(2.0))
    BS = float(1.0 / np.sqrt(2.0))
    INV2PI = float(np.float32(1.0 / (2 * np.pi)))
    MAGIC = 12582912.0
    CW1 = 6.25
    CW2 = 0.032958984375
    CW3 = 0.000226322804586232
    MEL_C = float(np.log(10.0) / 2595.0 * (TOP / 31.0))
    cpre = [ctx.constant(v) for v in (MEL_C, CS, BS)]

    fcol = ctx.sbuf("b2_fcol", [32, 1], F32)
    thc = ctx.sbuf("b2_thc", [32, 1], F32)
    throw = ctx.sbuf("b2_throw", [1, 1], F32)
    ones32 = ctx.sbuf("b2_ones32", [1, 32], F32)
    ones64 = ctx.sbuf("b2_ones64", [1, 64], F32)
    onescol = ctx.sbuf("b2_onescol", [128, 1], BF16)
    C128 = ctx.sbuf("b2_C128", [128, T], F32)
    S128 = ctx.sbuf("b2_S128", [128, T], F32)
    o_all = ctx.sbuf("b2_oall", [128, NPAIR * T], BF16)
    v_all = ctx.sbuf("b2_vall", [128, 32 * 130], BF16)
    vv = v_all[:].rearrange("p (k c) -> p k c", c=130)
    qbf = ctx.sbuf("b2_qbf", [128, T], BF16)
    kbf = ctx.sbuf("b2_kbf", [128, T], BF16)
    qt32 = ctx.sbuf("b2_qt", [128, TC], F32)
    qsw = ctx.sbuf("b2_qsw", [128, TC], F32)
    qcs = ctx.sbuf("b2_qcs", [128, TC], F32)
    ttb2 = [ctx.sbuf(f"b2_tt{j}", [128, 2 * TC], BF16) for j in range(2)]
    svb = [ctx.sbuf(f"b2_sv{h}", [128, 1], F32) for h in range(2)]
    osb = [ctx.sbuf(f"b2_osb{j}", [128, TC], F32) for j in range(2)]
    drow = [ctx.sbuf(f"b2_drow{j}", [1, TC], F32) for j in range(2)]
    bcsb = [ctx.sbuf(f"b2_bc{j}", [64, TC], F32) for j in range(2)]
    _tabscope = ExitStack()
    ang = ctx.sbuf("b2_ang", [32, T], F32, scope=_tabscope)
    kk = ctx.sbuf("b2_kk", [32, T], F32, scope=_tabscope)
    red = ctx.sbuf("b2_red", [32, T], F32, scope=_tabscope)
    sin32 = ang
    cos32 = kk
    pth = ctx.psum("b2_pth", [32, 1], F32, scope=_tabscope)

    _f32 = np.float32
    _top = _f32(2595.0) * np.log10(_f32(21.0))
    _lin = np.linspace(_f32(0.0), _top, 32, dtype=_f32)
    _mel = np.power(_f32(10.0), _lin / _f32(2595.0), dtype=_f32) - _f32(1.0)
    FB = (_f32(200.0) * _mel / _f32(1000.0)).astype(_f32)
    fbrow = ctx.sbuf("b2_fbrow", [1, 32], F32, scope=_tabscope)
    one11 = ctx.sbuf("b2_one11", [1, 1], F32, scope=_tabscope)
    ldth = P.dma("sync", lambda e: e.dma_start(out=throw[:], in_=theta_h[:]))
    o32 = P.op("dve", lambda e: e.memset(ones32[:], 1.0))
    o64 = P.op("dve", lambda e: e.memset(ones64[:], 1.0))
    ocol = P.op("dve", lambda e: e.memset(onescol[:], 1.0))
    o11 = P.op("dve", lambda e: e.memset(one11[:], 1.0))
    fbsets = [P.op("dve", lambda e, i=i: e.memset(fbrow[:, i:i + 1],
                   float(FB[i]))) for i in range(32)]
    t220 = P.op("dve", lambda e: e.tensor_scalar(
        throw[:], throw[:], float(np.float32(1.0) / np.float32(220.0)),
        None, ALU.mult), deps=[ldth])
    mth = P.op("pe", lambda e: e.matmul(pth[:], ones32[:], throw[:],
               start=True, stop=True), deps=[t220, o32])
    evth = P.op("dve", lambda e: e.tensor_copy(thc[:], pth[:]), deps=[mth])
    mfb = P.op("pe", lambda e: e.transpose(pth[:], fbrow[:], one11[:]),
               deps=fbsets + [o11, evth])
    evfb = P.op("dve", lambda e: e.tensor_copy(fcol[:], pth[:]), deps=[mfb])
    frq = P.op("dve", lambda e: e.tensor_mul(fcol[:], fcol[:], thc[:]),
               deps=[evfb, evth])
    iot = P.op("pool", lambda e: e.iota(ang[:], pattern=[[1, T]], base=0,
               channel_multiplier=0, allow_small_or_imprecise_dtypes=True))
    a1 = P.op("dve", lambda e: e.tensor_scalar(ang[:], ang[:], fcol[:, 0:1],
              None, ALU.mult), deps=[iot, frq])
    k1 = P.op("dve", lambda e: e.tensor_scalar(kk[:], ang[:], INV2PI, MAGIC,
              ALU.mult, ALU.add), deps=[a1])
    k2 = P.op("dve", lambda e: e.tensor_scalar(kk[:], kk[:], MAGIC, None,
              ALU.subtract), deps=[k1])
    rr1 = P.op("dve", lambda e: e.scalar_tensor_tensor(red[:], kk[:], -CW1,
               ang[:], ALU.mult, ALU.add), deps=[k2])
    rr2 = P.op("dve", lambda e: e.scalar_tensor_tensor(red[:], kk[:], -CW2,
               red[:], ALU.mult, ALU.add), deps=[rr1])
    rr3 = P.op("dve", lambda e: e.scalar_tensor_tensor(red[:], kk[:], -CW3,
               red[:], ALU.mult, ALU.add), deps=[rr2])
    sn = P.op("act", lambda e: e.activation(sin32[:], red[:], AF.Sin,
              bias=0.0, scale=1.0), deps=[rr3])
    w1 = P.op("dve", lambda e: e.tensor_scalar(red[:], red[:],
              float(np.pi / 2), None, ALU.add), deps=[sn])
    w2 = P.op("dve", lambda e: e.tensor_scalar(kk[:], red[:], float(np.pi),
              None, ALU.is_gt), deps=[w1])
    w3 = P.op("dve", lambda e: e.scalar_tensor_tensor(red[:], kk[:],
              float(-2 * np.pi), red[:], ALU.mult, ALU.add), deps=[w2])
    cs_ = P.op("act", lambda e: e.activation(cos32[:], red[:], AF.Sin,
               bias=0.0, scale=1.0), deps=[w3])
    tabops = []
    for blk in range(4):
        sl = slice(blk * 32, (blk + 1) * 32)
        eng = "dve" if blk % 2 == 0 else "pool"
        tabops.append(P.op(eng, lambda e, sl=sl: e.tensor_copy(
            C128[sl, :], cos32[:]), deps=[cs_]))
        if blk % 2 == 0:
            tabops.append(P.op(eng, lambda e, sl=sl: e.tensor_scalar(
                S128[sl, :], sin32[:], -1.0, None, ALU.mult), deps=[sn, cs_]))
        else:
            tabops.append(P.op(eng, lambda e, sl=sl: e.tensor_copy(
                S128[sl, :], sin32[:]), deps=[sn, cs_]))


    mo1 = P.op("dve", lambda e: e.memset(vv[:, :, 64:65], 1.0))
    mo2 = P.op("dve", lambda e: e.memset(vv[:, :, 129:130], 1.0))
    _tabscope.close()
    _qkvscope = ExitStack()
    wqp = [ctx.sbuf(f"b2_wqp{j}", [128, D], BF16, scope=_qkvscope) for j in range(2)]
    wkp = [ctx.sbuf(f"b2_wkp{j}", [128, D], BF16, scope=_qkvscope) for j in range(2)]
    wvp = [ctx.sbuf(f"b2_wvp{j}", [128, D], BF16, scope=_qkvscope) for j in range(2)]
    hbuf = [ctx.sbuf(f"b2_h{j}", [128, NDC * TC], BF16, scope=_qkvscope)
            for j in range(2)]
    pqkv = ctx.psum("b2_pqkv", [128, 2 * TC], F32, scope=_qkvscope)
    ps2 = [ctx.psum(f"b2_ps{j}", [128, 2 * TC], F32, scope=_qkvscope)
           for j in range(2)]
    po = ctx.psum("b2_po", [65, TC], F32, scope=_qkvscope)
    pmix = ctx.psum("b2_pmix", [128, TC], F32, scope=_qkvscope)
    pa = pqkv[:, 0:TC]
    pb = pqkv[:, TC:2 * TC]
    pv = pqkv[:, 0:128]
    psv = pmix[0:65, 0:1]
    pbc = pmix[0:64, :]

    tabdep = tabops + [cs_, sn]

    def wsrc(w, p):
        return w[:, p * 128:(p + 1) * 128].rearrange("(c p) f -> p c f", p=128)

    w_cons = {("q", 0): [], ("q", 1): [], ("k", 0): [], ("k", 1): [],
              ("v", 0): [], ("v", 1): []}
    h_cons = {0: [], 1: []}
    ep_by_grp = {}
    av_by_grp = {}
    av_by_kt = {}
    last_qev = None
    last_kev = None
    last_vev = None
    last_sve = None
    last_smm = None
    last_oev = None
    pending = None
    pend_d5 = {}
    pend_d4 = [None]
    prev_d5 = pend_d5
    sv_ref = {}

    def emit_div_epilogue(st8):
        pp, hh, qq, jo, oev = st8
        svop = sv_ref[hh]
        d1 = P.op("dve", lambda e, jo=jo, hh=hh: e.tensor_scalar(
            drow[jo][:], osb[jo][64:65, :], svb[hh][64:65, 0:1], None,
            ALU.add), deps=[oev, svop, pend_d4[0]])
        d2 = P.op("dve", lambda e, jo=jo: e.reciprocal(drow[jo][:],
                  drow[jo][:]), deps=[d1])
        d3 = P.op("pe", lambda e, jo=jo: e.matmul(
            pbc, ones64[:], drow[jo][:], start=True, stop=True),
            deps=[d2, o64, pend_d4[0]])
        d4 = P.op("dve", lambda e, jo=jo: e.tensor_copy(bcsb[jo][:], pbc),
                  deps=[d3, pend_d5.get(jo)])
        d5 = P.op("dve", lambda e, jo=jo, hh=hh, pp=pp, qq=qq:
                  e.scalar_tensor_tensor(
                      o_all[hh * 64:(hh + 1) * 64,
                            pp * T + qq * TC: pp * T + (qq + 1) * TC],
                      osb[jo][0:64, :], svb[hh][0:64, 0:1], bcsb[jo][:],
                      ALU.add, ALU.mult), deps=[d4, oev, svop])
        pend_d4[0] = d4
        pend_d5[jo] = d5

    for p in range(NPAIR):
        jw = p % 2
        extra = tabdep if p < 2 else []
        lwq = P.dma("pool", lambda e, p=p, t=wqp[jw]: e.dma_start(
            out=t[:].rearrange("p (c f) -> p c f", f=128), in_=wsrc(wq_h, p)),
            deps=w_cons[("q", jw)] + extra)
        lwk = P.dma("pool", lambda e, p=p, t=wkp[jw]: e.dma_start(
            out=t[:].rearrange("p (c f) -> p c f", f=128), in_=wsrc(wk_h, p)),
            deps=w_cons[("k", jw)] + extra)
        lwv = P.dma("pool", lambda e, p=p, t=wvp[jw]: e.dma_start(
            out=t[:].rearrange("p (c f) -> p c f", f=128), in_=wsrc(wv_h, p)),
            deps=w_cons[("v", jw)] + extra)
        qmms, kmms, vmms, vevs, ropes = [], [], [], [], []
        for tc in range(NTCH):
            j = tc % 2
            hb = hbuf[j]
            ldh = P.dma("sync", lambda e, tc=tc, hb=hb: e.dma_start(
                out=hb[:].rearrange("p (c t) -> p c t", t=TC),
                in_=h0T[:, tc * TC:(tc + 1) * TC].rearrange(
                    "(c p) t -> p c t", p=128)),
                deps=h_cons[j] + (tabdep if p == 0 and tc < 2 else []))
            h_cons[j] = []

            def qfn(e, w=wqp[jw], hb=hb):
                last = None
                for dc in range(NDC):
                    last = e.matmul(pa, w[:, dc * 128:(dc + 1) * 128],
                                    hb[:, dc * TC:(dc + 1) * TC],
                                    start=(dc == 0), stop=(dc == NDC - 1))
                return last
            qmm = P.op("pe", qfn, deps=[ldh, lwq, last_qev, last_vev])
            qev = P.op("dve", lambda e: e.tensor_copy(qt32[:], pa),
                       deps=[qmm, ropes[-1] if ropes else None])
            sw_ops = []
            for blk in range(4):
                d_, s_ = slice(blk * 32, (blk + 1) * 32),                     slice((blk ^ 1) * 32, ((blk ^ 1) + 1) * 32)
                sw_ops.append(P.op("pool" if blk % 2 else "dve",
                              lambda e, d=d_, s=s_: e.tensor_copy(
                                  qsw[d, :], qt32[s, :]), deps=[qev]))
            qc1 = P.op("pool", lambda e, tc=tc: e.tensor_mul(
                qcs[:], qt32[:], C128[:, tc * TC:(tc + 1) * TC]),
                deps=[qev] + tabops)
            qc2 = P.op("dve", lambda e, tc=tc: e.tensor_mul(
                qsw[:], qsw[:], S128[:, tc * TC:(tc + 1) * TC]),
                deps=sw_ops + tabops)
            qc3 = P.op("dve", lambda e, tc=tc: e.tensor_add(
                qbf[:, tc * TC:(tc + 1) * TC], qcs[:], qsw[:]),
                deps=[qc1, qc2, last_smm])
            last_qev = qev

            def kfn(e, w=wkp[jw], hb=hb):
                last = None
                for dc in range(NDC):
                    last = e.matmul(pb, w[:, dc * 128:(dc + 1) * 128],
                                    hb[:, dc * TC:(dc + 1) * TC],
                                    start=(dc == 0), stop=(dc == NDC - 1))
                return last
            kmm = P.op("pe", kfn, deps=[ldh, lwk, last_kev])
            kev = P.op("dve", lambda e: e.tensor_copy(qt32[:], pb),
                       deps=[kmm, qc1, qc3])
            ksw_ops = []
            for blk in range(4):
                d_, s_ = slice(blk * 32, (blk + 1) * 32),                     slice((blk ^ 1) * 32, ((blk ^ 1) + 1) * 32)
                ksw_ops.append(P.op("pool" if blk % 2 else "dve",
                               lambda e, d=d_, s=s_: e.tensor_copy(
                                   qsw[d, :], qt32[s, :]), deps=[kev, qc2]))
            kc1 = P.op("pool", lambda e, tc=tc: e.tensor_mul(
                qcs[:], qt32[:], C128[:, tc * TC:(tc + 1) * TC]),
                deps=[kev, qc3])
            kc2 = P.op("dve", lambda e, tc=tc: e.tensor_mul(
                qsw[:], qsw[:], S128[:, tc * TC:(tc + 1) * TC]), deps=ksw_ops)
            kc3 = P.op("dve", lambda e, tc=tc: e.tensor_add(
                kbf[:, tc * TC:(tc + 1) * TC], qcs[:], qsw[:]),
                deps=[kc1, kc2, last_smm])
            last_kev = kev
            ropes.extend([qc3, kc3])
            qmms.append(qmm)
            kmms.append(kmm)

            for tt in range(4):
                kt = tc * 4 + tt
                def vfn(e, w=wvp[jw], hb=hb, tt=tt):
                    last = None
                    for dc in range(NDC):
                        last = e.matmul(
                            pv, hb[:, dc * TC + tt * 128:
                                      dc * TC + (tt + 1) * 128],
                            w[:, dc * 128:(dc + 1) * 128],
                            start=(dc == 0), stop=(dc == NDC - 1))
                    return last
                vmm = P.op("pe", vfn, deps=[ldh, lwv, last_vev, qev])
                vev = P.op("dve", lambda e, kt=kt: e.tensor_copy(
                    vv[:, kt, 0:130].rearrange("p (b c) -> p b c", b=2, c=65)
                    [:, :, 0:64],
                    pv.rearrange("p (b c) -> p b c", b=2, c=64)),
                    deps=[vmm] + av_by_kt.get(kt, []))
                av_by_kt[kt] = []
                vmms.append(vmm)
                vevs.append(vev)
                last_vev = vev
            h_cons[j] = [qmm, kmm] + vmms[-4:]
        if "dbg_q" in ctx.hbm and p == 0:
            P.dma("sync", lambda e: e.dma_start(out=ctx.hbm["dbg_C"][:],
                  in_=C128[:]), deps=tabdep)
            P.dma("sync", lambda e: e.dma_start(out=ctx.hbm["dbg_S"][:],
                  in_=S128[:]), deps=tabdep)
            P.dma("sync", lambda e: e.dma_start(out=ctx.hbm["dbg_q"][:],
                  in_=qbf[:]), deps=ropes)
            P.dma("sync", lambda e: e.dma_start(out=ctx.hbm["dbg_k"][:],
                  in_=kbf[:]), deps=ropes)
            P.dma("sync", lambda e: e.dma_start(out=ctx.hbm["dbg_v"][:],
                  in_=v_all[:]), deps=vevs)
        w_cons[("q", jw)] = [qmms[-1]]
        w_cons[("k", jw)] = [kmms[-1]]
        w_cons[("v", jw)] = [vmms[-1]]
        qkv_done_pair = ropes + vevs

        svops = []
        for h in range(2):
            def svfn(e, h=h):
                last = None
                for kt in range(32):
                    last = e.matmul(psv, vv[:, kt, h * 65:(h + 1) * 65],
                                    onescol[:], start=(kt == 0), stop=(kt == 31))
                return last
            svm = P.op("pe", svfn,
                       deps=vevs + [ocol, mo1, mo2, last_sve, pend_d4[0]])
            sve = P.op("dve", lambda e, h=h: e.tensor_scalar(
                svb[h][0:65, 0:1], psv, 0.5, None, ALU.mult),
                deps=[svm, pend_d5.get(0), pend_d5.get(1)])
            svops.append(sve)
            sv_ref[h] = sve
            last_sve = sve

        for hq in range(2 * NTCH):
            h, qc = hq // NTCH, hq % NTCH
            jo = hq % 2
            av_last = None
            for g in range(16):
                jg = g % 2
                smms = []
                for half in range(2):
                    kt = 2 * g + half
                    prev_ep = ep_by_grp.get((g - 2) % 16 if g >= 2 else g + 14)
                    smm = P.op("pe", lambda e, h=h, qc=qc, kt=kt, jg=jg,
                               half=half: e.matmul(
                        ps2[jg][:, half * TC:(half + 1) * TC],
                        kbf[h * 64:(h + 1) * 64, kt * 128:(kt + 1) * 128],
                        qbf[h * 64:(h + 1) * 64, qc * TC:(qc + 1) * TC],
                        start=True, stop=True),
                        deps=[ropes[-1], ropes[-2], prev_ep,
                              last_qev if hq == 0 and g == 0 else None,
                              last_kev if hq == 0 and g == 0 else None,
                              last_vev if hq == 0 and g == 0 else None])
                    smms.append(smm)
                    last_smm = smm
                ttw = av_by_grp.get(g - 2 if g >= 2 else g + 14, [])
                ep = P.op("act", lambda e, jg=jg: e.activation(
                    ttb2[jg][:], ps2[jg][:], AF.Square, bias=BS, scale=CS),
                    deps=smms + ttw + cpre)
                ep_by_grp[g] = ep
                avs = []
                for half in range(2):
                    kt = 2 * g + half
                    avm = P.op("pe", lambda e, h=h, kt=kt, jg=jg, half=half:
                               e.matmul(
                        po[:], vv[:, kt, h * 65:(h + 1) * 65],
                        ttb2[jg][:, half * TC:(half + 1) * TC],
                        start=(kt == 0), stop=(kt == 31)),
                        deps=[ep, vevs[-1]] +
                        ([last_oev] if kt == 0 else []))
                    avs.append(avm)
                    av_by_kt[kt] = [avm]
                av_by_grp[g] = avs
                av_last = avs[-1]
            oev = P.op("dve", lambda e, jo=jo: e.tensor_copy(
                osb[jo][0:65, :], po[0:65, :]),
                deps=[av_last, pend_d5.get(jo)])
            last_oev = oev
            if pending is not None:
                emit_div_epilogue(pending)
            pending = (p, h, qc, jo, oev)
        if pending is not None:
            emit_div_epilogue(pending)
            pending = None
    _qkvscope.close()
    wo_sb = ctx.sbuf("b2_wo", [128, NPAIR * D], BF16)
    evb = [ctx.sbuf(f"b2_ev{j}", [128, TC], BF16) for j in range(2)]
    pw = [ctx.psum(f"b2_pw{j}", [128, TC], F32) for j in range(2)]
    oall_done = [x for x in (pend_d5.get(0), pend_d5.get(1)) if x]
    if "dbg_o" in ctx.hbm:
        P.dma("sync", lambda e: e.dma_start(out=ctx.hbm["dbg_o"][:],
              in_=o_all[:]), deps=[d for d in oall_done if d is not None])
    lws = [P.dma("pool", lambda e, p=p: e.dma_start(
        out=wo_sb[:, p * D:(p + 1) * D],
        in_=wo_h[p * 128:(p + 1) * 128, :]), deps=oall_done)
        for p in range(NPAIR)]
    prev = {}
    k = 0
    for dc in range(NDC):
        for qc in range(NTCH):
            j = k % 2
            def mfn(e, dc=dc, qc=qc, j=j):
                last = None
                for p in range(NPAIR):
                    last = e.matmul(
                        pw[j][:],
                        wo_sb[:, p * D + dc * 128: p * D + (dc + 1) * 128],
                        o_all[:, p * T + qc * TC: p * T + (qc + 1) * TC],
                        start=(p == 0), stop=(p == NPAIR - 1))
                return last
            mm = P.op("pe", mfn, deps=lws + oall_done + [prev.get(j)])
            evo = P.op("dve", lambda e, j=j: e.tensor_copy(evb[j][:], pw[j][:]),
                       deps=[mm, prev.get(("st", j))])
            sto = P.dma("sync", lambda e, dc=dc, qc=qc, j=j: e.dma_start(
                out=s0[dc * 128:(dc + 1) * 128, qc * TC:(qc + 1) * TC],
                in_=evb[j][:]), deps=[evo])
            prev[j] = evo
            prev[("st", j)] = sto
            k += 1


def ph_b4(P, ctx):
    xn0T, cw0, s0, p0 = (ctx.hbm["xn0T"], ctx.hbm["cw0"], ctx.hbm["s0"],
                         ctx.hbm["p0"])
    gu0_h, dn0_h = ctx.hbm["gu0"], ctx.hbm["dn0"]
    NE = E0 + 1
    QW = 1024
    NQ = T // QW

    y0 = ctx.sbuf("b4_y0", [128, NDC * QW], F32)
    xnq = ctx.sbuf("b4_xnq", [128, NDC * QW], BF16)
    gu = [ctx.sbuf(f"b4_gu{j}", [128, NDC * 2 * I0P], BF16) for j in range(2)]
    dn = [ctx.sbuf(f"b4_dn{j}", [128, 8 * D], BF16) for j in range(2)]
    sg = [ctx.sbuf(f"b4_sg{j}", [128, TC], F32) for j in range(2)]
    hsb = ctx.sbuf("b4_h", [128, 8 * TC], BF16)
    cwbc = ctx.sbuf("b4_cwbc", [128, TC], BF16)
    cwrow = ctx.sbuf("b4_cwrow", [1, TC], F32)
    ones1 = ctx.sbuf("b4_ones1", [1, 128], F32)
    gsb = [ctx.sbuf(f"b4_g{j}", [128, TC], BF16) for j in range(2)]
    ssb = [ctx.sbuf(f"b4_s{j}", [128, TC], BF16) for j in range(2)]
    psb = [ctx.sbuf(f"b4_p{j}", [128, TC], BF16) for j in range(2)]
    pg = [ctx.psum(f"b4_pg{j}", [128, TC], F32) for j in range(2)]
    pu = [ctx.psum(f"b4_pu{j}", [128, TC], F32) for j in range(2)]
    py = [ctx.psum(f"b4_py{j}", [128, TC], F32) for j in range(2)]
    pbc = ctx.psum("b4_pbc", [128, TC], F32)

    o1 = P.op("dve", lambda e: e.memset(ones1[:], 1.0))
    gu_cons = {0: [], 1: []}
    dn_cons = {0: [], 1: []}
    xn_cons = []
    y_cons = []
    h_cons = {}
    sg_cons = {}
    cw_cons = []
    bc_cons = []
    pg_cons = {}
    pu_cons = {}
    py_cons = {}
    bc_readers = []
    ya_prev = {}
    prev_comb = {}

    for q in range(NQ):
        ldx = P.dma("sync", lambda e, q=q: e.dma_start(
            out=xnq[:].rearrange("p (c t) -> p c t", t=QW),
            in_=xn0T[:, q * QW:(q + 1) * QW].rearrange(
                "(c p) t -> p c t", p=128)), deps=xn_cons)
        xn_cons = []
        ms = [P.op("dve", lambda e, dc=dc: e.memset(
            y0[:, dc * QW:(dc + 1) * QW], 0.0), deps=y_cons) for dc in range(NDC)]
        y_cons = []
        for en in range(NE):
            jw = en % 2
            lg = P.dma("pool", lambda e, en=en, t=gu[jw]: e.dma_start(
                out=t[:].rearrange("p (c r) -> p c r", r=2 * I0P),
                in_=gu0_h[en].rearrange("(c p) r -> p c r", p=128)),
                deps=gu_cons[jw])
            ldn = P.dma("pool", lambda e, en=en, t=dn[jw]: e.dma_start(
                out=t[:].rearrange("p (c d) -> p c d", d=D),
                in_=dn0_h[en].rearrange("(c p) d -> p c d", p=128)),
                deps=dn_cons[jw])
            gu_cons[jw], dn_cons[jw] = [], []
            for tcc in range(QW // TC):
                col0 = q * QW + tcc * TC
                if en > 0:
                    lcw = P.dma("sync", lambda e, en=en, col0=col0: e.dma_start(
                        out=cwrow[:], in_=cw0[en - 1:en, col0:col0 + TC]),
                        deps=cw_cons)
                    mbc = P.op("pe", lambda e: e.matmul(
                        pbc[:], ones1[:], cwrow[:], start=True, stop=True),
                        deps=[lcw, o1] + bc_cons)
                    ebc = P.op("dve", lambda e: e.tensor_copy(cwbc[:], pbc[:]),
                               deps=[mbc] + bc_readers)
                    cw_cons = [mbc]
                    bc_cons = [ebc]
                    bc_readers = []
                hops = []
                for ic in range(8):
                    jj = ic % 2
                    def gfn(e, w=gu[jw], ic=ic, tcc=tcc, jj=jj):
                        last = None
                        for dc in range(NDC):
                            last = e.matmul(
                                pg[jj][:],
                                w[:, dc * 2 * I0P + (2 * ic) * 128:
                                  dc * 2 * I0P + (2 * ic + 1) * 128],
                                xnq[:, dc * QW + tcc * TC:
                                    dc * QW + tcc * TC + TC],
                                start=(dc == 0), stop=(dc == NDC - 1))
                        return last
                    def ufn(e, w=gu[jw], ic=ic, tcc=tcc, jj=jj):
                        last = None
                        for dc in range(NDC):
                            last = e.matmul(
                                pu[jj][:],
                                w[:, dc * 2 * I0P + (2 * ic + 1) * 128:
                                  dc * 2 * I0P + (2 * ic + 2) * 128],
                                xnq[:, dc * QW + tcc * TC:
                                    dc * QW + tcc * TC + TC],
                                start=(dc == 0), stop=(dc == NDC - 1))
                        return last
                    gm = P.op("pe", gfn, deps=[lg, ldx] + pg_cons.get(jj, []))
                    um = P.op("pe", ufn, deps=[lg, ldx] + pu_cons.get(jj, []))
                    sl = P.op("act", lambda e, jj=jj: e.activation(
                        sg[jj][:], pg[jj][:], AF.Silu, bias=0.0, scale=1.0),
                        deps=[gm] + sg_cons.get(jj, []))
                    pg_cons[jj] = [sl]
                    hm = P.op("dve", lambda e, jj=jj, ic=ic: e.tensor_mul(
                        hsb[:, ic * TC:(ic + 1) * TC], sg[jj][:], pu[jj][:]),
                        deps=[sl, um] + h_cons.get(ic, []))
                    h_cons[ic] = [hm]
                    pu_cons[jj] = [hm]
                    sg_cons[jj] = [hm]
                    if en > 0:
                        hc = P.op("pool", lambda e, ic=ic: e.tensor_mul(
                            hsb[:, ic * TC:(ic + 1) * TC],
                            hsb[:, ic * TC:(ic + 1) * TC], cwbc[:]),
                            deps=[hm] + bc_cons)
                        hops.append(hc)
                        bc_readers.append(hc)
                    else:
                        hops.append(hm)
                for dc in range(NDC):
                    jj = dc % 2
                    def dfn(e, w=dn[jw], dc=dc, jj=jj):
                        last = None
                        for ic in range(8):
                            last = e.matmul(
                                py[jj][:],
                                w[:, ic * D + dc * 128: ic * D + (dc + 1) * 128],
                                hsb[:, ic * TC:(ic + 1) * TC],
                                start=(ic == 0), stop=(ic == 7))
                        return last
                    dm = P.op("pe", dfn, deps=hops + py_cons.get(jj, []))
                    ya = P.op("dve", lambda e, dc=dc, tcc=tcc, jj=jj:
                              e.tensor_add(
                                  y0[:, dc * QW + tcc * TC:
                                     dc * QW + tcc * TC + TC],
                                  y0[:, dc * QW + tcc * TC:
                                     dc * QW + tcc * TC + TC], py[jj][:]),
                              deps=[dm, ya_prev.get((dc, tcc))] + ms)
                    ya_prev[(dc, tcc)] = ya
                    py_cons[jj] = [ya]
                    y_cons.append(ya)
                    last_dm = dm
                if "dbg_h" in ctx.hbm and q == 0 and en == 0 and tcc == 0:
                    P.dma("sync", lambda e: e.dma_start(
                        out=ctx.hbm["dbg_h"][:], in_=hsb[:]), deps=hops)
                for ic in range(8):
                    h_cons[ic] = [last_dm]
                gu_cons[jw].extend([hops[-1]])
                dn_cons[jw].append(py_cons[0][0] if 0 in py_cons else None)
                dn_cons[jw].append(py_cons[1][0] if 1 in py_cons else None)
            gu_cons[jw] = [x for x in gu_cons[jw] if x]
            dn_cons[jw] = [x for x in dn_cons[jw] if x]
            xn_cons = y_cons[-1:]
        if "dbg_y" in ctx.hbm and q == 0:
            P.dma("sync", lambda e: e.dma_start(
                out=ctx.hbm["dbg_y"][:], in_=y0[:]), deps=list(y_cons))
        for tcc in range(QW // TC):
            for dc in range(NDC):
                k = tcc * NDC + dc
                jj = k % 2
                col0 = q * QW + tcc * TC
                lds = P.dma("sync", lambda e, dc=dc, col0=col0, jj=jj:
                            e.dma_start(out=ssb[jj][:],
                                        in_=s0[dc * 128:(dc + 1) * 128,
                                               col0:col0 + TC]),
                            deps=[prev_comb.get(("p", jj))])
                gg = P.op("act", lambda e, dc=dc, tcc=tcc, jj=jj: e.activation(
                    gsb[jj][:], y0[:, dc * QW + tcc * TC: dc * QW + tcc * TC + TC],
                    AF.Silu, bias=0.0, scale=1.0),
                    deps=y_cons + [prev_comb.get(("g", jj))])
                pp = P.op("dve", lambda e, jj=jj: e.tensor_mul(
                    psb[jj][:], gsb[jj][:], ssb[jj][:]),
                    deps=[gg, lds, prev_comb.get(("st", jj))])
                st = P.dma("sync", lambda e, dc=dc, col0=col0, jj=jj:
                           e.dma_start(out=p0[dc * 128:(dc + 1) * 128,
                                              col0:col0 + TC], in_=psb[jj][:]),
                           deps=[pp])
                prev_comb[("g", jj)] = pp
                prev_comb[("p", jj)] = pp
                prev_comb[("st", jj)] = st
                y_cons.append(gg)


def ph_b5(P, ctx):
    x, p0, x1, xn1 = (ctx.hbm["x"], ctx.hbm["p0"], ctx.hbm["x1"],
                      ctx.hbm["xn1"])
    ceps = ctx.constant(LN_EPS)
    cm1 = ctx.constant(-1.0)
    cmh = ctx.constant(-0.5)
    xb = [ctx.sbuf(f"b5_x{j}", [128, D], F32) for j in range(2)]
    pb_ = [ctx.sbuf(f"b5_p{j}", [128, D], BF16) for j in range(2)]
    x1b = [ctx.sbuf(f"b5_x1{j}", [128, D], F32) for j in range(2)]
    sq = [ctx.sbuf(f"b5_sq{j}", [128, D], F32) for j in range(2)]
    xnb = [ctx.sbuf(f"b5_xn{j}", [128, D], BF16) for j in range(2)]
    st = [ctx.sbuf(f"b5_st{j}", [128, 8], F32) for j in range(2)]
    C = 1.0 / D
    prev = {}
    for i in range(NTT):
        j = i % 2
        pv = prev.get(j, {})
        ld = P.dma("sync", lambda e, i=i, t=xb[j]: e.dma_start(
            out=t[:], in_=x[i * 128:(i + 1) * 128, :]), deps=[pv.get("add")])
        ldp = P.dma("sync", lambda e, i=i, t=pb_[j]: e.dma_start(
            out=t[:], in_=p0[:, i * 128:(i + 1) * 128], transpose=True),
            deps=[pv.get("add")])
        add = P.op("dve", lambda e, j=j: e.tensor_add(
            x1b[j][:], xb[j][:], pb_[j][:]),
            deps=[ld, ldp, pv.get("stx1"), pv.get("xn"), pv.get("sq1"),
                  pv.get("sum")])
        stx1 = P.dma("pool", lambda e, i=i, j=j: e.dma_start(
            out=x1[i * 128:(i + 1) * 128, :], in_=x1b[j][:]), deps=[add])
        stb = st[j]
        sx, sx2 = stb[:, 0:1], stb[:, 1:2]
        mu, nv, rstd = stb[:, 2:3], stb[:, 3:4], stb[:, 4:5]
        summ = P.op("dve", lambda e, sx=sx, j=j: e.reduce_sum(
            sx, x1b[j][:], axis=AX.X), deps=[add])
        sq1 = P.op("act", lambda e, j=j, sx2=sx2: e.activation(
            sq[j][:], x1b[j][:], AF.Square, bias=0.0, scale=1.0,
            accum_out=sx2), deps=[add, pv.get("sq_w")])
        ex2 = P.op("dve", lambda e, sx2=sx2: e.tensor_scalar(
            sx2, sx2, C, None, ALU.mult), deps=[sq1])
        muo = P.op("dve", lambda e, mu=mu, sx=sx: e.tensor_scalar(
            mu, sx, C, None, ALU.mult), deps=[summ])
        nvo = P.op("dve", lambda e, nv=nv, mu=mu, sx2=sx2:
                   e.scalar_tensor_tensor(nv, mu, mu, sx2, ALU.mult,
                                          ALU.subtract), deps=[muo, ex2])
        lnv = P.op("act", lambda e, nv=nv: e.activation(
            nv, nv, AF.Ln, bias=LN_EPS, scale=-1.0), deps=[nvo, ceps, cm1])
        rst = P.op("act", lambda e, rstd=rstd, nv=nv: e.activation(
            rstd, nv, AF.Exp, bias=0.0, scale=-0.5), deps=[lnv, cmh])
        xn = P.op("dve", lambda e, j=j, mu=mu, rstd=rstd: e.tensor_scalar(
            xnb[j][:], x1b[j][:], mu, rstd, ALU.subtract, ALU.mult),
            deps=[muo, rst, pv.get("stxn")])
        stxn = P.dma("pool", lambda e, i=i, j=j: e.dma_start(
            out=xn1[i * 128:(i + 1) * 128, :], in_=xnb[j][:]), deps=[xn])
        prev[j] = dict(add=add, stx1=stx1, xn=xn, sq1=sq1, sum=summ,
                       sq_w=nvo, stxn=stxn)


def ph_b6(P, ctx):
    xn1, xn1T = ctx.hbm["xn1"], ctx.hbm["xn1T"]
    wr1_h, br1_h, g1 = ctx.hbm["wr1"], ctx.hbm["br1"], ctx.hbm["g1"]

    xt = [[ctx.sbuf(f"b6_xt{j}_{dc}", [128, TC], BF16) for dc in range(NDC)]
          for j in range(2)]
    wr = ctx.sbuf("b6_wr", [128, NDC * E1], BF16)
    brrow = ctx.sbuf("b6_br", [1, E1], F32)
    brbc = ctx.sbuf("b6_brbc", [128, E1], F32)
    ones1 = ctx.sbuf("b6_ones", [1, 128], F32)
    lg = [ctx.sbuf(f"b6_lg{j}", [128, 4 * E1], F32) for j in range(2)]
    p2b = [ctx.sbuf(f"b6_p2{j}", [128, 4 * E1], F32) for j in range(2)]
    tmp = [ctx.sbuf(f"b6_tmp{j}", [128, 4 * E1], F32) for j in range(2)]
    msk = [ctx.sbuf(f"b6_msk{j}", [128, 4 * E1], F32) for j in range(2)]
    gt = [ctx.sbuf(f"b6_gt{j}", [128, 4 * E1], F32) for j in range(2)]
    r1 = [ctx.sbuf(f"b6_r1{j}", [128, 4], F32) for j in range(2)]
    gfm = ctx.sbuf("b6_gfm", [E1, T], F32)
    ident = ctx.sbuf("b6_id", [128, 128], F32)
    ia = ctx.sbuf("b6_ia", [128, 128], F32)
    plg = [ctx.psum(f"b6_plg{j}", [128, E1], F32) for j in range(2)]
    ptr = [ctx.psum(f"b6_ptr{j}", [E1, 128], F32) for j in range(2)]
    pbc = ctx.psum("b6_pbc", [128, E1], F32)

    i1 = P.op("pool", lambda e: e.iota(ia[:], pattern=[[0, 128]], base=0,
              channel_multiplier=1, allow_small_or_imprecise_dtypes=True))
    i2 = P.op("pool", lambda e: e.iota(ident[:], pattern=[[1, 128]], base=0,
              channel_multiplier=0, allow_small_or_imprecise_dtypes=True))
    idt = P.op("dve", lambda e: e.tensor_tensor(ident[:], ia[:], ident[:],
               ALU.is_equal), deps=[i1, i2])
    ldw = [P.dma("pool", lambda e, dc=dc: e.dma_start(
        out=wr[:, dc * E1:(dc + 1) * E1],
        in_=wr1_h[dc * 128:(dc + 1) * 128, :])) for dc in range(NDC)]
    ldb = P.dma("sync", lambda e: e.dma_start(out=brrow[:], in_=br1_h[:]))
    o1 = P.op("dve", lambda e: e.memset(ones1[:], 1.0))
    mbc = P.op("pe", lambda e: e.matmul(pbc[:], ones1[:], brrow[:],
               start=True, stop=True), deps=[ldb, o1])
    ebc = P.op("dve", lambda e: e.tensor_copy(brbc[:], pbc[:]), deps=[mbc])

    prev = {}
    ev_by_k = {}
    evg_by_k = {}
    for i in range(NTCH):
        j = i % 2
        pv = prev.get(j, {})
        lds = []
        for dc in range(NDC):
            lds.append(P.dma("sync", lambda e, i=i, dc=dc, t=xt[j][dc]:
                       e.dma_start(out=t[:],
                                   in_=xn1[i * TC:(i + 1) * TC,
                                           dc * 128:(dc + 1) * 128],
                                   transpose=True), deps=pv.get("xt_cons", [])))
        stws = [P.dma("pool", lambda e, i=i, dc=dc, t=xt[j][dc]: e.dma_start(
            out=xn1T[dc * 128:(dc + 1) * 128, i * TC:(i + 1) * TC], in_=t[:]),
            deps=[lds[dc]]) for dc in range(NDC)]
        mmops, evops = [], []
        for tt in range(4):
            k = i * 4 + tt
            jj = k % 2

            def mmfn(e, j=j, tt=tt, jj=jj):
                last = None
                for dc in range(NDC):
                    last = e.matmul(plg[jj][:],
                                    xt[j][dc][:, tt * 128:(tt + 1) * 128],
                                    wr[:, dc * E1:(dc + 1) * E1],
                                    start=(dc == 0), stop=(dc == NDC - 1))
                return last
            mm = P.op("pe", mmfn, deps=lds + ldw + [ev_by_k.get(k - 2)])
            ev = P.op("dve", lambda e, j=j, tt=tt, jj=jj: e.tensor_add(
                lg[j][:, tt * E1:(tt + 1) * E1], plg[jj][:], brbc[:]),
                deps=[mm, ebc, pv.get("chain_last")])
            ev_by_k[k] = ev
            mmops.append(mm)
            evops.append(ev)
        lgv = lg[j][:].rearrange("p (s e) -> p s e", e=E1)
        p2v = p2b[j][:].rearrange("p (s e) -> p s e", e=E1)
        tmpv = tmp[j][:].rearrange("p (s e) -> p s e", e=E1)
        mskv = msk[j][:].rearrange("p (s e) -> p s e", e=E1)
        gtv = gt[j][:].rearrange("p (s e) -> p s e", e=E1)
        r1v = r1[j][:].rearrange("p (s o) -> p s o", o=1)
        r1b = r1[j][:].rearrange("p (s o) -> p s o", o=1).to_broadcast(
            [128, 4, E1])
        c1 = P.op("dve", lambda e, lgv=lgv, r1v=r1v: e.tensor_reduce(
            r1v, lgv, axis=AX.X, op=ALU.max, negate=True),
            deps=evops + [pv.get("c_last")])
        c2 = P.op("dve", lambda e, p2v=p2v, lgv=lgv, r1b=r1b: e.tensor_tensor(
            p2v, lgv, r1b, ALU.add), deps=[c1, pv.get("c_last")])
        c3 = P.op("act", lambda e, s=p2b[j]: e.activation(
            s[:], s[:], AF.Exp, bias=0.0, scale=1.0), deps=[c2])
        c4 = P.op("dve", lambda e, p2v=p2v, r1v=r1v: e.tensor_reduce(
            r1v, p2v, axis=AX.X, op=ALU.add), deps=[c3])
        c5 = P.op("dve", lambda e, r=r1[j]: e.reciprocal(r[:], r[:]), deps=[c4])
        c6 = P.op("dve", lambda e, p2v=p2v, r1b=r1b: e.tensor_tensor(
            p2v, p2v, r1b, ALU.mult), deps=[c5])
        c7 = P.op("dve", lambda e, p2v=p2v, r1v=r1v: e.tensor_reduce(
            r1v, p2v, axis=AX.X, op=ALU.max), deps=[c6])
        c8 = P.op("dve", lambda e, mskv=mskv, p2v=p2v, r1b=r1b: e.tensor_tensor(
            mskv, p2v, r1b, ALU.is_equal), deps=[c7])
        c9 = P.op("dve", lambda e, tmpv=tmpv, p2v=p2v, mskv=mskv:
                  e.tensor_tensor(tmpv, p2v, mskv, ALU.mult), deps=[c8])
        c10 = P.op("dve", lambda e, tmpv=tmpv, p2v=p2v: e.tensor_tensor(
            tmpv, p2v, tmpv, ALU.subtract), deps=[c9])
        c11 = P.op("dve", lambda e, tmpv=tmpv, r1v=r1v: e.tensor_reduce(
            r1v, tmpv, axis=AX.X, op=ALU.max), deps=[c10])
        c12 = P.op("dve", lambda e, gtv=gtv, tmpv=tmpv, r1b=r1b:
                   e.tensor_tensor(gtv, tmpv, r1b, ALU.is_equal), deps=[c11])
        c13 = P.op("dve", lambda e, mskv=mskv, gtv=gtv: e.tensor_tensor(
            mskv, mskv, gtv, ALU.add), deps=[c12])
        c14 = P.op("dve", lambda e, tmpv=tmpv, gtv=gtv: e.tensor_tensor(
            gtv, tmpv, gtv, ALU.mult), deps=[c13])
        c15 = P.op("dve", lambda e, tmpv=tmpv, gtv=gtv: e.tensor_tensor(
            tmpv, tmpv, gtv, ALU.subtract), deps=[c14])
        c16 = P.op("dve", lambda e, tmpv=tmpv, r1v=r1v: e.tensor_reduce(
            r1v, tmpv, axis=AX.X, op=ALU.max), deps=[c15])
        c17 = P.op("dve", lambda e, gtv=gtv, tmpv=tmpv, r1b=r1b:
                   e.tensor_tensor(gtv, tmpv, r1b, ALU.is_equal), deps=[c16])
        c18 = P.op("dve", lambda e, mskv=mskv, gtv=gtv: e.tensor_tensor(
            mskv, mskv, gtv, ALU.add), deps=[c17])
        c19 = P.op("dve", lambda e, gtv=gtv, p2v=p2v, mskv=mskv:
                   e.tensor_tensor(gtv, p2v, mskv, ALU.mult), deps=[c18])
        tpev = []
        for tt in range(4):
            k = i * 4 + tt
            jj = k % 2
            tpo = P.op("pe", lambda e, j=j, tt=tt, jj=jj: e.transpose(
                ptr[jj][:], gt[j][:, tt * E1:(tt + 1) * E1], ident[:]),
                deps=[c19, idt, evg_by_k.get(k - 2)])
            evc = P.op("dve", lambda e, i=i, tt=tt, jj=jj: e.tensor_copy(
                gfm[:, i * TC + tt * 128: i * TC + (tt + 1) * 128], ptr[jj][:]),
                deps=[tpo])
            evg_by_k[k] = evc
            tpev.append(evc)
        prev[j] = {"xt_cons": mmops + stws, "chain_last": c19, "c_last": c19}
    P.dma("pool", lambda e: e.dma_start(out=g1[:], in_=gfm[:]),
          deps=list(evg_by_k.values()))


def ph_b7(P, ctx):
    xn1T = ctx.hbm["xn1T"]
    wq1_h, wk1_h, wv1_h = ctx.hbm["wq1"], ctx.hbm["wk1"], ctx.hbm["wv1"]
    bq1_h, bk1_h, bv1_h = ctx.hbm["bq1"], ctx.hbm["bk1"], ctx.hbm["bv1"]
    q1, attn1, s1 = ctx.hbm["q1"], ctx.hbm["attn1"], ctx.hbm["s1"]

    v_tm = ctx.sbuf("b7_vtm", [128, 32 * D], BF16)
    bq = ctx.sbuf("b7_bq", [128, NDC], F32)
    bk = ctx.sbuf("b7_bk", [128, NDC], F32)
    bvrow = ctx.sbuf("b7_bvrow", [1, D], F32)
    bvbc = ctx.sbuf("b7_bvbc", [128, D], F32)
    ones1 = ctx.sbuf("b7_ones", [1, 128], F32)
    _sc2 = ExitStack()
    k_fm = ctx.sbuf("b7_kfm", [128, NDC * T], BF16, scope=_sc2)
    pq = [ctx.psum(f"b7_pq{j}", [128, TC], F32, scope=_sc2) for j in range(2)]
    pv_ = [ctx.psum(f"b7_pv{j}", [128, TC], F32, scope=_sc2) for j in range(2)]
    pbc = ctx.psum("b7_pbc", [128, TC], F32, scope=_sc2)
    _sc3 = ExitStack()
    wq_sb = ctx.sbuf("b7_wq", [128, NDC * D], BF16, scope=_sc3)
    wk_sb = ctx.sbuf("b7_wk", [128, NDC * D], BF16, scope=_sc3)
    wv_sb = ctx.sbuf("b7_wv", [128, NDC * D], BF16, scope=_sc3)
    xnb = [ctx.sbuf(f"b7_xn{j}", [128, NDC * TC], BF16, scope=_sc3)
           for j in range(2)]
    qev = [ctx.sbuf(f"b7_qe{j}", [128, TC], BF16, scope=_sc3) for j in range(2)]

    ldbq = P.dma("sync", lambda e: e.dma_start(out=bq[:], in_=bq1_h[:]))
    ldbk = P.dma("sync", lambda e: e.dma_start(out=bk[:], in_=bk1_h[:]))
    ldbv = P.dma("sync", lambda e: e.dma_start(out=bvrow[:], in_=bv1_h[:]))
    o1 = P.op("dve", lambda e: e.memset(ones1[:], 1.0))
    mb1 = P.op("pe", lambda e: e.matmul(pbc[:], ones1[:], bvrow[:, 0:TC],
               start=True, stop=True), deps=[ldbv, o1])
    eb1 = P.op("dve", lambda e: e.tensor_copy(bvbc[:, 0:TC], pbc[:]),
               deps=[mb1])
    mb2 = P.op("pe", lambda e: e.matmul(pbc[:], ones1[:], bvrow[:, TC:D],
               start=True, stop=True), deps=[eb1])
    eb2 = P.op("dve", lambda e: e.tensor_copy(bvbc[:, TC:D], pbc[:]),
               deps=[mb2])
    lws = []
    for dc in range(NDC):
        lws.append(P.dma("pool", lambda e, dc=dc: e.dma_start(
            out=wq_sb[:, dc * D:(dc + 1) * D],
            in_=wq1_h[dc * 128:(dc + 1) * 128, :])))
        lws.append(P.dma("pool", lambda e, dc=dc: e.dma_start(
            out=wk_sb[:, dc * D:(dc + 1) * D],
            in_=wk1_h[dc * 128:(dc + 1) * 128, :])))
        lws.append(P.dma("pool", lambda e, dc=dc: e.dma_start(
            out=wv_sb[:, dc * D:(dc + 1) * D],
            in_=wv1_h[dc * 128:(dc + 1) * 128, :])))

    xn_cons = {0: [], 1: []}
    pq_cons = {0: [], 1: []}
    pv_cons = {0: [], 1: []}
    qst_prev = {0: None, 1: None}
    last_qkv = []
    for tc in range(NTCH):
        j = tc % 2
        ldx = P.dma("sync", lambda e, tc=tc, t=xnb[j]: e.dma_start(
            out=t[:].rearrange("p (c t) -> p c t", t=TC),
            in_=xn1T[:, tc * TC:(tc + 1) * TC].rearrange(
                "(c p) t -> p c t", p=128)), deps=xn_cons[j])
        xn_cons[j] = []
        for fc in range(NDC):
            kq = (tc * NDC + fc)
            jj = kq % 2
            def qfn(e, j=j, fc=fc, jj=jj):
                last = None
                for dc in range(NDC):
                    last = e.matmul(pq[jj][:],
                                    wq_sb[:, dc * D + fc * 128:
                                          dc * D + (fc + 1) * 128],
                                    xnb[j][:, dc * TC:(dc + 1) * TC],
                                    start=(dc == 0), stop=(dc == NDC - 1))
                return last
            qm = P.op("pe", qfn, deps=[ldx] + lws + pq_cons[jj])
            qe = P.op("dve", lambda e, fc=fc, jj=jj: e.tensor_scalar(
                qev[jj][:], pq[jj][:], bq[:, fc:fc + 1], None, ALU.add),
                deps=[qm, qst_prev[jj]])
            pq_cons[jj] = [qe]
            qs = P.dma("sync", lambda e, fc=fc, tc=tc, jj=jj: e.dma_start(
                out=q1[fc * 128:(fc + 1) * 128, tc * TC:(tc + 1) * TC],
                in_=qev[jj][:]), deps=[qe])
            qst_prev[jj] = qs
            xn_cons[j].append(qm)
            def kfn(e, j=j, fc=fc, jj=jj):
                last = None
                for dc in range(NDC):
                    last = e.matmul(pv_[jj][:],
                                    wk_sb[:, dc * D + fc * 128:
                                          dc * D + (fc + 1) * 128],
                                    xnb[j][:, dc * TC:(dc + 1) * TC],
                                    start=(dc == 0), stop=(dc == NDC - 1))
                return last
            km = P.op("pe", kfn, deps=[ldx] + lws + pv_cons[jj])
            ke = P.op("dve", lambda e, fc=fc, tc=tc, jj=jj: e.tensor_scalar(
                k_fm[:, fc * T + tc * TC: fc * T + tc * TC + TC], pv_[jj][:],
                bk[:, fc:fc + 1], None, ALU.add), deps=[km, ldbk])
            pv_cons[jj] = [ke]
            xn_cons[j].append(km)
            last_qkv = [qs, ke]
        for tt in range(4):
            kt = tc * 4 + tt
            for fh in range(2):
                jj = (kt * 2 + fh) % 2
                def vfn(e, j=j, tt=tt, fh=fh, jj=jj):
                    last = None
                    for dc in range(NDC):
                        last = e.matmul(
                            pv_[jj][:],
                            xnb[j][:, dc * TC + tt * 128:
                                dc * TC + (tt + 1) * 128],
                            wv_sb[:, dc * D + fh * TC: dc * D + (fh + 1) * TC],
                            start=(dc == 0), stop=(dc == NDC - 1))
                    return last
                vm = P.op("pe", vfn, deps=[ldx] + lws + pv_cons[jj])
                ve = P.op("dve", lambda e, kt=kt, fh=fh, jj=jj: e.tensor_add(
                    v_tm[:, kt * D + fh * TC: kt * D + (fh + 1) * TC],
                    pv_[jj][:], bvbc[:, fh * TC:(fh + 1) * TC]),
                    deps=[vm, eb1, eb2])
                pv_cons[jj] = [ve]
                xn_cons[j].append(vm)
                last_qkv.append(ve)

    _sc3.close()
    srow = ctx.sbuf("b7_srow", [128, T], F32, scope=_sc2)
    zap = ctx.sbuf("b7_zap", [128, T], F32, scope=_sc2)
    mx8 = ctx.sbuf("b7_mx8", [128, 8], F32, scope=_sc2)
    dcol = ctx.sbuf("b7_dcol", [128, 1], F32, scope=_sc2)
    asp = ctx.sbuf("b7_asp", [128, T], BF16, scope=_sc2)
    qt = [ctx.sbuf(f"b7_qt{j}", [128, NDC * 128], BF16, scope=_sc2)
          for j in range(2)]
    ep_prev = {0: None, 1: None}
    att_sts = []
    sc_prev = None
    mr_prev = None
    for qt_i in range(NTT):
        j = qt_i % 2
        ldq = P.dma("sync", lambda e, qt_i=qt_i, t=qt[j]: e.dma_start(
            out=t[:].rearrange("p (c f) -> p c f", f=128),
            in_=q1[:, qt_i * 128:(qt_i + 1) * 128].rearrange(
                "(c p) f -> p c f", p=128)),
            deps=last_qkv + (ep_prev[j] or []))
        evs = []
        for kc in range(NTCH):
            jj = kc % 2
            def sfn(e, j=j, kc=kc, jj=jj):
                last = None
                for ec in range(NDC):
                    last = e.matmul(pq[jj][:],
                                    qt[j][:, ec * 128:(ec + 1) * 128],
                                    k_fm[:, ec * T + kc * TC:
                                         ec * T + kc * TC + TC],
                                    start=(ec == 0), stop=(ec == NDC - 1))
                return last
            sm = P.op("pe", sfn, deps=[ldq] + last_qkv + pq_cons[jj])
            se = P.op("dve", lambda e, kc=kc, jj=jj: e.tensor_copy(
                srow[:, kc * TC:(kc + 1) * TC], pq[jj][:]),
                deps=[sm, sc_prev])
            pq_cons[jj] = [se]
            evs.append(se)
        cz = P.op("dve", lambda e: e.tensor_copy(zap[:], srow[:]),
                  deps=evs + [mr_prev])
        mrs = cz
        for r in range(4):
            m8 = P.op("dve", lambda e: e.max(out=mx8[:], in_=zap[:]),
                      deps=[mrs])
            mrs = P.op("dve", lambda e: e.match_replace(
                out=zap[:], in_to_replace=mx8[:], in_values=zap[:],
                imm_value=-1e30), deps=[m8])
        mk = P.op("dve", lambda e: e.tensor_tensor(zap[:], zap[:], srow[:],
                  ALU.not_equal), deps=[mrs])
        sp = P.op("dve", lambda e: e.scalar_tensor_tensor(
            asp[:], srow[:], 1.0, zap[:], ALU.mult, ALU.mult,
            accum_out=dcol[:]), deps=[mk, att_sts[-1] if att_sts else None])
        dn1 = P.op("dve", lambda e: e.tensor_scalar(
            dcol[:], dcol[:], 1e-10, None, ALU.add), deps=[sp])
        dn2 = P.op("dve", lambda e: e.reciprocal(dcol[:], dcol[:]), deps=[dn1])
        nrm = P.op("dve", lambda e: e.tensor_scalar(
            asp[:], asp[:], dcol[:, 0:1], None, ALU.mult), deps=[dn2])
        sc_prev = nrm
        mr_prev = sp
        ast = P.dma("sync", lambda e, qt_i=qt_i: e.dma_start(
            out=attn1[qt_i * 128:(qt_i + 1) * 128, :], in_=asp[:]),
            deps=[nrm])
        att_sts.append(ast)
        ep_prev[j] = [sm]

    _sc2.close()
    at = ctx.sbuf("b7_at", [128, 32 * TC], BF16)
    ev1 = [ctx.sbuf(f"b7_ev1{j}", [128, TC], BF16) for j in range(2)]
    pa = [ctx.psum(f"b7_pa{j}", [128, TC], F32) for j in range(2)]
    at_cons = []
    prev_st = {0: None, 1: None}
    prev_ev = {0: None, 1: None}
    for qc in range(NTCH):
        lats = [P.dma("sync", lambda e, qc=qc, kt=kt: e.dma_start(
            out=at[:, kt * TC:(kt + 1) * TC],
            in_=attn1[qc * TC:(qc + 1) * TC, kt * 128:(kt + 1) * 128],
            transpose=True), deps=att_sts + at_cons) for kt in range(32)]
        at_cons = []
        for vc in range(NDC):
            jj = vc % 2
            def avfn(e, qc=qc, vc=vc, jj=jj):
                last = None
                for kt in range(32):
                    last = e.matmul(pa[jj][:],
                                    v_tm[:, kt * D + vc * 128:
                                         kt * D + (vc + 1) * 128],
                                    at[:, kt * TC:(kt + 1) * TC],
                                    start=(kt == 0), stop=(kt == 31))
                return last
            av = P.op("pe", avfn, deps=lats + last_qkv + [prev_ev[jj]])
            eo = P.op("dve", lambda e, jj=jj: e.tensor_copy(ev1[jj][:],
                      pa[jj][:]), deps=[av, prev_st[jj]])
            st = P.dma("sync", lambda e, qc=qc, vc=vc, jj=jj: e.dma_start(
                out=s1[vc * 128:(vc + 1) * 128, qc * TC:(qc + 1) * TC],
                in_=ev1[jj][:]), deps=[eo])
            prev_ev[jj] = eo
            prev_st[jj] = st
            at_cons.append(av)


def _yview(y1, dcp, tcc):
    return (y1[:].rearrange("p (c t) -> p c t", t=2048)
            [:, 2 * dcp:2 * dcp + 2, tcc * TC: tcc * TC + TC])


def ph_b8(P, ctx):
    xn1T, g1, s1, p1 = (ctx.hbm["xn1T"], ctx.hbm["g1"], ctx.hbm["s1"],
                        ctx.hbm["p1"])
    w1f_h, w2f_h = ctx.hbm["w1f"], ctx.hbm["w2f"]
    b1f_h, bs2_h, be2_h = ctx.hbm["b1f"], ctx.hbm["bs2"], ctx.hbm["be2"]
    NE = 32
    HW = 2048
    NHALF = T // HW

    y1 = ctx.sbuf("b8_y1", [128, NDC * HW], F32)
    xnh = ctx.sbuf("b8_xnh", [128, NDC * HW], BF16)
    w1 = [ctx.sbuf(f"b8_w1{j}", [128, NDC * D], BF16) for j in range(2)]
    w2 = [ctx.sbuf(f"b8_w2{j}", [128, 8 * D], BF16) for j in range(2)]
    b1 = ctx.sbuf("b8_b1", [128, NE * NDC], F32)
    bs2 = ctx.sbuf("b8_bs2", [128, NDC], F32)
    be2 = ctx.sbuf("b8_be2", [E1, NDC * 128], BF16)
    g1b = ctx.sbuf("b8_g1b", [E1, HW], BF16)
    grow = ctx.sbuf("b8_grow", [1, TC], F32)
    gbc = ctx.sbuf("b8_gbc", [128, TC], BF16)
    ones1 = ctx.sbuf("b8_ones1", [1, 128], F32)
    hsb = ctx.sbuf("b8_h", [128, 8 * TC], BF16)
    tsc = [ctx.sbuf(f"b8_t{j}", [128, TC], F32) for j in range(2)]
    gsb = [ctx.sbuf(f"b8_g{j}", [128, TC], BF16) for j in range(2)]
    ssb = [ctx.sbuf(f"b8_s{j}", [128, TC], BF16) for j in range(2)]
    psb = [ctx.sbuf(f"b8_p{j}", [128, TC], BF16) for j in range(2)]
    ph_ = [ctx.psum(f"b8_ph{j}", [128, TC], F32) for j in range(2)]
    py = [ctx.psum(f"b8_py{j}", [128, 2 * TC], F32) for j in range(2)]
    pbc = ctx.psum("b8_pbc", [128, TC], F32)

    o1 = P.op("dve", lambda e: e.memset(ones1[:], 1.0))
    ldb1 = P.dma("sync", lambda e: e.dma_start(
        out=b1[:], in_=b1f_h[:].rearrange("p e c -> p (e c)")))
    ldbs2 = P.dma("sync", lambda e: e.dma_start(out=bs2[:], in_=bs2_h[:]))
    ldbe2 = P.dma("pool", lambda e: e.dma_start(
        out=be2[:], in_=be2_h[:]))
    w1_cons = {0: [], 1: []}
    w2_cons = {0: [], 1: []}
    xn_cons = []
    y_cons = []
    h_cons = {}
    ts_cons = {}
    gb_cons = []
    gr_cons = []
    ph_cons = {}
    py_cons = {}
    ya_prev = {}
    prev_comb = {}
    g1_cons = []

    for half in range(NHALF):
        col0h = half * HW
        ldx = P.dma("sync", lambda e, col0h=col0h: e.dma_start(
            out=xnh[:].rearrange("p (c t) -> p c t", t=HW),
            in_=xn1T[:, col0h:col0h + HW].rearrange("(c p) t -> p c t", p=128)),
            deps=xn_cons)
        xn_cons = []
        ldg = P.dma("pool", lambda e, col0h=col0h: e.dma_start(
            out=g1b[:], in_=g1[:, col0h:col0h + HW]), deps=g1_cons)
        g1_cons = []
        ms = [P.op("dve", lambda e, dc=dc: e.memset(
            y1[:, dc * HW:(dc + 1) * HW], 0.0), deps=y_cons)
            for dc in range(NDC)]
        y_cons = []
        for en in range(NE):
            jw = en % 2
            lw1 = P.dma("pool", lambda e, en=en, t=w1[jw]: e.dma_start(
                out=t[:].rearrange("p (c i) -> p c i", i=D),
                in_=w1f_h[en].rearrange("(c p) i -> p c i", p=128)),
                deps=w1_cons[jw])
            lw2 = P.dma("pool", lambda e, en=en, t=w2[jw]: e.dma_start(
                out=t[:].rearrange("p (c d) -> p c d", d=D),
                in_=w2f_h[en].rearrange("(c p) d -> p c d", p=128)),
                deps=w2_cons[jw])
            w1_cons[jw], w2_cons[jw] = [], []
            for tcc in range(HW // TC):
                col0 = tcc * TC
                if en > 0:
                    lgr = P.dma("sync", lambda e, en=en, col0=col0: e.dma_start(
                        out=grow[:], in_=g1[en - 1:en,
                                            col0h + col0:col0h + col0 + TC]),
                        deps=gr_cons)
                    mgb = P.op("pe", lambda e: e.matmul(
                        pbc[:], ones1[:], grow[:], start=True, stop=True),
                        deps=[lgr, o1] + gb_cons)
                    egb = P.op("dve", lambda e: e.tensor_copy(gbc[:], pbc[:]),
                               deps=[mgb] + gb_cons)
                    gr_cons = [mgb]
                    gb_cons = [egb]
                hops = []
                for ic in range(8):
                    jj = ic % 2
                    def hfn(e, jw=jw, ic=ic, tcc=tcc, jj=jj):
                        last = None
                        for dc in range(NDC):
                            last = e.matmul(
                                ph_[jj][:],
                                w1[jw][:, dc * D + ic * 128:
                                       dc * D + (ic + 1) * 128],
                                xnh[:, dc * HW + tcc * TC:
                                    dc * HW + tcc * TC + TC],
                                start=(dc == 0), stop=(dc == NDC - 1))
                        return last
                    hm = P.op("pe", hfn, deps=[lw1, ldx] + ph_cons.get(jj, []))
                    ge = P.op("act", lambda e, en=en, ic=ic, jj=jj:
                              e.activation(hsb[:, ic * TC:(ic + 1) * TC],
                                           ph_[jj][:], AF.Gelu,
                                           bias=b1[:, en * NDC + ic:
                                                   en * NDC + ic + 1],
                                           scale=1.0),
                              deps=[hm, ldb1] + h_cons.get(ic, []))
                    ph_cons[jj] = [ge]
                    if en > 0:
                        hc = P.op("pool", lambda e, ic=ic: e.tensor_mul(
                            hsb[:, ic * TC:(ic + 1) * TC],
                            hsb[:, ic * TC:(ic + 1) * TC], gbc[:]),
                            deps=[ge] + gb_cons)
                        gb_cons.append(hc)
                        hops.append(hc)
                    else:
                        hops.append(ge)
                last_dm = None
                for dcp in range(NDC // 2):
                    jj = dcp % 2
                    def dfn(e, jw=jw, dcp=dcp, en=en, tcc=tcc, jj=jj):
                        last = None
                        for half in range(2):
                            dc = 2 * dcp + half
                            for ic in range(8):
                                stop = (ic == 7) and (en != 0)
                                last = e.matmul(
                                    py[jj][:, half * TC:(half + 1) * TC],
                                    w2[jw][:, ic * D + dc * 128:
                                           ic * D + (dc + 1) * 128],
                                    hsb[:, ic * TC:(ic + 1) * TC],
                                    start=(ic == 0), stop=stop)
                            if en == 0:
                                last = e.matmul(
                                    py[jj][:, half * TC:(half + 1) * TC],
                                    be2[:, dc * 128:(dc + 1) * 128],
                                    g1b[:, tcc * TC:(tcc + 1) * TC],
                                    start=False, stop=True)
                        return last
                    dm = P.op("pe", dfn,
                              deps=hops + py_cons.get(jj, []) +
                              ([ldbe2, ldg] if en == 0 else []))
                    ya = P.op("dve", lambda e, dcp=dcp, tcc=tcc, jj=jj:
                              e.tensor_add(
                                  _yview(y1, dcp, tcc),
                                  _yview(y1, dcp, tcc),
                                  py[jj][:].rearrange("p (b t) -> p b t", t=TC)),
                              deps=[dm, ya_prev.get((dcp, tcc))] + ms)
                    ya_prev[(dcp, tcc)] = ya
                    py_cons[jj] = [ya]
                    y_cons.append(ya)
                    last_dm = dm
                for ic in range(8):
                    h_cons[ic] = [last_dm]
                w1_cons[jw].append(hops[-1])
                w2_cons[jw].append(y_cons[-1])
            xn_cons = y_cons[-1:]
            g1_cons = y_cons[-1:]
        for tcc in range(HW // TC):
            for dc in range(NDC):
                k = tcc * NDC + dc
                jj = k % 2
                colg = col0h + tcc * TC
                lds = P.dma("sync", lambda e, dc=dc, colg=colg, jj=jj:
                            e.dma_start(out=ssb[jj][:],
                                        in_=s1[dc * 128:(dc + 1) * 128,
                                               colg:colg + TC]),
                            deps=[prev_comb.get(("s", jj))])
                tadd = P.op("dve", lambda e, dc=dc, tcc=tcc, jj=jj:
                            e.tensor_add(
                                tsc[jj][:],
                                y1[:, dc * HW + tcc * TC:
                                   dc * HW + tcc * TC + TC],
                                xnh[:, dc * HW + tcc * TC:
                                    dc * HW + tcc * TC + TC]),
                            deps=y_cons + [prev_comb.get(("t", jj))])
                gg = P.op("act", lambda e, dc=dc, jj=jj: e.activation(
                    gsb[jj][:], tsc[jj][:], AF.Silu,
                    bias=bs2[:, dc:dc + 1], scale=1.0),
                    deps=[tadd, ldbs2, prev_comb.get(("g", jj))])
                pp = P.op("dve", lambda e, jj=jj: e.tensor_mul(
                    psb[jj][:], gsb[jj][:], ssb[jj][:]),
                    deps=[gg, lds, prev_comb.get(("st", jj))])
                st = P.dma("sync", lambda e, dc=dc, colg=colg, jj=jj:
                           e.dma_start(out=p1[dc * 128:(dc + 1) * 128,
                                              colg:colg + TC],
                                       in_=psb[jj][:]), deps=[pp])
                prev_comb[("t", jj)] = gg
                prev_comb[("g", jj)] = pp
                prev_comb[("s", jj)] = pp
                prev_comb[("st", jj)] = st
                y_cons.append(tadd)
                xn_cons.append(tadd)


def ph_b9(P, ctx):
    x1, p1, out = ctx.hbm["x1"], ctx.hbm["p1"], ctx.hbm["out"]
    xb = [ctx.sbuf(f"b9_x{j}", [128, D], F32) for j in range(2)]
    pb_ = [ctx.sbuf(f"b9_p{j}", [128, D], BF16) for j in range(2)]
    ob = [ctx.sbuf(f"b9_o{j}", [128, D], F32) for j in range(2)]
    prev = {}
    for i in range(NTT):
        j = i % 2
        pv = prev.get(j, {})
        ld = P.dma("sync", lambda e, i=i, t=xb[j]: e.dma_start(
            out=t[:], in_=x1[i * 128:(i + 1) * 128, :]), deps=[pv.get("add")])
        ldp = P.dma("sync", lambda e, i=i, t=pb_[j]: e.dma_start(
            out=t[:], in_=p1[:, i * 128:(i + 1) * 128], transpose=True),
            deps=[pv.get("add")])
        add = P.op("dve", lambda e, j=j: e.tensor_add(
            ob[j][:], xb[j][:], pb_[j][:]), deps=[ld, ldp, pv.get("st")])
        st = P.dma("pool", lambda e, i=i, j=j: e.dma_start(
            out=out[i * 128:(i + 1) * 128, :], in_=ob[j][:]), deps=[add])
        prev[j] = dict(add=add, st=st)


PHASES = {
    "b0": (ph_b0, ["x", "lnw", "lnb"], ["xn0", "h0"]),
    "b1": (ph_b1, ["xn0", "h0", "gw0T"], ["xn0T", "h0T", "cw0"]),
    "b2": (ph_b2, ["h0T", "wq", "wk", "wv", "wo", "theta"], ["s0"]),
    "b2dbg": (ph_b2, ["h0T", "wq", "wk", "wv", "wo", "theta"],
              ["s0", "dbg_q", "dbg_k", "dbg_v", "dbg_o", "dbg_C", "dbg_S"]),
    "b4": (ph_b4, ["xn0T", "cw0", "s0", "gu0", "dn0"], ["p0"]),
    "b4dbg": (ph_b4, ["xn0T", "cw0", "s0", "gu0", "dn0"],
              ["p0", "dbg_y", "dbg_h"]),
    "b5": (ph_b5, ["x", "p0"], ["x1", "xn1"]),
    "b6": (ph_b6, ["xn1", "wr1", "br1"], ["xn1T", "g1"]),
    "b7": (ph_b7, ["xn1T", "wq1", "wk1", "wv1", "bq1", "bk1", "bv1"],
           ["q1", "attn1", "s1"]),
    "b8": (ph_b8, ["xn1T", "g1", "s1", "w1f", "w2f", "b1f", "bs2", "be2"],
           ["p1"]),
    "b9": (ph_b9, ["x1", "p1"], ["out"]),
}


INTERNAL_WRITES = set()


def build_group(phase_names, feed_names):
    nc = bass.Bass("TRN2", target_bir_lowering=False)
    es = ExitStack()
    P = Prog()
    ctx = Ctx(nc, es, P)
    reads, writes = set(), set()
    for pn in phase_names:
        _, r, w = PHASES[pn]
        reads |= set(r)
        writes |= set(w)
    in_names = sorted((reads - writes) | (reads & feed_names))
    out_names = sorted(writes - INTERNAL_WRITES)
    for n in in_names:
        shape, dt = (WEIGHT_SPECS.get(n) or INTER_SPECS.get(n))
        ctx.hbm[n] = nc.dram_tensor(n, shape, dt, kind="ExternalInput").ap()
    for n in out_names:
        shape, dt = INTER_SPECS[n]
        ctx.hbm[n] = nc.dram_tensor(n, shape, dt, kind="ExternalOutput").ap()
    for n in sorted(writes & INTERNAL_WRITES):
        shape, dt = INTER_SPECS[n]
        ctx.hbm[n] = nc.dram_tensor(n, shape, dt, kind="Internal").ap()
    for pi, pn in enumerate(phase_names):
        fn, _, _ = PHASES[pn]
        ctx.ph = ExitStack()
        fn(P, ctx)
        ctx.ph.close()
        P.barrier()
    with nc.Block() as block:
        P.emit(nc, block, es)
    return nc, in_names, out_names


def run_groups(groups, shared, xs, keep=()):
    store = [dict(shared) for _ in range(NCORES)]
    for b in range(NCORES):
        store[b]["x"] = xs[b]
    for gi, gnames in enumerate(groups):
        nc, in_names, out_names = build_group(gnames, set(shared.keys()) | {"x"})
        in_maps = [{n: store[b][n] for n in in_names} for b in range(NCORES)]
        res = run_bass_kernel_spmd(nc, in_maps, core_ids=list(range(NCORES)))
        for b in range(NCORES):
            for n in out_names:
                store[b][n] = np.asarray(res.results[b][n])
    return store


ALL_PHASES = ["b0", "b1", "b2", "b4", "b5", "b6", "b7", "b8", "b9"]
FUSED = True
GROUPS = [ALL_PHASES] if FUSED else [[p] for p in ALL_PHASES]


def kernel(**inputs):
    global INTERNAL_WRITES
    if FUSED:
        INTERNAL_WRITES = set(INTER_SPECS) - {"out"}
    shared, xs = prep_host(inputs)
    store = run_groups(GROUPS, shared, xs)
    return np.stack([store[b]["out"] for b in range(NCORES)])


# version 05
# speedup vs baseline: 1.2042x; 1.1207x over previous
import numpy as np
from contextlib import ExitStack

import concourse.bass as bass
import concourse.mybir as mybir
from concourse.bass_utils import run_bass_kernel_spmd

F32 = mybir.dt.float32
BF16 = mybir.dt.bfloat16
ALU = mybir.AluOpType
AF = mybir.ActivationFunctionType
AX = mybir.AxisListType

NCORES = 8
T = 4096
D = 1024
NDC = D // 128
NTT = T // 128
TC = 512
NTCH = T // TC
NH = 16
HD = 64
NPAIR = 8
I0 = 938
I0P = 1024
E0 = 8
E1 = 31
LN_EPS = 1e-5

ENGS = ("pe", "act", "dve", "pool", "sync")
DEBUG_B2 = False


class Op:
    __slots__ = ("eng", "fn", "deps", "step", "cum")


class Bar:
    __slots__ = ("snap",)


class Prog:
    def __init__(self):
        self.items = {e: [] for e in ENGS}
        self.cum = {e: 0 for e in ENGS}

    def op(self, eng, fn, deps=(), step=1):
        o = Op()
        o.eng = eng
        o.fn = fn
        o.deps = tuple(d for d in deps if d is not None)
        o.step = step
        self.cum[eng] += step
        o.cum = self.cum[eng]
        self.items[eng].append(o)
        return o

    def dma(self, eng, fn, deps=()):
        return self.op(eng, fn, deps, step=16)

    def barrier(self):
        snap = dict(self.cum)
        for e in ENGS:
            b = Bar()
            b.snap = snap
            self.items[e].append(b)

    def emit(self, nc, block, es):
        sems = {e: es.enter_context(nc.semaphore(f"ch_{e}")) for e in ENGS}
        engmap = {"pe": block.tensor, "act": block.scalar, "dve": block.vector,
                  "pool": block.gpsimd, "sync": block.sync}
        self.barrier()
        for ename in ENGS:
            deco = engmap[ename]
            items = self.items[ename]

            def section(e, ename=ename, items=items):
                waited = {x: 0 for x in ENGS}
                for it in items:
                    if isinstance(it, Bar):
                        for oe in ENGS:
                            c = it.snap[oe]
                            if oe != ename and waited[oe] < c:
                                e.wait_ge(sems[oe], c)
                                waited[oe] = c
                        continue
                    needs = {}
                    for d in it.deps:
                        if needs.get(d.eng, 0) < d.cum:
                            needs[d.eng] = d.cum
                    for oe, c in needs.items():
                        if waited[oe] < c:
                            e.wait_ge(sems[oe], c)
                            waited[oe] = c
                    ins = it.fn(e)
                    ins.then_inc(sems[ename], it.step)
                    if it.eng == ename:
                        waited[ename] = max(waited[ename], 0)

            deco(section)


class Ctx:
    def __init__(self, nc, es, prog):
        self.nc = nc
        self.es = es
        self.P = prog
        self.hbm = {}
        self.ph = None
        self._consts = {}

    def sbuf(self, name, shape, dtype, scope=None):
        return (scope or self.ph).enter_context(
            self.nc.sbuf_tensor(name, shape, dtype))

    def psum(self, name, shape, dtype=F32, scope=None):
        return (scope or self.ph).enter_context(
            self.nc.psum_tensor(name, shape, dtype))

    def constant(self, value):
        value = float(value)
        if value in self._consts:
            return self._consts[value]
        t = self.es.enter_context(
            self.nc.sbuf_tensor(f"const_{len(self._consts)}", [128, 1], F32))
        op = self.P.op("dve", lambda e: e.memset(t[:], value))
        self.nc.const_aps.register(F32, value, t[:])
        self._consts[value] = op
        return op


def _perm_qk_cols():
    perm = []
    for h in range(NH):
        base = h * HD
        perm.extend(base + 2 * np.arange(32))
        perm.extend(base + 2 * np.arange(32) + 1)
    return np.array(perm)


def prep_host(inputs):
    g = {k: np.asarray(v) for k, v in inputs.items()}
    out = {}
    perm = _perm_qk_cols()
    out["wq"] = np.ascontiguousarray(g["L0_attn_w_q"][:, perm])
    out["wk"] = np.ascontiguousarray(g["L0_attn_w_kv"][:, :D][:, perm])
    out["wv"] = np.ascontiguousarray(g["L0_attn_w_kv"][:, D:])
    out["wo"] = np.ascontiguousarray(g["L0_attn_w_o"])
    out["lnw"] = np.ascontiguousarray(g["L0_attn_ln_w"].reshape(1, D))
    out["lnb"] = np.ascontiguousarray(g["L0_attn_ln_b"].reshape(1, D))
    out["theta"] = np.asarray(g["L0_attn_theta"], dtype=np.float32).reshape(1, 1)
    out["gw0T"] = np.ascontiguousarray(g["L0_ffn_gate_weight"].T)

    def prep_gu(gu):
        o = np.zeros((2 * I0P, D), dtype=np.float32)
        gp, up = gu[:I0], gu[I0:]
        for ic in range(8):
            lo, hi = ic * 128, min((ic + 1) * 128, I0)
            n = hi - lo
            if n > 0:
                o[ic * 256: ic * 256 + n] = gp[lo:hi]
                o[ic * 256 + 128: ic * 256 + 128 + n] = up[lo:hi]
        return o

    def prep_dn(dn):
        o = np.zeros((I0P, D), dtype=np.float32)
        o[:I0] = dn.T
        return o

    out["gu0"] = np.ascontiguousarray(np.stack(
        [prep_gu(g["L0_ffn_shared_gate_up"]).T] +
        [prep_gu(g["L0_ffn_experts_gate_up"][e]).T for e in range(E0)]))
    out["dn0"] = np.ascontiguousarray(np.stack(
        [prep_dn(g["L0_ffn_shared_down"])] +
        [prep_dn(g["L0_ffn_experts_down"][e]) for e in range(E0)]))
    out["wq1"] = np.ascontiguousarray(g["L1_attn_Wq"].T)
    out["wk1"] = np.ascontiguousarray(g["L1_attn_Wk"].T)
    out["wv1"] = np.ascontiguousarray(g["L1_attn_Wv"].T)
    out["bq1"] = np.ascontiguousarray(g["L1_attn_bq"].reshape(NDC, 128).T)
    out["bk1"] = np.ascontiguousarray(g["L1_attn_bk"].reshape(NDC, 128).T)
    out["bv1"] = np.ascontiguousarray(g["L1_attn_bv"].reshape(1, D))
    out["w1f"] = np.ascontiguousarray(np.stack(
        [g["L1_ffn_ws1"]] + [g["L1_ffn_We1"][e] for e in range(E1)]))
    out["w2f"] = np.ascontiguousarray(np.stack(
        [g["L1_ffn_ws2"]] + [g["L1_ffn_We2"][e] for e in range(E1)]))
    out["b1f"] = np.ascontiguousarray(np.stack(
        [g["L1_ffn_bs1"]] + [g["L1_ffn_Be1"][e] for e in range(E1)]
    ).reshape(32, NDC, 128).transpose(2, 0, 1))
    out["bs2"] = np.ascontiguousarray(g["L1_ffn_bs2"].reshape(NDC, 128).T)
    out["be2"] = np.ascontiguousarray(g["L1_ffn_Be2"])
    out["wr1"] = np.ascontiguousarray(g["L1_ffn_Wr"])
    out["br1"] = np.ascontiguousarray(g["L1_ffn_br"].reshape(1, E1))
    xs = [np.ascontiguousarray(g["L0_attn_x"][b]) for b in range(NCORES)]
    return out, xs


WEIGHT_SPECS = {
    "wq": ([D, D], F32), "wk": ([D, D], F32), "wv": ([D, D], F32),
    "wo": ([D, D], F32), "lnw": ([1, D], F32), "lnb": ([1, D], F32),
    "theta": ([1, 1], F32), "gw0T": ([D, E0], F32),
    "gu0": ([E0 + 1, D, 2 * I0P], F32), "dn0": ([E0 + 1, I0P, D], F32),
    "wq1": ([D, D], F32), "wk1": ([D, D], F32), "wv1": ([D, D], F32),
    "bq1": ([128, NDC], F32), "bk1": ([128, NDC], F32), "bv1": ([1, D], F32),
    "w1f": ([32, D, D], F32), "w2f": ([32, D, D], F32),
    "b1f": ([128, 32, NDC], F32), "bs2": ([128, NDC], F32),
    "be2": ([E1, D], F32), "wr1": ([D, E1], F32), "br1": ([1, E1], F32),
    "x": ([T, D], F32),
}
INTER_SPECS = {
    "xn0": ([T, D], BF16), "h0": ([T, D], BF16),
    "xn0T": ([D, T], BF16), "h0T": ([D, T], BF16),
    "cw0": ([E0, T], F32),
    "s0": ([D, T], BF16), "p0": ([D, T], BF16),
    "x1": ([T, D], F32), "xn1": ([T, D], BF16), "xn1T": ([D, T], BF16),
    "g1": ([E1, T], F32), "q1": ([D, T], BF16),
    "attn1": ([T, T], BF16), "s1": ([D, T], BF16), "p1": ([D, T], BF16),
    "out": ([T, D], F32),
    "dbg_q": ([128, T], BF16), "dbg_k": ([128, T], BF16),
    "dbg_C": ([128, T], F32), "dbg_S": ([128, T], F32),
    "dbg_y": ([128, NDC * 1024], F32), "dbg_h": ([128, 8 * TC], BF16),
    "dbg_v": ([128, 32 * 130], BF16), "dbg_o": ([128, NPAIR * T], BF16),
}


def ph_b0(P, ctx):
    x, xn0, h0 = ctx.hbm["x"], ctx.hbm["xn0"], ctx.hbm["h0"]
    ceps = ctx.constant(LN_EPS)
    cm1 = ctx.constant(-1.0)
    cmh = ctx.constant(-0.5)
    lnw_h, lnb_h = ctx.hbm["lnw"], ctx.hbm["lnb"]
    xbuf = [ctx.sbuf(f"b0_x{j}", [128, D], F32) for j in range(2)]
    sq = [ctx.sbuf(f"b0_sq{j}", [128, D], F32) for j in range(2)]
    xnf = [ctx.sbuf(f"b0_xn{j}", [128, D], F32) for j in range(2)]
    z2f = [ctx.sbuf(f"b0_z2{j}", [128, D], F32) for j in range(2)]
    hbf = [ctx.sbuf(f"b0_h{j}", [128, D], BF16) for j in range(2)]
    xnbf = [ctx.sbuf(f"b0_xnb{j}", [128, D], BF16) for j in range(2)]
    st = [ctx.sbuf(f"b0_st{j}", [128, 8], F32) for j in range(2)]
    lnw_bc = ctx.sbuf("b0_lnwbc", [128, D], F32)
    lnb_bc = ctx.sbuf("b0_lnbbc", [128, D], F32)
    lnrow = ctx.sbuf("b0_lnrow", [1, 2 * D], F32)
    ones1 = ctx.sbuf("b0_ones", [1, 128], F32)
    pb = [ctx.psum(f"b0_pb{i}", [128, 512], F32) for i in range(4)]

    ldr = P.dma("sync", lambda e: e.dma_start(out=lnrow[:, 0:D], in_=lnw_h[:]))
    ldr2 = P.dma("sync", lambda e: e.dma_start(out=lnrow[:, D:2 * D], in_=lnb_h[:]))
    on = P.op("dve", lambda e: e.memset(ones1[:], 1.0))
    mms, cps = [], []
    for k in range(4):
        src = lnrow[:, (k % 2) * 512 + (k // 2) * D: (k % 2) * 512 + (k // 2) * D + 512]
        mms.append(P.op("pe", lambda e, pbk=pb[k], s=src: e.matmul(
            pbk[:], ones1[:], s, start=True, stop=True), deps=[ldr, ldr2, on]))
    for k in range(4):
        dst = (lnw_bc if k < 2 else lnb_bc)[:, (k % 2) * 512:(k % 2) * 512 + 512]
        cps.append(P.op("dve", lambda e, d=dst, pbk=pb[k]: e.tensor_copy(d, pbk[:]),
                        deps=[mms[k]]))
    C = 1.0 / D
    prev = {}
    for i in range(NTT):
        j = i % 2
        xb, sqb, xnb, z2b, hb, xnbb, stb = (xbuf[j], sq[j], xnf[j], z2f[j],
                                            hbf[j], xnbf[j], st[j])
        sx, sx2 = stb[:, 0:1], stb[:, 1:2]
        mu, nv, rstd = stb[:, 2:3], stb[:, 3:4], stb[:, 4:5]
        mu2, nv2, rstd2 = stb[:, 5:6], stb[:, 6:7], stb[:, 7:8]
        pv = prev.get(j, {})
        ld = P.dma("sync", lambda e, i=i, xb=xb: e.dma_start(
            out=xb[:], in_=x[i * 128:(i + 1) * 128, :]),
            deps=[pv.get("xn"), pv.get("sq1")])
        summ = P.op("dve", lambda e, sx=sx, xb=xb: e.reduce_sum(sx, xb[:], axis=AX.X),
                    deps=[ld, pv.get("nv")])
        sq1 = P.op("act", lambda e, sqb=sqb, xb=xb, sx2=sx2: e.activation(
            sqb[:], xb[:], AF.Square, bias=0.0, scale=1.0, accum_out=sx2),
            deps=[ld, pv.get("t1"), pv.get("nv")])
        ex2 = P.op("dve", lambda e, sx2=sx2: e.tensor_scalar(
            sx2, sx2, C, None, ALU.mult), deps=[sq1])
        muo = P.op("dve", lambda e, mu=mu, sx=sx: e.tensor_scalar(
            mu, sx, C, None, ALU.mult), deps=[summ])
        nvo = P.op("dve", lambda e, nv=nv, mu=mu, sx2=sx2: e.scalar_tensor_tensor(
            nv, mu, mu, sx2, ALU.mult, ALU.subtract), deps=[muo, ex2])
        lnv = P.op("act", lambda e, nv=nv: e.activation(
            nv, nv, AF.Ln, bias=LN_EPS, scale=-1.0), deps=[nvo, ceps, cm1])
        rst = P.op("act", lambda e, rstd=rstd, nv=nv: e.activation(
            rstd, nv, AF.Exp, bias=0.0, scale=-0.5), deps=[lnv, cmh])
        xn = P.op("dve", lambda e, xnb=xnb, xb=xb, mu=mu, rstd=rstd: e.tensor_scalar(
            xnb[:], xb[:], mu, rstd, ALU.subtract, ALU.mult),
            deps=[muo, rst, pv.get("z2"), pv.get("xnbf"), pv.get("sum2"), pv.get("sq2")])
        sum2 = P.op("dve", lambda e, mu2=mu2, xnb=xnb: e.reduce_sum(
            mu2, xnb[:], axis=AX.X), deps=[xn])
        sq2 = P.op("act", lambda e, sqb=sqb, xnb=xnb, nv2=nv2: e.activation(
            sqb[:], xnb[:], AF.Square, bias=0.0, scale=1.0, accum_out=nv2),
            deps=[xn, sq1])
        ex2b = P.op("dve", lambda e, nv2=nv2: e.tensor_scalar(
            nv2, nv2, C, None, ALU.mult), deps=[sq2])
        mu2o = P.op("dve", lambda e, mu2=mu2: e.tensor_scalar(
            mu2, mu2, C, None, ALU.mult), deps=[sum2])
        nv2o = P.op("dve", lambda e, nv2=nv2, mu2=mu2: e.scalar_tensor_tensor(
            nv2, mu2, mu2, nv2, ALU.mult, ALU.subtract), deps=[mu2o, ex2b])
        lnv2 = P.op("act", lambda e, nv2=nv2: e.activation(
            nv2, nv2, AF.Ln, bias=LN_EPS, scale=-1.0), deps=[nv2o, ceps, cm1])
        rst2 = P.op("act", lambda e, rstd2=rstd2, nv2=nv2: e.activation(
            rstd2, nv2, AF.Exp, bias=0.0, scale=-0.5), deps=[lnv2, cmh])
        z2 = P.op("dve", lambda e, z2b=z2b, xnb=xnb, mu2=mu2, rstd2=rstd2:
                  e.tensor_scalar(z2b[:], xnb[:], mu2, rstd2, ALU.subtract, ALU.mult),
                  deps=[mu2o, rst2, pv.get("t1")])
        t1 = P.op("dve", lambda e, sqb=sqb, z2b=z2b: e.tensor_mul(
            sqb[:], z2b[:], lnw_bc[:]), deps=[z2, sq2] + cps[:2])
        h = P.op("dve", lambda e, hb=hb, sqb=sqb: e.tensor_add(
            hb[:], sqb[:], lnb_bc[:]), deps=[t1, pv.get("sth")] + cps[2:])
        xnbf_o = P.op("act", lambda e, xnbb=xnbb, xnb=xnb: e.activation(
            xnbb[:], xnb[:], AF.Copy, bias=0.0, scale=1.0),
            deps=[xn, pv.get("stx")])
        sth = P.dma("pool", lambda e, i=i, hb=hb: e.dma_start(
            out=h0[i * 128:(i + 1) * 128, :], in_=hb[:]), deps=[h])
        stx = P.dma("pool", lambda e, i=i, xnbb=xnbb: e.dma_start(
            out=xn0[i * 128:(i + 1) * 128, :], in_=xnbb[:]), deps=[xnbf_o])
        prev[j] = dict(xn=xn, sq1=sq1, z2=z2, t1=t1, sth=sth, stx=stx,
                       xnbf=xnbf_o, nv=nv2o, sum2=sum2, sq2=sq2)


def ph_b1(P, ctx):
    xn0, h0 = ctx.hbm["xn0"], ctx.hbm["h0"]
    xn0T, h0T = ctx.hbm["xn0T"], ctx.hbm["h0T"]
    gw0T_h, cw0 = ctx.hbm["gw0T"], ctx.hbm["cw0"]

    xt = [[ctx.sbuf(f"b1_xt{j}_{dc}", [128, TC], BF16) for dc in range(NDC)]
          for j in range(2)]
    ht = [[ctx.sbuf(f"b1_ht{j}_{dc}", [128, TC], BF16) for dc in range(NDC)]
          for j in range(2)]
    gw = ctx.sbuf("b1_gw", [128, NDC * E0], BF16)
    lg = [ctx.sbuf(f"b1_lg{j}", [128, 4 * E0], F32) for j in range(2)]
    sc2 = [ctx.sbuf(f"b1_sc2{j}", [128, 4 * E0], F32) for j in range(2)]
    tmp = [ctx.sbuf(f"b1_tmp{j}", [128, 4 * E0], F32) for j in range(2)]
    msk = [ctx.sbuf(f"b1_msk{j}", [128, 4 * E0], F32) for j in range(2)]
    cwt = [ctx.sbuf(f"b1_cwt{j}", [128, 4 * E0], F32) for j in range(2)]
    r1 = [ctx.sbuf(f"b1_r1{j}", [128, 4], F32) for j in range(2)]
    r2 = [ctx.sbuf(f"b1_r2{j}", [128, 4], F32) for j in range(2)]
    cwfm = ctx.sbuf("b1_cwfm", [E0, T], F32)
    ident = ctx.sbuf("b1_id", [128, 128], F32)
    ia = ctx.sbuf("b1_ia", [128, 128], F32)
    plg = [ctx.psum(f"b1_plg{j}", [128, E0], F32) for j in range(2)]
    ptr = [ctx.psum(f"b1_ptr{j}", [E0, 128], F32) for j in range(2)]

    i1 = P.op("pool", lambda e: e.iota(ia[:], pattern=[[0, 128]], base=0,
              channel_multiplier=1, allow_small_or_imprecise_dtypes=True))
    i2 = P.op("pool", lambda e: e.iota(ident[:], pattern=[[1, 128]], base=0,
              channel_multiplier=0, allow_small_or_imprecise_dtypes=True))
    idt = P.op("dve", lambda e: e.tensor_tensor(ident[:], ia[:], ident[:],
               ALU.is_equal), deps=[i1, i2])
    ldw = [P.dma("pool", lambda e, dc=dc: e.dma_start(
        out=gw[:, dc * E0:(dc + 1) * E0],
        in_=gw0T_h[dc * 128:(dc + 1) * 128, :])) for dc in range(NDC)]

    prev = {}
    evcw_by_k = {}
    ev_by_k = {}
    for i in range(NTCH):
        j = i % 2
        pv = prev.get(j, {})
        lds, ldsh = [], []
        for dc in range(NDC):
            lds.append(P.dma("sync", lambda e, i=i, dc=dc, t=xt[j][dc]: e.dma_start(
                out=t[:], in_=xn0[i * TC:(i + 1) * TC, dc * 128:(dc + 1) * 128],
                transpose=True), deps=pv.get("xt_cons", [])))
            ldsh.append(P.dma("sync", lambda e, i=i, dc=dc, t=ht[j][dc]: e.dma_start(
                out=t[:], in_=h0[i * TC:(i + 1) * TC, dc * 128:(dc + 1) * 128],
                transpose=True), deps=pv.get("ht_cons", [])))
        stws = []
        for dc in range(NDC):
            stws.append(P.dma("pool", lambda e, i=i, dc=dc, t=xt[j][dc]: e.dma_start(
                out=xn0T[dc * 128:(dc + 1) * 128, i * TC:(i + 1) * TC], in_=t[:]),
                deps=[lds[dc]]))
            stws.append(P.dma("pool", lambda e, i=i, dc=dc, t=ht[j][dc]: e.dma_start(
                out=h0T[dc * 128:(dc + 1) * 128, i * TC:(i + 1) * TC], in_=t[:]),
                deps=[ldsh[dc]]))
        mmops, evops = [], []
        for tt in range(4):
            k = i * 4 + tt
            jj = k % 2

            def mmfn(e, jj=jj, j=j, tt=tt):
                last = None
                for dc in range(NDC):
                    last = e.matmul(plg[jj][:],
                                    xt[j][dc][:, tt * 128:(tt + 1) * 128],
                                    gw[:, dc * E0:(dc + 1) * E0],
                                    start=(dc == 0), stop=(dc == NDC - 1))
                return last
            mm = P.op("pe", mmfn,
                      deps=lds + ldw + [ev_by_k.get(k - 2), pv.get("ev%d" % tt)])
            ev = P.op("dve", lambda e, j=j, tt=tt, jj=jj: e.tensor_copy(
                lg[j][:, tt * E0:(tt + 1) * E0], plg[jj][:]),
                deps=[mm, pv.get("chain_last")])
            ev_by_k[k] = ev
            mmops.append(mm)
            evops.append(ev)
        lgv = lg[j][:].rearrange("p (s e) -> p s e", e=E0)
        sc2v = sc2[j][:].rearrange("p (s e) -> p s e", e=E0)
        tmpv = tmp[j][:].rearrange("p (s e) -> p s e", e=E0)
        mskv = msk[j][:].rearrange("p (s e) -> p s e", e=E0)
        cwtv = cwt[j][:].rearrange("p (s e) -> p s e", e=E0)
        r1b = r1[j][:].rearrange("p (s o) -> p s o", o=1).to_broadcast([128, 4, E0])
        r2b = r2[j][:].rearrange("p (s o) -> p s o", o=1).to_broadcast([128, 4, E0])
        c1 = P.op("dve", lambda e, lgv=lgv, r=r1[j]: e.tensor_reduce(
            r[:].rearrange("p (s o) -> p s o", o=1), lgv, axis=AX.X, op=ALU.max, negate=True),
            deps=evops + [pv.get("c_last")])
        c2 = P.op("dve", lambda e, sc2v=sc2v, lgv=lgv, r1b=r1b: e.tensor_tensor(
            sc2v, lgv, r1b, ALU.add), deps=[c1] + [pv.get("c2w")])
        c3 = P.op("act", lambda e, s=sc2[j]: e.activation(
            s[:], s[:], AF.Exp, bias=0.0, scale=1.0), deps=[c2])
        c4 = P.op("dve", lambda e, sc2v=sc2v, r=r1[j]: e.tensor_reduce(
            r[:].rearrange("p (s o) -> p s o", o=1), sc2v, axis=AX.X, op=ALU.add), deps=[c3])
        c5 = P.op("dve", lambda e, r=r1[j]: e.reciprocal(r[:], r[:]), deps=[c4])
        c6 = P.op("dve", lambda e, sc2v=sc2v, r1b=r1b: e.tensor_tensor(
            sc2v, sc2v, r1b, ALU.mult), deps=[c5])
        c7 = P.op("dve", lambda e, sc2v=sc2v, r=r1[j]: e.tensor_reduce(
            r[:].rearrange("p (s o) -> p s o", o=1), sc2v, axis=AX.X, op=ALU.max), deps=[c6])
        c8 = P.op("dve", lambda e, mskv=mskv, sc2v=sc2v, r1b=r1b: e.tensor_tensor(
            mskv, sc2v, r1b, ALU.is_equal), deps=[c7])
        c9 = P.op("dve", lambda e, tmpv=tmpv, sc2v=sc2v, mskv=mskv: e.tensor_tensor(
            tmpv, sc2v, mskv, ALU.mult), deps=[c8])
        c10 = P.op("dve", lambda e, tmpv=tmpv, sc2v=sc2v: e.tensor_tensor(
            tmpv, sc2v, tmpv, ALU.subtract), deps=[c9])
        c11 = P.op("dve", lambda e, tmpv=tmpv, r=r2[j]: e.tensor_reduce(
            r[:].rearrange("p (s o) -> p s o", o=1), tmpv, axis=AX.X, op=ALU.max), deps=[c10])
        c12 = P.op("dve", lambda e, cwtv=cwtv, tmpv=tmpv, r2b=r2b: e.tensor_tensor(
            cwtv, tmpv, r2b, ALU.is_equal), deps=[c11])
        c13 = P.op("dve", lambda e, mskv=mskv, cwtv=cwtv: e.tensor_tensor(
            mskv, mskv, cwtv, ALU.add), deps=[c12])
        c14 = P.op("dve", lambda e, r1=r1[j], r2=r2[j]: e.tensor_tensor(
            r1[:], r1[:], r2[:], ALU.add), deps=[c11, c8])
        c15 = P.op("dve", lambda e, r1=r1[j]: e.tensor_scalar(
            r1[:], r1[:], 1e-8, None, ALU.add), deps=[c14])
        c16 = P.op("dve", lambda e, r1=r1[j]: e.reciprocal(r1[:], r1[:]), deps=[c15])
        c17 = P.op("dve", lambda e, cwtv=cwtv, sc2v=sc2v, mskv=mskv: e.tensor_tensor(
            cwtv, sc2v, mskv, ALU.mult), deps=[c13, c12])
        c18 = P.op("dve", lambda e, cwtv=cwtv, r1b=r1b: e.tensor_tensor(
            cwtv, cwtv, r1b, ALU.mult), deps=[c16, c17])
        tpev = []
        for tt in range(4):
            k = i * 4 + tt
            jj = k % 2
            tpo = P.op("pe", lambda e, j=j, tt=tt, jj=jj: e.transpose(
                ptr[jj][:], cwt[j][:, tt * E0:(tt + 1) * E0], ident[:]),
                deps=[c18, idt, evcw_by_k.get(k - 2)])
            evc = P.op("dve", lambda e, i=i, tt=tt, jj=jj: e.tensor_copy(
                cwfm[:, i * TC + tt * 128: i * TC + (tt + 1) * 128], ptr[jj][:]),
                deps=[tpo])
            evcw_by_k[k] = evc
            tpev.append(evc)
        prev[j] = {
            "xt_cons": mmops + [stws[0]] + stws[::2],
            "ht_cons": stws[1::2],
            "ev0": evops[0], "ev1": evops[1], "ev2": evops[2], "ev3": evops[3],
            "chain_last": c18, "c_last": c18, "c2w": c18,
        }
    stcw = P.dma("pool", lambda e: e.dma_start(out=cw0[:], in_=cwfm[:]),
                 deps=list(evcw_by_k.values()))


def ph_b2(P, ctx):
    h0T = ctx.hbm["h0T"]
    wq_h, wk_h, wv_h, wo_h = (ctx.hbm["wq"], ctx.hbm["wk"], ctx.hbm["wv"],
                              ctx.hbm["wo"])
    theta_h, s0 = ctx.hbm["theta"], ctx.hbm["s0"]

    TOP = 2595.0 * np.log10(np.float32(1.0 + 4000.0 / 200.0))
    CS = float(0.125 / np.sqrt(2.0))
    BS = float(1.0 / np.sqrt(2.0))
    INV2PI = float(np.float32(1.0 / (2 * np.pi)))
    MAGIC = 12582912.0
    CW1 = 6.25
    CW2 = 0.032958984375
    CW3 = 0.000226322804586232
    MEL_C = float(np.log(10.0) / 2595.0 * (TOP / 31.0))
    cpre = [ctx.constant(v) for v in (MEL_C, CS, BS)]

    fcol = ctx.sbuf("b2_fcol", [32, 1], F32)
    thc = ctx.sbuf("b2_thc", [32, 1], F32)
    throw = ctx.sbuf("b2_throw", [1, 1], F32)
    ones32 = ctx.sbuf("b2_ones32", [1, 32], F32)
    ones64 = ctx.sbuf("b2_ones64", [1, 64], F32)
    onescol = ctx.sbuf("b2_onescol", [128, 1], BF16)
    C128 = ctx.sbuf("b2_C128", [128, T], F32)
    S128 = ctx.sbuf("b2_S128", [128, T], F32)
    o_all = ctx.sbuf("b2_oall", [128, NPAIR * T], BF16)
    v_all = ctx.sbuf("b2_vall", [128, 32 * 130], BF16)
    vv = v_all[:].rearrange("p (k c) -> p k c", c=130)
    qbf = ctx.sbuf("b2_qbf", [128, T], BF16)
    kbf = ctx.sbuf("b2_kbf", [128, T], BF16)
    qt32 = ctx.sbuf("b2_qt", [128, TC], F32)
    qsw = ctx.sbuf("b2_qsw", [128, TC], F32)
    qcs = ctx.sbuf("b2_qcs", [128, TC], F32)
    ttb2 = [ctx.sbuf(f"b2_tt{j}", [128, 2 * TC], BF16) for j in range(2)]
    svb = [ctx.sbuf(f"b2_sv{h}", [128, 1], F32) for h in range(2)]
    osb = [ctx.sbuf(f"b2_osb{j}", [128, TC], F32) for j in range(2)]
    drow = [ctx.sbuf(f"b2_drow{j}", [1, TC], F32) for j in range(2)]
    bcsb = [ctx.sbuf(f"b2_bc{j}", [64, TC], F32) for j in range(2)]
    _tabscope = ExitStack()
    ang = ctx.sbuf("b2_ang", [32, T], F32, scope=_tabscope)
    kk = ctx.sbuf("b2_kk", [32, T], F32, scope=_tabscope)
    red = ctx.sbuf("b2_red", [32, T], F32, scope=_tabscope)
    sin32 = ang
    cos32 = kk
    pth = ctx.psum("b2_pth", [32, 1], F32, scope=_tabscope)

    _f32 = np.float32
    _top = _f32(2595.0) * np.log10(_f32(21.0))
    _lin = np.linspace(_f32(0.0), _top, 32, dtype=_f32)
    _mel = np.power(_f32(10.0), _lin / _f32(2595.0), dtype=_f32) - _f32(1.0)
    FB = (_f32(200.0) * _mel / _f32(1000.0)).astype(_f32)
    fbrow = ctx.sbuf("b2_fbrow", [1, 32], F32, scope=_tabscope)
    one11 = ctx.sbuf("b2_one11", [1, 1], F32, scope=_tabscope)
    ldth = P.dma("sync", lambda e: e.dma_start(out=throw[:], in_=theta_h[:]))
    o32 = P.op("dve", lambda e: e.memset(ones32[:], 1.0))
    o64 = P.op("dve", lambda e: e.memset(ones64[:], 1.0))
    ocol = P.op("dve", lambda e: e.memset(onescol[:], 1.0))
    o11 = P.op("dve", lambda e: e.memset(one11[:], 1.0))
    fbsets = [P.op("dve", lambda e, i=i: e.memset(fbrow[:, i:i + 1],
                   float(FB[i]))) for i in range(32)]
    t220 = P.op("dve", lambda e: e.tensor_scalar(
        throw[:], throw[:], float(np.float32(1.0) / np.float32(220.0)),
        None, ALU.mult), deps=[ldth])
    mth = P.op("pe", lambda e: e.matmul(pth[:], ones32[:], throw[:],
               start=True, stop=True), deps=[t220, o32])
    evth = P.op("dve", lambda e: e.tensor_copy(thc[:], pth[:]), deps=[mth])
    mfb = P.op("pe", lambda e: e.transpose(pth[:], fbrow[:], one11[:]),
               deps=fbsets + [o11, evth])
    evfb = P.op("dve", lambda e: e.tensor_copy(fcol[:], pth[:]), deps=[mfb])
    frq = P.op("dve", lambda e: e.tensor_mul(fcol[:], fcol[:], thc[:]),
               deps=[evfb, evth])
    iot = P.op("pool", lambda e: e.iota(ang[:], pattern=[[1, T]], base=0,
               channel_multiplier=0, allow_small_or_imprecise_dtypes=True))
    a1 = P.op("dve", lambda e: e.tensor_scalar(ang[:], ang[:], fcol[:, 0:1],
              None, ALU.mult), deps=[iot, frq])
    k1 = P.op("dve", lambda e: e.tensor_scalar(kk[:], ang[:], INV2PI, MAGIC,
              ALU.mult, ALU.add), deps=[a1])
    k2 = P.op("dve", lambda e: e.tensor_scalar(kk[:], kk[:], MAGIC, None,
              ALU.subtract), deps=[k1])
    rr1 = P.op("dve", lambda e: e.scalar_tensor_tensor(red[:], kk[:], -CW1,
               ang[:], ALU.mult, ALU.add), deps=[k2])
    rr2 = P.op("dve", lambda e: e.scalar_tensor_tensor(red[:], kk[:], -CW2,
               red[:], ALU.mult, ALU.add), deps=[rr1])
    rr3 = P.op("dve", lambda e: e.scalar_tensor_tensor(red[:], kk[:], -CW3,
               red[:], ALU.mult, ALU.add), deps=[rr2])
    sn = P.op("act", lambda e: e.activation(sin32[:], red[:], AF.Sin,
              bias=0.0, scale=1.0), deps=[rr3])
    w1 = P.op("dve", lambda e: e.tensor_scalar(red[:], red[:],
              float(np.pi / 2), None, ALU.add), deps=[sn])
    w2 = P.op("dve", lambda e: e.tensor_scalar(kk[:], red[:], float(np.pi),
              None, ALU.is_gt), deps=[w1])
    w3 = P.op("dve", lambda e: e.scalar_tensor_tensor(red[:], kk[:],
              float(-2 * np.pi), red[:], ALU.mult, ALU.add), deps=[w2])
    cs_ = P.op("act", lambda e: e.activation(cos32[:], red[:], AF.Sin,
               bias=0.0, scale=1.0), deps=[w3])
    tabops = []
    for blk in range(4):
        sl = slice(blk * 32, (blk + 1) * 32)
        eng = "dve" if blk % 2 == 0 else "pool"
        tabops.append(P.op(eng, lambda e, sl=sl: e.tensor_copy(
            C128[sl, :], cos32[:]), deps=[cs_]))
        if blk % 2 == 0:
            tabops.append(P.op(eng, lambda e, sl=sl: e.tensor_scalar(
                S128[sl, :], sin32[:], -1.0, None, ALU.mult), deps=[sn, cs_]))
        else:
            tabops.append(P.op(eng, lambda e, sl=sl: e.tensor_copy(
                S128[sl, :], sin32[:]), deps=[sn, cs_]))


    mo1 = P.op("dve", lambda e: e.memset(vv[:, :, 64:65], 1.0))
    mo2 = P.op("dve", lambda e: e.memset(vv[:, :, 129:130], 1.0))
    _tabscope.close()
    _qkvscope = ExitStack()
    wqp = [ctx.sbuf(f"b2_wqp{j}", [128, D], BF16, scope=_qkvscope) for j in range(2)]
    wkp = [ctx.sbuf(f"b2_wkp{j}", [128, D], BF16, scope=_qkvscope) for j in range(2)]
    wvp = [ctx.sbuf(f"b2_wvp{j}", [128, D], BF16, scope=_qkvscope) for j in range(2)]
    hbuf = [ctx.sbuf(f"b2_h{j}", [128, NDC * TC], BF16, scope=_qkvscope)
            for j in range(2)]
    pqkv = ctx.psum("b2_pqkv", [128, 2 * TC], F32, scope=_qkvscope)
    ps2 = [ctx.psum(f"b2_ps{j}", [128, 2 * TC], F32, scope=_qkvscope)
           for j in range(2)]
    po = ctx.psum("b2_po", [65, TC], F32, scope=_qkvscope)
    pmix = ctx.psum("b2_pmix", [128, TC], F32, scope=_qkvscope)
    pa = pqkv[:, 0:TC]
    pb = pqkv[:, TC:2 * TC]
    pv = pqkv[:, 0:128]
    psv = pmix[0:65, 0:1]
    pbc = pmix[0:64, :]

    tabdep = tabops + [cs_, sn]

    def wsrc(w, p):
        return w[:, p * 128:(p + 1) * 128].rearrange("(c p) f -> p c f", p=128)

    w_cons = {("q", 0): [], ("q", 1): [], ("k", 0): [], ("k", 1): [],
              ("v", 0): [], ("v", 1): []}
    h_cons = {0: [], 1: []}
    ep_by_grp = {}
    av_by_grp = {}
    av_by_kt = {}
    last_qev = None
    last_kev = None
    last_vev = None
    last_sve = None
    last_smm = None
    last_oev = None
    pending = None
    pend_d5 = {}
    pend_d4 = [None]
    prev_d5 = pend_d5
    sv_ref = {}

    def emit_div_epilogue(st8):
        pp, hh, qq, jo, oev = st8
        svop = sv_ref[hh]
        d1 = P.op("dve", lambda e, jo=jo, hh=hh: e.tensor_scalar(
            drow[jo][:], osb[jo][64:65, :], svb[hh][64:65, 0:1], None,
            ALU.add), deps=[oev, svop, pend_d4[0]])
        d2 = P.op("dve", lambda e, jo=jo: e.reciprocal(drow[jo][:],
                  drow[jo][:]), deps=[d1])
        d3 = P.op("pe", lambda e, jo=jo: e.matmul(
            pbc, ones64[:], drow[jo][:], start=True, stop=True),
            deps=[d2, o64, pend_d4[0]])
        d4 = P.op("dve", lambda e, jo=jo: e.tensor_copy(bcsb[jo][:], pbc),
                  deps=[d3, pend_d5.get(jo)])
        d5 = P.op("dve", lambda e, jo=jo, hh=hh, pp=pp, qq=qq:
                  e.scalar_tensor_tensor(
                      o_all[hh * 64:(hh + 1) * 64,
                            pp * T + qq * TC: pp * T + (qq + 1) * TC],
                      osb[jo][0:64, :], svb[hh][0:64, 0:1], bcsb[jo][:],
                      ALU.add, ALU.mult), deps=[d4, oev, svop])
        pend_d4[0] = d4
        pend_d5[jo] = d5

    for p in range(NPAIR):
        jw = p % 2
        extra = tabdep if p < 2 else []
        lwq = P.dma("pool", lambda e, p=p, t=wqp[jw]: e.dma_start(
            out=t[:].rearrange("p (c f) -> p c f", f=128), in_=wsrc(wq_h, p)),
            deps=w_cons[("q", jw)] + extra)
        lwk = P.dma("pool", lambda e, p=p, t=wkp[jw]: e.dma_start(
            out=t[:].rearrange("p (c f) -> p c f", f=128), in_=wsrc(wk_h, p)),
            deps=w_cons[("k", jw)] + extra)
        lwv = P.dma("pool", lambda e, p=p, t=wvp[jw]: e.dma_start(
            out=t[:].rearrange("p (c f) -> p c f", f=128), in_=wsrc(wv_h, p)),
            deps=w_cons[("v", jw)] + extra)
        qmms, kmms, vmms, vevs, ropes = [], [], [], [], []
        for tc in range(NTCH):
            j = tc % 2
            hb = hbuf[j]
            ldh = P.dma("sync", lambda e, tc=tc, hb=hb: e.dma_start(
                out=hb[:].rearrange("p (c t) -> p c t", t=TC),
                in_=h0T[:, tc * TC:(tc + 1) * TC].rearrange(
                    "(c p) t -> p c t", p=128)),
                deps=h_cons[j] + (tabdep if p == 0 and tc < 2 else []))
            h_cons[j] = []

            def qfn(e, w=wqp[jw], hb=hb):
                last = None
                for dc in range(NDC):
                    last = e.matmul(pa, w[:, dc * 128:(dc + 1) * 128],
                                    hb[:, dc * TC:(dc + 1) * TC],
                                    start=(dc == 0), stop=(dc == NDC - 1))
                return last
            qmm = P.op("pe", qfn, deps=[ldh, lwq, last_qev, last_vev])
            qev = P.op("dve", lambda e: e.tensor_copy(qt32[:], pa),
                       deps=[qmm, ropes[-1] if ropes else None])
            sw_ops = []
            for blk in range(4):
                d_, s_ = slice(blk * 32, (blk + 1) * 32),                     slice((blk ^ 1) * 32, ((blk ^ 1) + 1) * 32)
                sw_ops.append(P.op("pool" if blk % 2 else "dve",
                              lambda e, d=d_, s=s_: e.tensor_copy(
                                  qsw[d, :], qt32[s, :]), deps=[qev]))
            qc1 = P.op("pool", lambda e, tc=tc: e.tensor_mul(
                qcs[:], qt32[:], C128[:, tc * TC:(tc + 1) * TC]),
                deps=[qev] + tabops)
            qc2 = P.op("dve", lambda e, tc=tc: e.tensor_mul(
                qsw[:], qsw[:], S128[:, tc * TC:(tc + 1) * TC]),
                deps=sw_ops + tabops)
            qc3 = P.op("dve", lambda e, tc=tc: e.tensor_add(
                qbf[:, tc * TC:(tc + 1) * TC], qcs[:], qsw[:]),
                deps=[qc1, qc2, last_smm])
            last_qev = qev

            def kfn(e, w=wkp[jw], hb=hb):
                last = None
                for dc in range(NDC):
                    last = e.matmul(pb, w[:, dc * 128:(dc + 1) * 128],
                                    hb[:, dc * TC:(dc + 1) * TC],
                                    start=(dc == 0), stop=(dc == NDC - 1))
                return last
            kmm = P.op("pe", kfn, deps=[ldh, lwk, last_kev])
            kev = P.op("dve", lambda e: e.tensor_copy(qt32[:], pb),
                       deps=[kmm, qc1, qc3])
            ksw_ops = []
            for blk in range(4):
                d_, s_ = slice(blk * 32, (blk + 1) * 32),                     slice((blk ^ 1) * 32, ((blk ^ 1) + 1) * 32)
                ksw_ops.append(P.op("pool" if blk % 2 else "dve",
                               lambda e, d=d_, s=s_: e.tensor_copy(
                                   qsw[d, :], qt32[s, :]), deps=[kev, qc2]))
            kc1 = P.op("pool", lambda e, tc=tc: e.tensor_mul(
                qcs[:], qt32[:], C128[:, tc * TC:(tc + 1) * TC]),
                deps=[kev, qc3])
            kc2 = P.op("dve", lambda e, tc=tc: e.tensor_mul(
                qsw[:], qsw[:], S128[:, tc * TC:(tc + 1) * TC]), deps=ksw_ops)
            kc3 = P.op("dve", lambda e, tc=tc: e.tensor_add(
                kbf[:, tc * TC:(tc + 1) * TC], qcs[:], qsw[:]),
                deps=[kc1, kc2, last_smm])
            last_kev = kev
            ropes.extend([qc3, kc3])
            qmms.append(qmm)
            kmms.append(kmm)

            for tt in range(4):
                kt = tc * 4 + tt
                def vfn(e, w=wvp[jw], hb=hb, tt=tt):
                    last = None
                    for dc in range(NDC):
                        last = e.matmul(
                            pv, hb[:, dc * TC + tt * 128:
                                      dc * TC + (tt + 1) * 128],
                            w[:, dc * 128:(dc + 1) * 128],
                            start=(dc == 0), stop=(dc == NDC - 1))
                    return last
                vmm = P.op("pe", vfn, deps=[ldh, lwv, last_vev, qev])
                vev = P.op("dve", lambda e, kt=kt: e.tensor_copy(
                    vv[:, kt, 0:130].rearrange("p (b c) -> p b c", b=2, c=65)
                    [:, :, 0:64],
                    pv.rearrange("p (b c) -> p b c", b=2, c=64)),
                    deps=[vmm] + av_by_kt.get(kt, []))
                av_by_kt[kt] = []
                vmms.append(vmm)
                vevs.append(vev)
                last_vev = vev
            h_cons[j] = [qmm, kmm] + vmms[-4:]
        if "dbg_q" in ctx.hbm and p == 0:
            P.dma("sync", lambda e: e.dma_start(out=ctx.hbm["dbg_C"][:],
                  in_=C128[:]), deps=tabdep)
            P.dma("sync", lambda e: e.dma_start(out=ctx.hbm["dbg_S"][:],
                  in_=S128[:]), deps=tabdep)
            P.dma("sync", lambda e: e.dma_start(out=ctx.hbm["dbg_q"][:],
                  in_=qbf[:]), deps=ropes)
            P.dma("sync", lambda e: e.dma_start(out=ctx.hbm["dbg_k"][:],
                  in_=kbf[:]), deps=ropes)
            P.dma("sync", lambda e: e.dma_start(out=ctx.hbm["dbg_v"][:],
                  in_=v_all[:]), deps=vevs)
        w_cons[("q", jw)] = [qmms[-1]]
        w_cons[("k", jw)] = [kmms[-1]]
        w_cons[("v", jw)] = [vmms[-1]]
        qkv_done_pair = ropes + vevs

        svops = []
        for h in range(2):
            def svfn(e, h=h):
                last = None
                for kt in range(32):
                    last = e.matmul(psv, vv[:, kt, h * 65:(h + 1) * 65],
                                    onescol[:], start=(kt == 0), stop=(kt == 31))
                return last
            svm = P.op("pe", svfn,
                       deps=vevs + [ocol, mo1, mo2, last_sve, pend_d4[0]])
            sve = P.op("dve", lambda e, h=h: e.tensor_scalar(
                svb[h][0:65, 0:1], psv, 0.5, None, ALU.mult),
                deps=[svm, pend_d5.get(0), pend_d5.get(1)])
            svops.append(sve)
            sv_ref[h] = sve
            last_sve = sve

        for hq in range(2 * NTCH):
            h, qc = hq // NTCH, hq % NTCH
            jo = hq % 2
            av_last = None

            def emit_av(g, h):
                nonlocal av_last
                jg = g % 2
                avs = []
                for half in range(2):
                    kt = 2 * g + half
                    avm = P.op("pe", lambda e, h=h, kt=kt, jg=jg, half=half:
                               e.matmul(
                        po[:], vv[:, kt, h * 65:(h + 1) * 65],
                        ttb2[jg][:, half * TC:(half + 1) * TC],
                        start=(kt == 0), stop=(kt == 31)),
                        deps=[ep_by_grp[g], vevs[-1]] +
                        ([last_oev] if kt == 0 else []))
                    avs.append(avm)
                    av_by_kt[kt] = [avm]
                av_by_grp[g] = avs
                av_last = avs[-1]

            for g in range(16):
                jg = g % 2
                smms = []
                for half in range(2):
                    kt = 2 * g + half
                    prev_ep = ep_by_grp.get(g - 2 if g >= 2 else g + 14)
                    smm = P.op("pe", lambda e, h=h, qc=qc, kt=kt, jg=jg,
                               half=half: e.matmul(
                        ps2[jg][:, half * TC:(half + 1) * TC],
                        kbf[h * 64:(h + 1) * 64, kt * 128:(kt + 1) * 128],
                        qbf[h * 64:(h + 1) * 64, qc * TC:(qc + 1) * TC],
                        start=True, stop=True),
                        deps=[ropes[-1], ropes[-2], prev_ep,
                              last_qev if hq == 0 and g == 0 else None,
                              last_kev if hq == 0 and g == 0 else None,
                              last_vev if hq == 0 and g == 0 else None])
                    smms.append(smm)
                    last_smm = smm
                ttw = av_by_grp.get(g - 2 if g >= 2 else g + 14, [])
                ep = P.op("act", lambda e, jg=jg: e.activation(
                    ttb2[jg][:], ps2[jg][:], AF.Square, bias=BS, scale=CS),
                    deps=smms + ttw + cpre)
                ep_by_grp[g] = ep
                if g >= 1:
                    emit_av(g - 1, h)
            emit_av(15, h)
            oev = P.op("dve", lambda e, jo=jo: e.tensor_copy(
                osb[jo][0:65, :], po[0:65, :]),
                deps=[av_last, pend_d5.get(jo)])
            last_oev = oev
            if pending is not None:
                emit_div_epilogue(pending)
            pending = (p, h, qc, jo, oev)
        if pending is not None:
            emit_div_epilogue(pending)
            pending = None
    _qkvscope.close()
    wo_sb = ctx.sbuf("b2_wo", [128, NPAIR * D], BF16)
    evb = [ctx.sbuf(f"b2_ev{j}", [128, TC], BF16) for j in range(2)]
    pw = [ctx.psum(f"b2_pw{j}", [128, TC], F32) for j in range(2)]
    oall_done = [x for x in (pend_d5.get(0), pend_d5.get(1)) if x]
    if "dbg_o" in ctx.hbm:
        P.dma("sync", lambda e: e.dma_start(out=ctx.hbm["dbg_o"][:],
              in_=o_all[:]), deps=[d for d in oall_done if d is not None])
    lws = [P.dma("pool", lambda e, p=p: e.dma_start(
        out=wo_sb[:, p * D:(p + 1) * D],
        in_=wo_h[p * 128:(p + 1) * 128, :]), deps=oall_done)
        for p in range(NPAIR)]
    prev = {}
    k = 0
    for dc in range(NDC):
        for qc in range(NTCH):
            j = k % 2
            def mfn(e, dc=dc, qc=qc, j=j):
                last = None
                for p in range(NPAIR):
                    last = e.matmul(
                        pw[j][:],
                        wo_sb[:, p * D + dc * 128: p * D + (dc + 1) * 128],
                        o_all[:, p * T + qc * TC: p * T + (qc + 1) * TC],
                        start=(p == 0), stop=(p == NPAIR - 1))
                return last
            mm = P.op("pe", mfn, deps=lws + oall_done + [prev.get(j)])
            evo = P.op("dve", lambda e, j=j: e.tensor_copy(evb[j][:], pw[j][:]),
                       deps=[mm, prev.get(("st", j))])
            sto = P.dma("sync", lambda e, dc=dc, qc=qc, j=j: e.dma_start(
                out=s0[dc * 128:(dc + 1) * 128, qc * TC:(qc + 1) * TC],
                in_=evb[j][:]), deps=[evo])
            prev[j] = evo
            prev[("st", j)] = sto
            k += 1


def _y4view(y0, dcp, tcc):
    return (y0[:].rearrange("p (c t) -> p c t", t=1024)
            [:, 2 * dcp:2 * dcp + 2, tcc * TC: tcc * TC + TC])


def ph_b4(P, ctx):
    xn0T, cw0, s0, p0 = (ctx.hbm["xn0T"], ctx.hbm["cw0"], ctx.hbm["s0"],
                         ctx.hbm["p0"])
    gu0_h, dn0_h = ctx.hbm["gu0"], ctx.hbm["dn0"]
    NE = E0 + 1
    QW = 1024
    NQ = T // QW

    y0 = ctx.sbuf("b4_y0", [128, NDC * QW], F32)
    xnq = ctx.sbuf("b4_xnq", [128, NDC * QW], BF16)
    gu = [ctx.sbuf(f"b4_gu{j}", [128, NDC * 2 * I0P], BF16) for j in range(2)]
    dn = [ctx.sbuf(f"b4_dn{j}", [128, 8 * D], BF16) for j in range(2)]
    sg = [ctx.sbuf(f"b4_sg{j}", [128, TC], F32) for j in range(2)]
    hsb2 = [ctx.sbuf(f"b4_h{j}", [128, 8 * TC], BF16) for j in range(2)]
    cwbc = ctx.sbuf("b4_cwbc", [128, TC], BF16)
    cwrow = ctx.sbuf("b4_cwrow", [1, TC], F32)
    ones1 = ctx.sbuf("b4_ones1", [1, 128], F32)
    gsb = [ctx.sbuf(f"b4_g{j}", [128, TC], BF16) for j in range(2)]
    ssb = [ctx.sbuf(f"b4_s{j}", [128, TC], BF16) for j in range(2)]
    psb = [ctx.sbuf(f"b4_p{j}", [128, TC], BF16) for j in range(2)]
    pg = [ctx.psum(f"b4_pg{j}", [128, TC], F32) for j in range(2)]
    pu0 = ctx.psum("b4_pu", [128, TC], F32)
    pu = [pu0, pu0]
    py = [ctx.psum(f"b4_py{j}", [128, 2 * TC], F32) for j in range(2)]
    pbc = ctx.psum("b4_pbc", [128, TC], F32)

    o1 = P.op("dve", lambda e: e.memset(ones1[:], 1.0))
    gu_cons = {0: [], 1: []}
    dn_cons = {0: [], 1: []}
    xn_cons = []
    y_cons = []
    h_cons = {}
    sg_cons = {}
    cw_cons = []
    bc_cons = []
    pg_cons = {}
    pu_cons = {}
    py_cons = {}
    bc_readers = []
    ya_prev = {}
    prev_comb = {}
    ci = [0]
    pend_down = [None]
    ms_ref = [None]

    def make_down(en, q, tcc, jw, hsb, hops, par):
        ms_l = ms_ref[0]

        def emit():
            last_dm = None
            for dcp in range(NDC // 2):
                jj = dcp % 2

                def dfn(e, jw=jw, dcp=dcp, jj=jj, hsb=hsb):
                    last = None
                    for hf in range(2):
                        dc = 2 * dcp + hf
                        for ic in range(8):
                            last = e.matmul(
                                py[jj][:, hf * TC:(hf + 1) * TC],
                                dn[jw][:, ic * D + dc * 128:
                                       ic * D + (dc + 1) * 128],
                                hsb[:, ic * TC:(ic + 1) * TC],
                                start=(ic == 0), stop=(ic == 7))
                    return last
                dm = P.op("pe", dfn, deps=hops + py_cons.get(jj, []))
                ya = P.op("dve", lambda e, dcp=dcp, q=q, tcc=tcc, jj=jj:
                          e.tensor_add(
                              _y4view(y0, dcp, tcc),
                              _y4view(y0, dcp, tcc),
                              py[jj][:].rearrange("p (b t) -> p b t", t=TC)),
                          deps=[dm, ya_prev.get((dcp, tcc))] + ms_l)
                ya_prev[(dcp, tcc)] = ya
                py_cons[jj] = [ya]
                y_cons.append(ya)
                last_dm = dm
            for ic in range(8):
                h_cons[(par, ic)] = [last_dm]
            dn_cons[jw].append(y_cons[-1])
        return emit

    for q in range(NQ):
        ldx = P.dma("sync", lambda e, q=q: e.dma_start(
            out=xnq[:].rearrange("p (c t) -> p c t", t=QW),
            in_=xn0T[:, q * QW:(q + 1) * QW].rearrange(
                "(c p) t -> p c t", p=128)), deps=xn_cons)
        xn_cons = []
        ms = [P.op("dve", lambda e, dc=dc: e.memset(
            y0[:, dc * QW:(dc + 1) * QW], 0.0), deps=y_cons) for dc in range(NDC)]
        y_cons = []
        ms_ref[0] = ms
        for en in range(NE):
            jw = en % 2
            lg = P.dma("pool", lambda e, en=en, t=gu[jw]: e.dma_start(
                out=t[:].rearrange("p (c r) -> p c r", r=2 * I0P),
                in_=gu0_h[en].rearrange("(c p) r -> p c r", p=128)),
                deps=gu_cons[jw])
            ldn = P.dma("pool", lambda e, en=en, t=dn[jw]: e.dma_start(
                out=t[:].rearrange("p (c d) -> p c d", d=D),
                in_=dn0_h[en].rearrange("(c p) d -> p c d", p=128)),
                deps=dn_cons[jw])
            gu_cons[jw], dn_cons[jw] = [], []
            for tcc in range(QW // TC):
                ci[0] += 1
                hsb = hsb2[ci[0] % 2]
                col0 = q * QW + tcc * TC
                if en > 0:
                    lcw = P.dma("sync", lambda e, en=en, col0=col0: e.dma_start(
                        out=cwrow[:], in_=cw0[en - 1:en, col0:col0 + TC]),
                        deps=cw_cons)
                    mbc = P.op("pe", lambda e: e.matmul(
                        pbc[:], ones1[:], cwrow[:], start=True, stop=True),
                        deps=[lcw, o1] + bc_cons)
                    ebc = P.op("dve", lambda e: e.tensor_copy(cwbc[:], pbc[:]),
                               deps=[mbc] + bc_readers)
                    cw_cons = [mbc]
                    bc_cons = [ebc]
                    bc_readers = []
                hops = []
                for ic in range(8):
                    jj = ic % 2
                    def gfn(e, w=gu[jw], ic=ic, tcc=tcc, jj=jj):
                        last = None
                        for dc in range(NDC):
                            last = e.matmul(
                                pg[jj][:],
                                w[:, dc * 2 * I0P + (2 * ic) * 128:
                                  dc * 2 * I0P + (2 * ic + 1) * 128],
                                xnq[:, dc * QW + tcc * TC:
                                    dc * QW + tcc * TC + TC],
                                start=(dc == 0), stop=(dc == NDC - 1))
                        return last
                    def ufn(e, w=gu[jw], ic=ic, tcc=tcc, jj=jj):
                        last = None
                        for dc in range(NDC):
                            last = e.matmul(
                                pu[jj][:],
                                w[:, dc * 2 * I0P + (2 * ic + 1) * 128:
                                  dc * 2 * I0P + (2 * ic + 2) * 128],
                                xnq[:, dc * QW + tcc * TC:
                                    dc * QW + tcc * TC + TC],
                                start=(dc == 0), stop=(dc == NDC - 1))
                        return last
                    gm = P.op("pe", gfn, deps=[lg, ldx] + pg_cons.get(jj, []))
                    um = P.op("pe", ufn, deps=[lg, ldx] + pu_cons.get(jj, []))
                    sl = P.op("act", lambda e, jj=jj: e.activation(
                        sg[jj][:], pg[jj][:], AF.Silu, bias=0.0, scale=1.0),
                        deps=[gm] + sg_cons.get(jj, []))
                    pg_cons[jj] = [sl]
                    hm = P.op("dve", lambda e, jj=jj, ic=ic, hsb=hsb:
                              e.tensor_mul(
                        hsb[:, ic * TC:(ic + 1) * TC], sg[jj][:], pu[jj][:]),
                        deps=[sl, um] + h_cons.get((ci[0] % 2, ic), []))
                    pu_cons[0] = pu_cons[1] = [hm]
                    sg_cons[jj] = [hm]
                    if en > 0:
                        hc = P.op("pool", lambda e, ic=ic, hsb=hsb: e.tensor_mul(
                            hsb[:, ic * TC:(ic + 1) * TC],
                            hsb[:, ic * TC:(ic + 1) * TC], cwbc[:]),
                            deps=[hm] + bc_cons)
                        hops.append(hc)
                        bc_readers.append(hc)
                    else:
                        hops.append(hm)
                if pend_down[0] is not None:
                    pend_down[0]()
                pend_down[0] = make_down(en, q, tcc, jw, hsb, list(hops),
                                         ci[0] % 2)
                gu_cons[jw].append(hops[-1])
        if pend_down[0] is not None:
            pend_down[0]()
            pend_down[0] = None
        xn_cons = y_cons[-1:]
        if "dbg_y" in ctx.hbm and q == 0:
            P.dma("sync", lambda e: e.dma_start(
                out=ctx.hbm["dbg_y"][:], in_=y0[:]), deps=list(y_cons))
        for tcc in range(QW // TC):
            for dc in range(NDC):
                k = tcc * NDC + dc
                jj = k % 2
                col0 = q * QW + tcc * TC
                lds = P.dma("sync", lambda e, dc=dc, col0=col0, jj=jj:
                            e.dma_start(out=ssb[jj][:],
                                        in_=s0[dc * 128:(dc + 1) * 128,
                                               col0:col0 + TC]),
                            deps=[prev_comb.get(("p", jj))])
                gg = P.op("act", lambda e, dc=dc, tcc=tcc, jj=jj: e.activation(
                    gsb[jj][:], y0[:, dc * QW + tcc * TC: dc * QW + tcc * TC + TC],
                    AF.Silu, bias=0.0, scale=1.0),
                    deps=y_cons + [prev_comb.get(("g", jj))])
                pp = P.op("dve", lambda e, jj=jj: e.tensor_mul(
                    psb[jj][:], gsb[jj][:], ssb[jj][:]),
                    deps=[gg, lds, prev_comb.get(("st", jj))])
                st = P.dma("sync", lambda e, dc=dc, col0=col0, jj=jj:
                           e.dma_start(out=p0[dc * 128:(dc + 1) * 128,
                                              col0:col0 + TC], in_=psb[jj][:]),
                           deps=[pp])
                prev_comb[("g", jj)] = pp
                prev_comb[("p", jj)] = pp
                prev_comb[("st", jj)] = st
                y_cons.append(gg)


def ph_b5(P, ctx):
    x, p0, x1, xn1 = (ctx.hbm["x"], ctx.hbm["p0"], ctx.hbm["x1"],
                      ctx.hbm["xn1"])
    ceps = ctx.constant(LN_EPS)
    cm1 = ctx.constant(-1.0)
    cmh = ctx.constant(-0.5)
    xb = [ctx.sbuf(f"b5_x{j}", [128, D], F32) for j in range(2)]
    pb_ = [ctx.sbuf(f"b5_p{j}", [128, D], BF16) for j in range(2)]
    x1b = [ctx.sbuf(f"b5_x1{j}", [128, D], F32) for j in range(2)]
    sq = [ctx.sbuf(f"b5_sq{j}", [128, D], F32) for j in range(2)]
    xnb = [ctx.sbuf(f"b5_xn{j}", [128, D], BF16) for j in range(2)]
    st = [ctx.sbuf(f"b5_st{j}", [128, 8], F32) for j in range(2)]
    C = 1.0 / D
    prev = {}
    for i in range(NTT):
        j = i % 2
        pv = prev.get(j, {})
        ld = P.dma("sync", lambda e, i=i, t=xb[j]: e.dma_start(
            out=t[:], in_=x[i * 128:(i + 1) * 128, :]), deps=[pv.get("add")])
        ldp = P.dma("sync", lambda e, i=i, t=pb_[j]: e.dma_start(
            out=t[:], in_=p0[:, i * 128:(i + 1) * 128], transpose=True),
            deps=[pv.get("add")])
        add = P.op("dve", lambda e, j=j: e.tensor_add(
            x1b[j][:], xb[j][:], pb_[j][:]),
            deps=[ld, ldp, pv.get("stx1"), pv.get("xn"), pv.get("sq1"),
                  pv.get("sum")])
        stx1 = P.dma("pool", lambda e, i=i, j=j: e.dma_start(
            out=x1[i * 128:(i + 1) * 128, :], in_=x1b[j][:]), deps=[add])
        stb = st[j]
        sx, sx2 = stb[:, 0:1], stb[:, 1:2]
        mu, nv, rstd = stb[:, 2:3], stb[:, 3:4], stb[:, 4:5]
        summ = P.op("dve", lambda e, sx=sx, j=j: e.reduce_sum(
            sx, x1b[j][:], axis=AX.X), deps=[add])
        sq1 = P.op("act", lambda e, j=j, sx2=sx2: e.activation(
            sq[j][:], x1b[j][:], AF.Square, bias=0.0, scale=1.0,
            accum_out=sx2), deps=[add, pv.get("sq_w")])
        ex2 = P.op("dve", lambda e, sx2=sx2: e.tensor_scalar(
            sx2, sx2, C, None, ALU.mult), deps=[sq1])
        muo = P.op("dve", lambda e, mu=mu, sx=sx: e.tensor_scalar(
            mu, sx, C, None, ALU.mult), deps=[summ])
        nvo = P.op("dve", lambda e, nv=nv, mu=mu, sx2=sx2:
                   e.scalar_tensor_tensor(nv, mu, mu, sx2, ALU.mult,
                                          ALU.subtract), deps=[muo, ex2])
        lnv = P.op("act", lambda e, nv=nv: e.activation(
            nv, nv, AF.Ln, bias=LN_EPS, scale=-1.0), deps=[nvo, ceps, cm1])
        rst = P.op("act", lambda e, rstd=rstd, nv=nv: e.activation(
            rstd, nv, AF.Exp, bias=0.0, scale=-0.5), deps=[lnv, cmh])
        xn = P.op("dve", lambda e, j=j, mu=mu, rstd=rstd: e.tensor_scalar(
            xnb[j][:], x1b[j][:], mu, rstd, ALU.subtract, ALU.mult),
            deps=[muo, rst, pv.get("stxn")])
        stxn = P.dma("pool", lambda e, i=i, j=j: e.dma_start(
            out=xn1[i * 128:(i + 1) * 128, :], in_=xnb[j][:]), deps=[xn])
        prev[j] = dict(add=add, stx1=stx1, xn=xn, sq1=sq1, sum=summ,
                       sq_w=nvo, stxn=stxn)


def ph_b6(P, ctx):
    xn1, xn1T = ctx.hbm["xn1"], ctx.hbm["xn1T"]
    wr1_h, br1_h, g1 = ctx.hbm["wr1"], ctx.hbm["br1"], ctx.hbm["g1"]

    xt = [[ctx.sbuf(f"b6_xt{j}_{dc}", [128, TC], BF16) for dc in range(NDC)]
          for j in range(2)]
    wr = ctx.sbuf("b6_wr", [128, NDC * E1], BF16)
    brrow = ctx.sbuf("b6_br", [1, E1], F32)
    brbc = ctx.sbuf("b6_brbc", [128, E1], F32)
    ones1 = ctx.sbuf("b6_ones", [1, 128], F32)
    lg = [ctx.sbuf(f"b6_lg{j}", [128, 4 * E1], F32) for j in range(2)]
    p2b = [ctx.sbuf(f"b6_p2{j}", [128, 4 * E1], F32) for j in range(2)]
    tmp = [ctx.sbuf(f"b6_tmp{j}", [128, 4 * E1], F32) for j in range(2)]
    msk = [ctx.sbuf(f"b6_msk{j}", [128, 4 * E1], F32) for j in range(2)]
    gt = [ctx.sbuf(f"b6_gt{j}", [128, 4 * E1], F32) for j in range(2)]
    r1 = [ctx.sbuf(f"b6_r1{j}", [128, 4], F32) for j in range(2)]
    gfm = ctx.sbuf("b6_gfm", [E1, T], F32)
    ident = ctx.sbuf("b6_id", [128, 128], F32)
    ia = ctx.sbuf("b6_ia", [128, 128], F32)
    plg = [ctx.psum(f"b6_plg{j}", [128, E1], F32) for j in range(2)]
    ptr = [ctx.psum(f"b6_ptr{j}", [E1, 128], F32) for j in range(2)]
    pbc = ctx.psum("b6_pbc", [128, E1], F32)

    i1 = P.op("pool", lambda e: e.iota(ia[:], pattern=[[0, 128]], base=0,
              channel_multiplier=1, allow_small_or_imprecise_dtypes=True))
    i2 = P.op("pool", lambda e: e.iota(ident[:], pattern=[[1, 128]], base=0,
              channel_multiplier=0, allow_small_or_imprecise_dtypes=True))
    idt = P.op("dve", lambda e: e.tensor_tensor(ident[:], ia[:], ident[:],
               ALU.is_equal), deps=[i1, i2])
    ldw = [P.dma("pool", lambda e, dc=dc: e.dma_start(
        out=wr[:, dc * E1:(dc + 1) * E1],
        in_=wr1_h[dc * 128:(dc + 1) * 128, :])) for dc in range(NDC)]
    ldb = P.dma("sync", lambda e: e.dma_start(out=brrow[:], in_=br1_h[:]))
    o1 = P.op("dve", lambda e: e.memset(ones1[:], 1.0))
    mbc = P.op("pe", lambda e: e.matmul(pbc[:], ones1[:], brrow[:],
               start=True, stop=True), deps=[ldb, o1])
    ebc = P.op("dve", lambda e: e.tensor_copy(brbc[:], pbc[:]), deps=[mbc])

    prev = {}
    ev_by_k = {}
    evg_by_k = {}
    for i in range(NTCH):
        j = i % 2
        pv = prev.get(j, {})
        lds = []
        for dc in range(NDC):
            lds.append(P.dma("sync", lambda e, i=i, dc=dc, t=xt[j][dc]:
                       e.dma_start(out=t[:],
                                   in_=xn1[i * TC:(i + 1) * TC,
                                           dc * 128:(dc + 1) * 128],
                                   transpose=True), deps=pv.get("xt_cons", [])))
        stws = [P.dma("pool", lambda e, i=i, dc=dc, t=xt[j][dc]: e.dma_start(
            out=xn1T[dc * 128:(dc + 1) * 128, i * TC:(i + 1) * TC], in_=t[:]),
            deps=[lds[dc]]) for dc in range(NDC)]
        mmops, evops = [], []
        for tt in range(4):
            k = i * 4 + tt
            jj = k % 2

            def mmfn(e, j=j, tt=tt, jj=jj):
                last = None
                for dc in range(NDC):
                    last = e.matmul(plg[jj][:],
                                    xt[j][dc][:, tt * 128:(tt + 1) * 128],
                                    wr[:, dc * E1:(dc + 1) * E1],
                                    start=(dc == 0), stop=(dc == NDC - 1))
                return last
            mm = P.op("pe", mmfn, deps=lds + ldw + [ev_by_k.get(k - 2)])
            ev = P.op("dve", lambda e, j=j, tt=tt, jj=jj: e.tensor_add(
                lg[j][:, tt * E1:(tt + 1) * E1], plg[jj][:], brbc[:]),
                deps=[mm, ebc, pv.get("chain_last")])
            ev_by_k[k] = ev
            mmops.append(mm)
            evops.append(ev)
        lgv = lg[j][:].rearrange("p (s e) -> p s e", e=E1)
        p2v = p2b[j][:].rearrange("p (s e) -> p s e", e=E1)
        tmpv = tmp[j][:].rearrange("p (s e) -> p s e", e=E1)
        mskv = msk[j][:].rearrange("p (s e) -> p s e", e=E1)
        gtv = gt[j][:].rearrange("p (s e) -> p s e", e=E1)
        r1v = r1[j][:].rearrange("p (s o) -> p s o", o=1)
        r1b = r1[j][:].rearrange("p (s o) -> p s o", o=1).to_broadcast(
            [128, 4, E1])
        c1 = P.op("dve", lambda e, lgv=lgv, r1v=r1v: e.tensor_reduce(
            r1v, lgv, axis=AX.X, op=ALU.max, negate=True),
            deps=evops + [pv.get("c_last")])
        c2 = P.op("dve", lambda e, p2v=p2v, lgv=lgv, r1b=r1b: e.tensor_tensor(
            p2v, lgv, r1b, ALU.add), deps=[c1, pv.get("c_last")])
        c3 = P.op("act", lambda e, s=p2b[j]: e.activation(
            s[:], s[:], AF.Exp, bias=0.0, scale=1.0), deps=[c2])
        c4 = P.op("dve", lambda e, p2v=p2v, r1v=r1v: e.tensor_reduce(
            r1v, p2v, axis=AX.X, op=ALU.add), deps=[c3])
        c5 = P.op("dve", lambda e, r=r1[j]: e.reciprocal(r[:], r[:]), deps=[c4])
        c6 = P.op("dve", lambda e, p2v=p2v, r1b=r1b: e.tensor_tensor(
            p2v, p2v, r1b, ALU.mult), deps=[c5])
        c7 = P.op("dve", lambda e, p2v=p2v, r1v=r1v: e.tensor_reduce(
            r1v, p2v, axis=AX.X, op=ALU.max), deps=[c6])
        c8 = P.op("dve", lambda e, mskv=mskv, p2v=p2v, r1b=r1b: e.tensor_tensor(
            mskv, p2v, r1b, ALU.is_equal), deps=[c7])
        c9 = P.op("dve", lambda e, tmpv=tmpv, p2v=p2v, mskv=mskv:
                  e.tensor_tensor(tmpv, p2v, mskv, ALU.mult), deps=[c8])
        c10 = P.op("dve", lambda e, tmpv=tmpv, p2v=p2v: e.tensor_tensor(
            tmpv, p2v, tmpv, ALU.subtract), deps=[c9])
        c11 = P.op("dve", lambda e, tmpv=tmpv, r1v=r1v: e.tensor_reduce(
            r1v, tmpv, axis=AX.X, op=ALU.max), deps=[c10])
        c12 = P.op("dve", lambda e, gtv=gtv, tmpv=tmpv, r1b=r1b:
                   e.tensor_tensor(gtv, tmpv, r1b, ALU.is_equal), deps=[c11])
        c13 = P.op("dve", lambda e, mskv=mskv, gtv=gtv: e.tensor_tensor(
            mskv, mskv, gtv, ALU.add), deps=[c12])
        c14 = P.op("dve", lambda e, tmpv=tmpv, gtv=gtv: e.tensor_tensor(
            gtv, tmpv, gtv, ALU.mult), deps=[c13])
        c15 = P.op("dve", lambda e, tmpv=tmpv, gtv=gtv: e.tensor_tensor(
            tmpv, tmpv, gtv, ALU.subtract), deps=[c14])
        c16 = P.op("dve", lambda e, tmpv=tmpv, r1v=r1v: e.tensor_reduce(
            r1v, tmpv, axis=AX.X, op=ALU.max), deps=[c15])
        c17 = P.op("dve", lambda e, gtv=gtv, tmpv=tmpv, r1b=r1b:
                   e.tensor_tensor(gtv, tmpv, r1b, ALU.is_equal), deps=[c16])
        c18 = P.op("dve", lambda e, mskv=mskv, gtv=gtv: e.tensor_tensor(
            mskv, mskv, gtv, ALU.add), deps=[c17])
        c19 = P.op("dve", lambda e, gtv=gtv, p2v=p2v, mskv=mskv:
                   e.tensor_tensor(gtv, p2v, mskv, ALU.mult), deps=[c18])
        tpev = []
        for tt in range(4):
            k = i * 4 + tt
            jj = k % 2
            tpo = P.op("pe", lambda e, j=j, tt=tt, jj=jj: e.transpose(
                ptr[jj][:], gt[j][:, tt * E1:(tt + 1) * E1], ident[:]),
                deps=[c19, idt, evg_by_k.get(k - 2)])
            evc = P.op("dve", lambda e, i=i, tt=tt, jj=jj: e.tensor_copy(
                gfm[:, i * TC + tt * 128: i * TC + (tt + 1) * 128], ptr[jj][:]),
                deps=[tpo])
            evg_by_k[k] = evc
            tpev.append(evc)
        prev[j] = {"xt_cons": mmops + stws, "chain_last": c19, "c_last": c19}
    P.dma("pool", lambda e: e.dma_start(out=g1[:], in_=gfm[:]),
          deps=list(evg_by_k.values()))


def ph_b7(P, ctx):
    xn1T = ctx.hbm["xn1T"]
    wq1_h, wk1_h, wv1_h = ctx.hbm["wq1"], ctx.hbm["wk1"], ctx.hbm["wv1"]
    bq1_h, bk1_h, bv1_h = ctx.hbm["bq1"], ctx.hbm["bk1"], ctx.hbm["bv1"]
    q1, attn1, s1 = ctx.hbm["q1"], ctx.hbm["attn1"], ctx.hbm["s1"]

    v_tm = ctx.sbuf("b7_vtm", [128, 32 * D], BF16)
    bq = ctx.sbuf("b7_bq", [128, NDC], F32)
    bk = ctx.sbuf("b7_bk", [128, NDC], F32)
    bvrow = ctx.sbuf("b7_bvrow", [1, D], F32)
    bvbc = ctx.sbuf("b7_bvbc", [128, D], F32)
    ones1 = ctx.sbuf("b7_ones", [1, 128], F32)
    _sc2 = ExitStack()
    k_fm = ctx.sbuf("b7_kfm", [128, NDC * T], BF16, scope=_sc2)
    pq = [ctx.psum(f"b7_pq{j}", [128, TC], F32, scope=_sc2) for j in range(2)]
    pv_ = [ctx.psum(f"b7_pv{j}", [128, TC], F32, scope=_sc2) for j in range(2)]
    pbc = ctx.psum("b7_pbc", [128, TC], F32, scope=_sc2)
    _sc3 = ExitStack()
    wq_sb = ctx.sbuf("b7_wq", [128, NDC * D], BF16, scope=_sc3)
    wk_sb = ctx.sbuf("b7_wk", [128, NDC * D], BF16, scope=_sc3)
    wv_sb = ctx.sbuf("b7_wv", [128, NDC * D], BF16, scope=_sc3)
    xnb = [ctx.sbuf(f"b7_xn{j}", [128, NDC * TC], BF16, scope=_sc3)
           for j in range(2)]
    qev = [ctx.sbuf(f"b7_qe{j}", [128, TC], BF16, scope=_sc3) for j in range(2)]

    ldbq = P.dma("sync", lambda e: e.dma_start(out=bq[:], in_=bq1_h[:]))
    ldbk = P.dma("sync", lambda e: e.dma_start(out=bk[:], in_=bk1_h[:]))
    ldbv = P.dma("sync", lambda e: e.dma_start(out=bvrow[:], in_=bv1_h[:]))
    o1 = P.op("dve", lambda e: e.memset(ones1[:], 1.0))
    mb1 = P.op("pe", lambda e: e.matmul(pbc[:], ones1[:], bvrow[:, 0:TC],
               start=True, stop=True), deps=[ldbv, o1])
    eb1 = P.op("dve", lambda e: e.tensor_copy(bvbc[:, 0:TC], pbc[:]),
               deps=[mb1])
    mb2 = P.op("pe", lambda e: e.matmul(pbc[:], ones1[:], bvrow[:, TC:D],
               start=True, stop=True), deps=[eb1])
    eb2 = P.op("dve", lambda e: e.tensor_copy(bvbc[:, TC:D], pbc[:]),
               deps=[mb2])
    lws = []
    for dc in range(NDC):
        lws.append(P.dma("pool", lambda e, dc=dc: e.dma_start(
            out=wq_sb[:, dc * D:(dc + 1) * D],
            in_=wq1_h[dc * 128:(dc + 1) * 128, :])))
        lws.append(P.dma("pool", lambda e, dc=dc: e.dma_start(
            out=wk_sb[:, dc * D:(dc + 1) * D],
            in_=wk1_h[dc * 128:(dc + 1) * 128, :])))
        lws.append(P.dma("pool", lambda e, dc=dc: e.dma_start(
            out=wv_sb[:, dc * D:(dc + 1) * D],
            in_=wv1_h[dc * 128:(dc + 1) * 128, :])))

    xn_cons = {0: [], 1: []}
    pq_cons = {0: [], 1: []}
    pv_cons = {0: [], 1: []}
    qst_prev = {0: None, 1: None}
    last_qkv = []
    for tc in range(NTCH):
        j = tc % 2
        ldx = P.dma("sync", lambda e, tc=tc, t=xnb[j]: e.dma_start(
            out=t[:].rearrange("p (c t) -> p c t", t=TC),
            in_=xn1T[:, tc * TC:(tc + 1) * TC].rearrange(
                "(c p) t -> p c t", p=128)), deps=xn_cons[j])
        xn_cons[j] = []
        for fc in range(NDC):
            kq = (tc * NDC + fc)
            jj = kq % 2
            def qfn(e, j=j, fc=fc, jj=jj):
                last = None
                for dc in range(NDC):
                    last = e.matmul(pq[jj][:],
                                    wq_sb[:, dc * D + fc * 128:
                                          dc * D + (fc + 1) * 128],
                                    xnb[j][:, dc * TC:(dc + 1) * TC],
                                    start=(dc == 0), stop=(dc == NDC - 1))
                return last
            qm = P.op("pe", qfn, deps=[ldx] + lws + pq_cons[jj])
            qe = P.op("dve", lambda e, fc=fc, jj=jj: e.tensor_scalar(
                qev[jj][:], pq[jj][:], bq[:, fc:fc + 1], None, ALU.add),
                deps=[qm, qst_prev[jj]])
            pq_cons[jj] = [qe]
            qs = P.dma("sync", lambda e, fc=fc, tc=tc, jj=jj: e.dma_start(
                out=q1[fc * 128:(fc + 1) * 128, tc * TC:(tc + 1) * TC],
                in_=qev[jj][:]), deps=[qe])
            qst_prev[jj] = qs
            xn_cons[j].append(qm)
            def kfn(e, j=j, fc=fc, jj=jj):
                last = None
                for dc in range(NDC):
                    last = e.matmul(pv_[jj][:],
                                    wk_sb[:, dc * D + fc * 128:
                                          dc * D + (fc + 1) * 128],
                                    xnb[j][:, dc * TC:(dc + 1) * TC],
                                    start=(dc == 0), stop=(dc == NDC - 1))
                return last
            km = P.op("pe", kfn, deps=[ldx] + lws + pv_cons[jj])
            ke = P.op("dve", lambda e, fc=fc, tc=tc, jj=jj: e.tensor_scalar(
                k_fm[:, fc * T + tc * TC: fc * T + tc * TC + TC], pv_[jj][:],
                bk[:, fc:fc + 1], None, ALU.add), deps=[km, ldbk])
            pv_cons[jj] = [ke]
            xn_cons[j].append(km)
            last_qkv = [qs, ke]
        for tt in range(4):
            kt = tc * 4 + tt
            for fh in range(2):
                jj = (kt * 2 + fh) % 2
                def vfn(e, j=j, tt=tt, fh=fh, jj=jj):
                    last = None
                    for dc in range(NDC):
                        last = e.matmul(
                            pv_[jj][:],
                            xnb[j][:, dc * TC + tt * 128:
                                dc * TC + (tt + 1) * 128],
                            wv_sb[:, dc * D + fh * TC: dc * D + (fh + 1) * TC],
                            start=(dc == 0), stop=(dc == NDC - 1))
                    return last
                vm = P.op("pe", vfn, deps=[ldx] + lws + pv_cons[jj])
                ve = P.op("dve", lambda e, kt=kt, fh=fh, jj=jj: e.tensor_add(
                    v_tm[:, kt * D + fh * TC: kt * D + (fh + 1) * TC],
                    pv_[jj][:], bvbc[:, fh * TC:(fh + 1) * TC]),
                    deps=[vm, eb1, eb2])
                pv_cons[jj] = [ve]
                xn_cons[j].append(vm)
                last_qkv.append(ve)

    _sc3.close()
    srow = ctx.sbuf("b7_srow", [128, T], F32, scope=_sc2)
    zap = ctx.sbuf("b7_zap", [128, T], F32, scope=_sc2)
    mx8 = ctx.sbuf("b7_mx8", [128, 8], F32, scope=_sc2)
    dcol = ctx.sbuf("b7_dcol", [128, 1], F32, scope=_sc2)
    asp = ctx.sbuf("b7_asp", [128, T], BF16, scope=_sc2)
    qt = [ctx.sbuf(f"b7_qt{j}", [128, NDC * 128], BF16, scope=_sc2)
          for j in range(2)]
    ep_prev = {0: None, 1: None}
    att_sts = []
    sc_prev = None
    mr_prev = None
    for qt_i in range(NTT):
        j = qt_i % 2
        ldq = P.dma("sync", lambda e, qt_i=qt_i, t=qt[j]: e.dma_start(
            out=t[:].rearrange("p (c f) -> p c f", f=128),
            in_=q1[:, qt_i * 128:(qt_i + 1) * 128].rearrange(
                "(c p) f -> p c f", p=128)),
            deps=last_qkv + (ep_prev[j] or []))
        evs = []
        for kc in range(NTCH):
            jj = kc % 2
            def sfn(e, j=j, kc=kc, jj=jj):
                last = None
                for ec in range(NDC):
                    last = e.matmul(pq[jj][:],
                                    qt[j][:, ec * 128:(ec + 1) * 128],
                                    k_fm[:, ec * T + kc * TC:
                                         ec * T + kc * TC + TC],
                                    start=(ec == 0), stop=(ec == NDC - 1))
                return last
            sm = P.op("pe", sfn, deps=[ldq] + last_qkv + pq_cons[jj])
            se = P.op("dve", lambda e, kc=kc, jj=jj: e.tensor_copy(
                srow[:, kc * TC:(kc + 1) * TC], pq[jj][:]),
                deps=[sm, sc_prev])
            pq_cons[jj] = [se]
            evs.append(se)
        cz = P.op("dve", lambda e: e.tensor_copy(zap[:], srow[:]),
                  deps=evs + [mr_prev])
        mrs = cz
        for r in range(4):
            m8 = P.op("dve", lambda e: e.max(out=mx8[:], in_=zap[:]),
                      deps=[mrs])
            mrs = P.op("dve", lambda e: e.match_replace(
                out=zap[:], in_to_replace=mx8[:], in_values=zap[:],
                imm_value=-1e30), deps=[m8])
        mk = P.op("dve", lambda e: e.tensor_tensor(zap[:], zap[:], srow[:],
                  ALU.not_equal), deps=[mrs])
        sp = P.op("dve", lambda e: e.scalar_tensor_tensor(
            asp[:], srow[:], 1.0, zap[:], ALU.mult, ALU.mult,
            accum_out=dcol[:]), deps=[mk, att_sts[-1] if att_sts else None])
        dn1 = P.op("dve", lambda e: e.tensor_scalar(
            dcol[:], dcol[:], 1e-10, None, ALU.add), deps=[sp])
        dn2 = P.op("dve", lambda e: e.reciprocal(dcol[:], dcol[:]), deps=[dn1])
        nrm = P.op("dve", lambda e: e.tensor_scalar(
            asp[:], asp[:], dcol[:, 0:1], None, ALU.mult), deps=[dn2])
        sc_prev = nrm
        mr_prev = sp
        ast = P.dma("sync", lambda e, qt_i=qt_i: e.dma_start(
            out=attn1[qt_i * 128:(qt_i + 1) * 128, :], in_=asp[:]),
            deps=[nrm])
        att_sts.append(ast)
        ep_prev[j] = [sm]

    _sc2.close()
    at = ctx.sbuf("b7_at", [128, 32 * TC], BF16)
    ev1 = [ctx.sbuf(f"b7_ev1{j}", [128, TC], BF16) for j in range(2)]
    pa = [ctx.psum(f"b7_pa{j}", [128, TC], F32) for j in range(2)]
    at_cons = []
    prev_st = {0: None, 1: None}
    prev_ev = {0: None, 1: None}
    for qc in range(NTCH):
        lats = [P.dma("sync", lambda e, qc=qc, kt=kt: e.dma_start(
            out=at[:, kt * TC:(kt + 1) * TC],
            in_=attn1[qc * TC:(qc + 1) * TC, kt * 128:(kt + 1) * 128],
            transpose=True), deps=att_sts + at_cons) for kt in range(32)]
        at_cons = []
        for vc in range(NDC):
            jj = vc % 2
            def avfn(e, qc=qc, vc=vc, jj=jj):
                last = None
                for kt in range(32):
                    last = e.matmul(pa[jj][:],
                                    v_tm[:, kt * D + vc * 128:
                                         kt * D + (vc + 1) * 128],
                                    at[:, kt * TC:(kt + 1) * TC],
                                    start=(kt == 0), stop=(kt == 31))
                return last
            av = P.op("pe", avfn, deps=lats + last_qkv + [prev_ev[jj]])
            eo = P.op("dve", lambda e, jj=jj: e.tensor_copy(ev1[jj][:],
                      pa[jj][:]), deps=[av, prev_st[jj]])
            st = P.dma("sync", lambda e, qc=qc, vc=vc, jj=jj: e.dma_start(
                out=s1[vc * 128:(vc + 1) * 128, qc * TC:(qc + 1) * TC],
                in_=ev1[jj][:]), deps=[eo])
            prev_ev[jj] = eo
            prev_st[jj] = st
            at_cons.append(av)


def _yview(y1, dcp, tcc):
    return (y1[:].rearrange("p (c t) -> p c t", t=2048)
            [:, 2 * dcp:2 * dcp + 2, tcc * TC: tcc * TC + TC])


def ph_b8(P, ctx):
    xn1T, g1, s1, p1 = (ctx.hbm["xn1T"], ctx.hbm["g1"], ctx.hbm["s1"],
                        ctx.hbm["p1"])
    w1f_h, w2f_h = ctx.hbm["w1f"], ctx.hbm["w2f"]
    b1f_h, bs2_h, be2_h = ctx.hbm["b1f"], ctx.hbm["bs2"], ctx.hbm["be2"]
    NE = 32
    HW = 2048
    NHALF = T // HW

    y1 = ctx.sbuf("b8_y1", [128, NDC * HW], F32)
    xnh = ctx.sbuf("b8_xnh", [128, NDC * HW], BF16)
    w1 = [ctx.sbuf(f"b8_w1{j}", [128, NDC * D], BF16) for j in range(2)]
    w2 = [ctx.sbuf(f"b8_w2{j}", [128, 8 * D], BF16) for j in range(2)]
    b1 = ctx.sbuf("b8_b1", [128, NE * NDC], F32)
    bs2 = ctx.sbuf("b8_bs2", [128, NDC], F32)
    be2 = ctx.sbuf("b8_be2", [E1, NDC * 128], BF16)
    g1b = ctx.sbuf("b8_g1b", [E1, HW], BF16)
    grow = ctx.sbuf("b8_grow", [1, TC], F32)
    gbc = ctx.sbuf("b8_gbc", [128, TC], BF16)
    ones1 = ctx.sbuf("b8_ones1", [1, 128], F32)
    hsb2 = [ctx.sbuf(f"b8_h{j}", [128, 8 * TC], BF16) for j in range(2)]
    tsc = [ctx.sbuf(f"b8_t{j}", [128, TC], F32) for j in range(2)]
    gsb = [ctx.sbuf(f"b8_g{j}", [128, TC], BF16) for j in range(2)]
    ssb = [ctx.sbuf(f"b8_s{j}", [128, TC], BF16) for j in range(2)]
    psb = [ctx.sbuf(f"b8_p{j}", [128, TC], BF16) for j in range(2)]
    ph_ = [ctx.psum(f"b8_ph{j}", [128, TC], F32) for j in range(2)]
    py = [ctx.psum(f"b8_py{j}", [128, 2 * TC], F32) for j in range(2)]
    pbc = ctx.psum("b8_pbc", [128, TC], F32)

    o1 = P.op("dve", lambda e: e.memset(ones1[:], 1.0))
    ldb1 = P.dma("sync", lambda e: e.dma_start(
        out=b1[:], in_=b1f_h[:].rearrange("p e c -> p (e c)")))
    ldbs2 = P.dma("sync", lambda e: e.dma_start(out=bs2[:], in_=bs2_h[:]))
    ldbe2 = P.dma("pool", lambda e: e.dma_start(
        out=be2[:], in_=be2_h[:]))
    w1_cons = {0: [], 1: []}
    w2_cons = {0: [], 1: []}
    xn_cons = []
    y_cons = []
    h_cons = {}
    ts_cons = {}
    gb_cons = []
    gr_cons = []
    ph_cons = {}
    py_cons = {}
    ya_prev = {}
    prev_comb = {}
    g1_cons = []
    ci = [0]
    pend_down = [None]
    ldg_ref = [None]
    ms_ref = [None]

    def make_down(en, tcc, jw, hsb, hops, par):
        ldg_l, ms_l = ldg_ref[0], ms_ref[0]

        def emit():
            last_dm = None
            for dcp in range(NDC // 2):
                jj = dcp % 2

                def dfn(e, jw=jw, dcp=dcp, en=en, tcc=tcc, jj=jj, hsb=hsb):
                    last = None
                    for hf in range(2):
                        dc = 2 * dcp + hf
                        for ic in range(8):
                            stop = (ic == 7) and (en != 0)
                            last = e.matmul(
                                py[jj][:, hf * TC:(hf + 1) * TC],
                                w2[jw][:, ic * D + dc * 128:
                                       ic * D + (dc + 1) * 128],
                                hsb[:, ic * TC:(ic + 1) * TC],
                                start=(ic == 0), stop=stop)
                        if en == 0:
                            last = e.matmul(
                                py[jj][:, hf * TC:(hf + 1) * TC],
                                be2[:, dc * 128:(dc + 1) * 128],
                                g1b[:, tcc * TC:(tcc + 1) * TC],
                                start=False, stop=True)
                    return last
                dm = P.op("pe", dfn,
                          deps=hops + py_cons.get(jj, []) +
                          ([ldbe2, ldg_l] if en == 0 else []))
                ya = P.op("dve", lambda e, dcp=dcp, tcc=tcc, jj=jj:
                          e.tensor_add(
                              _yview(y1, dcp, tcc),
                              _yview(y1, dcp, tcc),
                              py[jj][:].rearrange("p (b t) -> p b t", t=TC)),
                          deps=[dm, ya_prev.get((dcp, tcc))] + ms_l)
                ya_prev[(dcp, tcc)] = ya
                py_cons[jj] = [ya]
                y_cons.append(ya)
                last_dm = dm
            for ic in range(8):
                h_cons[(par, ic)] = [last_dm]
            w2_cons[jw].append(y_cons[-1])
        return emit

    for half in range(NHALF):
        col0h = half * HW
        ldx = P.dma("sync", lambda e, col0h=col0h: e.dma_start(
            out=xnh[:].rearrange("p (c t) -> p c t", t=HW),
            in_=xn1T[:, col0h:col0h + HW].rearrange("(c p) t -> p c t", p=128)),
            deps=xn_cons)
        xn_cons = []
        ldg = P.dma("pool", lambda e, col0h=col0h: e.dma_start(
            out=g1b[:], in_=g1[:, col0h:col0h + HW]), deps=g1_cons)
        g1_cons = []
        ms = [P.op("dve", lambda e, dc=dc: e.memset(
            y1[:, dc * HW:(dc + 1) * HW], 0.0), deps=y_cons)
            for dc in range(NDC)]
        y_cons = []
        ldg_ref[0] = ldg
        ms_ref[0] = ms
        for en in range(NE):
            jw = en % 2
            lw1 = P.dma("pool", lambda e, en=en, t=w1[jw]: e.dma_start(
                out=t[:].rearrange("p (c i) -> p c i", i=D),
                in_=w1f_h[en].rearrange("(c p) i -> p c i", p=128)),
                deps=w1_cons[jw])
            lw2 = P.dma("pool", lambda e, en=en, t=w2[jw]: e.dma_start(
                out=t[:].rearrange("p (c d) -> p c d", d=D),
                in_=w2f_h[en].rearrange("(c p) d -> p c d", p=128)),
                deps=w2_cons[jw])
            w1_cons[jw], w2_cons[jw] = [], []
            for tcc in range(HW // TC):
                ci[0] += 1
                hsb = hsb2[ci[0] % 2]
                col0 = tcc * TC
                if en > 0:
                    lgr = P.dma("sync", lambda e, en=en, col0=col0: e.dma_start(
                        out=grow[:], in_=g1[en - 1:en,
                                            col0h + col0:col0h + col0 + TC]),
                        deps=gr_cons)
                    mgb = P.op("pe", lambda e: e.matmul(
                        pbc[:], ones1[:], grow[:], start=True, stop=True),
                        deps=[lgr, o1] + gb_cons)
                    egb = P.op("dve", lambda e: e.tensor_copy(gbc[:], pbc[:]),
                               deps=[mgb] + gb_cons)
                    gr_cons = [mgb]
                    gb_cons = [egb]
                hops = []
                for ic in range(8):
                    jj = ic % 2
                    def hfn(e, jw=jw, ic=ic, tcc=tcc, jj=jj):
                        last = None
                        for dc in range(NDC):
                            last = e.matmul(
                                ph_[jj][:],
                                w1[jw][:, dc * D + ic * 128:
                                       dc * D + (ic + 1) * 128],
                                xnh[:, dc * HW + tcc * TC:
                                    dc * HW + tcc * TC + TC],
                                start=(dc == 0), stop=(dc == NDC - 1))
                        return last
                    hm = P.op("pe", hfn, deps=[lw1, ldx] + ph_cons.get(jj, []))
                    ge = P.op("act", lambda e, en=en, ic=ic, jj=jj, hsb=hsb:
                              e.activation(hsb[:, ic * TC:(ic + 1) * TC],
                                           ph_[jj][:], AF.Gelu,
                                           bias=b1[:, en * NDC + ic:
                                                   en * NDC + ic + 1],
                                           scale=1.0),
                              deps=[hm, ldb1] + h_cons.get((ci[0] % 2, ic), []))
                    ph_cons[jj] = [ge]
                    if en > 0:
                        hc = P.op("pool", lambda e, ic=ic, hsb=hsb: e.tensor_mul(
                            hsb[:, ic * TC:(ic + 1) * TC],
                            hsb[:, ic * TC:(ic + 1) * TC], gbc[:]),
                            deps=[ge] + gb_cons)
                        gb_cons.append(hc)
                        hops.append(hc)
                    else:
                        hops.append(ge)
                if pend_down[0] is not None:
                    pend_down[0]()
                pend_down[0] = make_down(en, tcc, jw, hsb, list(hops),
                                         ci[0] % 2)
                w1_cons[jw].append(hops[-1])
        if pend_down[0] is not None:
            pend_down[0]()
            pend_down[0] = None
        xn_cons = y_cons[-1:]
        g1_cons = y_cons[-1:]
        for tcc in range(HW // TC):
            for dc in range(NDC):
                k = tcc * NDC + dc
                jj = k % 2
                colg = col0h + tcc * TC
                lds = P.dma("sync", lambda e, dc=dc, colg=colg, jj=jj:
                            e.dma_start(out=ssb[jj][:],
                                        in_=s1[dc * 128:(dc + 1) * 128,
                                               colg:colg + TC]),
                            deps=[prev_comb.get(("s", jj))])
                tadd = P.op("dve", lambda e, dc=dc, tcc=tcc, jj=jj:
                            e.tensor_add(
                                tsc[jj][:],
                                y1[:, dc * HW + tcc * TC:
                                   dc * HW + tcc * TC + TC],
                                xnh[:, dc * HW + tcc * TC:
                                    dc * HW + tcc * TC + TC]),
                            deps=y_cons + [prev_comb.get(("t", jj))])
                gg = P.op("act", lambda e, dc=dc, jj=jj: e.activation(
                    gsb[jj][:], tsc[jj][:], AF.Silu,
                    bias=bs2[:, dc:dc + 1], scale=1.0),
                    deps=[tadd, ldbs2, prev_comb.get(("g", jj))])
                pp = P.op("dve", lambda e, jj=jj: e.tensor_mul(
                    psb[jj][:], gsb[jj][:], ssb[jj][:]),
                    deps=[gg, lds, prev_comb.get(("st", jj))])
                st = P.dma("sync", lambda e, dc=dc, colg=colg, jj=jj:
                           e.dma_start(out=p1[dc * 128:(dc + 1) * 128,
                                              colg:colg + TC],
                                       in_=psb[jj][:]), deps=[pp])
                prev_comb[("t", jj)] = gg
                prev_comb[("g", jj)] = pp
                prev_comb[("s", jj)] = pp
                prev_comb[("st", jj)] = st
                y_cons.append(tadd)
                xn_cons.append(tadd)


def ph_b9(P, ctx):
    x1, p1, out = ctx.hbm["x1"], ctx.hbm["p1"], ctx.hbm["out"]
    xb = [ctx.sbuf(f"b9_x{j}", [128, D], F32) for j in range(2)]
    pb_ = [ctx.sbuf(f"b9_p{j}", [128, D], BF16) for j in range(2)]
    ob = [ctx.sbuf(f"b9_o{j}", [128, D], F32) for j in range(2)]
    prev = {}
    for i in range(NTT):
        j = i % 2
        pv = prev.get(j, {})
        ld = P.dma("sync", lambda e, i=i, t=xb[j]: e.dma_start(
            out=t[:], in_=x1[i * 128:(i + 1) * 128, :]), deps=[pv.get("add")])
        ldp = P.dma("sync", lambda e, i=i, t=pb_[j]: e.dma_start(
            out=t[:], in_=p1[:, i * 128:(i + 1) * 128], transpose=True),
            deps=[pv.get("add")])
        add = P.op("dve", lambda e, j=j: e.tensor_add(
            ob[j][:], xb[j][:], pb_[j][:]), deps=[ld, ldp, pv.get("st")])
        st = P.dma("pool", lambda e, i=i, j=j: e.dma_start(
            out=out[i * 128:(i + 1) * 128, :], in_=ob[j][:]), deps=[add])
        prev[j] = dict(add=add, st=st)


PHASES = {
    "b0": (ph_b0, ["x", "lnw", "lnb"], ["xn0", "h0"]),
    "b1": (ph_b1, ["xn0", "h0", "gw0T"], ["xn0T", "h0T", "cw0"]),
    "b2": (ph_b2, ["h0T", "wq", "wk", "wv", "wo", "theta"], ["s0"]),
    "b2dbg": (ph_b2, ["h0T", "wq", "wk", "wv", "wo", "theta"],
              ["s0", "dbg_q", "dbg_k", "dbg_v", "dbg_o", "dbg_C", "dbg_S"]),
    "b4": (ph_b4, ["xn0T", "cw0", "s0", "gu0", "dn0"], ["p0"]),
    "b4dbg": (ph_b4, ["xn0T", "cw0", "s0", "gu0", "dn0"],
              ["p0", "dbg_y", "dbg_h"]),
    "b5": (ph_b5, ["x", "p0"], ["x1", "xn1"]),
    "b6": (ph_b6, ["xn1", "wr1", "br1"], ["xn1T", "g1"]),
    "b7": (ph_b7, ["xn1T", "wq1", "wk1", "wv1", "bq1", "bk1", "bv1"],
           ["q1", "attn1", "s1"]),
    "b8": (ph_b8, ["xn1T", "g1", "s1", "w1f", "w2f", "b1f", "bs2", "be2"],
           ["p1"]),
    "b9": (ph_b9, ["x1", "p1"], ["out"]),
}


INTERNAL_WRITES = set()


def build_group(phase_names, feed_names):
    nc = bass.Bass("TRN2", target_bir_lowering=False)
    es = ExitStack()
    P = Prog()
    ctx = Ctx(nc, es, P)
    reads, writes = set(), set()
    for pn in phase_names:
        _, r, w = PHASES[pn]
        reads |= set(r)
        writes |= set(w)
    in_names = sorted((reads - writes) | (reads & feed_names))
    out_names = sorted(writes - INTERNAL_WRITES)
    for n in in_names:
        shape, dt = (WEIGHT_SPECS.get(n) or INTER_SPECS.get(n))
        ctx.hbm[n] = nc.dram_tensor(n, shape, dt, kind="ExternalInput").ap()
    for n in out_names:
        shape, dt = INTER_SPECS[n]
        ctx.hbm[n] = nc.dram_tensor(n, shape, dt, kind="ExternalOutput").ap()
    for n in sorted(writes & INTERNAL_WRITES):
        shape, dt = INTER_SPECS[n]
        ctx.hbm[n] = nc.dram_tensor(n, shape, dt, kind="Internal").ap()
    for pi, pn in enumerate(phase_names):
        fn, _, _ = PHASES[pn]
        ctx.ph = ExitStack()
        fn(P, ctx)
        ctx.ph.close()
        P.barrier()
    with nc.Block() as block:
        P.emit(nc, block, es)
    return nc, in_names, out_names


def run_groups(groups, shared, xs, keep=()):
    store = [dict(shared) for _ in range(NCORES)]
    for b in range(NCORES):
        store[b]["x"] = xs[b]
    for gi, gnames in enumerate(groups):
        nc, in_names, out_names = build_group(gnames, set(shared.keys()) | {"x"})
        in_maps = [{n: store[b][n] for n in in_names} for b in range(NCORES)]
        res = run_bass_kernel_spmd(nc, in_maps, core_ids=list(range(NCORES)))
        for b in range(NCORES):
            for n in out_names:
                store[b][n] = np.asarray(res.results[b][n])
    return store


ALL_PHASES = ["b0", "b1", "b2", "b4", "b5", "b6", "b7", "b8", "b9"]
FUSED = True
GROUPS = [ALL_PHASES] if FUSED else [[p] for p in ALL_PHASES]


def kernel(**inputs):
    global INTERNAL_WRITES
    if FUSED:
        INTERNAL_WRITES = set(INTER_SPECS) - {"out"}
    shared, xs = prep_host(inputs)
    store = run_groups(GROUPS, shared, xs)
    return np.stack([store[b]["out"] for b in range(NCORES)])


# version 06
# speedup vs baseline: 1.2761x; 1.0597x over previous
import numpy as np
from contextlib import ExitStack

import concourse.bass as bass
import concourse.mybir as mybir
from concourse.bass_utils import run_bass_kernel_spmd

F32 = mybir.dt.float32
BF16 = mybir.dt.bfloat16
ALU = mybir.AluOpType
AF = mybir.ActivationFunctionType
AX = mybir.AxisListType

NCORES = 8
T = 4096
D = 1024
NDC = D // 128
NTT = T // 128
TC = 512
NTCH = T // TC
NH = 16
HD = 64
NPAIR = 8
I0 = 938
I0P = 1024
E0 = 8
E1 = 31
LN_EPS = 1e-5

ENGS = ("pe", "act", "dve", "pool", "sync")
DEBUG_B2 = False


class Op:
    __slots__ = ("eng", "fn", "deps", "step", "cum")


class Bar:
    __slots__ = ("snap",)


class Prog:
    def __init__(self):
        self.items = {e: [] for e in ENGS}
        self.cum = {e: 0 for e in ENGS}

    def op(self, eng, fn, deps=(), step=1):
        o = Op()
        o.eng = eng
        o.fn = fn
        o.deps = tuple(d for d in deps if d is not None)
        o.step = step
        self.cum[eng] += step
        o.cum = self.cum[eng]
        self.items[eng].append(o)
        return o

    def dma(self, eng, fn, deps=()):
        return self.op(eng, fn, deps, step=16)

    def barrier(self):
        snap = dict(self.cum)
        for e in ENGS:
            b = Bar()
            b.snap = snap
            self.items[e].append(b)

    def emit(self, nc, block, es):
        sems = {e: es.enter_context(nc.semaphore(f"ch_{e}")) for e in ENGS}
        engmap = {"pe": block.tensor, "act": block.scalar, "dve": block.vector,
                  "pool": block.gpsimd, "sync": block.sync}
        self.barrier()
        for ename in ENGS:
            deco = engmap[ename]
            items = self.items[ename]

            def section(e, ename=ename, items=items):
                waited = {x: 0 for x in ENGS}
                for it in items:
                    if isinstance(it, Bar):
                        for oe in ENGS:
                            c = it.snap[oe]
                            if oe != ename and waited[oe] < c:
                                e.wait_ge(sems[oe], c)
                                waited[oe] = c
                        continue
                    needs = {}
                    for d in it.deps:
                        if needs.get(d.eng, 0) < d.cum:
                            needs[d.eng] = d.cum
                    for oe, c in needs.items():
                        if waited[oe] < c:
                            e.wait_ge(sems[oe], c)
                            waited[oe] = c
                    ins = it.fn(e)
                    ins.then_inc(sems[ename], it.step)
                    if it.eng == ename:
                        waited[ename] = max(waited[ename], 0)

            deco(section)


class Ctx:
    def __init__(self, nc, es, prog):
        self.nc = nc
        self.es = es
        self.P = prog
        self.hbm = {}
        self.ph = None
        self._consts = {}

    def sbuf(self, name, shape, dtype, scope=None):
        return (scope or self.ph).enter_context(
            self.nc.sbuf_tensor(name, shape, dtype))

    def psum(self, name, shape, dtype=F32, scope=None):
        return (scope or self.ph).enter_context(
            self.nc.psum_tensor(name, shape, dtype))

    def constant(self, value):
        value = float(value)
        if value in self._consts:
            return self._consts[value]
        t = self.es.enter_context(
            self.nc.sbuf_tensor(f"const_{len(self._consts)}", [128, 1], F32))
        op = self.P.op("dve", lambda e: e.memset(t[:], value))
        self.nc.const_aps.register(F32, value, t[:])
        self._consts[value] = op
        return op


def _perm_qk_cols():
    perm = []
    for h in range(NH):
        base = h * HD
        perm.extend(base + 2 * np.arange(32))
        perm.extend(base + 2 * np.arange(32) + 1)
    return np.array(perm)


def prep_host(inputs):
    g = {k: np.asarray(v) for k, v in inputs.items()}
    out = {}
    perm = _perm_qk_cols()
    out["wq"] = np.ascontiguousarray(g["L0_attn_w_q"][:, perm])
    out["wk"] = np.ascontiguousarray(g["L0_attn_w_kv"][:, :D][:, perm])
    out["wv"] = np.ascontiguousarray(g["L0_attn_w_kv"][:, D:])
    out["wo"] = np.ascontiguousarray(g["L0_attn_w_o"])
    out["lnw"] = np.ascontiguousarray(g["L0_attn_ln_w"].reshape(1, D))
    out["lnb"] = np.ascontiguousarray(g["L0_attn_ln_b"].reshape(1, D))
    out["theta"] = np.asarray(g["L0_attn_theta"], dtype=np.float32).reshape(1, 1)
    out["gw0T"] = np.ascontiguousarray(g["L0_ffn_gate_weight"].T)

    def prep_gu(gu):
        o = np.zeros((2 * I0P, D), dtype=np.float32)
        gp, up = gu[:I0], gu[I0:]
        for ic in range(8):
            lo, hi = ic * 128, min((ic + 1) * 128, I0)
            n = hi - lo
            if n > 0:
                o[ic * 256: ic * 256 + n] = gp[lo:hi]
                o[ic * 256 + 128: ic * 256 + 128 + n] = up[lo:hi]
        return o

    def prep_dn(dn):
        o = np.zeros((I0P, D), dtype=np.float32)
        o[:I0] = dn.T
        return o

    out["gu0"] = np.ascontiguousarray(np.stack(
        [prep_gu(g["L0_ffn_shared_gate_up"]).T] +
        [prep_gu(g["L0_ffn_experts_gate_up"][e]).T for e in range(E0)]))
    out["dn0"] = np.ascontiguousarray(np.stack(
        [prep_dn(g["L0_ffn_shared_down"])] +
        [prep_dn(g["L0_ffn_experts_down"][e]) for e in range(E0)]))
    out["wq1"] = np.ascontiguousarray(g["L1_attn_Wq"].T)
    out["wk1"] = np.ascontiguousarray(g["L1_attn_Wk"].T)
    out["wv1"] = np.ascontiguousarray(g["L1_attn_Wv"].T)
    out["bq1"] = np.ascontiguousarray(g["L1_attn_bq"].reshape(NDC, 128).T)
    out["bk1"] = np.ascontiguousarray(g["L1_attn_bk"].reshape(NDC, 128).T)
    out["bv1"] = np.ascontiguousarray(g["L1_attn_bv"].reshape(1, D))
    out["w1f"] = np.ascontiguousarray(np.stack(
        [g["L1_ffn_ws1"]] + [g["L1_ffn_We1"][e] for e in range(E1)]))
    out["w2f"] = np.ascontiguousarray(np.stack(
        [g["L1_ffn_ws2"]] + [g["L1_ffn_We2"][e] for e in range(E1)]))
    out["b1f"] = np.ascontiguousarray(np.stack(
        [g["L1_ffn_bs1"]] + [g["L1_ffn_Be1"][e] for e in range(E1)]
    ).reshape(32, NDC, 128).transpose(2, 0, 1))
    out["bs2"] = np.ascontiguousarray(g["L1_ffn_bs2"].reshape(NDC, 128).T)
    out["be2"] = np.ascontiguousarray(g["L1_ffn_Be2"])
    out["wr1"] = np.ascontiguousarray(g["L1_ffn_Wr"])
    out["br1"] = np.ascontiguousarray(g["L1_ffn_br"].reshape(1, E1))
    xs = [np.ascontiguousarray(g["L0_attn_x"][b]) for b in range(NCORES)]
    return out, xs


WEIGHT_SPECS = {
    "wq": ([D, D], F32), "wk": ([D, D], F32), "wv": ([D, D], F32),
    "wo": ([D, D], F32), "lnw": ([1, D], F32), "lnb": ([1, D], F32),
    "theta": ([1, 1], F32), "gw0T": ([D, E0], F32),
    "gu0": ([E0 + 1, D, 2 * I0P], F32), "dn0": ([E0 + 1, I0P, D], F32),
    "wq1": ([D, D], F32), "wk1": ([D, D], F32), "wv1": ([D, D], F32),
    "bq1": ([128, NDC], F32), "bk1": ([128, NDC], F32), "bv1": ([1, D], F32),
    "w1f": ([32, D, D], F32), "w2f": ([32, D, D], F32),
    "b1f": ([128, 32, NDC], F32), "bs2": ([128, NDC], F32),
    "be2": ([E1, D], F32), "wr1": ([D, E1], F32), "br1": ([1, E1], F32),
    "x": ([T, D], F32),
}
INTER_SPECS = {
    "xn0": ([T, D], BF16), "h0": ([T, D], BF16),
    "xn0T": ([D, T], BF16), "h0T": ([D, T], BF16),
    "cw0": ([E0, T], F32),
    "s0": ([D, T], BF16), "p0": ([D, T], BF16),
    "x1": ([T, D], F32), "xn1": ([T, D], BF16), "xn1T": ([D, T], BF16),
    "g1": ([E1, T], F32), "q1": ([D, T], BF16),
    "attn1": ([T, T], BF16), "s1": ([D, T], BF16), "p1": ([D, T], BF16),
    "out": ([T, D], F32),
    "dbg_q": ([128, T], BF16), "dbg_k": ([128, T], BF16),
    "dbg_C": ([128, T], F32), "dbg_S": ([128, T], F32),
    "dbg_y": ([128, NDC * 1024], F32), "dbg_h": ([128, 8 * TC], BF16),
    "dbg_v": ([128, 32 * 130], BF16), "dbg_o": ([128, NPAIR * T], BF16),
}


def ph_b0(P, ctx):
    x, xn0, h0 = ctx.hbm["x"], ctx.hbm["xn0"], ctx.hbm["h0"]
    ceps = ctx.constant(LN_EPS)
    cm1 = ctx.constant(-1.0)
    cmh = ctx.constant(-0.5)
    lnw_h, lnb_h = ctx.hbm["lnw"], ctx.hbm["lnb"]
    xbuf = [ctx.sbuf(f"b0_x{j}", [128, D], F32) for j in range(2)]
    sq = [ctx.sbuf(f"b0_sq{j}", [128, D], F32) for j in range(2)]
    xnf = [ctx.sbuf(f"b0_xn{j}", [128, D], F32) for j in range(2)]
    z2f = [ctx.sbuf(f"b0_z2{j}", [128, D], F32) for j in range(2)]
    hbf = [ctx.sbuf(f"b0_h{j}", [128, D], BF16) for j in range(2)]
    xnbf = [ctx.sbuf(f"b0_xnb{j}", [128, D], BF16) for j in range(2)]
    st = [ctx.sbuf(f"b0_st{j}", [128, 8], F32) for j in range(2)]
    lnw_bc = ctx.sbuf("b0_lnwbc", [128, D], F32)
    lnb_bc = ctx.sbuf("b0_lnbbc", [128, D], F32)
    lnrow = ctx.sbuf("b0_lnrow", [1, 2 * D], F32)
    ones1 = ctx.sbuf("b0_ones", [1, 128], F32)
    pb = [ctx.psum(f"b0_pb{i}", [128, 512], F32) for i in range(4)]

    ldr = P.dma("sync", lambda e: e.dma_start(out=lnrow[:, 0:D], in_=lnw_h[:]))
    ldr2 = P.dma("sync", lambda e: e.dma_start(out=lnrow[:, D:2 * D], in_=lnb_h[:]))
    on = P.op("dve", lambda e: e.memset(ones1[:], 1.0))
    mms, cps = [], []
    for k in range(4):
        src = lnrow[:, (k % 2) * 512 + (k // 2) * D: (k % 2) * 512 + (k // 2) * D + 512]
        mms.append(P.op("pe", lambda e, pbk=pb[k], s=src: e.matmul(
            pbk[:], ones1[:], s, start=True, stop=True), deps=[ldr, ldr2, on]))
    for k in range(4):
        dst = (lnw_bc if k < 2 else lnb_bc)[:, (k % 2) * 512:(k % 2) * 512 + 512]
        cps.append(P.op("dve", lambda e, d=dst, pbk=pb[k]: e.tensor_copy(d, pbk[:]),
                        deps=[mms[k]]))
    C = 1.0 / D
    prev = {}
    for i in range(NTT):
        j = i % 2
        xb, sqb, xnb, z2b, hb, xnbb, stb = (xbuf[j], sq[j], xnf[j], z2f[j],
                                            hbf[j], xnbf[j], st[j])
        sx, sx2 = stb[:, 0:1], stb[:, 1:2]
        mu, nv, rstd = stb[:, 2:3], stb[:, 3:4], stb[:, 4:5]
        mu2, nv2, rstd2 = stb[:, 5:6], stb[:, 6:7], stb[:, 7:8]
        pv = prev.get(j, {})
        ld = P.dma("sync", lambda e, i=i, xb=xb: e.dma_start(
            out=xb[:], in_=x[i * 128:(i + 1) * 128, :]),
            deps=[pv.get("xn"), pv.get("sq1")])
        summ = P.op("dve", lambda e, sx=sx, xb=xb: e.reduce_sum(sx, xb[:], axis=AX.X),
                    deps=[ld, pv.get("nv")])
        sq1 = P.op("act", lambda e, sqb=sqb, xb=xb, sx2=sx2: e.activation(
            sqb[:], xb[:], AF.Square, bias=0.0, scale=1.0, accum_out=sx2),
            deps=[ld, pv.get("t1"), pv.get("nv")])
        ex2 = P.op("dve", lambda e, sx2=sx2: e.tensor_scalar(
            sx2, sx2, C, None, ALU.mult), deps=[sq1])
        muo = P.op("dve", lambda e, mu=mu, sx=sx: e.tensor_scalar(
            mu, sx, C, None, ALU.mult), deps=[summ])
        nvo = P.op("dve", lambda e, nv=nv, mu=mu, sx2=sx2: e.scalar_tensor_tensor(
            nv, mu, mu, sx2, ALU.mult, ALU.subtract), deps=[muo, ex2])
        lnv = P.op("act", lambda e, nv=nv: e.activation(
            nv, nv, AF.Ln, bias=LN_EPS, scale=-1.0), deps=[nvo, ceps, cm1])
        rst = P.op("act", lambda e, rstd=rstd, nv=nv: e.activation(
            rstd, nv, AF.Exp, bias=0.0, scale=-0.5), deps=[lnv, cmh])
        xn = P.op("dve", lambda e, xnb=xnb, xb=xb, mu=mu, rstd=rstd: e.tensor_scalar(
            xnb[:], xb[:], mu, rstd, ALU.subtract, ALU.mult),
            deps=[muo, rst, pv.get("z2"), pv.get("xnbf"), pv.get("sum2"), pv.get("sq2")])
        sum2 = P.op("dve", lambda e, mu2=mu2, xnb=xnb: e.reduce_sum(
            mu2, xnb[:], axis=AX.X), deps=[xn])
        sq2 = P.op("act", lambda e, sqb=sqb, xnb=xnb, nv2=nv2: e.activation(
            sqb[:], xnb[:], AF.Square, bias=0.0, scale=1.0, accum_out=nv2),
            deps=[xn, sq1])
        ex2b = P.op("dve", lambda e, nv2=nv2: e.tensor_scalar(
            nv2, nv2, C, None, ALU.mult), deps=[sq2])
        mu2o = P.op("dve", lambda e, mu2=mu2: e.tensor_scalar(
            mu2, mu2, C, None, ALU.mult), deps=[sum2])
        nv2o = P.op("dve", lambda e, nv2=nv2, mu2=mu2: e.scalar_tensor_tensor(
            nv2, mu2, mu2, nv2, ALU.mult, ALU.subtract), deps=[mu2o, ex2b])
        lnv2 = P.op("act", lambda e, nv2=nv2: e.activation(
            nv2, nv2, AF.Ln, bias=LN_EPS, scale=-1.0), deps=[nv2o, ceps, cm1])
        rst2 = P.op("act", lambda e, rstd2=rstd2, nv2=nv2: e.activation(
            rstd2, nv2, AF.Exp, bias=0.0, scale=-0.5), deps=[lnv2, cmh])
        z2 = P.op("dve", lambda e, z2b=z2b, xnb=xnb, mu2=mu2, rstd2=rstd2:
                  e.tensor_scalar(z2b[:], xnb[:], mu2, rstd2, ALU.subtract, ALU.mult),
                  deps=[mu2o, rst2, pv.get("t1")])
        t1 = P.op("dve", lambda e, sqb=sqb, z2b=z2b: e.tensor_mul(
            sqb[:], z2b[:], lnw_bc[:]), deps=[z2, sq2] + cps[:2])
        h = P.op("dve", lambda e, hb=hb, sqb=sqb: e.tensor_add(
            hb[:], sqb[:], lnb_bc[:]), deps=[t1, pv.get("sth")] + cps[2:])
        xnbf_o = P.op("act", lambda e, xnbb=xnbb, xnb=xnb: e.activation(
            xnbb[:], xnb[:], AF.Copy, bias=0.0, scale=1.0),
            deps=[xn, pv.get("stx")])
        sth = P.dma("pool", lambda e, i=i, hb=hb: e.dma_start(
            out=h0[i * 128:(i + 1) * 128, :], in_=hb[:]), deps=[h])
        stx = P.dma("pool", lambda e, i=i, xnbb=xnbb: e.dma_start(
            out=xn0[i * 128:(i + 1) * 128, :], in_=xnbb[:]), deps=[xnbf_o])
        prev[j] = dict(xn=xn, sq1=sq1, z2=z2, t1=t1, sth=sth, stx=stx,
                       xnbf=xnbf_o, nv=nv2o, sum2=sum2, sq2=sq2)


def ph_b1(P, ctx):
    xn0, h0 = ctx.hbm["xn0"], ctx.hbm["h0"]
    xn0T, h0T = ctx.hbm["xn0T"], ctx.hbm["h0T"]
    gw0T_h, cw0 = ctx.hbm["gw0T"], ctx.hbm["cw0"]

    xt = [[ctx.sbuf(f"b1_xt{j}_{dc}", [128, TC], BF16) for dc in range(NDC)]
          for j in range(2)]
    ht = [[ctx.sbuf(f"b1_ht{j}_{dc}", [128, TC], BF16) for dc in range(NDC)]
          for j in range(2)]
    gw = ctx.sbuf("b1_gw", [128, NDC * E0], BF16)
    lg = [ctx.sbuf(f"b1_lg{j}", [128, 4 * E0], F32) for j in range(2)]
    sc2 = [ctx.sbuf(f"b1_sc2{j}", [128, 4 * E0], F32) for j in range(2)]
    tmp = [ctx.sbuf(f"b1_tmp{j}", [128, 4 * E0], F32) for j in range(2)]
    msk = [ctx.sbuf(f"b1_msk{j}", [128, 4 * E0], F32) for j in range(2)]
    cwt = [ctx.sbuf(f"b1_cwt{j}", [128, 4 * E0], F32) for j in range(2)]
    r1 = [ctx.sbuf(f"b1_r1{j}", [128, 4], F32) for j in range(2)]
    r2 = [ctx.sbuf(f"b1_r2{j}", [128, 4], F32) for j in range(2)]
    cwfm = ctx.sbuf("b1_cwfm", [E0, T], F32)
    ident = ctx.sbuf("b1_id", [128, 128], F32)
    ia = ctx.sbuf("b1_ia", [128, 128], F32)
    plg = [ctx.psum(f"b1_plg{j}", [128, E0], F32) for j in range(2)]
    ptr = [ctx.psum(f"b1_ptr{j}", [E0, 128], F32) for j in range(2)]

    i1 = P.op("pool", lambda e: e.iota(ia[:], pattern=[[0, 128]], base=0,
              channel_multiplier=1, allow_small_or_imprecise_dtypes=True))
    i2 = P.op("pool", lambda e: e.iota(ident[:], pattern=[[1, 128]], base=0,
              channel_multiplier=0, allow_small_or_imprecise_dtypes=True))
    idt = P.op("dve", lambda e: e.tensor_tensor(ident[:], ia[:], ident[:],
               ALU.is_equal), deps=[i1, i2])
    ldw = [P.dma("pool", lambda e, dc=dc: e.dma_start(
        out=gw[:, dc * E0:(dc + 1) * E0],
        in_=gw0T_h[dc * 128:(dc + 1) * 128, :])) for dc in range(NDC)]

    prev = {}
    evcw_by_k = {}
    ev_by_k = {}
    for i in range(NTCH):
        j = i % 2
        pv = prev.get(j, {})
        lds, ldsh = [], []
        for dc in range(NDC):
            lds.append(P.dma("sync", lambda e, i=i, dc=dc, t=xt[j][dc]: e.dma_start(
                out=t[:], in_=xn0[i * TC:(i + 1) * TC, dc * 128:(dc + 1) * 128],
                transpose=True), deps=pv.get("xt_cons", [])))
            ldsh.append(P.dma("sync", lambda e, i=i, dc=dc, t=ht[j][dc]: e.dma_start(
                out=t[:], in_=h0[i * TC:(i + 1) * TC, dc * 128:(dc + 1) * 128],
                transpose=True), deps=pv.get("ht_cons", [])))
        stws = []
        for dc in range(NDC):
            stws.append(P.dma("pool", lambda e, i=i, dc=dc, t=xt[j][dc]: e.dma_start(
                out=xn0T[dc * 128:(dc + 1) * 128, i * TC:(i + 1) * TC], in_=t[:]),
                deps=[lds[dc]]))
            stws.append(P.dma("pool", lambda e, i=i, dc=dc, t=ht[j][dc]: e.dma_start(
                out=h0T[dc * 128:(dc + 1) * 128, i * TC:(i + 1) * TC], in_=t[:]),
                deps=[ldsh[dc]]))
        mmops, evops = [], []
        for tt in range(4):
            k = i * 4 + tt
            jj = k % 2

            def mmfn(e, jj=jj, j=j, tt=tt):
                last = None
                for dc in range(NDC):
                    last = e.matmul(plg[jj][:],
                                    xt[j][dc][:, tt * 128:(tt + 1) * 128],
                                    gw[:, dc * E0:(dc + 1) * E0],
                                    start=(dc == 0), stop=(dc == NDC - 1))
                return last
            mm = P.op("pe", mmfn,
                      deps=lds + ldw + [ev_by_k.get(k - 2), pv.get("ev%d" % tt)])
            ev = P.op("dve", lambda e, j=j, tt=tt, jj=jj: e.tensor_copy(
                lg[j][:, tt * E0:(tt + 1) * E0], plg[jj][:]),
                deps=[mm, pv.get("chain_last")])
            ev_by_k[k] = ev
            mmops.append(mm)
            evops.append(ev)
        lgv = lg[j][:].rearrange("p (s e) -> p s e", e=E0)
        sc2v = sc2[j][:].rearrange("p (s e) -> p s e", e=E0)
        tmpv = tmp[j][:].rearrange("p (s e) -> p s e", e=E0)
        mskv = msk[j][:].rearrange("p (s e) -> p s e", e=E0)
        cwtv = cwt[j][:].rearrange("p (s e) -> p s e", e=E0)
        r1b = r1[j][:].rearrange("p (s o) -> p s o", o=1).to_broadcast([128, 4, E0])
        r2b = r2[j][:].rearrange("p (s o) -> p s o", o=1).to_broadcast([128, 4, E0])
        c1 = P.op("dve", lambda e, lgv=lgv, r=r1[j]: e.tensor_reduce(
            r[:].rearrange("p (s o) -> p s o", o=1), lgv, axis=AX.X, op=ALU.max, negate=True),
            deps=evops + [pv.get("c_last")])
        c2 = P.op("dve", lambda e, sc2v=sc2v, lgv=lgv, r1b=r1b: e.tensor_tensor(
            sc2v, lgv, r1b, ALU.add), deps=[c1] + [pv.get("c2w")])
        c3 = P.op("act", lambda e, s=sc2[j]: e.activation(
            s[:], s[:], AF.Exp, bias=0.0, scale=1.0), deps=[c2])
        c4 = P.op("dve", lambda e, sc2v=sc2v, r=r1[j]: e.tensor_reduce(
            r[:].rearrange("p (s o) -> p s o", o=1), sc2v, axis=AX.X, op=ALU.add), deps=[c3])
        c5 = P.op("dve", lambda e, r=r1[j]: e.reciprocal(r[:], r[:]), deps=[c4])
        c6 = P.op("dve", lambda e, sc2v=sc2v, r1b=r1b: e.tensor_tensor(
            sc2v, sc2v, r1b, ALU.mult), deps=[c5])
        c7 = P.op("dve", lambda e, sc2v=sc2v, r=r1[j]: e.tensor_reduce(
            r[:].rearrange("p (s o) -> p s o", o=1), sc2v, axis=AX.X, op=ALU.max), deps=[c6])
        c8 = P.op("dve", lambda e, mskv=mskv, sc2v=sc2v, r1b=r1b: e.tensor_tensor(
            mskv, sc2v, r1b, ALU.is_equal), deps=[c7])
        c9 = P.op("dve", lambda e, tmpv=tmpv, sc2v=sc2v, mskv=mskv: e.tensor_tensor(
            tmpv, sc2v, mskv, ALU.mult), deps=[c8])
        c10 = P.op("dve", lambda e, tmpv=tmpv, sc2v=sc2v: e.tensor_tensor(
            tmpv, sc2v, tmpv, ALU.subtract), deps=[c9])
        c11 = P.op("dve", lambda e, tmpv=tmpv, r=r2[j]: e.tensor_reduce(
            r[:].rearrange("p (s o) -> p s o", o=1), tmpv, axis=AX.X, op=ALU.max), deps=[c10])
        c12 = P.op("dve", lambda e, cwtv=cwtv, tmpv=tmpv, r2b=r2b: e.tensor_tensor(
            cwtv, tmpv, r2b, ALU.is_equal), deps=[c11])
        c13 = P.op("dve", lambda e, mskv=mskv, cwtv=cwtv: e.tensor_tensor(
            mskv, mskv, cwtv, ALU.add), deps=[c12])
        c14 = P.op("dve", lambda e, r1=r1[j], r2=r2[j]: e.tensor_tensor(
            r1[:], r1[:], r2[:], ALU.add), deps=[c11, c8])
        c15 = P.op("dve", lambda e, r1=r1[j]: e.tensor_scalar(
            r1[:], r1[:], 1e-8, None, ALU.add), deps=[c14])
        c16 = P.op("dve", lambda e, r1=r1[j]: e.reciprocal(r1[:], r1[:]), deps=[c15])
        c17 = P.op("dve", lambda e, cwtv=cwtv, sc2v=sc2v, mskv=mskv: e.tensor_tensor(
            cwtv, sc2v, mskv, ALU.mult), deps=[c13, c12])
        c18 = P.op("dve", lambda e, cwtv=cwtv, r1b=r1b: e.tensor_tensor(
            cwtv, cwtv, r1b, ALU.mult), deps=[c16, c17])
        tpev = []
        for tt in range(4):
            k = i * 4 + tt
            jj = k % 2
            tpo = P.op("pe", lambda e, j=j, tt=tt, jj=jj: e.transpose(
                ptr[jj][:], cwt[j][:, tt * E0:(tt + 1) * E0], ident[:]),
                deps=[c18, idt, evcw_by_k.get(k - 2)])
            evc = P.op("dve", lambda e, i=i, tt=tt, jj=jj: e.tensor_copy(
                cwfm[:, i * TC + tt * 128: i * TC + (tt + 1) * 128], ptr[jj][:]),
                deps=[tpo])
            evcw_by_k[k] = evc
            tpev.append(evc)
        prev[j] = {
            "xt_cons": mmops + [stws[0]] + stws[::2],
            "ht_cons": stws[1::2],
            "ev0": evops[0], "ev1": evops[1], "ev2": evops[2], "ev3": evops[3],
            "chain_last": c18, "c_last": c18, "c2w": c18,
        }
    stcw = P.dma("pool", lambda e: e.dma_start(out=cw0[:], in_=cwfm[:]),
                 deps=list(evcw_by_k.values()))


def ph_b2(P, ctx):
    h0T = ctx.hbm["h0T"]
    wq_h, wk_h, wv_h, wo_h = (ctx.hbm["wq"], ctx.hbm["wk"], ctx.hbm["wv"],
                              ctx.hbm["wo"])
    theta_h, s0 = ctx.hbm["theta"], ctx.hbm["s0"]

    TOP = 2595.0 * np.log10(np.float32(1.0 + 4000.0 / 200.0))
    CS = float(0.125 / np.sqrt(2.0))
    BS = float(1.0 / np.sqrt(2.0))
    INV2PI = float(np.float32(1.0 / (2 * np.pi)))
    MAGIC = 12582912.0
    CW1 = 6.25
    CW2 = 0.032958984375
    CW3 = 0.000226322804586232
    MEL_C = float(np.log(10.0) / 2595.0 * (TOP / 31.0))
    cpre = [ctx.constant(v) for v in (MEL_C, CS, BS)]

    fcol = ctx.sbuf("b2_fcol", [32, 1], F32)
    thc = ctx.sbuf("b2_thc", [32, 1], F32)
    throw = ctx.sbuf("b2_throw", [1, 1], F32)
    ones32 = ctx.sbuf("b2_ones32", [1, 32], F32)
    ones64 = ctx.sbuf("b2_ones64", [1, 64], F32)
    onescol = ctx.sbuf("b2_onescol", [128, 1], BF16)
    C128 = ctx.sbuf("b2_C128", [128, T], F32)
    S128 = ctx.sbuf("b2_S128", [128, T], F32)
    o_all = ctx.sbuf("b2_oall", [128, NPAIR * T], BF16)
    v_all = ctx.sbuf("b2_vall", [128, 32 * 130], BF16)
    vv = v_all[:].rearrange("p (k c) -> p k c", c=130)
    qbf = ctx.sbuf("b2_qbf", [128, T], BF16)
    kbf = ctx.sbuf("b2_kbf", [128, T], BF16)
    qt32 = ctx.sbuf("b2_qt", [128, TC], F32)
    qsw = ctx.sbuf("b2_qsw", [128, TC], F32)
    qcs = ctx.sbuf("b2_qcs", [128, TC], F32)
    ttb2 = [ctx.sbuf(f"b2_tt{j}", [128, 2 * TC], BF16) for j in range(4)]
    svb = [ctx.sbuf(f"b2_sv{h}", [128, 1], F32) for h in range(2)]
    osb = [ctx.sbuf(f"b2_osb{j}", [128, TC], F32) for j in range(2)]
    drow = [ctx.sbuf(f"b2_drow{j}", [1, TC], F32) for j in range(2)]
    bcsb = [ctx.sbuf(f"b2_bc{j}", [64, TC], F32) for j in range(2)]
    _tabscope = ExitStack()
    ang = ctx.sbuf("b2_ang", [32, T], F32, scope=_tabscope)
    kk = ctx.sbuf("b2_kk", [32, T], F32, scope=_tabscope)
    red = ctx.sbuf("b2_red", [32, T], F32, scope=_tabscope)
    sin32 = ang
    cos32 = kk
    pth = ctx.psum("b2_pth", [32, 1], F32, scope=_tabscope)

    _f32 = np.float32
    _top = _f32(2595.0) * np.log10(_f32(21.0))
    _lin = np.linspace(_f32(0.0), _top, 32, dtype=_f32)
    _mel = np.power(_f32(10.0), _lin / _f32(2595.0), dtype=_f32) - _f32(1.0)
    FB = (_f32(200.0) * _mel / _f32(1000.0)).astype(_f32)
    fbrow = ctx.sbuf("b2_fbrow", [1, 32], F32, scope=_tabscope)
    one11 = ctx.sbuf("b2_one11", [1, 1], F32, scope=_tabscope)
    ldth = P.dma("sync", lambda e: e.dma_start(out=throw[:], in_=theta_h[:]))
    o32 = P.op("dve", lambda e: e.memset(ones32[:], 1.0))
    o64 = P.op("dve", lambda e: e.memset(ones64[:], 1.0))
    ocol = P.op("dve", lambda e: e.memset(onescol[:], 1.0))
    o11 = P.op("dve", lambda e: e.memset(one11[:], 1.0))
    fbsets = [P.op("dve", lambda e, i=i: e.memset(fbrow[:, i:i + 1],
                   float(FB[i]))) for i in range(32)]
    t220 = P.op("dve", lambda e: e.tensor_scalar(
        throw[:], throw[:], float(np.float32(1.0) / np.float32(220.0)),
        None, ALU.mult), deps=[ldth])
    mth = P.op("pe", lambda e: e.matmul(pth[:], ones32[:], throw[:],
               start=True, stop=True), deps=[t220, o32])
    evth = P.op("dve", lambda e: e.tensor_copy(thc[:], pth[:]), deps=[mth])
    mfb = P.op("pe", lambda e: e.transpose(pth[:], fbrow[:], one11[:]),
               deps=fbsets + [o11, evth])
    evfb = P.op("dve", lambda e: e.tensor_copy(fcol[:], pth[:]), deps=[mfb])
    frq = P.op("dve", lambda e: e.tensor_mul(fcol[:], fcol[:], thc[:]),
               deps=[evfb, evth])
    iot = P.op("pool", lambda e: e.iota(ang[:], pattern=[[1, T]], base=0,
               channel_multiplier=0, allow_small_or_imprecise_dtypes=True))
    a1 = P.op("dve", lambda e: e.tensor_scalar(ang[:], ang[:], fcol[:, 0:1],
              None, ALU.mult), deps=[iot, frq])
    k1 = P.op("dve", lambda e: e.tensor_scalar(kk[:], ang[:], INV2PI, MAGIC,
              ALU.mult, ALU.add), deps=[a1])
    k2 = P.op("dve", lambda e: e.tensor_scalar(kk[:], kk[:], MAGIC, None,
              ALU.subtract), deps=[k1])
    rr1 = P.op("dve", lambda e: e.scalar_tensor_tensor(red[:], kk[:], -CW1,
               ang[:], ALU.mult, ALU.add), deps=[k2])
    rr2 = P.op("dve", lambda e: e.scalar_tensor_tensor(red[:], kk[:], -CW2,
               red[:], ALU.mult, ALU.add), deps=[rr1])
    rr3 = P.op("dve", lambda e: e.scalar_tensor_tensor(red[:], kk[:], -CW3,
               red[:], ALU.mult, ALU.add), deps=[rr2])
    sn = P.op("act", lambda e: e.activation(sin32[:], red[:], AF.Sin,
              bias=0.0, scale=1.0), deps=[rr3])
    w1 = P.op("dve", lambda e: e.tensor_scalar(red[:], red[:],
              float(np.pi / 2), None, ALU.add), deps=[sn])
    w2 = P.op("dve", lambda e: e.tensor_scalar(kk[:], red[:], float(np.pi),
              None, ALU.is_gt), deps=[w1])
    w3 = P.op("dve", lambda e: e.scalar_tensor_tensor(red[:], kk[:],
              float(-2 * np.pi), red[:], ALU.mult, ALU.add), deps=[w2])
    cs_ = P.op("act", lambda e: e.activation(cos32[:], red[:], AF.Sin,
               bias=0.0, scale=1.0), deps=[w3])
    tabops = []
    for blk in range(4):
        sl = slice(blk * 32, (blk + 1) * 32)
        eng = "dve" if blk % 2 == 0 else "pool"
        tabops.append(P.op(eng, lambda e, sl=sl: e.tensor_copy(
            C128[sl, :], cos32[:]), deps=[cs_]))
        if blk % 2 == 0:
            tabops.append(P.op(eng, lambda e, sl=sl: e.tensor_scalar(
                S128[sl, :], sin32[:], -1.0, None, ALU.mult), deps=[sn, cs_]))
        else:
            tabops.append(P.op(eng, lambda e, sl=sl: e.tensor_copy(
                S128[sl, :], sin32[:]), deps=[sn, cs_]))


    mo1 = P.op("dve", lambda e: e.memset(vv[:, :, 64:65], 1.0))
    mo2 = P.op("dve", lambda e: e.memset(vv[:, :, 129:130], 1.0))
    _tabscope.close()
    _qkvscope = ExitStack()
    wqp = [ctx.sbuf(f"b2_wqp{j}", [128, D], BF16, scope=_qkvscope) for j in range(2)]
    wkp = [ctx.sbuf(f"b2_wkp{j}", [128, D], BF16, scope=_qkvscope) for j in range(2)]
    wvp = [ctx.sbuf(f"b2_wvp{j}", [128, D], BF16, scope=_qkvscope) for j in range(2)]
    hbuf = [ctx.sbuf(f"b2_h{j}", [128, NDC * TC], BF16, scope=_qkvscope)
            for j in range(2)]
    pqkv = ctx.psum("b2_pqkv", [128, 2 * TC], F32, scope=_qkvscope)
    ps2 = [ctx.psum(f"b2_ps{j}", [128, 2 * TC], F32, scope=_qkvscope)
           for j in range(2)]
    po = ctx.psum("b2_po", [65, TC], F32, scope=_qkvscope)
    pmix = ctx.psum("b2_pmix", [128, TC], F32, scope=_qkvscope)
    pa = pqkv[:, 0:TC]
    pb = pqkv[:, TC:2 * TC]
    pv = pqkv[:, 0:128]
    psv = pmix[0:65, 0:1]
    pbc = pmix[0:64, :]

    tabdep = tabops + [cs_, sn]

    def wsrc(w, p):
        return w[:, p * 128:(p + 1) * 128].rearrange("(c p) f -> p c f", p=128)

    w_cons = {("q", 0): [], ("q", 1): [], ("k", 0): [], ("k", 1): [],
              ("v", 0): [], ("v", 1): []}
    h_cons = {0: [], 1: []}
    ep_by_grp = {}
    av_by_grp = {}
    av_by_kt = {}
    last_qev = None
    last_kev = None
    last_vev = None
    last_sve = None
    last_smm = None
    last_oev = None
    pending = None
    pend_d5 = {}
    pend_d4 = [None]
    prev_d5 = pend_d5
    sv_ref = {}

    def emit_div_epilogue(st8):
        pp, hh, qq, jo, oev = st8
        svop = sv_ref[hh]
        d1 = P.op("dve", lambda e, jo=jo, hh=hh: e.tensor_scalar(
            drow[jo][:], osb[jo][64:65, :], svb[hh][64:65, 0:1], None,
            ALU.add), deps=[oev, svop, pend_d4[0]])
        d2 = P.op("dve", lambda e, jo=jo: e.reciprocal(drow[jo][:],
                  drow[jo][:]), deps=[d1])
        d3 = P.op("pe", lambda e, jo=jo: e.matmul(
            pbc, ones64[:], drow[jo][:], start=True, stop=True),
            deps=[d2, o64, pend_d4[0]])
        d4 = P.op("dve", lambda e, jo=jo: e.tensor_copy(bcsb[jo][:], pbc),
                  deps=[d3, pend_d5.get(jo)])
        d5 = P.op("dve", lambda e, jo=jo, hh=hh, pp=pp, qq=qq:
                  e.scalar_tensor_tensor(
                      o_all[hh * 64:(hh + 1) * 64,
                            pp * T + qq * TC: pp * T + (qq + 1) * TC],
                      osb[jo][0:64, :], svb[hh][0:64, 0:1], bcsb[jo][:],
                      ALU.add, ALU.mult), deps=[d4, oev, svop])
        pend_d4[0] = d4
        pend_d5[jo] = d5

    for p in range(NPAIR):
        jw = p % 2
        extra = tabdep if p < 2 else []
        lwq = P.dma("pool", lambda e, p=p, t=wqp[jw]: e.dma_start(
            out=t[:].rearrange("p (c f) -> p c f", f=128), in_=wsrc(wq_h, p)),
            deps=w_cons[("q", jw)] + extra)
        lwk = P.dma("pool", lambda e, p=p, t=wkp[jw]: e.dma_start(
            out=t[:].rearrange("p (c f) -> p c f", f=128), in_=wsrc(wk_h, p)),
            deps=w_cons[("k", jw)] + extra)
        lwv = P.dma("pool", lambda e, p=p, t=wvp[jw]: e.dma_start(
            out=t[:].rearrange("p (c f) -> p c f", f=128), in_=wsrc(wv_h, p)),
            deps=w_cons[("v", jw)] + extra)
        qmms, kmms, vmms, vevs, ropes = [], [], [], [], []
        for tc in range(NTCH):
            j = tc % 2
            hb = hbuf[j]
            ldh = P.dma("sync", lambda e, tc=tc, hb=hb: e.dma_start(
                out=hb[:].rearrange("p (c t) -> p c t", t=TC),
                in_=h0T[:, tc * TC:(tc + 1) * TC].rearrange(
                    "(c p) t -> p c t", p=128)),
                deps=h_cons[j] + (tabdep if p == 0 and tc < 2 else []))
            h_cons[j] = []

            def qfn(e, w=wqp[jw], hb=hb):
                last = None
                for dc in range(NDC):
                    last = e.matmul(pa, w[:, dc * 128:(dc + 1) * 128],
                                    hb[:, dc * TC:(dc + 1) * TC],
                                    start=(dc == 0), stop=(dc == NDC - 1))
                return last
            qmm = P.op("pe", qfn, deps=[ldh, lwq, last_qev, last_vev])
            qev = P.op("dve", lambda e: e.tensor_copy(qt32[:], pa),
                       deps=[qmm, ropes[-1] if ropes else None])
            sw_ops = []
            for blk in range(4):
                d_, s_ = slice(blk * 32, (blk + 1) * 32),                     slice((blk ^ 1) * 32, ((blk ^ 1) + 1) * 32)
                sw_ops.append(P.op("pool" if blk % 2 else "dve",
                              lambda e, d=d_, s=s_: e.tensor_copy(
                                  qsw[d, :], qt32[s, :]), deps=[qev]))
            qc1 = P.op("pool", lambda e, tc=tc: e.tensor_mul(
                qcs[:], qt32[:], C128[:, tc * TC:(tc + 1) * TC]),
                deps=[qev] + tabops)
            qc2 = P.op("dve", lambda e, tc=tc: e.tensor_mul(
                qsw[:], qsw[:], S128[:, tc * TC:(tc + 1) * TC]),
                deps=sw_ops + tabops)
            qc3 = P.op("dve", lambda e, tc=tc: e.tensor_add(
                qbf[:, tc * TC:(tc + 1) * TC], qcs[:], qsw[:]),
                deps=[qc1, qc2, last_smm])
            last_qev = qev

            def kfn(e, w=wkp[jw], hb=hb):
                last = None
                for dc in range(NDC):
                    last = e.matmul(pb, w[:, dc * 128:(dc + 1) * 128],
                                    hb[:, dc * TC:(dc + 1) * TC],
                                    start=(dc == 0), stop=(dc == NDC - 1))
                return last
            kmm = P.op("pe", kfn, deps=[ldh, lwk, last_kev])
            kev = P.op("dve", lambda e: e.tensor_copy(qt32[:], pb),
                       deps=[kmm, qc1, qc3])
            ksw_ops = []
            for blk in range(4):
                d_, s_ = slice(blk * 32, (blk + 1) * 32),                     slice((blk ^ 1) * 32, ((blk ^ 1) + 1) * 32)
                ksw_ops.append(P.op("pool" if blk % 2 else "dve",
                               lambda e, d=d_, s=s_: e.tensor_copy(
                                   qsw[d, :], qt32[s, :]), deps=[kev, qc2]))
            kc1 = P.op("pool", lambda e, tc=tc: e.tensor_mul(
                qcs[:], qt32[:], C128[:, tc * TC:(tc + 1) * TC]),
                deps=[kev, qc3])
            kc2 = P.op("dve", lambda e, tc=tc: e.tensor_mul(
                qsw[:], qsw[:], S128[:, tc * TC:(tc + 1) * TC]), deps=ksw_ops)
            kc3 = P.op("dve", lambda e, tc=tc: e.tensor_add(
                kbf[:, tc * TC:(tc + 1) * TC], qcs[:], qsw[:]),
                deps=[kc1, kc2, last_smm])
            last_kev = kev
            ropes.extend([qc3, kc3])
            qmms.append(qmm)
            kmms.append(kmm)

            for tt in range(4):
                kt = tc * 4 + tt
                def vfn(e, w=wvp[jw], hb=hb, tt=tt):
                    last = None
                    for dc in range(NDC):
                        last = e.matmul(
                            pv, hb[:, dc * TC + tt * 128:
                                      dc * TC + (tt + 1) * 128],
                            w[:, dc * 128:(dc + 1) * 128],
                            start=(dc == 0), stop=(dc == NDC - 1))
                    return last
                vmm = P.op("pe", vfn, deps=[ldh, lwv, last_vev, qev])
                vev = P.op("dve", lambda e, kt=kt: e.tensor_copy(
                    vv[:, kt, 0:130].rearrange("p (b c) -> p b c", b=2, c=65)
                    [:, :, 0:64],
                    pv.rearrange("p (b c) -> p b c", b=2, c=64)),
                    deps=[vmm] + av_by_kt.get(kt, []))
                av_by_kt[kt] = []
                vmms.append(vmm)
                vevs.append(vev)
                last_vev = vev
            h_cons[j] = [qmm, kmm] + vmms[-4:]
        if "dbg_q" in ctx.hbm and p == 0:
            P.dma("sync", lambda e: e.dma_start(out=ctx.hbm["dbg_C"][:],
                  in_=C128[:]), deps=tabdep)
            P.dma("sync", lambda e: e.dma_start(out=ctx.hbm["dbg_S"][:],
                  in_=S128[:]), deps=tabdep)
            P.dma("sync", lambda e: e.dma_start(out=ctx.hbm["dbg_q"][:],
                  in_=qbf[:]), deps=ropes)
            P.dma("sync", lambda e: e.dma_start(out=ctx.hbm["dbg_k"][:],
                  in_=kbf[:]), deps=ropes)
            P.dma("sync", lambda e: e.dma_start(out=ctx.hbm["dbg_v"][:],
                  in_=v_all[:]), deps=vevs)
        w_cons[("q", jw)] = [qmms[-1]]
        w_cons[("k", jw)] = [kmms[-1]]
        w_cons[("v", jw)] = [vmms[-1]]
        qkv_done_pair = ropes + vevs

        svops = []
        for h in range(2):
            def svfn(e, h=h):
                last = None
                for kt in range(32):
                    last = e.matmul(psv, vv[:, kt, h * 65:(h + 1) * 65],
                                    onescol[:], start=(kt == 0), stop=(kt == 31))
                return last
            svm = P.op("pe", svfn,
                       deps=vevs + [ocol, mo1, mo2, last_sve, pend_d4[0]])
            sve = P.op("dve", lambda e, h=h: e.tensor_scalar(
                svb[h][0:65, 0:1], psv, 0.5, None, ALU.mult),
                deps=[svm, pend_d5.get(0), pend_d5.get(1)])
            svops.append(sve)
            sv_ref[h] = sve
            last_sve = sve

        for hq in range(2 * NTCH):
            h, qc = hq // NTCH, hq % NTCH
            jo = hq % 2
            av_last = None

            def emit_av(g, h):
                nonlocal av_last
                jt = g % 4
                avs = []
                for half in range(2):
                    kt = 2 * g + half
                    avm = P.op("pe", lambda e, h=h, kt=kt, jt=jt, half=half:
                               e.matmul(
                        po[:], vv[:, kt, h * 65:(h + 1) * 65],
                        ttb2[jt][:, half * TC:(half + 1) * TC],
                        start=(kt == 0), stop=(kt == 31)),
                        deps=[ep_by_grp[g], vevs[-1]] +
                        ([last_oev] if kt == 0 else []))
                    avs.append(avm)
                    av_by_kt[kt] = [avm]
                av_by_grp[g] = avs
                av_last = avs[-1]

            for g in range(16):
                jg = g % 2
                smms = []
                for half in range(2):
                    kt = 2 * g + half
                    prev_ep = ep_by_grp.get(g - 2 if g >= 2 else g + 14)
                    smm = P.op("pe", lambda e, h=h, qc=qc, kt=kt, jg=jg,
                               half=half: e.matmul(
                        ps2[jg][:, half * TC:(half + 1) * TC],
                        kbf[h * 64:(h + 1) * 64, kt * 128:(kt + 1) * 128],
                        qbf[h * 64:(h + 1) * 64, qc * TC:(qc + 1) * TC],
                        start=True, stop=True),
                        deps=[ropes[-1], ropes[-2], prev_ep,
                              last_qev if hq == 0 and g == 0 else None,
                              last_kev if hq == 0 and g == 0 else None,
                              last_vev if hq == 0 and g == 0 else None])
                    smms.append(smm)
                    last_smm = smm
                ttw = av_by_grp.get(g - 4 if g >= 4 else g + 12, [])
                ep = P.op("act", lambda e, jg=jg, jt=g % 4: e.activation(
                    ttb2[jt][:], ps2[jg][:], AF.Square, bias=BS, scale=CS),
                    deps=smms + ttw + cpre)
                ep_by_grp[g] = ep
                if g >= 2:
                    emit_av(g - 2, h)
            emit_av(14, h)
            emit_av(15, h)
            oev = P.op("dve", lambda e, jo=jo: e.tensor_copy(
                osb[jo][0:65, :], po[0:65, :]),
                deps=[av_last, pend_d5.get(jo)])
            last_oev = oev
            if pending is not None:
                emit_div_epilogue(pending)
            pending = (p, h, qc, jo, oev)
        if pending is not None:
            emit_div_epilogue(pending)
            pending = None
    _qkvscope.close()
    wo_sb = ctx.sbuf("b2_wo", [128, NPAIR * D], BF16)
    evb = [ctx.sbuf(f"b2_ev{j}", [128, TC], BF16) for j in range(2)]
    pw = [ctx.psum(f"b2_pw{j}", [128, TC], F32) for j in range(2)]
    oall_done = [x for x in (pend_d5.get(0), pend_d5.get(1)) if x]
    if "dbg_o" in ctx.hbm:
        P.dma("sync", lambda e: e.dma_start(out=ctx.hbm["dbg_o"][:],
              in_=o_all[:]), deps=[d for d in oall_done if d is not None])
    lws = [P.dma("pool", lambda e, p=p: e.dma_start(
        out=wo_sb[:, p * D:(p + 1) * D],
        in_=wo_h[p * 128:(p + 1) * 128, :]), deps=oall_done)
        for p in range(NPAIR)]
    prev = {}
    k = 0
    for dc in range(NDC):
        for qc in range(NTCH):
            j = k % 2
            def mfn(e, dc=dc, qc=qc, j=j):
                last = None
                for p in range(NPAIR):
                    last = e.matmul(
                        pw[j][:],
                        wo_sb[:, p * D + dc * 128: p * D + (dc + 1) * 128],
                        o_all[:, p * T + qc * TC: p * T + (qc + 1) * TC],
                        start=(p == 0), stop=(p == NPAIR - 1))
                return last
            mm = P.op("pe", mfn, deps=lws + oall_done + [prev.get(j)])
            evo = P.op("dve", lambda e, j=j: e.tensor_copy(evb[j][:], pw[j][:]),
                       deps=[mm, prev.get(("st", j))])
            sto = P.dma("sync", lambda e, dc=dc, qc=qc, j=j: e.dma_start(
                out=s0[dc * 128:(dc + 1) * 128, qc * TC:(qc + 1) * TC],
                in_=evb[j][:]), deps=[evo])
            prev[j] = evo
            prev[("st", j)] = sto
            k += 1


def _y4view(y0, dcp, tcc):
    return (y0[:].rearrange("p (c t) -> p c t", t=1024)
            [:, 2 * dcp:2 * dcp + 2, tcc * TC: tcc * TC + TC])


def ph_b4(P, ctx):
    xn0T, cw0, s0, p0 = (ctx.hbm["xn0T"], ctx.hbm["cw0"], ctx.hbm["s0"],
                         ctx.hbm["p0"])
    gu0_h, dn0_h = ctx.hbm["gu0"], ctx.hbm["dn0"]
    NE = E0 + 1
    QW = 1024
    NQ = T // QW

    y0 = ctx.sbuf("b4_y0", [128, NDC * QW], F32)
    xnq = ctx.sbuf("b4_xnq", [128, NDC * QW], BF16)
    gu = [ctx.sbuf(f"b4_gu{j}", [128, NDC * 2 * I0P], BF16) for j in range(2)]
    dn = [ctx.sbuf(f"b4_dn{j}", [128, 8 * D], BF16) for j in range(2)]
    sg = [ctx.sbuf(f"b4_sg{j}", [128, TC], F32) for j in range(2)]
    hsb2 = [ctx.sbuf(f"b4_h{j}", [128, 8 * TC], BF16) for j in range(2)]
    cwbc = ctx.sbuf("b4_cwbc", [128, TC], BF16)
    cwrow = ctx.sbuf("b4_cwrow", [1, TC], F32)
    ones1 = ctx.sbuf("b4_ones1", [1, 128], F32)
    gsb = [ctx.sbuf(f"b4_g{j}", [128, TC], BF16) for j in range(2)]
    ssb = [ctx.sbuf(f"b4_s{j}", [128, TC], BF16) for j in range(2)]
    psb = [ctx.sbuf(f"b4_p{j}", [128, TC], BF16) for j in range(2)]
    pg = [ctx.psum(f"b4_pg{j}", [128, TC], F32) for j in range(2)]
    pu0 = ctx.psum("b4_pu", [128, TC], F32)
    pu = [pu0, pu0]
    py = [ctx.psum(f"b4_py{j}", [128, 2 * TC], F32) for j in range(2)]
    pbc = ctx.psum("b4_pbc", [128, TC], F32)

    o1 = P.op("dve", lambda e: e.memset(ones1[:], 1.0))
    gu_cons = {0: [], 1: []}
    dn_cons = {0: [], 1: []}
    xn_cons = []
    y_cons = []
    h_cons = {}
    sg_cons = {}
    cw_cons = []
    bc_cons = []
    pg_cons = {}
    pu_cons = {}
    py_cons = {}
    bc_readers = []
    ya_prev = {}
    prev_comb = {}
    ci = [0]
    pend_down = [None]
    ms_ref = [None]

    def make_down(en, q, tcc, jw, hsb, hops, par):
        ms_l = ms_ref[0]

        def emit():
            last_dm = None
            for dcp in range(NDC // 2):
                jj = dcp % 2

                def dfn(e, jw=jw, dcp=dcp, jj=jj, hsb=hsb):
                    last = None
                    for hf in range(2):
                        dc = 2 * dcp + hf
                        for ic in range(8):
                            last = e.matmul(
                                py[jj][:, hf * TC:(hf + 1) * TC],
                                dn[jw][:, ic * D + dc * 128:
                                       ic * D + (dc + 1) * 128],
                                hsb[:, ic * TC:(ic + 1) * TC],
                                start=(ic == 0), stop=(ic == 7))
                    return last
                dm = P.op("pe", dfn, deps=hops + py_cons.get(jj, []))
                ya = P.op("dve", lambda e, dcp=dcp, q=q, tcc=tcc, jj=jj:
                          e.tensor_add(
                              _y4view(y0, dcp, tcc),
                              _y4view(y0, dcp, tcc),
                              py[jj][:].rearrange("p (b t) -> p b t", t=TC)),
                          deps=[dm, ya_prev.get((dcp, tcc))] + ms_l)
                ya_prev[(dcp, tcc)] = ya
                py_cons[jj] = [ya]
                y_cons.append(ya)
                last_dm = dm
            for ic in range(8):
                h_cons[(par, ic)] = [last_dm]
            dn_cons[jw].append(y_cons[-1])
        return emit

    for q in range(NQ):
        ldx = P.dma("sync", lambda e, q=q: e.dma_start(
            out=xnq[:].rearrange("p (c t) -> p c t", t=QW),
            in_=xn0T[:, q * QW:(q + 1) * QW].rearrange(
                "(c p) t -> p c t", p=128)), deps=xn_cons)
        xn_cons = []
        ms = [P.op("dve", lambda e, dc=dc: e.memset(
            y0[:, dc * QW:(dc + 1) * QW], 0.0), deps=y_cons) for dc in range(NDC)]
        y_cons = []
        ms_ref[0] = ms
        for en in range(NE):
            jw = en % 2
            lg = P.dma("pool", lambda e, en=en, t=gu[jw]: e.dma_start(
                out=t[:].rearrange("p (c r) -> p c r", r=2 * I0P),
                in_=gu0_h[en].rearrange("(c p) r -> p c r", p=128)),
                deps=gu_cons[jw])
            ldn = P.dma("pool", lambda e, en=en, t=dn[jw]: e.dma_start(
                out=t[:].rearrange("p (c d) -> p c d", d=D),
                in_=dn0_h[en].rearrange("(c p) d -> p c d", p=128)),
                deps=dn_cons[jw])
            gu_cons[jw], dn_cons[jw] = [], []
            for tcc in range(QW // TC):
                ci[0] += 1
                hsb = hsb2[ci[0] % 2]
                col0 = q * QW + tcc * TC
                if en > 0:
                    lcw = P.dma("sync", lambda e, en=en, col0=col0: e.dma_start(
                        out=cwrow[:], in_=cw0[en - 1:en, col0:col0 + TC]),
                        deps=cw_cons)
                    mbc = P.op("pe", lambda e: e.matmul(
                        pbc[:], ones1[:], cwrow[:], start=True, stop=True),
                        deps=[lcw, o1] + bc_cons)
                    ebc = P.op("dve", lambda e: e.tensor_copy(cwbc[:], pbc[:]),
                               deps=[mbc] + bc_readers)
                    cw_cons = [mbc]
                    bc_cons = [ebc]
                    bc_readers = []
                hops = []
                for ic in range(8):
                    jj = ic % 2
                    def gfn(e, w=gu[jw], ic=ic, tcc=tcc, jj=jj):
                        last = None
                        for dc in range(NDC):
                            last = e.matmul(
                                pg[jj][:],
                                w[:, dc * 2 * I0P + (2 * ic) * 128:
                                  dc * 2 * I0P + (2 * ic + 1) * 128],
                                xnq[:, dc * QW + tcc * TC:
                                    dc * QW + tcc * TC + TC],
                                start=(dc == 0), stop=(dc == NDC - 1))
                        return last
                    def ufn(e, w=gu[jw], ic=ic, tcc=tcc, jj=jj):
                        last = None
                        for dc in range(NDC):
                            last = e.matmul(
                                pu[jj][:],
                                w[:, dc * 2 * I0P + (2 * ic + 1) * 128:
                                  dc * 2 * I0P + (2 * ic + 2) * 128],
                                xnq[:, dc * QW + tcc * TC:
                                    dc * QW + tcc * TC + TC],
                                start=(dc == 0), stop=(dc == NDC - 1))
                        return last
                    gm = P.op("pe", gfn, deps=[lg, ldx] + pg_cons.get(jj, []))
                    um = P.op("pe", ufn, deps=[lg, ldx] + pu_cons.get(jj, []))
                    sl = P.op("act", lambda e, jj=jj: e.activation(
                        sg[jj][:], pg[jj][:], AF.Silu, bias=0.0, scale=1.0),
                        deps=[gm] + sg_cons.get(jj, []))
                    pg_cons[jj] = [sl]
                    hm = P.op("dve", lambda e, jj=jj, ic=ic, hsb=hsb:
                              e.tensor_mul(
                        hsb[:, ic * TC:(ic + 1) * TC], sg[jj][:], pu[jj][:]),
                        deps=[sl, um] + h_cons.get((ci[0] % 2, ic), []))
                    pu_cons[0] = pu_cons[1] = [hm]
                    sg_cons[jj] = [hm]
                    if en > 0:
                        hc = P.op("pool", lambda e, ic=ic, hsb=hsb: e.tensor_mul(
                            hsb[:, ic * TC:(ic + 1) * TC],
                            hsb[:, ic * TC:(ic + 1) * TC], cwbc[:]),
                            deps=[hm] + bc_cons)
                        hops.append(hc)
                        bc_readers.append(hc)
                    else:
                        hops.append(hm)
                if pend_down[0] is not None:
                    pend_down[0]()
                pend_down[0] = make_down(en, q, tcc, jw, hsb, list(hops),
                                         ci[0] % 2)
                gu_cons[jw].append(hops[-1])
        if pend_down[0] is not None:
            pend_down[0]()
            pend_down[0] = None
        xn_cons = y_cons[-1:]
        if "dbg_y" in ctx.hbm and q == 0:
            P.dma("sync", lambda e: e.dma_start(
                out=ctx.hbm["dbg_y"][:], in_=y0[:]), deps=list(y_cons))
        for tcc in range(QW // TC):
            for dc in range(NDC):
                k = tcc * NDC + dc
                jj = k % 2
                col0 = q * QW + tcc * TC
                lds = P.dma("sync", lambda e, dc=dc, col0=col0, jj=jj:
                            e.dma_start(out=ssb[jj][:],
                                        in_=s0[dc * 128:(dc + 1) * 128,
                                               col0:col0 + TC]),
                            deps=[prev_comb.get(("p", jj))])
                gg = P.op("act", lambda e, dc=dc, tcc=tcc, jj=jj: e.activation(
                    gsb[jj][:], y0[:, dc * QW + tcc * TC: dc * QW + tcc * TC + TC],
                    AF.Silu, bias=0.0, scale=1.0),
                    deps=y_cons + [prev_comb.get(("g", jj))])
                pp = P.op("dve", lambda e, jj=jj: e.tensor_mul(
                    psb[jj][:], gsb[jj][:], ssb[jj][:]),
                    deps=[gg, lds, prev_comb.get(("st", jj))])
                st = P.dma("sync", lambda e, dc=dc, col0=col0, jj=jj:
                           e.dma_start(out=p0[dc * 128:(dc + 1) * 128,
                                              col0:col0 + TC], in_=psb[jj][:]),
                           deps=[pp])
                prev_comb[("g", jj)] = pp
                prev_comb[("p", jj)] = pp
                prev_comb[("st", jj)] = st
                y_cons.append(gg)


def ph_b5(P, ctx):
    x, p0, x1, xn1 = (ctx.hbm["x"], ctx.hbm["p0"], ctx.hbm["x1"],
                      ctx.hbm["xn1"])
    ceps = ctx.constant(LN_EPS)
    cm1 = ctx.constant(-1.0)
    cmh = ctx.constant(-0.5)
    xb = [ctx.sbuf(f"b5_x{j}", [128, D], F32) for j in range(2)]
    pb_ = [ctx.sbuf(f"b5_p{j}", [128, D], BF16) for j in range(2)]
    x1b = [ctx.sbuf(f"b5_x1{j}", [128, D], F32) for j in range(2)]
    sq = [ctx.sbuf(f"b5_sq{j}", [128, D], F32) for j in range(2)]
    xnb = [ctx.sbuf(f"b5_xn{j}", [128, D], BF16) for j in range(2)]
    st = [ctx.sbuf(f"b5_st{j}", [128, 8], F32) for j in range(2)]
    C = 1.0 / D
    prev = {}
    for i in range(NTT):
        j = i % 2
        pv = prev.get(j, {})
        ld = P.dma("sync", lambda e, i=i, t=xb[j]: e.dma_start(
            out=t[:], in_=x[i * 128:(i + 1) * 128, :]), deps=[pv.get("add")])
        ldp = P.dma("sync", lambda e, i=i, t=pb_[j]: e.dma_start(
            out=t[:], in_=p0[:, i * 128:(i + 1) * 128], transpose=True),
            deps=[pv.get("add")])
        add = P.op("dve", lambda e, j=j: e.tensor_add(
            x1b[j][:], xb[j][:], pb_[j][:]),
            deps=[ld, ldp, pv.get("stx1"), pv.get("xn"), pv.get("sq1"),
                  pv.get("sum")])
        stx1 = P.dma("pool", lambda e, i=i, j=j: e.dma_start(
            out=x1[i * 128:(i + 1) * 128, :], in_=x1b[j][:]), deps=[add])
        stb = st[j]
        sx, sx2 = stb[:, 0:1], stb[:, 1:2]
        mu, nv, rstd = stb[:, 2:3], stb[:, 3:4], stb[:, 4:5]
        summ = P.op("dve", lambda e, sx=sx, j=j: e.reduce_sum(
            sx, x1b[j][:], axis=AX.X), deps=[add])
        sq1 = P.op("act", lambda e, j=j, sx2=sx2: e.activation(
            sq[j][:], x1b[j][:], AF.Square, bias=0.0, scale=1.0,
            accum_out=sx2), deps=[add, pv.get("sq_w")])
        ex2 = P.op("dve", lambda e, sx2=sx2: e.tensor_scalar(
            sx2, sx2, C, None, ALU.mult), deps=[sq1])
        muo = P.op("dve", lambda e, mu=mu, sx=sx: e.tensor_scalar(
            mu, sx, C, None, ALU.mult), deps=[summ])
        nvo = P.op("dve", lambda e, nv=nv, mu=mu, sx2=sx2:
                   e.scalar_tensor_tensor(nv, mu, mu, sx2, ALU.mult,
                                          ALU.subtract), deps=[muo, ex2])
        lnv = P.op("act", lambda e, nv=nv: e.activation(
            nv, nv, AF.Ln, bias=LN_EPS, scale=-1.0), deps=[nvo, ceps, cm1])
        rst = P.op("act", lambda e, rstd=rstd, nv=nv: e.activation(
            rstd, nv, AF.Exp, bias=0.0, scale=-0.5), deps=[lnv, cmh])
        xn = P.op("dve", lambda e, j=j, mu=mu, rstd=rstd: e.tensor_scalar(
            xnb[j][:], x1b[j][:], mu, rstd, ALU.subtract, ALU.mult),
            deps=[muo, rst, pv.get("stxn")])
        stxn = P.dma("pool", lambda e, i=i, j=j: e.dma_start(
            out=xn1[i * 128:(i + 1) * 128, :], in_=xnb[j][:]), deps=[xn])
        prev[j] = dict(add=add, stx1=stx1, xn=xn, sq1=sq1, sum=summ,
                       sq_w=nvo, stxn=stxn)


def ph_b6(P, ctx):
    xn1, xn1T = ctx.hbm["xn1"], ctx.hbm["xn1T"]
    wr1_h, br1_h, g1 = ctx.hbm["wr1"], ctx.hbm["br1"], ctx.hbm["g1"]

    xt = [[ctx.sbuf(f"b6_xt{j}_{dc}", [128, TC], BF16) for dc in range(NDC)]
          for j in range(2)]
    wr = ctx.sbuf("b6_wr", [128, NDC * E1], BF16)
    brrow = ctx.sbuf("b6_br", [1, E1], F32)
    brbc = ctx.sbuf("b6_brbc", [128, E1], F32)
    ones1 = ctx.sbuf("b6_ones", [1, 128], F32)
    lg = [ctx.sbuf(f"b6_lg{j}", [128, 4 * E1], F32) for j in range(2)]
    p2b = [ctx.sbuf(f"b6_p2{j}", [128, 4 * E1], F32) for j in range(2)]
    tmp = [ctx.sbuf(f"b6_tmp{j}", [128, 4 * E1], F32) for j in range(2)]
    msk = [ctx.sbuf(f"b6_msk{j}", [128, 4 * E1], F32) for j in range(2)]
    gt = [ctx.sbuf(f"b6_gt{j}", [128, 4 * E1], F32) for j in range(2)]
    r1 = [ctx.sbuf(f"b6_r1{j}", [128, 4], F32) for j in range(2)]
    gfm = ctx.sbuf("b6_gfm", [E1, T], F32)
    ident = ctx.sbuf("b6_id", [128, 128], F32)
    ia = ctx.sbuf("b6_ia", [128, 128], F32)
    plg = [ctx.psum(f"b6_plg{j}", [128, E1], F32) for j in range(2)]
    ptr = [ctx.psum(f"b6_ptr{j}", [E1, 128], F32) for j in range(2)]
    pbc = ctx.psum("b6_pbc", [128, E1], F32)

    i1 = P.op("pool", lambda e: e.iota(ia[:], pattern=[[0, 128]], base=0,
              channel_multiplier=1, allow_small_or_imprecise_dtypes=True))
    i2 = P.op("pool", lambda e: e.iota(ident[:], pattern=[[1, 128]], base=0,
              channel_multiplier=0, allow_small_or_imprecise_dtypes=True))
    idt = P.op("dve", lambda e: e.tensor_tensor(ident[:], ia[:], ident[:],
               ALU.is_equal), deps=[i1, i2])
    ldw = [P.dma("pool", lambda e, dc=dc: e.dma_start(
        out=wr[:, dc * E1:(dc + 1) * E1],
        in_=wr1_h[dc * 128:(dc + 1) * 128, :])) for dc in range(NDC)]
    ldb = P.dma("sync", lambda e: e.dma_start(out=brrow[:], in_=br1_h[:]))
    o1 = P.op("dve", lambda e: e.memset(ones1[:], 1.0))
    mbc = P.op("pe", lambda e: e.matmul(pbc[:], ones1[:], brrow[:],
               start=True, stop=True), deps=[ldb, o1])
    ebc = P.op("dve", lambda e: e.tensor_copy(brbc[:], pbc[:]), deps=[mbc])

    prev = {}
    ev_by_k = {}
    evg_by_k = {}
    for i in range(NTCH):
        j = i % 2
        pv = prev.get(j, {})
        lds = []
        for dc in range(NDC):
            lds.append(P.dma("sync", lambda e, i=i, dc=dc, t=xt[j][dc]:
                       e.dma_start(out=t[:],
                                   in_=xn1[i * TC:(i + 1) * TC,
                                           dc * 128:(dc + 1) * 128],
                                   transpose=True), deps=pv.get("xt_cons", [])))
        stws = [P.dma("pool", lambda e, i=i, dc=dc, t=xt[j][dc]: e.dma_start(
            out=xn1T[dc * 128:(dc + 1) * 128, i * TC:(i + 1) * TC], in_=t[:]),
            deps=[lds[dc]]) for dc in range(NDC)]
        mmops, evops = [], []
        for tt in range(4):
            k = i * 4 + tt
            jj = k % 2

            def mmfn(e, j=j, tt=tt, jj=jj):
                last = None
                for dc in range(NDC):
                    last = e.matmul(plg[jj][:],
                                    xt[j][dc][:, tt * 128:(tt + 1) * 128],
                                    wr[:, dc * E1:(dc + 1) * E1],
                                    start=(dc == 0), stop=(dc == NDC - 1))
                return last
            mm = P.op("pe", mmfn, deps=lds + ldw + [ev_by_k.get(k - 2)])
            ev = P.op("dve", lambda e, j=j, tt=tt, jj=jj: e.tensor_add(
                lg[j][:, tt * E1:(tt + 1) * E1], plg[jj][:], brbc[:]),
                deps=[mm, ebc, pv.get("chain_last")])
            ev_by_k[k] = ev
            mmops.append(mm)
            evops.append(ev)
        lgv = lg[j][:].rearrange("p (s e) -> p s e", e=E1)
        p2v = p2b[j][:].rearrange("p (s e) -> p s e", e=E1)
        tmpv = tmp[j][:].rearrange("p (s e) -> p s e", e=E1)
        mskv = msk[j][:].rearrange("p (s e) -> p s e", e=E1)
        gtv = gt[j][:].rearrange("p (s e) -> p s e", e=E1)
        r1v = r1[j][:].rearrange("p (s o) -> p s o", o=1)
        r1b = r1[j][:].rearrange("p (s o) -> p s o", o=1).to_broadcast(
            [128, 4, E1])
        c1 = P.op("dve", lambda e, lgv=lgv, r1v=r1v: e.tensor_reduce(
            r1v, lgv, axis=AX.X, op=ALU.max, negate=True),
            deps=evops + [pv.get("c_last")])
        c2 = P.op("dve", lambda e, p2v=p2v, lgv=lgv, r1b=r1b: e.tensor_tensor(
            p2v, lgv, r1b, ALU.add), deps=[c1, pv.get("c_last")])
        c3 = P.op("act", lambda e, s=p2b[j]: e.activation(
            s[:], s[:], AF.Exp, bias=0.0, scale=1.0), deps=[c2])
        c4 = P.op("dve", lambda e, p2v=p2v, r1v=r1v: e.tensor_reduce(
            r1v, p2v, axis=AX.X, op=ALU.add), deps=[c3])
        c5 = P.op("dve", lambda e, r=r1[j]: e.reciprocal(r[:], r[:]), deps=[c4])
        c6 = P.op("dve", lambda e, p2v=p2v, r1b=r1b: e.tensor_tensor(
            p2v, p2v, r1b, ALU.mult), deps=[c5])
        c7 = P.op("dve", lambda e, p2v=p2v, r1v=r1v: e.tensor_reduce(
            r1v, p2v, axis=AX.X, op=ALU.max), deps=[c6])
        c8 = P.op("dve", lambda e, mskv=mskv, p2v=p2v, r1b=r1b: e.tensor_tensor(
            mskv, p2v, r1b, ALU.is_equal), deps=[c7])
        c9 = P.op("dve", lambda e, tmpv=tmpv, p2v=p2v, mskv=mskv:
                  e.tensor_tensor(tmpv, p2v, mskv, ALU.mult), deps=[c8])
        c10 = P.op("dve", lambda e, tmpv=tmpv, p2v=p2v: e.tensor_tensor(
            tmpv, p2v, tmpv, ALU.subtract), deps=[c9])
        c11 = P.op("dve", lambda e, tmpv=tmpv, r1v=r1v: e.tensor_reduce(
            r1v, tmpv, axis=AX.X, op=ALU.max), deps=[c10])
        c12 = P.op("dve", lambda e, gtv=gtv, tmpv=tmpv, r1b=r1b:
                   e.tensor_tensor(gtv, tmpv, r1b, ALU.is_equal), deps=[c11])
        c13 = P.op("dve", lambda e, mskv=mskv, gtv=gtv: e.tensor_tensor(
            mskv, mskv, gtv, ALU.add), deps=[c12])
        c14 = P.op("dve", lambda e, tmpv=tmpv, gtv=gtv: e.tensor_tensor(
            gtv, tmpv, gtv, ALU.mult), deps=[c13])
        c15 = P.op("dve", lambda e, tmpv=tmpv, gtv=gtv: e.tensor_tensor(
            tmpv, tmpv, gtv, ALU.subtract), deps=[c14])
        c16 = P.op("dve", lambda e, tmpv=tmpv, r1v=r1v: e.tensor_reduce(
            r1v, tmpv, axis=AX.X, op=ALU.max), deps=[c15])
        c17 = P.op("dve", lambda e, gtv=gtv, tmpv=tmpv, r1b=r1b:
                   e.tensor_tensor(gtv, tmpv, r1b, ALU.is_equal), deps=[c16])
        c18 = P.op("dve", lambda e, mskv=mskv, gtv=gtv: e.tensor_tensor(
            mskv, mskv, gtv, ALU.add), deps=[c17])
        c19 = P.op("dve", lambda e, gtv=gtv, p2v=p2v, mskv=mskv:
                   e.tensor_tensor(gtv, p2v, mskv, ALU.mult), deps=[c18])
        tpev = []
        for tt in range(4):
            k = i * 4 + tt
            jj = k % 2
            tpo = P.op("pe", lambda e, j=j, tt=tt, jj=jj: e.transpose(
                ptr[jj][:], gt[j][:, tt * E1:(tt + 1) * E1], ident[:]),
                deps=[c19, idt, evg_by_k.get(k - 2)])
            evc = P.op("dve", lambda e, i=i, tt=tt, jj=jj: e.tensor_copy(
                gfm[:, i * TC + tt * 128: i * TC + (tt + 1) * 128], ptr[jj][:]),
                deps=[tpo])
            evg_by_k[k] = evc
            tpev.append(evc)
        prev[j] = {"xt_cons": mmops + stws, "chain_last": c19, "c_last": c19}
    P.dma("pool", lambda e: e.dma_start(out=g1[:], in_=gfm[:]),
          deps=list(evg_by_k.values()))


def ph_b7(P, ctx):
    xn1T = ctx.hbm["xn1T"]
    wq1_h, wk1_h, wv1_h = ctx.hbm["wq1"], ctx.hbm["wk1"], ctx.hbm["wv1"]
    bq1_h, bk1_h, bv1_h = ctx.hbm["bq1"], ctx.hbm["bk1"], ctx.hbm["bv1"]
    q1, attn1, s1 = ctx.hbm["q1"], ctx.hbm["attn1"], ctx.hbm["s1"]

    v_tm = ctx.sbuf("b7_vtm", [128, 32 * D], BF16)
    bq = ctx.sbuf("b7_bq", [128, NDC], F32)
    bk = ctx.sbuf("b7_bk", [128, NDC], F32)
    bvrow = ctx.sbuf("b7_bvrow", [1, D], F32)
    bvbc = ctx.sbuf("b7_bvbc", [128, D], F32)
    ones1 = ctx.sbuf("b7_ones", [1, 128], F32)
    _sc2 = ExitStack()
    k_fm = ctx.sbuf("b7_kfm", [128, NDC * T], BF16, scope=_sc2)
    pq = [ctx.psum(f"b7_pq{j}", [128, TC], F32, scope=_sc2) for j in range(2)]
    pv_ = [ctx.psum(f"b7_pv{j}", [128, TC], F32, scope=_sc2) for j in range(2)]
    pbc = ctx.psum("b7_pbc", [128, TC], F32, scope=_sc2)
    _sc3 = ExitStack()
    wq_sb = ctx.sbuf("b7_wq", [128, NDC * D], BF16, scope=_sc3)
    wk_sb = ctx.sbuf("b7_wk", [128, NDC * D], BF16, scope=_sc3)
    wv_sb = ctx.sbuf("b7_wv", [128, NDC * D], BF16, scope=_sc3)
    xnb = [ctx.sbuf(f"b7_xn{j}", [128, NDC * TC], BF16, scope=_sc3)
           for j in range(2)]
    qev = [ctx.sbuf(f"b7_qe{j}", [128, TC], BF16, scope=_sc3) for j in range(2)]

    ldbq = P.dma("sync", lambda e: e.dma_start(out=bq[:], in_=bq1_h[:]))
    ldbk = P.dma("sync", lambda e: e.dma_start(out=bk[:], in_=bk1_h[:]))
    ldbv = P.dma("sync", lambda e: e.dma_start(out=bvrow[:], in_=bv1_h[:]))
    o1 = P.op("dve", lambda e: e.memset(ones1[:], 1.0))
    mb1 = P.op("pe", lambda e: e.matmul(pbc[:], ones1[:], bvrow[:, 0:TC],
               start=True, stop=True), deps=[ldbv, o1])
    eb1 = P.op("dve", lambda e: e.tensor_copy(bvbc[:, 0:TC], pbc[:]),
               deps=[mb1])
    mb2 = P.op("pe", lambda e: e.matmul(pbc[:], ones1[:], bvrow[:, TC:D],
               start=True, stop=True), deps=[eb1])
    eb2 = P.op("dve", lambda e: e.tensor_copy(bvbc[:, TC:D], pbc[:]),
               deps=[mb2])
    lws = []
    for dc in range(NDC):
        lws.append(P.dma("pool", lambda e, dc=dc: e.dma_start(
            out=wq_sb[:, dc * D:(dc + 1) * D],
            in_=wq1_h[dc * 128:(dc + 1) * 128, :])))
        lws.append(P.dma("pool", lambda e, dc=dc: e.dma_start(
            out=wk_sb[:, dc * D:(dc + 1) * D],
            in_=wk1_h[dc * 128:(dc + 1) * 128, :])))
        lws.append(P.dma("pool", lambda e, dc=dc: e.dma_start(
            out=wv_sb[:, dc * D:(dc + 1) * D],
            in_=wv1_h[dc * 128:(dc + 1) * 128, :])))

    xn_cons = {0: [], 1: []}
    pq_cons = {0: [], 1: []}
    pv_cons = {0: [], 1: []}
    qst_prev = {0: None, 1: None}
    last_qkv = []
    for tc in range(NTCH):
        j = tc % 2
        ldx = P.dma("sync", lambda e, tc=tc, t=xnb[j]: e.dma_start(
            out=t[:].rearrange("p (c t) -> p c t", t=TC),
            in_=xn1T[:, tc * TC:(tc + 1) * TC].rearrange(
                "(c p) t -> p c t", p=128)), deps=xn_cons[j])
        xn_cons[j] = []
        for fc in range(NDC):
            kq = (tc * NDC + fc)
            jj = kq % 2
            def qfn(e, j=j, fc=fc, jj=jj):
                last = None
                for dc in range(NDC):
                    last = e.matmul(pq[jj][:],
                                    wq_sb[:, dc * D + fc * 128:
                                          dc * D + (fc + 1) * 128],
                                    xnb[j][:, dc * TC:(dc + 1) * TC],
                                    start=(dc == 0), stop=(dc == NDC - 1))
                return last
            qm = P.op("pe", qfn, deps=[ldx] + lws + pq_cons[jj])
            qe = P.op("dve", lambda e, fc=fc, jj=jj: e.tensor_scalar(
                qev[jj][:], pq[jj][:], bq[:, fc:fc + 1], None, ALU.add),
                deps=[qm, qst_prev[jj]])
            pq_cons[jj] = [qe]
            qs = P.dma("sync", lambda e, fc=fc, tc=tc, jj=jj: e.dma_start(
                out=q1[fc * 128:(fc + 1) * 128, tc * TC:(tc + 1) * TC],
                in_=qev[jj][:]), deps=[qe])
            qst_prev[jj] = qs
            xn_cons[j].append(qm)
            def kfn(e, j=j, fc=fc, jj=jj):
                last = None
                for dc in range(NDC):
                    last = e.matmul(pv_[jj][:],
                                    wk_sb[:, dc * D + fc * 128:
                                          dc * D + (fc + 1) * 128],
                                    xnb[j][:, dc * TC:(dc + 1) * TC],
                                    start=(dc == 0), stop=(dc == NDC - 1))
                return last
            km = P.op("pe", kfn, deps=[ldx] + lws + pv_cons[jj])
            ke = P.op("dve", lambda e, fc=fc, tc=tc, jj=jj: e.tensor_scalar(
                k_fm[:, fc * T + tc * TC: fc * T + tc * TC + TC], pv_[jj][:],
                bk[:, fc:fc + 1], None, ALU.add), deps=[km, ldbk])
            pv_cons[jj] = [ke]
            xn_cons[j].append(km)
            last_qkv = [qs, ke]
        for tt in range(4):
            kt = tc * 4 + tt
            for fh in range(2):
                jj = (kt * 2 + fh) % 2
                def vfn(e, j=j, tt=tt, fh=fh, jj=jj):
                    last = None
                    for dc in range(NDC):
                        last = e.matmul(
                            pv_[jj][:],
                            xnb[j][:, dc * TC + tt * 128:
                                dc * TC + (tt + 1) * 128],
                            wv_sb[:, dc * D + fh * TC: dc * D + (fh + 1) * TC],
                            start=(dc == 0), stop=(dc == NDC - 1))
                    return last
                vm = P.op("pe", vfn, deps=[ldx] + lws + pv_cons[jj])
                ve = P.op("dve", lambda e, kt=kt, fh=fh, jj=jj: e.tensor_add(
                    v_tm[:, kt * D + fh * TC: kt * D + (fh + 1) * TC],
                    pv_[jj][:], bvbc[:, fh * TC:(fh + 1) * TC]),
                    deps=[vm, eb1, eb2])
                pv_cons[jj] = [ve]
                xn_cons[j].append(vm)
                last_qkv.append(ve)

    _sc3.close()
    srow = ctx.sbuf("b7_srow", [128, T], F32, scope=_sc2)
    zap = ctx.sbuf("b7_zap", [128, T], F32, scope=_sc2)
    mx8 = ctx.sbuf("b7_mx8", [128, 8], F32, scope=_sc2)
    dcol = ctx.sbuf("b7_dcol", [128, 1], F32, scope=_sc2)
    asp = ctx.sbuf("b7_asp", [128, T], BF16, scope=_sc2)
    qt = [ctx.sbuf(f"b7_qt{j}", [128, NDC * 128], BF16, scope=_sc2)
          for j in range(2)]
    ep_prev = {0: None, 1: None}
    att_sts = []
    sc_prev = None
    mr_prev = None
    for qt_i in range(NTT):
        j = qt_i % 2
        ldq = P.dma("sync", lambda e, qt_i=qt_i, t=qt[j]: e.dma_start(
            out=t[:].rearrange("p (c f) -> p c f", f=128),
            in_=q1[:, qt_i * 128:(qt_i + 1) * 128].rearrange(
                "(c p) f -> p c f", p=128)),
            deps=last_qkv + (ep_prev[j] or []))
        evs = []
        for kc in range(NTCH):
            jj = kc % 2
            def sfn(e, j=j, kc=kc, jj=jj):
                last = None
                for ec in range(NDC):
                    last = e.matmul(pq[jj][:],
                                    qt[j][:, ec * 128:(ec + 1) * 128],
                                    k_fm[:, ec * T + kc * TC:
                                         ec * T + kc * TC + TC],
                                    start=(ec == 0), stop=(ec == NDC - 1))
                return last
            sm = P.op("pe", sfn, deps=[ldq] + last_qkv + pq_cons[jj])
            se = P.op("dve", lambda e, kc=kc, jj=jj: e.tensor_copy(
                srow[:, kc * TC:(kc + 1) * TC], pq[jj][:]),
                deps=[sm, sc_prev])
            pq_cons[jj] = [se]
            evs.append(se)
        cz = P.op("dve", lambda e: e.tensor_copy(zap[:], srow[:]),
                  deps=evs + [mr_prev])
        mrs = cz
        for r in range(4):
            m8 = P.op("dve", lambda e: e.max(out=mx8[:], in_=zap[:]),
                      deps=[mrs])
            mrs = P.op("dve", lambda e: e.match_replace(
                out=zap[:], in_to_replace=mx8[:], in_values=zap[:],
                imm_value=-1e30), deps=[m8])
        mk = P.op("dve", lambda e: e.tensor_tensor(zap[:], zap[:], srow[:],
                  ALU.not_equal), deps=[mrs])
        sp = P.op("dve", lambda e: e.scalar_tensor_tensor(
            asp[:], srow[:], 1.0, zap[:], ALU.mult, ALU.mult,
            accum_out=dcol[:]), deps=[mk, att_sts[-1] if att_sts else None])
        dn1 = P.op("dve", lambda e: e.tensor_scalar(
            dcol[:], dcol[:], 1e-10, None, ALU.add), deps=[sp])
        dn2 = P.op("dve", lambda e: e.reciprocal(dcol[:], dcol[:]), deps=[dn1])
        nrm = P.op("dve", lambda e: e.tensor_scalar(
            asp[:], asp[:], dcol[:, 0:1], None, ALU.mult), deps=[dn2])
        sc_prev = nrm
        mr_prev = sp
        ast = P.dma("sync", lambda e, qt_i=qt_i: e.dma_start(
            out=attn1[qt_i * 128:(qt_i + 1) * 128, :], in_=asp[:]),
            deps=[nrm])
        att_sts.append(ast)
        ep_prev[j] = [sm]

    _sc2.close()
    at = ctx.sbuf("b7_at", [128, 32 * TC], BF16)
    ev1 = [ctx.sbuf(f"b7_ev1{j}", [128, TC], BF16) for j in range(2)]
    pa = [ctx.psum(f"b7_pa{j}", [128, TC], F32) for j in range(2)]
    at_cons = []
    prev_st = {0: None, 1: None}
    prev_ev = {0: None, 1: None}
    for qc in range(NTCH):
        lats = [P.dma("sync", lambda e, qc=qc, kt=kt: e.dma_start(
            out=at[:, kt * TC:(kt + 1) * TC],
            in_=attn1[qc * TC:(qc + 1) * TC, kt * 128:(kt + 1) * 128],
            transpose=True), deps=att_sts + at_cons) for kt in range(32)]
        at_cons = []
        for vc in range(NDC):
            jj = vc % 2
            def avfn(e, qc=qc, vc=vc, jj=jj):
                last = None
                for kt in range(32):
                    last = e.matmul(pa[jj][:],
                                    v_tm[:, kt * D + vc * 128:
                                         kt * D + (vc + 1) * 128],
                                    at[:, kt * TC:(kt + 1) * TC],
                                    start=(kt == 0), stop=(kt == 31))
                return last
            av = P.op("pe", avfn, deps=lats + last_qkv + [prev_ev[jj]])
            eo = P.op("dve", lambda e, jj=jj: e.tensor_copy(ev1[jj][:],
                      pa[jj][:]), deps=[av, prev_st[jj]])
            st = P.dma("sync", lambda e, qc=qc, vc=vc, jj=jj: e.dma_start(
                out=s1[vc * 128:(vc + 1) * 128, qc * TC:(qc + 1) * TC],
                in_=ev1[jj][:]), deps=[eo])
            prev_ev[jj] = eo
            prev_st[jj] = st
            at_cons.append(av)


def _yview(y1, dcp, tcc):
    return (y1[:].rearrange("p (c t) -> p c t", t=2048)
            [:, 2 * dcp:2 * dcp + 2, tcc * TC: tcc * TC + TC])


def ph_b8(P, ctx):
    xn1T, g1, s1, p1 = (ctx.hbm["xn1T"], ctx.hbm["g1"], ctx.hbm["s1"],
                        ctx.hbm["p1"])
    w1f_h, w2f_h = ctx.hbm["w1f"], ctx.hbm["w2f"]
    b1f_h, bs2_h, be2_h = ctx.hbm["b1f"], ctx.hbm["bs2"], ctx.hbm["be2"]
    NE = 32
    HW = 2048
    NHALF = T // HW

    y1 = ctx.sbuf("b8_y1", [128, NDC * HW], F32)
    xnh = ctx.sbuf("b8_xnh", [128, NDC * HW], BF16)
    w1 = [ctx.sbuf(f"b8_w1{j}", [128, NDC * D], BF16) for j in range(2)]
    w2 = [ctx.sbuf(f"b8_w2{j}", [128, 8 * D], BF16) for j in range(2)]
    b1 = ctx.sbuf("b8_b1", [128, NE * NDC], F32)
    bs2 = ctx.sbuf("b8_bs2", [128, NDC], F32)
    be2 = ctx.sbuf("b8_be2", [E1, NDC * 128], BF16)
    g1b = ctx.sbuf("b8_g1b", [E1, HW], BF16)
    grow = ctx.sbuf("b8_grow", [1, TC], F32)
    gbc = ctx.sbuf("b8_gbc", [128, TC], BF16)
    ones1 = ctx.sbuf("b8_ones1", [1, 128], F32)
    hsb2 = [ctx.sbuf(f"b8_h{j}", [128, 8 * TC], BF16) for j in range(2)]
    tsc = [ctx.sbuf(f"b8_t{j}", [128, TC], F32) for j in range(2)]
    gsb = [ctx.sbuf(f"b8_g{j}", [128, TC], BF16) for j in range(2)]
    ssb = [ctx.sbuf(f"b8_s{j}", [128, TC], BF16) for j in range(2)]
    psb = [ctx.sbuf(f"b8_p{j}", [128, TC], BF16) for j in range(2)]
    ph_ = [ctx.psum(f"b8_ph{j}", [128, TC], F32) for j in range(2)]
    py = [ctx.psum(f"b8_py{j}", [128, 2 * TC], F32) for j in range(2)]
    pbc = ctx.psum("b8_pbc", [128, TC], F32)

    o1 = P.op("dve", lambda e: e.memset(ones1[:], 1.0))
    ldb1 = P.dma("sync", lambda e: e.dma_start(
        out=b1[:], in_=b1f_h[:].rearrange("p e c -> p (e c)")))
    ldbs2 = P.dma("sync", lambda e: e.dma_start(out=bs2[:], in_=bs2_h[:]))
    ldbe2 = P.dma("pool", lambda e: e.dma_start(
        out=be2[:], in_=be2_h[:]))
    w1_cons = {0: [], 1: []}
    w2_cons = {0: [], 1: []}
    xn_cons = []
    y_cons = []
    h_cons = {}
    ts_cons = {}
    gb_cons = []
    gr_cons = []
    ph_cons = {}
    py_cons = {}
    ya_prev = {}
    prev_comb = {}
    g1_cons = []
    ci = [0]
    pend_down = [None]
    ldg_ref = [None]
    ms_ref = [None]

    def make_down(en, tcc, jw, hsb, hops, par):
        ldg_l, ms_l = ldg_ref[0], ms_ref[0]

        def emit():
            last_dm = None
            for dcp in range(NDC // 2):
                jj = dcp % 2

                def dfn(e, jw=jw, dcp=dcp, en=en, tcc=tcc, jj=jj, hsb=hsb):
                    last = None
                    for hf in range(2):
                        dc = 2 * dcp + hf
                        for ic in range(8):
                            stop = (ic == 7) and (en != 0)
                            last = e.matmul(
                                py[jj][:, hf * TC:(hf + 1) * TC],
                                w2[jw][:, ic * D + dc * 128:
                                       ic * D + (dc + 1) * 128],
                                hsb[:, ic * TC:(ic + 1) * TC],
                                start=(ic == 0), stop=stop)
                        if en == 0:
                            last = e.matmul(
                                py[jj][:, hf * TC:(hf + 1) * TC],
                                be2[:, dc * 128:(dc + 1) * 128],
                                g1b[:, tcc * TC:(tcc + 1) * TC],
                                start=False, stop=True)
                    return last
                dm = P.op("pe", dfn,
                          deps=hops + py_cons.get(jj, []) +
                          ([ldbe2, ldg_l] if en == 0 else []))
                ya = P.op("dve", lambda e, dcp=dcp, tcc=tcc, jj=jj:
                          e.tensor_add(
                              _yview(y1, dcp, tcc),
                              _yview(y1, dcp, tcc),
                              py[jj][:].rearrange("p (b t) -> p b t", t=TC)),
                          deps=[dm, ya_prev.get((dcp, tcc))] + ms_l)
                ya_prev[(dcp, tcc)] = ya
                py_cons[jj] = [ya]
                y_cons.append(ya)
                last_dm = dm
            for ic in range(8):
                h_cons[(par, ic)] = [last_dm]
            w2_cons[jw].append(y_cons[-1])
        return emit

    for half in range(NHALF):
        col0h = half * HW
        ldx = P.dma("sync", lambda e, col0h=col0h: e.dma_start(
            out=xnh[:].rearrange("p (c t) -> p c t", t=HW),
            in_=xn1T[:, col0h:col0h + HW].rearrange("(c p) t -> p c t", p=128)),
            deps=xn_cons)
        xn_cons = []
        ldg = P.dma("pool", lambda e, col0h=col0h: e.dma_start(
            out=g1b[:], in_=g1[:, col0h:col0h + HW]), deps=g1_cons)
        g1_cons = []
        ms = [P.op("dve", lambda e, dc=dc: e.memset(
            y1[:, dc * HW:(dc + 1) * HW], 0.0), deps=y_cons)
            for dc in range(NDC)]
        y_cons = []
        ldg_ref[0] = ldg
        ms_ref[0] = ms
        for en in range(NE):
            jw = en % 2
            lw1 = P.dma("pool", lambda e, en=en, t=w1[jw]: e.dma_start(
                out=t[:].rearrange("p (c i) -> p c i", i=D),
                in_=w1f_h[en].rearrange("(c p) i -> p c i", p=128)),
                deps=w1_cons[jw])
            lw2 = P.dma("pool", lambda e, en=en, t=w2[jw]: e.dma_start(
                out=t[:].rearrange("p (c d) -> p c d", d=D),
                in_=w2f_h[en].rearrange("(c p) d -> p c d", p=128)),
                deps=w2_cons[jw])
            w1_cons[jw], w2_cons[jw] = [], []
            for tcc in range(HW // TC):
                ci[0] += 1
                hsb = hsb2[ci[0] % 2]
                col0 = tcc * TC
                if en > 0:
                    lgr = P.dma("sync", lambda e, en=en, col0=col0: e.dma_start(
                        out=grow[:], in_=g1[en - 1:en,
                                            col0h + col0:col0h + col0 + TC]),
                        deps=gr_cons)
                    mgb = P.op("pe", lambda e: e.matmul(
                        pbc[:], ones1[:], grow[:], start=True, stop=True),
                        deps=[lgr, o1] + gb_cons)
                    egb = P.op("dve", lambda e: e.tensor_copy(gbc[:], pbc[:]),
                               deps=[mgb] + gb_cons)
                    gr_cons = [mgb]
                    gb_cons = [egb]
                hops = []
                for ic in range(8):
                    jj = ic % 2
                    def hfn(e, jw=jw, ic=ic, tcc=tcc, jj=jj):
                        last = None
                        for dc in range(NDC):
                            last = e.matmul(
                                ph_[jj][:],
                                w1[jw][:, dc * D + ic * 128:
                                       dc * D + (ic + 1) * 128],
                                xnh[:, dc * HW + tcc * TC:
                                    dc * HW + tcc * TC + TC],
                                start=(dc == 0), stop=(dc == NDC - 1))
                        return last
                    hm = P.op("pe", hfn, deps=[lw1, ldx] + ph_cons.get(jj, []))
                    ge = P.op("act", lambda e, en=en, ic=ic, jj=jj, hsb=hsb:
                              e.activation(hsb[:, ic * TC:(ic + 1) * TC],
                                           ph_[jj][:], AF.Gelu,
                                           bias=b1[:, en * NDC + ic:
                                                   en * NDC + ic + 1],
                                           scale=1.0),
                              deps=[hm, ldb1] + h_cons.get((ci[0] % 2, ic), []))
                    ph_cons[jj] = [ge]
                    if en > 0:
                        hc = P.op("pool", lambda e, ic=ic, hsb=hsb: e.tensor_mul(
                            hsb[:, ic * TC:(ic + 1) * TC],
                            hsb[:, ic * TC:(ic + 1) * TC], gbc[:]),
                            deps=[ge] + gb_cons)
                        gb_cons.append(hc)
                        hops.append(hc)
                    else:
                        hops.append(ge)
                if pend_down[0] is not None:
                    pend_down[0]()
                pend_down[0] = make_down(en, tcc, jw, hsb, list(hops),
                                         ci[0] % 2)
                w1_cons[jw].append(hops[-1])
        if pend_down[0] is not None:
            pend_down[0]()
            pend_down[0] = None
        xn_cons = y_cons[-1:]
        g1_cons = y_cons[-1:]
        for tcc in range(HW // TC):
            for dc in range(NDC):
                k = tcc * NDC + dc
                jj = k % 2
                colg = col0h + tcc * TC
                lds = P.dma("sync", lambda e, dc=dc, colg=colg, jj=jj:
                            e.dma_start(out=ssb[jj][:],
                                        in_=s1[dc * 128:(dc + 1) * 128,
                                               colg:colg + TC]),
                            deps=[prev_comb.get(("s", jj))])
                tadd = P.op("dve", lambda e, dc=dc, tcc=tcc, jj=jj:
                            e.tensor_add(
                                tsc[jj][:],
                                y1[:, dc * HW + tcc * TC:
                                   dc * HW + tcc * TC + TC],
                                xnh[:, dc * HW + tcc * TC:
                                    dc * HW + tcc * TC + TC]),
                            deps=y_cons + [prev_comb.get(("t", jj))])
                gg = P.op("act", lambda e, dc=dc, jj=jj: e.activation(
                    gsb[jj][:], tsc[jj][:], AF.Silu,
                    bias=bs2[:, dc:dc + 1], scale=1.0),
                    deps=[tadd, ldbs2, prev_comb.get(("g", jj))])
                pp = P.op("dve", lambda e, jj=jj: e.tensor_mul(
                    psb[jj][:], gsb[jj][:], ssb[jj][:]),
                    deps=[gg, lds, prev_comb.get(("st", jj))])
                st = P.dma("sync", lambda e, dc=dc, colg=colg, jj=jj:
                           e.dma_start(out=p1[dc * 128:(dc + 1) * 128,
                                              colg:colg + TC],
                                       in_=psb[jj][:]), deps=[pp])
                prev_comb[("t", jj)] = gg
                prev_comb[("g", jj)] = pp
                prev_comb[("s", jj)] = pp
                prev_comb[("st", jj)] = st
                y_cons.append(tadd)
                xn_cons.append(tadd)


def ph_b9(P, ctx):
    x1, p1, out = ctx.hbm["x1"], ctx.hbm["p1"], ctx.hbm["out"]
    xb = [ctx.sbuf(f"b9_x{j}", [128, D], F32) for j in range(2)]
    pb_ = [ctx.sbuf(f"b9_p{j}", [128, D], BF16) for j in range(2)]
    ob = [ctx.sbuf(f"b9_o{j}", [128, D], F32) for j in range(2)]
    prev = {}
    for i in range(NTT):
        j = i % 2
        pv = prev.get(j, {})
        ld = P.dma("sync", lambda e, i=i, t=xb[j]: e.dma_start(
            out=t[:], in_=x1[i * 128:(i + 1) * 128, :]), deps=[pv.get("add")])
        ldp = P.dma("sync", lambda e, i=i, t=pb_[j]: e.dma_start(
            out=t[:], in_=p1[:, i * 128:(i + 1) * 128], transpose=True),
            deps=[pv.get("add")])
        add = P.op("dve", lambda e, j=j: e.tensor_add(
            ob[j][:], xb[j][:], pb_[j][:]), deps=[ld, ldp, pv.get("st")])
        st = P.dma("pool", lambda e, i=i, j=j: e.dma_start(
            out=out[i * 128:(i + 1) * 128, :], in_=ob[j][:]), deps=[add])
        prev[j] = dict(add=add, st=st)


PHASES = {
    "b0": (ph_b0, ["x", "lnw", "lnb"], ["xn0", "h0"]),
    "b1": (ph_b1, ["xn0", "h0", "gw0T"], ["xn0T", "h0T", "cw0"]),
    "b2": (ph_b2, ["h0T", "wq", "wk", "wv", "wo", "theta"], ["s0"]),
    "b2dbg": (ph_b2, ["h0T", "wq", "wk", "wv", "wo", "theta"],
              ["s0", "dbg_q", "dbg_k", "dbg_v", "dbg_o", "dbg_C", "dbg_S"]),
    "b4": (ph_b4, ["xn0T", "cw0", "s0", "gu0", "dn0"], ["p0"]),
    "b4dbg": (ph_b4, ["xn0T", "cw0", "s0", "gu0", "dn0"],
              ["p0", "dbg_y", "dbg_h"]),
    "b5": (ph_b5, ["x", "p0"], ["x1", "xn1"]),
    "b6": (ph_b6, ["xn1", "wr1", "br1"], ["xn1T", "g1"]),
    "b7": (ph_b7, ["xn1T", "wq1", "wk1", "wv1", "bq1", "bk1", "bv1"],
           ["q1", "attn1", "s1"]),
    "b8": (ph_b8, ["xn1T", "g1", "s1", "w1f", "w2f", "b1f", "bs2", "be2"],
           ["p1"]),
    "b9": (ph_b9, ["x1", "p1"], ["out"]),
}


INTERNAL_WRITES = set()


def build_group(phase_names, feed_names):
    nc = bass.Bass("TRN2", target_bir_lowering=False)
    es = ExitStack()
    P = Prog()
    ctx = Ctx(nc, es, P)
    reads, writes = set(), set()
    for pn in phase_names:
        _, r, w = PHASES[pn]
        reads |= set(r)
        writes |= set(w)
    in_names = sorted((reads - writes) | (reads & feed_names))
    out_names = sorted(writes - INTERNAL_WRITES)
    for n in in_names:
        shape, dt = (WEIGHT_SPECS.get(n) or INTER_SPECS.get(n))
        ctx.hbm[n] = nc.dram_tensor(n, shape, dt, kind="ExternalInput").ap()
    for n in out_names:
        shape, dt = INTER_SPECS[n]
        ctx.hbm[n] = nc.dram_tensor(n, shape, dt, kind="ExternalOutput").ap()
    for n in sorted(writes & INTERNAL_WRITES):
        shape, dt = INTER_SPECS[n]
        ctx.hbm[n] = nc.dram_tensor(n, shape, dt, kind="Internal").ap()
    for pi, pn in enumerate(phase_names):
        fn, _, _ = PHASES[pn]
        ctx.ph = ExitStack()
        fn(P, ctx)
        ctx.ph.close()
        P.barrier()
    with nc.Block() as block:
        P.emit(nc, block, es)
    return nc, in_names, out_names


def run_groups(groups, shared, xs, keep=()):
    store = [dict(shared) for _ in range(NCORES)]
    for b in range(NCORES):
        store[b]["x"] = xs[b]
    for gi, gnames in enumerate(groups):
        nc, in_names, out_names = build_group(gnames, set(shared.keys()) | {"x"})
        in_maps = [{n: store[b][n] for n in in_names} for b in range(NCORES)]
        res = run_bass_kernel_spmd(nc, in_maps, core_ids=list(range(NCORES)))
        for b in range(NCORES):
            for n in out_names:
                store[b][n] = np.asarray(res.results[b][n])
    return store


ALL_PHASES = ["b0", "b1", "b2", "b4", "b5", "b6", "b7", "b8", "b9"]
FUSED = True
GROUPS = [ALL_PHASES] if FUSED else [[p] for p in ALL_PHASES]


def kernel(**inputs):
    global INTERNAL_WRITES
    if FUSED:
        INTERNAL_WRITES = set(INTER_SPECS) - {"out"}
    shared, xs = prep_host(inputs)
    store = run_groups(GROUPS, shared, xs)
    return np.stack([store[b]["out"] for b in range(NCORES)])
